# version 01
# baseline (speedup 1.0000x reference)
import numpy as np
import ml_dtypes
import concourse.bass as bass
import concourse.mybir as mybir
from concourse.bass_utils import run_bass_kernel_spmd

F32 = mybir.dt.float32
BF16 = mybir.dt.bfloat16
AF = mybir.ActivationFunctionType
ALU = mybir.AluOpType
BFH = ml_dtypes.bfloat16

B, T, M, G, H, D = 2, 4096, 2048, 4, 4, 128
HID = 8192
NT = T // 128
OTN = 8
MC = M // 128
QK2 = float(D) ** -0.5
SHIFT = 30.0
EPS = 1e-5
DEBUG = False

ENGINES = ("sync", "gpsimd", "tensor", "scalar", "vector")


class SemT:
    def __init__(self, nc, name):
        self.h = nc.alloc_semaphore(name)
        self.n = 0
        self.name = name

    def inc(self, k=1):
        self.n += k
        return self.n


class Op:
    __slots__ = ("eng", "fn", "waits", "incs", "tag")

    def __init__(self, eng, fn, waits, incs, tag):
        self.eng, self.fn, self.waits, self.incs, self.tag = eng, fn, waits, incs, tag


class Plan:
    def __init__(self, nc):
        self.nc = nc
        self.seq = []
        self.pid = {}
        self.sl1024 = {}

    def op(self, eng, fn=None, wait=(), inc=(), tag=""):
        self.seq.append(Op(eng, fn, [w for w in wait if w is not None], list(inc), tag))

    def barrier(self, sem):
        for e in ENGINES:
            self.op(e, None, inc=[(sem, 1)], tag="bar-inc")
            sem.inc(1)
        tgt = sem.n
        for e in ENGINES:
            self.op(e, None, wait=[(sem, tgt)], tag="bar-wait")

    def simulate(self):
        vals = {}
        qs = {e: [o for o in self.seq if o.eng == e] for e in ENGINES}
        ptr = {e: 0 for e in ENGINES}
        progress = True
        while progress:
            progress = False
            for e in ENGINES:
                while ptr[e] < len(qs[e]):
                    o = qs[e][ptr[e]]
                    if all(vals.get(s.name, 0) >= v for s, v in o.waits):
                        for s, k in o.incs:
                            vals[s.name] = vals.get(s.name, 0) + k
                        ptr[e] += 1
                        progress = True
                    else:
                        break
        stuck = {e: ptr[e] for e in ENGINES if ptr[e] < len(qs[e])}
        if stuck:
            msg = []
            for e, p in stuck.items():
                o = qs[e][p]
                why = [(s.name, v, vals.get(s.name, 0)) for s, v in o.waits
                       if vals.get(s.name, 0) < v]
                msg.append(f"{e}[{p}/{len(qs[e])}] tag={o.tag} blocked on {why}")
            raise RuntimeError("PLAN DEADLOCK:\n" + "\n".join(msg))

    def replay(self):
        nc = self.nc
        with nc.Block() as block:
            def run_engine(eng_obj, name):
                try:
                    self.pid[name] = eng_obj.partition_id()
                    self.sl1024[name] = self.pid[name] % 4 * 1024
                except Exception:
                    pass
                for o in self.seq:
                    if o.eng != name:
                        continue
                    for s, v in o.waits:
                        eng_obj.wait_ge(s.h, v)
                    if o.fn is not None:
                        inst = o.fn(eng_obj)
                        for s, k in o.incs:
                            inst = inst.then_inc(s.h, k)
                    else:
                        for s, k in o.incs:
                            eng_obj.sem_inc(s.h, k)

            @block.sync
            def _(e):
                run_engine(e, "sync")

            @block.gpsimd
            def _(e):
                run_engine(e, "gpsimd")

            @block.tensor
            def _(e):
                run_engine(e, "tensor")

            @block.scalar
            def _(e):
                run_engine(e, "scalar")

            @block.vector
            def _(e):
                run_engine(e, "vector")


class SbufMap:
    def __init__(self, nc, base, top):
        self.nc = nc
        self.base = base
        self.top = top
        self.items = []

    def add(self, name, shape, dtype, live):
        esz = 4 if dtype == F32 else 2
        bpp = int(np.prod(shape[1:])) * esz
        bpp = (bpp + 63) // 64 * 64
        self.items.append([name, shape, dtype, bpp, live])

    def finalize(self):
        placed = []
        out = {}
        for name, shape, dtype, bpp, live in sorted(self.items, key=lambda it: -it[3]):
            off = 0
            while True:
                clash = None
                for poff, psz, plive, pname in placed:
                    if not (live[1] < plive[0] or plive[1] < live[0]):
                        if not (off + bpp <= poff or poff + psz <= off):
                            clash = poff + psz
                            break
                if clash is None:
                    break
                off = (clash + 63) // 64 * 64
            assert self.base + off + bpp <= self.top, (
                f"SBUF overflow placing {name}: off={off} bpp={bpp}")
            placed.append((off, bpp, live, name))
            out[name] = self.nc.alloc_sbuf_tensor_at(
                name, list(shape), dtype, offset=self.base + off)
        return out


def build_program():
    nc = bass.Bass("TRN2", target_bir_lowering=False)

    x_full = nc.dram_tensor("x_full", [T, M], F32, kind="ExternalInput").ap()
    w_q = nc.dram_tensor("w_q", [H, M, D], F32, kind="ExternalInput").ap()
    w_k = nc.dram_tensor("w_k", [M, D], F32, kind="ExternalInput").ap()
    w_v = nc.dram_tensor("w_v", [M, D], F32, kind="ExternalInput").ap()
    w_o = nc.dram_tensor("w_o", [G * H, D, M], F32, kind="ExternalInput").ap()
    wg_t = nc.dram_tensor("wg_t", [M, HID], F32, kind="ExternalInput").ap()
    wu_t = nc.dram_tensor("wu_t", [M, HID], F32, kind="ExternalInput").ap()
    wd_t = nc.dram_tensor("wd_t", [HID, M], F32, kind="ExternalInput").ap()
    rope1 = nc.dram_tensor("rope1", [128, T], BF16, kind="ExternalInput").ap()
    rope2 = nc.dram_tensor("rope2", [128, T], BF16, kind="ExternalInput").ap()
    masks_i = nc.dram_tensor("masks", [4, 128, 512], BF16, kind="ExternalInput").ap()
    out_own = nc.dram_tensor("out_own", [1024, M], F32, kind="ExternalOutput").ap()

    h_dram = nc.dram_tensor("h_dram", [T, M], BF16)
    h2_dram = nc.dram_tensor("h2_dram", [1024, M], BF16)
    xmid_dram = nc.dram_tensor("xmid_dram", [1024, M], F32)
    oc_dram = nc.dram_tensor("oc_dram", [H * D, T], BF16)
    og_dram = nc.dram_tensor("og_dram", [G * H * D, T], BF16)
    wg_bf = nc.dram_tensor("wg_bf", [M, HID], BF16)
    wu_bf = nc.dram_tensor("wu_bf", [M, HID], BF16)
    wd_bf = nc.dram_tensor("wd_bf", [HID, M], BF16)

    dbg = {}
    if DEBUG:
        dbg["d_h"] = nc.dram_tensor("d_h", [T, M], BF16, kind="ExternalOutput").ap()
        dbg["d_q"] = nc.dram_tensor("d_q", [128, H * T], BF16, kind="ExternalOutput").ap()
        dbg["d_k"] = nc.dram_tensor("d_k", [128, T], BF16, kind="ExternalOutput").ap()
        dbg["d_v"] = nc.dram_tensor("d_v", [128, T], BF16, kind="ExternalOutput").ap()
        dbg["d_o"] = nc.dram_tensor("d_o", [128, H * T], BF16, kind="ExternalOutput").ap()
        dbg["d_xmid"] = nc.dram_tensor("d_xmid", [1024, M], F32, kind="ExternalOutput").ap()
        dbg["d_h2ft"] = nc.dram_tensor("d_h2ft", [128, MC * 1024], BF16, kind="ExternalOutput").ap()
        dbg["d_g"] = nc.dram_tensor("d_g", [128, 16 * 1024], BF16, kind="ExternalOutput").ap()
        dbg["d_r2"] = nc.dram_tensor("d_r2", [128, OTN * 2048], F32, kind="ExternalOutput").ap()

    smap = SbufMap(nc, (nc.sbuf_base + 63) // 64 * 64, nc.sbuf_top)
    smap.add("rope1_sb", [128, T], BF16, (0, 1))
    smap.add("rope2_sb", [128, T], BF16, (0, 1))
    smap.add("masks_sb", [128, 2048], BF16, (0, 2))
    smap.add("ones_bf", [128, 16], BF16, (0, 6))
    smap.add("ones_f", [1, 128], F32, (0, 6))
    smap.add("stats", [128, 64], F32, (0, 6))
    smap.add("recip_sb", [1, 512], F32, (2, 2))
    smap.add("bcast_sb", [128, 512], F32, (2, 2))
    smap.add("ptile", [128, 2048], BF16, (2, 2))
    smap.add("dump", [128, 2048], F32, (0, 6))
    smap.add("h_ft", [128, MC * 1024], BF16, (1, 1))
    smap.add("xt", [128, 2 * 2048], F32, (1, 1))
    smap.add("ht", [128, 2 * 2048], BF16, (1, 1))
    smap.add("qtmp", [128, 2 * 1024], BF16, (1, 1))
    smap.add("rtmp", [128, 2048], BF16, (1, 1))
    smap.add("rswp", [128, 1024], BF16, (1, 1))
    smap.add("wq_sb", [128, H * MC * D], BF16, (0, 1))
    smap.add("wk_sb", [128, MC * D], BF16, (0, 1))
    smap.add("wv_sb", [128, MC * D], BF16, (0, 1))
    smap.add("q_ft", [128, H * T], BF16, (1, 2))
    smap.add("k_ft", [128, T], BF16, (1, 2))
    smap.add("v_sb", [128, T], BF16, (1, 2))
    smap.add("o_ft", [128, H * T], BF16, (2, 3))
    smap.add("wo_sb", [128, G * H * M], BF16, (2, 4))
    smap.add("o16_sb", [128, G * H * 1024], BF16, (4, 4))
    smap.add("h2_ft", [128, MC * 1024], BF16, (4, 5))
    smap.add("xr", [128, 2 * 2048], F32, (4, 4))
    smap.add("ln_t", [128, 2 * 2048], F32, (4, 4))
    smap.add("xmid_t", [128, 2 * 2048], F32, (4, 4))
    smap.add("h2t", [128, 2 * 2048], BF16, (4, 4))
    smap.add("g_slab", [128, 16 * 1024], BF16, (5, 5))
    smap.add("r2", [128, OTN * 2048], F32, (5, 6))
    smap.add("wgu", [128, 2 * MC * 256], BF16, (5, 5))
    smap.add("wuu", [128, 2 * MC * 256], BF16, (5, 5))
    smap.add("wdu", [128, 8 * 2048], BF16, (5, 5))
    smap.add("silu_t", [128, 1024], F32, (5, 5))
    smap.add("xr6", [128, 2 * 2048], F32, (6, 6))
    smap.add("ln6", [128, 2 * 2048], F32, (6, 6))
    smap.add("xo6", [128, 2 * 2048], F32, (6, 6))
    sb = smap.finalize()

    P = [nc.alloc_psum_tensor(f"ps{i}", [128, 1024], F32) for i in range(4)]

    pl = Plan(nc)
    sCst = SemT(nc, "sCst")
    sXp = [SemT(nc, "sXa"), SemT(nc, "sXb")]
    sHp = [SemT(nc, "sHa"), SemT(nc, "sHb")]
    sTT = SemT(nc, "sTT")
    sWq = SemT(nc, "sWq")
    sWo = SemT(nc, "sWo")
    sWc = SemT(nc, "sWc")
    sFg = [SemT(nc, "sFga"), SemT(nc, "sFgb")]
    sFu = [SemT(nc, "sFua"), SemT(nc, "sFub")]
    sFd = SemT(nc, "sFd")
    sPE = SemT(nc, "sPE")
    sSE = SemT(nc, "sSE")
    sVE = SemT(nc, "sVE")
    sCC = SemT(nc, "sCC")
    sC1 = SemT(nc, "sC1")
    sG = SemT(nc, "sG")
    sXRp = [SemT(nc, "sXRa"), SemT(nc, "sXRb")]
    sXMp = [SemT(nc, "sXMa"), SemT(nc, "sXMb")]
    sSTp = [SemT(nc, "sSTa"), SemT(nc, "sSTb")]
    sB = SemT(nc, "sB")

    rope1_sb, rope2_sb = sb["rope1_sb"], sb["rope2_sb"]
    masks_sb, stats, dump = sb["masks_sb"], sb["stats"], sb["dump"]
    h_ft, q_ft, k_ft, v_sb, o_ft = (sb["h_ft"], sb["q_ft"], sb["k_ft"],
                                    sb["v_sb"], sb["o_ft"])

    pl.op("vector", lambda e: e.memset(sb["ones_bf"][:, :], 1.0))
    pl.op("vector", lambda e: e.memset(sb["ones_f"][:, :], 1.0), inc=[(sVE, 1)])
    ones_ready = sVE.inc()
    pl.op("sync", lambda e: e.dma_start(out=rope1_sb[:, :], in_=rope1), inc=[(sCst, 16)])
    sCst.inc(16)
    pl.op("sync", lambda e: e.dma_start(out=rope2_sb[:, :], in_=rope2), inc=[(sCst, 16)])
    sCst.inc(16)
    pl.op("sync",
          lambda e: e.dma_start(out=masks_sb[:, :].rearrange("p (r i) -> p r i", r=4),
                                in_=masks_i.rearrange("r p i -> p r i")),
          inc=[(sCst, 16)])
    rope_ready = sCst.inc(16)

    pl.op("gpsimd",
          lambda e: e.dma_start(out=sb["wk_sb"][:, :].rearrange("p (mc d) -> p mc d", mc=MC),
                                in_=w_k.rearrange("(mc p) d -> p mc d", p=128)),
          inc=[(sWq, 16)], tag="wk")
    sWq.inc(16)
    pl.op("gpsimd",
          lambda e: e.dma_start(out=sb["wv_sb"][:, :].rearrange("p (mc d) -> p mc d", mc=MC),
                                in_=w_v.rearrange("(mc p) d -> p mc d", p=128)),
          inc=[(sWq, 16)], tag="wv")
    sWq.inc(16)
    for hd in range(H):
        pl.op("gpsimd",
              lambda e, hd=hd: e.dma_start(
                  out=sb["wq_sb"][:, hd * MC * D:(hd + 1) * MC * D].rearrange(
                      "p (mc d) -> p mc d", mc=MC),
                  in_=w_q[hd, :, :].rearrange("(mc p) d -> p mc d", p=128)),
              inc=[(sWq, 16)], tag=f"wq{hd}")
        sWq.inc(16)
    wqkv_ready = sWq.n
    for nm, srcw, dstw in (("wg", wg_t, wg_bf), ("wu", wu_t, wu_bf), ("wd", wd_t, wd_bf)):
        pl.op("gpsimd", lambda e, srcw=srcw, dstw=dstw: e.dma_start(out=dstw.ap(), in_=srcw),
              inc=[(sWc, 16)], tag=f"cast-{nm}")
        sWc.inc(16)
    ffn_cast_ready = sWc.n

    xt, ht = sb["xt"], sb["ht"]
    ln_done = [0] * NT
    h_mark = [0] * NT

    def emit_ln_tile(ti):
        slot = ti % 2
        xa = xt[:, slot * 2048:(slot + 1) * 2048]
        ha = ht[:, slot * 2048:(slot + 1) * 2048]
        w = [(sVE, ln_done[ti - 2])] if ti >= 2 else []
        pl.op("sync", lambda e, ti=ti, xa=xa: e.dma_start(
            out=xa, in_=x_full[ti * 128:(ti + 1) * 128, :]),
            wait=w, inc=[(sXp[slot], 16)], tag=f"x{ti}")
        xv = sXp[slot].inc(16)
        sl = slot * 8
        sx = stats[:, sl:sl + 1]
        sxx = stats[:, sl + 1:sl + 2]
        v0 = stats[:, sl + 2:sl + 3]
        negm = stats[:, sl + 3:sl + 4]
        rstd = stats[:, sl + 4:sl + 5]
        pl.op("scalar", lambda e, xa=xa, sx=sx: e.activation(
            dump[:, :], xa, AF.Copy, accum_out=sx),
            wait=[(sXp[slot], xv)], tag=f"ln{ti}-sx")
        pl.op("scalar", lambda e, xa=xa, sxx=sxx: e.activation(
            dump[:, :], xa, AF.Square, accum_out=sxx), inc=[(sSE, 1)], tag=f"ln{ti}-sxx")
        m_se = sSE.inc()
        pl.op("vector", lambda e, sx=sx, v0=v0: e.tensor_mul(v0, sx, sx),
              wait=[(sSE, m_se)])
        pl.op("vector", lambda e, v0=v0, sxx=sxx: e.scalar_tensor_tensor(
            v0, v0, -1.0 / 2048, sxx, ALU.mult, ALU.add))
        pl.op("vector", lambda e, sx=sx, negm=negm: e.tensor_scalar(
            negm, sx, -1.0 / 2048, None, ALU.mult), inc=[(sVE, 1)], tag=f"ln{ti}-v")
        m_ve = sVE.inc()
        pl.op("scalar", lambda e, v0=v0: e.activation(v0, v0, AF.Ln,
                                                      scale=np.float32(1.0 / 2048), bias=np.float32(EPS)),
              wait=[(sVE, m_ve)])
        pl.op("scalar", lambda e, v0=v0, rstd=rstd: e.activation(
            rstd, v0, AF.Exp, scale=np.float32(-0.5)), inc=[(sSE, 1)], tag=f"ln{ti}-r")
        m_se2 = sSE.inc()
        wn = [(sSE, m_se2)]
        if ti >= 2 and h_mark[ti - 2]:
            wn.append((sHp[slot], h_mark[ti - 2]))
        pl.op("vector", lambda e, ha=ha, xa=xa, negm=negm, rstd=rstd: e.tensor_scalar(
            ha, xa, negm, rstd, ALU.add, ALU.mult),
            wait=wn, inc=[(sVE, 1)], tag=f"ln{ti}-n")
        ln_done[ti] = sVE.inc()
        pl.op("sync", lambda e, ti=ti, ha=ha: e.dma_start(
            out=h_dram.ap()[ti * 128:(ti + 1) * 128, :], in_=ha),
            wait=[(sVE, ln_done[ti])], inc=[(sHp[slot], 16)], tag=f"h{ti}")
        h_mark[ti] = sHp[slot].inc(16)

    qkv_state = {"unit": 0, "vunit": 0, "sse_unit": [0, 0], "sve_unit": [0, 0],
                 "sse_v": [0, 0]}
    trans_done = [0] * 4

    def emit_qkv_quarter(qu):
        tbase = qu * 1024
        t_ready = sTT.n
        for u in range(5):
            ps = P[qkv_state["unit"] % 2]
            w = [(sTT, t_ready)]
            w.append((sWq, wqkv_ready))
            prev = qkv_state["sse_unit"][qkv_state["unit"] % 2]
            if prev:
                w.append((sSE, prev))
            for mc in range(MC):
                if u < 4:
                    lhsT = sb["wq_sb"][:, (u * MC + mc) * D:(u * MC + mc + 1) * D]
                else:
                    lhsT = sb["wk_sb"][:, mc * D:(mc + 1) * D]
                pl.op("tensor", lambda e, lhsT=lhsT: e.ldweights(lhsT),
                      wait=(w if mc == 0 else []), tag=f"qkvL{qu}.{u}.{mc}")
                for half in range(2):
                    pl.op("tensor", lambda e, lhsT=lhsT, half=half, mc=mc, ps=ps: e.matmul(
                        ps[:, half * 512:(half + 1) * 512], lhsT,
                        h_ft[:, mc * 1024 + half * 512:mc * 1024 + (half + 1) * 512],
                        start=(mc == 0), stop=(mc == MC - 1)),
                        inc=([(sPE, 1)] if (mc == MC - 1 and half == 1) else []),
                        tag=f"qkvM{qu}.{u}")
            m_pe = sPE.inc()
            qslot = qkv_state["unit"] % 2
            qa = sb["qtmp"][:, qslot * 1024:(qslot + 1) * 1024]
            wse = [(sPE, m_pe)]
            prev_ve = qkv_state["sve_unit"][qslot]
            if prev_ve:
                wse.append((sVE, prev_ve))
            pl.op("scalar", lambda e, qa=qa, ps=ps: e.copy(qa, ps[:, :]),
                  wait=wse, inc=[(sSE, 1)], tag=f"qkvE{qu}.{u}")
            qkv_state["sse_unit"][qslot] = sSE.inc()
            if u < 4:
                dst = q_ft[:, u * T + tbase:u * T + tbase + 1024]
            else:
                dst = k_ft[:, tbase:tbase + 1024]
            cs1 = rope1_sb[:, tbase:tbase + 1024]
            cs2 = rope2_sb[:, tbase:tbase + 1024]
            rt1 = sb["rtmp"][:, 0:1024]
            rt2 = sb["rtmp"][:, 1024:2048]
            sw = sb["rswp"]
            m_se = qkv_state["sse_unit"][qslot]
            pl.op("vector", lambda e, qa=qa, cs1=cs1, rt1=rt1: e.tensor_mul(rt1, qa, cs1),
                  wait=[(sSE, m_se), (sCst, rope_ready)], tag=f"rp{qu}.{u}")
            pl.op("vector", lambda e, qa=qa, cs2=cs2, rt2=rt2: e.tensor_mul(rt2, qa, cs2))
            pl.op("vector", lambda e, rt1=rt1, sw=sw: e.tensor_copy(
                sw[0:64, :], rt1[64:128, :]))
            pl.op("vector", lambda e, rt2=rt2, sw=sw: e.tensor_copy(
                sw[64:128, :], rt2[0:64, :]))
            pl.op("vector", lambda e, dst=dst, rt1=rt1, sw=sw: e.tensor_sub(
                dst[0:64, :], rt1[0:64, :], sw[0:64, :]))
            pl.op("vector", lambda e, dst=dst, rt2=rt2, sw=sw: e.tensor_add(
                dst[64:128, :], rt2[64:128, :], sw[64:128, :]),
                inc=[(sVE, 1)], tag=f"rpd{qu}.{u}")
            qkv_state["sve_unit"][qslot] = sVE.inc()
            qkv_state["unit"] += 1
        for tt in range(8):
            vp = P[2 + qkv_state["vunit"] % 2]
            w = [(sTT, t_ready)]
            prev = qkv_state["sse_v"][qkv_state["vunit"] % 2]
            if prev:
                w.append((sSE, prev))
            for mc in range(MC):
                lhsT = h_ft[:, mc * 1024 + tt * 128:mc * 1024 + (tt + 1) * 128]
                pl.op("tensor", lambda e, lhsT=lhsT: e.ldweights(lhsT),
                      wait=(w if mc == 0 else []), tag=f"vL{qu}.{tt}")
                pl.op("tensor", lambda e, lhsT=lhsT, mc=mc, vp=vp: e.matmul(
                    vp[:, 0:128], lhsT, sb["wv_sb"][:, mc * D:(mc + 1) * D],
                    start=(mc == 0), stop=(mc == MC - 1)),
                    inc=([(sPE, 1)] if mc == MC - 1 else []), tag=f"vM{qu}.{tt}")
            m_pe = sPE.inc()
            gt = qu * 8 + tt
            pl.op("scalar", lambda e, gt=gt, vp=vp: e.copy(
                v_sb[:, gt * 128:(gt + 1) * 128], vp[:, 0:128]),
                wait=[(sPE, m_pe)], inc=[(sSE, 1)], tag=f"vE{qu}.{tt}")
            qkv_state["sse_v"][qkv_state["vunit"] % 2] = sSE.inc()
            qkv_state["vunit"] += 1
        trans_done[qu] = sPE.n

    def emit_transposes(qu):
        for mc in range(MC):
            w = []
            if mc == 0:
                w.append((sHp[0], sHp[0].n))
                w.append((sHp[1], sHp[1].n))
                if qu >= 1:
                    w.append((sPE, trans_done[qu - 1]))
            pl.op("sync", lambda e, qu=qu, mc=mc: e.dma_start(
                out=h_ft[:, mc * 1024:(mc + 1) * 1024],
                in_=h_dram.ap()[qu * 1024:(qu + 1) * 1024, mc * 128:(mc + 1) * 128],
                transpose=True),
                wait=w, inc=[(sTT, 16)], tag=f"T{qu}.{mc}")
            sTT.inc(16)

    for ti in range(8):
        emit_ln_tile(ti)
    emit_transposes(0)
    for ti in range(8, 16):
        emit_ln_tile(ti)
    emit_qkv_quarter(0)
    emit_transposes(1)
    for ti in range(16, 24):
        emit_ln_tile(ti)
    emit_qkv_quarter(1)
    emit_transposes(2)
    for ti in range(24, 32):
        emit_ln_tile(ti)
    emit_qkv_quarter(2)
    emit_transposes(3)
    emit_qkv_quarter(3)

    if DEBUG:
        pl.op("sync", lambda e: e.dma_start(out=dbg["d_h"], in_=h_dram.ap()),
              wait=[(sHp[0], sHp[0].n), (sHp[1], sHp[1].n)], inc=[(sSTp[0], 16)])
        sSTp[0].inc(16)

    pl.barrier(sB)
    for gg in range(G):
        pl.op("gpsimd",
              lambda e, gg=gg: e.dma_start(
                  out=sb["wo_sb"][:, gg * H * M:(gg + 1) * H * M].rearrange(
                      "p (h m) -> p h m", h=H),
                  in_=w_o[gg * H:(gg + 1) * H, :, :].rearrange("h d m -> d h m")),
              inc=[(sWo, 16)], tag=f"wo{gg}")
        sWo.inc(16)
    wo_ready = sWo.n
    ptile = sb["ptile"]
    att = {"jglob": 0, "norm_done": [0, 0], "recip_done": [0, 0], "bcast_copy": 0,
           "exp_mark": {}, "pv_done": {}}

    if DEBUG:
        pl.op("sync", lambda e: e.dma_start(out=dbg["d_q"], in_=q_ft[:, :]),
              inc=[(sSTp[0], 16)])
        sSTp[0].inc(16)
        pl.op("sync", lambda e: e.dma_start(out=dbg["d_k"], in_=k_ft[:, :]),
              inc=[(sSTp[0], 16)])
        sSTp[0].inc(16)
        pl.op("sync", lambda e: e.dma_start(out=dbg["d_v"], in_=v_sb[:, :]),
              inc=[(sSTp[0], 16)])
        sSTp[0].inc(16)

    unit = 0
    for hd in range(H):
        for qc in range(8):
            jmax = 4 * (qc + 1)
            q0 = 512 * qc
            opp = unit % 2
            spp = unit % 2
            for j in range(jmax):
                jg = att["jglob"] + j
                sp = jg % 2
                ring = jg % 4
                w = []
                pe_prev = att["exp_mark"].get(jg - 2)
                if pe_prev:
                    w.append(pe_prev)
                pl.op("tensor", lambda e, j=j: e.ldweights(k_ft[:, j * 128:(j + 1) * 128]),
                      wait=w, tag=f"aK{unit}.{j}")
                pl.op("tensor", lambda e, sp=sp, j=j, hd=hd, q0=q0: e.matmul(
                    P[0][:, sp * 512:(sp + 1) * 512],
                    k_ft[:, j * 128:(j + 1) * 128],
                    q_ft[:, hd * T + q0:hd * T + q0 + 512],
                    start=True, stop=True),
                    inc=[(sPE, 1)], tag=f"aS{unit}.{j}")
                m_s = sPE.inc()
                wse = [(sPE, m_s)]
                pvd = att["pv_done"].get(jg - 4)
                if pvd:
                    wse.append((sPE, pvd))
                pl.op("scalar", lambda e, ring=ring, sp=sp: e.activation(
                    ptile[:, ring * 512:(ring + 1) * 512],
                    P[0][:, sp * 512:(sp + 1) * 512],
                    AF.Exp, scale=np.float32(QK2), bias=np.float32(-SHIFT)),
                    wait=wse, inc=[(sSE, 1)], tag=f"aE{unit}.{j}")
                m_e = sSE.inc()
                if j >= jmax - 4:
                    r = j - (jmax - 4)
                    pl.op("vector", lambda e, ring=ring, r=r: e.tensor_mul(
                        ptile[:, ring * 512:(ring + 1) * 512],
                        ptile[:, ring * 512:(ring + 1) * 512],
                        masks_sb[:, r * 512:(r + 1) * 512]),
                        wait=[(sSE, m_e)], inc=[(sVE, 1)], tag=f"aM{unit}.{j}")
                    p_ready = (sVE, sVE.inc())
                    att["exp_mark"][jg] = p_ready
                else:
                    p_ready = (sSE, m_e)
                    att["exp_mark"][jg] = p_ready
                w2 = [p_ready]
                if j == 0:
                    if att["recip_done"][spp]:
                        w2.append((sVE, att["recip_done"][spp]))
                    if att["norm_done"][opp]:
                        w2.append((sVE, att["norm_done"][opp]))
                pl.op("tensor", lambda e: e.ldweights(sb["ones_bf"][:, 0:1]),
                      wait=w2, tag=f"aO{unit}.{j}")
                pl.op("tensor", lambda e, spp=spp, ring=ring, j=j, jmax=jmax: e.matmul(
                    P[2][0:1, spp * 512:(spp + 1) * 512],
                    sb["ones_bf"][:, 0:1],
                    ptile[:, ring * 512:(ring + 1) * 512],
                    start=(j == 0), stop=(j == jmax - 1)),
                    tag=f"aU{unit}.{j}")
                pl.op("tensor", lambda e, j=j: e.ldweights(v_sb[:, j * 128:(j + 1) * 128]),
                      tag=f"aV{unit}.{j}")
                pl.op("tensor", lambda e, opp=opp, ring=ring, j=j, jmax=jmax: e.matmul(
                    P[1][:, opp * 512:(opp + 1) * 512],
                    v_sb[:, j * 128:(j + 1) * 128],
                    ptile[:, ring * 512:(ring + 1) * 512],
                    start=(j == 0), stop=(j == jmax - 1)),
                    inc=[(sPE, 1)], tag=f"aP{unit}.{j}")
                att["pv_done"][jg] = sPE.inc()
            m_pe_unit = att["pv_done"][att["jglob"] + jmax - 1]
            pl.op("vector", lambda e, spp=spp: e.reciprocal(
                sb["recip_sb"][0:1, :], P[2][0:1, spp * 512:(spp + 1) * 512]),
                wait=[(sPE, m_pe_unit)], inc=[(sVE, 1)], tag=f"aR{unit}")
            att["recip_done"][spp] = sVE.inc()
            wb = [(sVE, att["recip_done"][spp])]
            if att["bcast_copy"]:
                wb.append((sSE, att["bcast_copy"]))
            pl.op("tensor", lambda e: e.matmul(
                P[3][:, 0:512], sb["ones_f"][0:1, :], sb["recip_sb"][0:1, :],
                start=True, stop=True),
                wait=wb, inc=[(sPE, 1)], tag=f"aB{unit}")
            m_b = sPE.inc()
            pl.op("scalar", lambda e: e.copy(sb["bcast_sb"][:, :], P[3][:, 0:512]),
                  wait=[(sPE, m_b)], inc=[(sSE, 1)], tag=f"aC{unit}")
            att["bcast_copy"] = sSE.inc()
            pl.op("vector", lambda e, hd=hd, q0=q0, opp=opp: e.tensor_mul(
                o_ft[:, hd * T + q0:hd * T + q0 + 512],
                P[1][:, opp * 512:(opp + 1) * 512],
                sb["bcast_sb"][:, :]),
                wait=[(sSE, att["bcast_copy"])], inc=[(sVE, 1)], tag=f"aN{unit}")
            att["norm_done"][opp] = sVE.inc()
            att["jglob"] += jmax
            unit += 1
    o_done = sVE.n

    if DEBUG:
        pl.op("sync", lambda e: e.dma_start(out=dbg["d_o"], in_=o_ft[:, :]),
              wait=[(sVE, o_done)], inc=[(sSTp[0], 16)])
        sSTp[0].inc(16)

    for hd in range(H):
        pl.op("sync", lambda e, hd=hd: e.dma_start(
            out=oc_dram.ap()[hd * D:(hd + 1) * D, :],
            in_=o_ft[:, hd * T:(hd + 1) * T]),
            wait=([(sVE, o_done)] if hd == 0 else []), inc=[(sC1, 16)], tag=f"oc{hd}")
        sC1.inc(16)
    pl.op("gpsimd", lambda e: e.collective_compute(
        "AllGather", ALU.bypass,
        replica_groups=[[0, 1, 2, 3], [4, 5, 6, 7]],
        ins=[oc_dram.ap()], outs=[og_dram.ap()]),
        wait=[(sC1, sC1.n)], inc=[(sCC, 1)], tag="allgather")
    sCC.inc(1)

    pl.barrier(sB)

    pl.op("sync", lambda e: e.dma_start(
        out=sb["o16_sb"][:, :].rearrange("p (gh t) -> p gh t", gh=G * H),
        in_=og_dram.ap().rearrange("(gh d) t -> d gh t", d=128)[
            :, :, bass.ds(pl.sl1024["sync"], 1024)]),
        wait=[(sCC, 1)], inc=[(sG, 16)], tag="gather")
    sG.inc(16)

    p4 = {"psum_free": [0, 0], "h2dma": [0, 0], "xmdma": [0, 0], "xr_used": [0, 0]}
    for ot in range(OTN):
        pp = ot % 2
        PA, PB = (P[0], P[1]) if pp == 0 else (P[2], P[3])
        wxr = [(sVE, p4["xr_used"][pp])] if p4["xr_used"][pp] else []
        pl.op("sync", lambda e, ot=ot, pp=pp: e.dma_start(
            out=sb["xr"][:, pp * 2048:(pp + 1) * 2048],
            in_=x_full[bass.ds(pl.sl1024["sync"] + ot * 128, 128), :]),
            wait=wxr, inc=[(sXRp[pp], 16)], tag=f"xr{ot}")
        m_xr = sXRp[pp].inc(16)
        w = [(sG, 16), (sWo, wo_ready)]
        if p4["psum_free"][pp]:
            w.append((sVE, p4["psum_free"][pp]))
        for gh in range(G * H):
            lhsT = sb["o16_sb"][:, gh * 1024 + ot * 128:gh * 1024 + (ot + 1) * 128]
            pl.op("tensor", lambda e, lhsT=lhsT: e.ldweights(lhsT),
                  wait=(w if gh == 0 else []), tag=f"oL{ot}.{gh}")
            for k in range(4):
                tgt = PA if k < 2 else PB
                pl.op("tensor", lambda e, lhsT=lhsT, gh=gh, k=k, tgt=tgt: e.matmul(
                    tgt[:, (k % 2) * 512:(k % 2 + 1) * 512], lhsT,
                    sb["wo_sb"][:, gh * M + k * 512:gh * M + (k + 1) * 512],
                    start=(gh == 0), stop=(gh == G * H - 1)),
                    inc=([(sPE, 1)] if (gh == G * H - 1 and k == 3) else []),
                    tag=f"oM{ot}.{gh}.{k}")
        m_pe = sPE.inc()
        sl = pp * 16
        sxa, sxb = stats[:, sl:sl + 1], stats[:, sl + 1:sl + 2]
        sxxa, sxxb = stats[:, sl + 2:sl + 3], stats[:, sl + 3:sl + 4]
        v0 = stats[:, sl + 4:sl + 5]
        negm = stats[:, sl + 5:sl + 6]
        rstd = stats[:, sl + 6:sl + 7]
        pl.op("scalar", lambda e, PA=PA, sxa=sxa: e.activation(
            dump[:, 0:1024], PA[:, :], AF.Copy, accum_out=sxa),
            wait=[(sPE, m_pe)], tag=f"rA{ot}")
        pl.op("scalar", lambda e, PB=PB, sxb=sxb: e.activation(
            dump[:, 0:1024], PB[:, :], AF.Copy, accum_out=sxb))
        pl.op("scalar", lambda e, PA=PA, sxxa=sxxa: e.activation(
            dump[:, 0:1024], PA[:, :], AF.Square, accum_out=sxxa))
        pl.op("scalar", lambda e, PB=PB, sxxb=sxxb: e.activation(
            dump[:, 0:1024], PB[:, :], AF.Square, accum_out=sxxb),
            inc=[(sSE, 1)], tag=f"rS{ot}")
        m_se = sSE.inc()
        pl.op("vector", lambda e, sxa=sxa, sxb=sxb: e.tensor_add(sxa, sxa, sxb),
              wait=[(sSE, m_se)])
        pl.op("vector", lambda e, sxxa=sxxa, sxxb=sxxb: e.tensor_add(sxxa, sxxa, sxxb))
        pl.op("vector", lambda e, sxa=sxa, v0=v0: e.tensor_mul(v0, sxa, sxa))
        pl.op("vector", lambda e, v0=v0, sxxa=sxxa: e.scalar_tensor_tensor(
            v0, v0, -1.0 / 2048, sxxa, ALU.mult, ALU.add))
        pl.op("vector", lambda e, sxa=sxa, negm=negm: e.tensor_scalar(
            negm, sxa, -1.0 / 2048, None, ALU.mult), inc=[(sVE, 1)], tag=f"rV{ot}")
        m_ve = sVE.inc()
        pl.op("scalar", lambda e, v0=v0: e.activation(v0, v0, AF.Ln,
                                                      scale=np.float32(1.0 / 2048), bias=np.float32(EPS)),
              wait=[(sVE, m_ve)])
        pl.op("scalar", lambda e, v0=v0, rstd=rstd: e.activation(
            rstd, v0, AF.Exp, scale=np.float32(-0.5)), inc=[(sSE, 1)], tag=f"rR{ot}")
        m_se2 = sSE.inc()
        lnta = sb["ln_t"][:, pp * 2048:pp * 2048 + 1024]
        lntb = sb["ln_t"][:, pp * 2048 + 1024:(pp + 1) * 2048]
        pl.op("vector", lambda e, PA=PA, lnta=lnta, negm=negm, rstd=rstd: e.tensor_scalar(
            lnta, PA[:, :], negm, rstd, ALU.add, ALU.mult),
            wait=[(sSE, m_se2)], tag=f"rlA{ot}")
        pl.op("vector", lambda e, PB=PB, lntb=lntb, negm=negm, rstd=rstd: e.tensor_scalar(
            lntb, PB[:, :], negm, rstd, ALU.add, ALU.mult),
            inc=[(sVE, 1)], tag=f"rlB{ot}")
        p4["psum_free"][pp] = sVE.inc()
        sx2 = stats[:, sl + 7:sl + 8]
        xma = sb["xmid_t"][:, pp * 2048:(pp + 1) * 2048]
        lnt = sb["ln_t"][:, pp * 2048:(pp + 1) * 2048]
        xra = sb["xr"][:, pp * 2048:(pp + 1) * 2048]
        wxm = [(sXRp[pp], m_xr)]
        if p4["xmdma"][pp]:
            wxm.append((sXMp[pp], p4["xmdma"][pp]))
        pl.op("vector", lambda e, xma=xma, lnt=lnt, xra=xra, sx2=sx2:
              e.scalar_tensor_tensor(xma, lnt, 1.0, xra, ALU.mult, ALU.add,
                                     accum_out=sx2),
              wait=wxm, inc=[(sVE, 1)], tag=f"xm{ot}")
        p4["xr_used"][pp] = sVE.inc()
        m_xmid = p4["xr_used"][pp]
        sxx2 = stats[:, sl + 8:sl + 9]
        v02 = stats[:, sl + 9:sl + 10]
        negm2 = stats[:, sl + 10:sl + 11]
        rstd2 = stats[:, sl + 11:sl + 12]
        pl.op("scalar", lambda e, xma=xma, sxx2=sxx2: e.activation(
            dump[:, :], xma, AF.Square, accum_out=sxx2),
            wait=[(sVE, m_xmid)], inc=[(sSE, 1)], tag=f"l2S{ot}")
        m_se3 = sSE.inc()
        pl.op("vector", lambda e, sx2=sx2, v02=v02: e.tensor_mul(v02, sx2, sx2),
              wait=[(sSE, m_se3)])
        pl.op("vector", lambda e, v02=v02, sxx2=sxx2: e.scalar_tensor_tensor(
            v02, v02, -1.0 / 2048, sxx2, ALU.mult, ALU.add))
        pl.op("vector", lambda e, sx2=sx2, negm2=negm2: e.tensor_scalar(
            negm2, sx2, -1.0 / 2048, None, ALU.mult), inc=[(sVE, 1)], tag=f"l2V{ot}")
        m_ve2 = sVE.inc()
        pl.op("scalar", lambda e, v02=v02: e.activation(v02, v02, AF.Ln,
                                                        scale=np.float32(1.0 / 2048), bias=np.float32(EPS)),
              wait=[(sVE, m_ve2)])
        pl.op("scalar", lambda e, v02=v02, rstd2=rstd2: e.activation(
            rstd2, v02, AF.Exp, scale=np.float32(-0.5)), inc=[(sSE, 1)], tag=f"l2R{ot}")
        m_se4 = sSE.inc()
        h2a = sb["h2t"][:, pp * 2048:(pp + 1) * 2048]
        wh2 = [(sSE, m_se4)]
        if p4["h2dma"][pp]:
            wh2.append((sXMp[pp], p4["h2dma"][pp]))
        pl.op("vector", lambda e, h2a=h2a, xma=xma, negm2=negm2, rstd2=rstd2:
              e.tensor_scalar(h2a, xma, negm2, rstd2, ALU.add, ALU.mult),
              wait=wh2, inc=[(sVE, 1)], tag=f"h2{ot}")
        m_h2 = sVE.inc()
        pl.op("sync", lambda e, ot=ot, xma=xma: e.dma_start(
            out=xmid_dram.ap()[ot * 128:(ot + 1) * 128, :], in_=xma),
            wait=[(sVE, m_h2)], inc=[(sXMp[pp], 16)], tag=f"sxm{ot}")
        p4["xmdma"][pp] = sXMp[pp].inc(16)
        pl.op("sync", lambda e, ot=ot, h2a=h2a: e.dma_start(
            out=h2_dram.ap()[ot * 128:(ot + 1) * 128, :], in_=h2a),
            inc=[(sXMp[pp], 16)], tag=f"sh2{ot}")
        p4["h2dma"][pp] = sXMp[pp].inc(16)
    xm_all_w = [(sXMp[0], sXMp[0].n), (sXMp[1], sXMp[1].n)]

    if DEBUG:
        pl.op("sync", lambda e: e.dma_start(out=dbg["d_xmid"], in_=xmid_dram.ap()),
              wait=xm_all_w, inc=[(sSTp[0], 16)])
        sSTp[0].inc(16)

    for mc in range(MC):
        pl.op("sync", lambda e, mc=mc: e.dma_start(
            out=sb["h2_ft"][:, mc * 1024:(mc + 1) * 1024],
            in_=h2_dram.ap()[:, mc * 128:(mc + 1) * 128], transpose=True),
            wait=(xm_all_w if mc == 0 else []), inc=[(sTT, 16)], tag=f"T2.{mc}")
        sTT.inc(16)
    h2ft_ready = sTT.n

    if DEBUG:
        pl.op("sync", lambda e: e.dma_start(out=dbg["d_h2ft"], in_=sb["h2_ft"][:, :]),
              wait=[(sTT, h2ft_ready)], inc=[(sSTp[0], 16)])
        sSTp[0].inc(16)

    pl.barrier(sB)
    ffn = {"gu_pe": {}, "gu_used": [0, 0], "silu_prev": 0,
           "r2_evac": [0, 0], "down_pe_all": 0}
    load_marks = {}

    def load_gu(which, slab, u):
        gidx = slab * 8 + u
        slot = gidx % 2
        h0 = slab * 2048 + u * 256
        dst = sb["wgu"] if which == "wg" else sb["wuu"]
        srcw = wg_bf if which == "wg" else wu_bf
        sem = sFg[slot] if which == "wg" else sFu[slot]
        wk_ = []
        if gidx >= 2 and (gidx - 2) in ffn["gu_pe"]:
            wk_.append((sPE, ffn["gu_pe"][gidx - 2]))
        if gidx == 0:
            wk_.append((sWc, ffn_cast_ready))
            wk_.append((sTT, h2ft_ready))
        pl.op("sync", lambda e, dst=dst, srcw=srcw, h0=h0, slot=slot: e.dma_start(
            out=dst[:, slot * MC * 256:(slot + 1) * MC * 256].rearrange(
                "p (mc h) -> p mc h", mc=MC),
            in_=srcw.ap()[:, h0:h0 + 256].rearrange("(mc p) h -> p mc h", p=128)),
            wait=wk_, inc=[(sem, 16)], tag=f"u{which}{slab}.{u}")
        return (sem, sem.inc(16))

    def load_wd_half(slab, hs):
        for ul in range(4):
            h0 = slab * 2048 + hs * 1024 + ul * 256
            wk_ = []
            if ul == 0:
                if ffn["down_pe_all"]:
                    wk_.append((sPE, ffn["down_pe_all"]))
                if slab == 0 and hs == 0:
                    wk_.append((sWc, ffn_cast_ready))
            pl.op("sync", lambda e, h0=h0, ul=ul: e.dma_start(
                out=sb["wdu"][:, ul * 2 * 2048:(ul + 1) * 2 * 2048].rearrange(
                    "p (hs2 m) -> p hs2 m", hs2=2),
                in_=wd_bf.ap()[h0:h0 + 256, :].rearrange("(hs2 p) m -> p hs2 m", p=128)),
                wait=wk_, inc=[(sFd, 16)], tag=f"uwd{slab}.{hs}.{ul}")
            sFd.inc(16)
        return (sFd, sFd.n)

    for slab in range(4):
        wd_mark = load_wd_half(slab, 0)
        for hc in range(16):
            u, sub = hc // 2, hc % 2
            if sub == 0:
                load_marks[("wg", slab, u)] = load_gu("wg", slab, u)
                load_marks[("wu", slab, u)] = load_gu("wu", slab, u)
            gp = P[0] if hc % 2 == 0 else P[2]
            up = P[1] if hc % 2 == 0 else P[3]
            gidx = slab * 8 + u
            slot = gidx % 2
            w = [load_marks[("wg", slab, u)], load_marks[("wu", slab, u)]]
            prev = ffn["gu_used"][hc % 2]
            if prev:
                w.append((sVE, prev))
            if slab > 0 and hc < 2:
                if ffn["r2_evac"][0]:
                    w.append((sVE, ffn["r2_evac"][0]))
                if ffn["r2_evac"][1]:
                    w.append((sVE, ffn["r2_evac"][1]))
            for mc in range(MC):
                base = slot * MC * 256 + mc * 256 + sub * 128
                lg = sb["wgu"][:, base:base + 128]
                lu = sb["wuu"][:, base:base + 128]
                pl.op("tensor", lambda e, lg=lg: e.ldweights(lg),
                      wait=(w if mc == 0 else []), tag=f"gL{slab}.{hc}.{mc}")
                for half in range(2):
                    pl.op("tensor", lambda e, lg=lg, gp=gp, mc=mc, half=half: e.matmul(
                        gp[:, half * 512:(half + 1) * 512], lg,
                        sb["h2_ft"][:, mc * 1024 + half * 512:mc * 1024 + (half + 1) * 512],
                        start=(mc == 0), stop=(mc == MC - 1)),
                        inc=([(sPE, 1)] if (mc == MC - 1 and half == 1) else []),
                        tag=f"gM{slab}.{hc}")
                pl.op("tensor", lambda e, lu=lu: e.ldweights(lu))
                for half in range(2):
                    pl.op("tensor", lambda e, lu=lu, up=up, mc=mc, half=half: e.matmul(
                        up[:, half * 512:(half + 1) * 512], lu,
                        sb["h2_ft"][:, mc * 1024 + half * 512:mc * 1024 + (half + 1) * 512],
                        start=(mc == 0), stop=(mc == MC - 1)),
                        inc=([(sPE, 1)] if (mc == MC - 1 and half == 1) else []),
                        tag=f"uM{slab}.{hc}")
            m_gate = sPE.inc()
            m_up = sPE.inc()
            if sub == 1:
                ffn["gu_pe"][gidx] = m_up
            wse = [(sPE, m_gate)]
            if ffn["silu_prev"]:
                wse.append((sVE, ffn["silu_prev"]))
            pl.op("scalar", lambda e, gp=gp: e.activation(
                sb["silu_t"][:, :], gp[:, :], AF.Silu),
                wait=wse, inc=[(sSE, 1)], tag=f"si{slab}.{hc}")
            m_si = sSE.inc()
            pl.op("vector", lambda e, up=up, hc=hc: e.tensor_mul(
                sb["g_slab"][:, hc * 1024:(hc + 1) * 1024], up[:, :],
                sb["silu_t"][:, :]),
                wait=[(sSE, m_si), (sPE, m_up)], inc=[(sVE, 1)], tag=f"gm{slab}.{hc}")
            ffn["gu_used"][hc % 2] = sVE.inc()
            ffn["silu_prev"] = ffn["gu_used"][hc % 2]
        g_ready = sVE.n
        if DEBUG and slab == 0:
            pl.op("sync", lambda e: e.dma_start(out=dbg["d_g"], in_=sb["g_slab"][:, :]),
                  wait=[(sVE, g_ready)], inc=[(sSTp[0], 16)])
            sSTp[0].inc(16)
        for hs in range(2):
            if hs == 1:
                wd_mark = load_wd_half(slab, 1)
            for ot in range(OTN):
                pp = ot % 2
                PA, PB = (P[0], P[1]) if pp == 0 else (P[2], P[3])
                w = [(sVE, g_ready), wd_mark]
                if ffn["r2_evac"][pp]:
                    w.append((sVE, ffn["r2_evac"][pp]))
                for hl in range(8):
                    hc = hs * 8 + hl
                    ul, sub = hl // 2, hl % 2
                    lg = sb["g_slab"][:, hc * 1024 + ot * 128:hc * 1024 + (ot + 1) * 128]
                    pl.op("tensor", lambda e, lg=lg: e.ldweights(lg),
                          wait=(w if hl == 0 else []), tag=f"dL{slab}.{hs}.{ot}.{hl}")
                    for k in range(4):
                        tgt = PA if k < 2 else PB
                        pl.op("tensor",
                              lambda e, lg=lg, tgt=tgt, k=k, ul=ul, sub=sub, hl=hl: e.matmul(
                                  tgt[:, (k % 2) * 512:(k % 2 + 1) * 512], lg,
                                  sb["wdu"][:, ul * 2 * 2048 + sub * 2048 + k * 512:
                                            ul * 2 * 2048 + sub * 2048 + (k + 1) * 512],
                                  start=(hl == 0), stop=(hl == 7)),
                              inc=([(sPE, 1)] if (hl == 7 and k == 3) else []),
                              tag=f"dM{slab}.{hs}.{ot}.{hl}.{k}")
                m_pe = sPE.inc()
                r2a = sb["r2"][:, ot * 2048:ot * 2048 + 1024]
                r2b = sb["r2"][:, ot * 2048 + 1024:(ot + 1) * 2048]
                first = (slab == 0 and hs == 0)
                if first:
                    pl.op("vector", lambda e, r2a=r2a, PA=PA: e.tensor_copy(r2a, PA[:, :]),
                          wait=[(sPE, m_pe)], tag=f"rc{slab}.{hs}.{ot}")
                    pl.op("vector", lambda e, r2b=r2b, PB=PB: e.tensor_copy(r2b, PB[:, :]),
                          inc=[(sVE, 1)], tag=f"rcB{slab}.{hs}.{ot}")
                else:
                    pl.op("vector", lambda e, r2a=r2a, PA=PA: e.tensor_add(r2a, r2a, PA[:, :]),
                          wait=[(sPE, m_pe)], tag=f"ra{slab}.{hs}.{ot}")
                    pl.op("vector", lambda e, r2b=r2b, PB=PB: e.tensor_add(r2b, r2b, PB[:, :]),
                          inc=[(sVE, 1)], tag=f"raB{slab}.{hs}.{ot}")
                ffn["r2_evac"][pp] = sVE.inc()
            ffn["down_pe_all"] = sPE.n
    r2_all = sVE.n

    if DEBUG:
        pl.op("sync", lambda e: e.dma_start(out=dbg["d_r2"], in_=sb["r2"][:, :]),
              wait=[(sVE, r2_all)], inc=[(sSTp[0], 16)])
        sSTp[0].inc(16)

    pl.barrier(sB)
    p6_ve = [0, 0]
    p6_st = [0, 0]
    for ot in range(OTN):
        pp = ot % 2
        xra = sb["xr6"][:, pp * 2048:(pp + 1) * 2048]
        wxr = list(xm_all_w)
        if p6_ve[pp]:
            wxr.append((sVE, p6_ve[pp]))
        pl.op("sync", lambda e, ot=ot, xra=xra: e.dma_start(
            out=xra, in_=xmid_dram.ap()[ot * 128:(ot + 1) * 128, :]),
            wait=wxr, inc=[(sXRp[pp], 16)], tag=f"fx{ot}")
        m_xr = sXRp[pp].inc(16)
        sl = pp * 16
        sx = stats[:, sl:sl + 1]
        sxx = stats[:, sl + 1:sl + 2]
        v0 = stats[:, sl + 2:sl + 3]
        negm = stats[:, sl + 3:sl + 4]
        rstd = stats[:, sl + 4:sl + 5]
        r2t = sb["r2"][:, ot * 2048:(ot + 1) * 2048]
        pl.op("scalar", lambda e, r2t=r2t, sx=sx: e.activation(
            dump[:, :], r2t, AF.Copy, accum_out=sx),
            wait=[(sVE, r2_all)], tag=f"fS{ot}")
        pl.op("scalar", lambda e, r2t=r2t, sxx=sxx: e.activation(
            dump[:, :], r2t, AF.Square, accum_out=sxx), inc=[(sSE, 1)], tag=f"fQ{ot}")
        m_se = sSE.inc()
        pl.op("vector", lambda e, sx=sx, v0=v0: e.tensor_mul(v0, sx, sx),
              wait=[(sSE, m_se)])
        pl.op("vector", lambda e, v0=v0, sxx=sxx: e.scalar_tensor_tensor(
            v0, v0, -1.0 / 2048, sxx, ALU.mult, ALU.add))
        pl.op("vector", lambda e, sx=sx, negm=negm: e.tensor_scalar(
            negm, sx, -1.0 / 2048, None, ALU.mult), inc=[(sVE, 1)], tag=f"fV{ot}")
        m_ve = sVE.inc()
        pl.op("scalar", lambda e, v0=v0: e.activation(v0, v0, AF.Ln,
                                                      scale=np.float32(1.0 / 2048), bias=np.float32(EPS)),
              wait=[(sVE, m_ve)])
        pl.op("scalar", lambda e, v0=v0, rstd=rstd: e.activation(
            rstd, v0, AF.Exp, scale=np.float32(-0.5)), inc=[(sSE, 1)], tag=f"fR{ot}")
        m_se2 = sSE.inc()
        lna = sb["ln6"][:, pp * 2048:(pp + 1) * 2048]
        xoa = sb["xo6"][:, pp * 2048:(pp + 1) * 2048]
        pl.op("vector", lambda e, lna=lna, r2t=r2t, negm=negm, rstd=rstd: e.tensor_scalar(
            lna, r2t, negm, rstd, ALU.add, ALU.mult),
            wait=[(sSE, m_se2)], tag=f"fL{ot}")
        wadd = [(sXRp[pp], m_xr)]
        if p6_st[pp]:
            wadd.append((sSTp[pp], p6_st[pp]))
        pl.op("vector", lambda e, xoa=xoa, lna=lna, xra=xra: e.scalar_tensor_tensor(
            xoa, lna, 1.0, xra, ALU.mult, ALU.add),
            wait=wadd, inc=[(sVE, 1)], tag=f"fA{ot}")
        p6_ve[pp] = sVE.inc()
        pl.op("sync", lambda e, ot=ot, xoa=xoa: e.dma_start(
            out=out_own[ot * 128:(ot + 1) * 128, :], in_=xoa),
            wait=[(sVE, p6_ve[pp])], inc=[(sSTp[pp], 16)], tag=f"st{ot}")
        p6_st[pp] = sSTp[pp].inc(16)

    pl.op("sync", None, wait=[(sSTp[0], sSTp[0].n), (sSTp[1], sSTp[1].n)],
          tag="final-wait")

    pl.simulate()
    pl.replay()
    return nc


_CACHE = {}


def kernel(L0_attn_x, L0_attn_w_aq, L0_attn_w_ak, L0_attn_w_av, L0_attn_w_ao,
           L0_ffn_w_gate, L0_ffn_w_up, L0_ffn_w_down):
    x = np.ascontiguousarray(np.asarray(L0_attn_x, dtype=np.float32))
    waq = np.ascontiguousarray(np.asarray(L0_attn_w_aq, np.float32))
    wak = np.ascontiguousarray(np.asarray(L0_attn_w_ak, np.float32))
    wav = np.ascontiguousarray(np.asarray(L0_attn_w_av, np.float32))
    wao = np.ascontiguousarray(np.asarray(L0_attn_w_ao, np.float32).reshape(G * H, D, M))
    wgt = np.ascontiguousarray(np.asarray(L0_ffn_w_gate, np.float32).T)
    wut = np.ascontiguousarray(np.asarray(L0_ffn_w_up, np.float32).T)
    wdt = np.ascontiguousarray(np.asarray(L0_ffn_w_down, np.float32).T)

    pos = np.arange(T, dtype=np.float64)
    freqs = 10000.0 ** (-np.arange(64, dtype=np.float64) / 64.0)
    ang = np.outer(freqs, pos)
    rope1 = np.concatenate([np.cos(ang), np.sin(ang)], axis=0).astype(BFH)
    rope2 = np.concatenate([np.sin(ang), np.cos(ang)], axis=0).astype(BFH)
    masks = np.zeros((4, 128, 512), np.float32)
    jj = np.arange(128)[:, None]
    ii = np.arange(512)[None, :]
    for r in range(4):
        masks[r] = ((jj + 128 * r) <= ii).astype(np.float32)
    masks = masks.astype(BFH)

    if "nc" not in _CACHE:
        _CACHE["nc"] = build_program()
    nc = _CACHE["nc"]

    in_maps = []
    for r in range(8):
        b, g = r // 4, r % 4
        in_maps.append({
            "x_full": x[b],
            "w_q": waq[g],
            "w_k": wak[g],
            "w_v": wav[g],
            "w_o": wao,
            "wg_t": wgt,
            "wu_t": wut,
            "wd_t": wdt,
            "rope1": np.asarray(rope1),
            "rope2": np.asarray(rope2),
            "masks": np.asarray(masks),
        })
    res = run_bass_kernel_spmd(nc, in_maps, core_ids=list(range(8)))
    out = np.empty((B, T, M), np.float32)
    for r in range(8):
        b, s = r // 4, r % 4
        out[b, s * 1024:(s + 1) * 1024, :] = np.asarray(res.results[r]["out_own"])
    if DEBUG:
        kernel._last = res
    return out


# version 02
# speedup vs baseline: 1.2210x; 1.2210x over previous
import numpy as np
import ml_dtypes
import concourse.bass as bass
import concourse.mybir as mybir
from concourse.bass_utils import run_bass_kernel_spmd

F32 = mybir.dt.float32
BF16 = mybir.dt.bfloat16
AF = mybir.ActivationFunctionType
ALU = mybir.AluOpType
BFH = ml_dtypes.bfloat16

B, T, M, G, H, D = 2, 4096, 2048, 4, 4, 128
HID = 8192
NT = T // 128
OTN = 8
MC = M // 128
QK2 = float(D) ** -0.5
SHIFT = 30.0
EPS = 1e-5
DEBUG = False

ENGINES = ("sync", "gpsimd", "tensor", "scalar", "vector")


class SemT:
    def __init__(self, nc, name):
        self.h = nc.alloc_semaphore(name)
        self.n = 0
        self.name = name

    def inc(self, k=1):
        self.n += k
        return self.n


class Op:
    __slots__ = ("eng", "fn", "waits", "incs", "tag")

    def __init__(self, eng, fn, waits, incs, tag):
        self.eng, self.fn, self.waits, self.incs, self.tag = eng, fn, waits, incs, tag


class Plan:
    def __init__(self, nc):
        self.nc = nc
        self.seq = []
        self.pid = {}
        self.sl1024 = {}

    def op(self, eng, fn=None, wait=(), inc=(), tag=""):
        self.seq.append(Op(eng, fn, [w for w in wait if w is not None], list(inc), tag))

    def barrier(self, sem):
        for e in ENGINES:
            self.op(e, None, inc=[(sem, 1)], tag="bar-inc")
            sem.inc(1)
        tgt = sem.n
        for e in ENGINES:
            self.op(e, None, wait=[(sem, tgt)], tag="bar-wait")

    def simulate(self):
        vals = {}
        qs = {e: [o for o in self.seq if o.eng == e] for e in ENGINES}
        ptr = {e: 0 for e in ENGINES}
        progress = True
        while progress:
            progress = False
            for e in ENGINES:
                while ptr[e] < len(qs[e]):
                    o = qs[e][ptr[e]]
                    if all(vals.get(s.name, 0) >= v for s, v in o.waits):
                        for s, k in o.incs:
                            vals[s.name] = vals.get(s.name, 0) + k
                        ptr[e] += 1
                        progress = True
                    else:
                        break
        stuck = {e: ptr[e] for e in ENGINES if ptr[e] < len(qs[e])}
        if stuck:
            msg = []
            for e, p in stuck.items():
                o = qs[e][p]
                why = [(s.name, v, vals.get(s.name, 0)) for s, v in o.waits
                       if vals.get(s.name, 0) < v]
                msg.append(f"{e}[{p}/{len(qs[e])}] tag={o.tag} blocked on {why}")
            raise RuntimeError("PLAN DEADLOCK:\n" + "\n".join(msg))

    def replay(self):
        nc = self.nc
        with nc.Block() as block:
            def run_engine(eng_obj, name):
                try:
                    self.pid[name] = eng_obj.partition_id()
                    self.sl1024[name] = self.pid[name] % 4 * 1024
                except Exception:
                    pass
                for o in self.seq:
                    if o.eng != name:
                        continue
                    for s, v in o.waits:
                        eng_obj.wait_ge(s.h, v)
                    if o.fn is not None:
                        inst = o.fn(eng_obj)
                        for s, k in o.incs:
                            inst = inst.then_inc(s.h, k)
                    else:
                        for s, k in o.incs:
                            eng_obj.sem_inc(s.h, k)

            @block.sync
            def _(e):
                run_engine(e, "sync")

            @block.gpsimd
            def _(e):
                run_engine(e, "gpsimd")

            @block.tensor
            def _(e):
                run_engine(e, "tensor")

            @block.scalar
            def _(e):
                run_engine(e, "scalar")

            @block.vector
            def _(e):
                run_engine(e, "vector")


class SbufMap:
    def __init__(self, nc, base, top):
        self.nc = nc
        self.base = base
        self.top = top
        self.items = []

    def add(self, name, shape, dtype, live):
        esz = 4 if dtype == F32 else 2
        bpp = int(np.prod(shape[1:])) * esz
        bpp = (bpp + 63) // 64 * 64
        self.items.append([name, shape, dtype, bpp, live])

    def finalize(self):
        placed = []
        out = {}
        for name, shape, dtype, bpp, live in sorted(self.items, key=lambda it: -it[3]):
            off = 0
            while True:
                clash = None
                for poff, psz, plive, pname in placed:
                    if not (live[1] < plive[0] or plive[1] < live[0]):
                        if not (off + bpp <= poff or poff + psz <= off):
                            clash = poff + psz
                            break
                if clash is None:
                    break
                off = (clash + 63) // 64 * 64
            assert self.base + off + bpp <= self.top, (
                f"SBUF overflow placing {name}: off={off} bpp={bpp}")
            placed.append((off, bpp, live, name))
            out[name] = self.nc.alloc_sbuf_tensor_at(
                name, list(shape), dtype, offset=self.base + off)
        return out


def build_program():
    nc = bass.Bass("TRN2", target_bir_lowering=False)

    x_full = nc.dram_tensor("x_full", [T, M], F32, kind="ExternalInput").ap()
    w_q = nc.dram_tensor("w_q", [128, H * MC * D], F32, kind="ExternalInput").ap()
    w_k = nc.dram_tensor("w_k", [128, MC * D], F32, kind="ExternalInput").ap()
    w_v = nc.dram_tensor("w_v", [128, MC * D], F32, kind="ExternalInput").ap()
    w_o = nc.dram_tensor("w_o", [128, G * H * M], F32, kind="ExternalInput").ap()
    wg_t = nc.dram_tensor("wg_t", [128, MC * HID], F32, kind="ExternalInput").ap()
    wu_t = nc.dram_tensor("wu_t", [128, MC * HID], F32, kind="ExternalInput").ap()
    wd_t = nc.dram_tensor("wd_t", [128, 64 * M], F32, kind="ExternalInput").ap()
    rope1 = nc.dram_tensor("rope1", [128, T], BF16, kind="ExternalInput").ap()
    rope2 = nc.dram_tensor("rope2", [128, T], BF16, kind="ExternalInput").ap()
    masks_i = nc.dram_tensor("masks", [4, 128, 512], BF16, kind="ExternalInput").ap()
    out_own = nc.dram_tensor("out_own", [1024, M], F32, kind="ExternalOutput").ap()

    h_dram = nc.dram_tensor("h_dram", [T, M], BF16)
    h2_dram = nc.dram_tensor("h2_dram", [1024, M], BF16)
    xmid_dram = nc.dram_tensor("xmid_dram", [1024, M], F32)
    oc_dram = [nc.dram_tensor(f"oc{i}_dram", [D, T], BF16) for i in range(H)]
    og_dram = [nc.dram_tensor(f"og{i}_dram", [G * D, T], BF16) for i in range(H)]
    wg_bf = nc.dram_tensor("wg_bf", [128, MC * HID], BF16)
    wu_bf = nc.dram_tensor("wu_bf", [128, MC * HID], BF16)
    wd_bf = nc.dram_tensor("wd_bf", [128, 64 * M], BF16)

    dbg = {}
    if DEBUG:
        dbg["d_h"] = nc.dram_tensor("d_h", [T, M], BF16, kind="ExternalOutput").ap()
        dbg["d_q"] = nc.dram_tensor("d_q", [128, H * T], BF16, kind="ExternalOutput").ap()
        dbg["d_k"] = nc.dram_tensor("d_k", [128, T], BF16, kind="ExternalOutput").ap()
        dbg["d_v"] = nc.dram_tensor("d_v", [128, T], BF16, kind="ExternalOutput").ap()
        dbg["d_o"] = nc.dram_tensor("d_o", [128, H * T], BF16, kind="ExternalOutput").ap()
        dbg["d_xmid"] = nc.dram_tensor("d_xmid", [1024, M], F32, kind="ExternalOutput").ap()
        dbg["d_h2ft"] = nc.dram_tensor("d_h2ft", [128, MC * 1024], BF16, kind="ExternalOutput").ap()
        dbg["d_g"] = nc.dram_tensor("d_g", [128, 16 * 1024], BF16, kind="ExternalOutput").ap()
        dbg["d_r2"] = nc.dram_tensor("d_r2", [128, OTN * 2048], F32, kind="ExternalOutput").ap()

    smap = SbufMap(nc, (nc.sbuf_base + 63) // 64 * 64, nc.sbuf_top)
    smap.add("rope1_sb", [128, T], BF16, (0, 1))
    smap.add("rope2_sb", [128, T], BF16, (0, 1))
    smap.add("masks_sb", [128, 2048], BF16, (0, 2))
    smap.add("ones_bf", [128, 16], BF16, (0, 6))
    smap.add("ones_f", [1, 128], F32, (0, 6))
    smap.add("stats", [128, 64], F32, (0, 6))
    smap.add("recip_sb", [1, 512], F32, (2, 2))
    smap.add("bcast_sb", [128, 512], F32, (2, 2))
    smap.add("ptile", [128, 32 * 512], BF16, (2, 2))
    smap.add("dump", [128, 2048], F32, (0, 6))
    smap.add("h_ft", [128, 2 * MC * 1024], BF16, (1, 1))
    smap.add("xt", [128, 2 * 2048], F32, (1, 1))
    smap.add("ht", [128, 2 * 2048], BF16, (1, 1))
    smap.add("qtmp", [128, 2 * 1024], BF16, (1, 1))
    smap.add("rtmp", [128, 2048], BF16, (1, 1))
    smap.add("rswp", [128, 1024], BF16, (1, 1))
    smap.add("wq_sb", [128, H * MC * D], BF16, (0, 1))
    smap.add("wk_sb", [128, MC * D], BF16, (0, 1))
    smap.add("wv_sb", [128, MC * D], BF16, (0, 1))
    smap.add("q_ft", [128, H * T], BF16, (1, 2))
    smap.add("k_ft", [128, T], BF16, (1, 2))
    smap.add("v_sb", [128, T], BF16, (1, 2))
    smap.add("o_ft", [128, H * T], BF16, (2, 3))
    smap.add("wo_sb", [128, G * H * M], BF16, (2, 4))
    smap.add("o16_sb", [128, G * H * 1024], BF16, (4, 4))
    smap.add("h2_ft", [128, MC * 1024], BF16, (4, 5))
    smap.add("xr", [128, 2 * 2048], F32, (4, 4))
    smap.add("ln_t", [128, 2 * 2048], F32, (4, 4))
    smap.add("xmid_t", [128, 2 * 2048], F32, (4, 4))
    smap.add("h2t", [128, 2 * 2048], BF16, (4, 4))
    smap.add("g_slab", [128, 16 * 1024], BF16, (5, 5))
    smap.add("r2", [128, OTN * 2048], F32, (5, 6))
    smap.add("wgu", [128, 2 * MC * 256], BF16, (5, 5))
    smap.add("wuu", [128, 2 * MC * 256], BF16, (5, 5))
    smap.add("wdu", [128, 8 * 2048], BF16, (5, 5))
    smap.add("silu_t", [128, 1024], F32, (5, 5))
    smap.add("xr6", [128, 2 * 2048], F32, (6, 6))
    smap.add("ln6", [128, 2 * 2048], F32, (6, 6))
    smap.add("xo6", [128, 2 * 2048], F32, (6, 6))
    sb = smap.finalize()

    P = [nc.alloc_psum_tensor(f"ps{i}", [128, 1024], F32) for i in range(4)]

    pl = Plan(nc)
    sCst = SemT(nc, "sCst")
    sXp = [SemT(nc, "sXa"), SemT(nc, "sXb")]
    sHp = [SemT(nc, "sHa"), SemT(nc, "sHb")]
    sTT = SemT(nc, "sTT")
    sWq = SemT(nc, "sWq")
    sWo = SemT(nc, "sWo")
    sWc = SemT(nc, "sWc")
    sFg = [SemT(nc, "sFga"), SemT(nc, "sFgb")]
    sFu = [SemT(nc, "sFua"), SemT(nc, "sFub")]
    sFd = SemT(nc, "sFd")
    sPE = SemT(nc, "sPE")
    sSE = SemT(nc, "sSE")
    sVE = SemT(nc, "sVE")
    sCCh = [SemT(nc, f"sCC{i}") for i in range(H)]
    sC1h = [SemT(nc, f"sC1{i}") for i in range(H)]
    sG = SemT(nc, "sG")
    sXRp = [SemT(nc, "sXRa"), SemT(nc, "sXRb")]
    sXMp = [SemT(nc, "sXMa"), SemT(nc, "sXMb")]
    sSTp = [SemT(nc, "sSTa"), SemT(nc, "sSTb")]
    sB = SemT(nc, "sB")

    rope1_sb, rope2_sb = sb["rope1_sb"], sb["rope2_sb"]
    masks_sb, stats, dump = sb["masks_sb"], sb["stats"], sb["dump"]
    h_ft, q_ft, k_ft, v_sb, o_ft = (sb["h_ft"], sb["q_ft"], sb["k_ft"],
                                    sb["v_sb"], sb["o_ft"])

    pl.op("vector", lambda e: e.memset(sb["ones_bf"][:, :], 1.0))
    pl.op("vector", lambda e: e.memset(sb["ones_f"][:, :], 1.0), inc=[(sVE, 1)])
    ones_ready = sVE.inc()
    pl.op("sync", lambda e: e.dma_start(out=rope1_sb[:, :], in_=rope1), inc=[(sCst, 16)])
    sCst.inc(16)
    pl.op("sync", lambda e: e.dma_start(out=rope2_sb[:, :], in_=rope2), inc=[(sCst, 16)])
    sCst.inc(16)
    pl.op("sync",
          lambda e: e.dma_start(out=masks_sb[:, :].rearrange("p (r i) -> p r i", r=4),
                                in_=masks_i.rearrange("r p i -> p r i")),
          inc=[(sCst, 16)])
    rope_ready = sCst.inc(16)

    pl.op("gpsimd", lambda e: e.dma_start(out=sb["wk_sb"][:, :], in_=w_k),
          inc=[(sWq, 16)], tag="wk")
    sWq.inc(16)
    pl.op("gpsimd", lambda e: e.dma_start(out=sb["wv_sb"][:, :], in_=w_v),
          inc=[(sWq, 16)], tag="wv")
    sWq.inc(16)
    pl.op("gpsimd", lambda e: e.dma_start(out=sb["wq_sb"][:, :], in_=w_q),
          inc=[(sWq, 16)], tag="wq")
    sWq.inc(16)
    wqkv_ready = sWq.n
    for nm, srcw, dstw in (("wg", wg_t, wg_bf), ("wu", wu_t, wu_bf), ("wd", wd_t, wd_bf)):
        pl.op("gpsimd", lambda e, srcw=srcw, dstw=dstw: e.dma_start(out=dstw.ap(), in_=srcw),
              inc=[(sWc, 16)], tag=f"cast-{nm}")
        sWc.inc(16)
    ffn_cast_ready = sWc.n

    xt, ht = sb["xt"], sb["ht"]
    ln_done = [0] * NT
    h_mark = [0] * NT

    def emit_ln_tile(ti):
        slot = ti % 2
        xa = xt[:, slot * 2048:(slot + 1) * 2048]
        ha = ht[:, slot * 2048:(slot + 1) * 2048]
        w = [(sVE, ln_done[ti - 2])] if ti >= 2 else []
        pl.op("sync", lambda e, ti=ti, xa=xa: e.dma_start(
            out=xa, in_=x_full[ti * 128:(ti + 1) * 128, :]),
            wait=w, inc=[(sXp[slot], 16)], tag=f"x{ti}")
        xv = sXp[slot].inc(16)
        sl = slot * 8
        sx = stats[:, sl:sl + 1]
        sxx = stats[:, sl + 1:sl + 2]
        v0 = stats[:, sl + 2:sl + 3]
        negm = stats[:, sl + 3:sl + 4]
        rstd = stats[:, sl + 4:sl + 5]
        pl.op("scalar", lambda e, xa=xa, sx=sx: e.activation(
            dump[:, :], xa, AF.Copy, accum_out=sx),
            wait=[(sXp[slot], xv)], tag=f"ln{ti}-sx")
        pl.op("scalar", lambda e, xa=xa, sxx=sxx: e.activation(
            dump[:, :], xa, AF.Square, accum_out=sxx), inc=[(sSE, 1)], tag=f"ln{ti}-sxx")
        m_se = sSE.inc()
        pl.op("vector", lambda e, sx=sx, v0=v0: e.tensor_mul(v0, sx, sx),
              wait=[(sSE, m_se)])
        pl.op("vector", lambda e, v0=v0, sxx=sxx: e.scalar_tensor_tensor(
            v0, v0, -1.0 / 2048, sxx, ALU.mult, ALU.add))
        pl.op("vector", lambda e, sx=sx, negm=negm: e.tensor_scalar(
            negm, sx, -1.0 / 2048, None, ALU.mult), inc=[(sVE, 1)], tag=f"ln{ti}-v")
        m_ve = sVE.inc()
        pl.op("scalar", lambda e, v0=v0: e.activation(v0, v0, AF.Ln,
                                                      scale=np.float32(1.0 / 2048), bias=np.float32(EPS)),
              wait=[(sVE, m_ve)])
        pl.op("scalar", lambda e, v0=v0, rstd=rstd: e.activation(
            rstd, v0, AF.Exp, scale=np.float32(-0.5)), inc=[(sSE, 1)], tag=f"ln{ti}-r")
        m_se2 = sSE.inc()
        wn = [(sSE, m_se2)]
        if ti >= 2 and h_mark[ti - 2]:
            wn.append((sHp[slot], h_mark[ti - 2]))
        pl.op("vector", lambda e, ha=ha, xa=xa, negm=negm, rstd=rstd: e.tensor_scalar(
            ha, xa, negm, rstd, ALU.add, ALU.mult),
            wait=wn, inc=[(sVE, 1)], tag=f"ln{ti}-n")
        ln_done[ti] = sVE.inc()
        pl.op("sync", lambda e, ti=ti, ha=ha: e.dma_start(
            out=h_dram.ap()[ti * 128:(ti + 1) * 128, :], in_=ha),
            wait=[(sVE, ln_done[ti])], inc=[(sHp[slot], 16)], tag=f"h{ti}")
        h_mark[ti] = sHp[slot].inc(16)

    qkv_state = {"unit": 0, "vunit": 0, "sse_unit": [0, 0], "sve_unit": [0, 0],
                 "sse_v": [0, 0]}
    trans_done = [0] * 4

    def emit_qkv_quarter(qu):
        tbase = qu * 1024
        t_ready = sTT.n
        hbase = (qu % 2) * MC * 1024
        for u in range(5):
            ps = P[qkv_state["unit"] % 2]
            w = [(sTT, t_ready)]
            w.append((sWq, wqkv_ready))
            prev = qkv_state["sse_unit"][qkv_state["unit"] % 2]
            if prev:
                w.append((sSE, prev))
            for mc in range(MC):
                if u < 4:
                    lhsT = sb["wq_sb"][:, (u * MC + mc) * D:(u * MC + mc + 1) * D]
                else:
                    lhsT = sb["wk_sb"][:, mc * D:(mc + 1) * D]
                for half in range(2):
                    pl.op("tensor", lambda e, lhsT=lhsT, half=half, mc=mc, ps=ps, hbase=hbase:
                          e.matmul(
                        ps[:, half * 512:(half + 1) * 512], lhsT,
                        h_ft[:, hbase + mc * 1024 + half * 512:
                             hbase + mc * 1024 + (half + 1) * 512],
                        start=(mc == 0), stop=(mc == MC - 1)),
                        wait=(w if (mc == 0 and half == 0) else []),
                        inc=([(sPE, 1)] if (mc == MC - 1 and half == 1) else []),
                        tag=f"qkvM{qu}.{u}")
            m_pe = sPE.inc()
            qslot = qkv_state["unit"] % 2
            qa = sb["qtmp"][:, qslot * 1024:(qslot + 1) * 1024]
            wse = [(sPE, m_pe)]
            prev_ve = qkv_state["sve_unit"][qslot]
            if prev_ve:
                wse.append((sVE, prev_ve))
            pl.op("scalar", lambda e, qa=qa, ps=ps: e.copy(qa, ps[:, :]),
                  wait=wse, inc=[(sSE, 1)], tag=f"qkvE{qu}.{u}")
            qkv_state["sse_unit"][qslot] = sSE.inc()
            if u < 4:
                dst = q_ft[:, u * T + tbase:u * T + tbase + 1024]
            else:
                dst = k_ft[:, tbase:tbase + 1024]
            cs1 = rope1_sb[:, tbase:tbase + 1024]
            cs2 = rope2_sb[:, tbase:tbase + 1024]
            rt1 = sb["rtmp"][:, 0:1024]
            rt2 = sb["rtmp"][:, 1024:2048]
            sw = sb["rswp"]
            m_se = qkv_state["sse_unit"][qslot]
            pl.op("vector", lambda e, qa=qa, cs1=cs1, rt1=rt1: e.tensor_mul(rt1, qa, cs1),
                  wait=[(sSE, m_se), (sCst, rope_ready)], tag=f"rp{qu}.{u}")
            pl.op("vector", lambda e, qa=qa, cs2=cs2, rt2=rt2: e.tensor_mul(rt2, qa, cs2))
            pl.op("vector", lambda e, rt1=rt1, sw=sw: e.tensor_copy(
                sw[0:64, :], rt1[64:128, :]))
            pl.op("vector", lambda e, rt2=rt2, sw=sw: e.tensor_copy(
                sw[64:128, :], rt2[0:64, :]))
            pl.op("vector", lambda e, dst=dst, rt1=rt1, sw=sw: e.tensor_sub(
                dst[0:64, :], rt1[0:64, :], sw[0:64, :]))
            pl.op("vector", lambda e, dst=dst, rt2=rt2, sw=sw: e.tensor_add(
                dst[64:128, :], rt2[64:128, :], sw[64:128, :]),
                inc=[(sVE, 1)], tag=f"rpd{qu}.{u}")
            qkv_state["sve_unit"][qslot] = sVE.inc()
            qkv_state["unit"] += 1
        for tt in range(8):
            vp = P[2 + qkv_state["vunit"] % 2]
            w = [(sTT, t_ready)]
            prev = qkv_state["sse_v"][qkv_state["vunit"] % 2]
            if prev:
                w.append((sSE, prev))
            for mc in range(MC):
                lhsT = h_ft[:, hbase + mc * 1024 + tt * 128:
                            hbase + mc * 1024 + (tt + 1) * 128]
                pl.op("tensor", lambda e, lhsT=lhsT, mc=mc, vp=vp: e.matmul(
                    vp[:, 0:128], lhsT, sb["wv_sb"][:, mc * D:(mc + 1) * D],
                    start=(mc == 0), stop=(mc == MC - 1)),
                    wait=(w if mc == 0 else []),
                    inc=([(sPE, 1)] if mc == MC - 1 else []), tag=f"vM{qu}.{tt}")
            m_pe = sPE.inc()
            gt = qu * 8 + tt
            pl.op("scalar", lambda e, gt=gt, vp=vp: e.copy(
                v_sb[:, gt * 128:(gt + 1) * 128], vp[:, 0:128]),
                wait=[(sPE, m_pe)], inc=[(sSE, 1)], tag=f"vE{qu}.{tt}")
            qkv_state["sse_v"][qkv_state["vunit"] % 2] = sSE.inc()
            qkv_state["vunit"] += 1
        trans_done[qu] = sPE.n

    def emit_transposes(qu):
        hbase = (qu % 2) * MC * 1024
        for mc in range(MC):
            w = []
            if mc == 0:
                w.append((sHp[0], sHp[0].n))
                w.append((sHp[1], sHp[1].n))
                if qu >= 2:
                    w.append((sPE, trans_done[qu - 2]))
            pl.op("sync", lambda e, qu=qu, mc=mc, hbase=hbase: e.dma_start(
                out=h_ft[:, hbase + mc * 1024:hbase + (mc + 1) * 1024],
                in_=h_dram.ap()[qu * 1024:(qu + 1) * 1024, mc * 128:(mc + 1) * 128],
                transpose=True),
                wait=w, inc=[(sTT, 16)], tag=f"T{qu}.{mc}")
            sTT.inc(16)

    for ti in range(8):
        emit_ln_tile(ti)
    emit_transposes(0)
    for ti in range(8, 16):
        emit_ln_tile(ti)
    emit_qkv_quarter(0)
    emit_transposes(1)
    for ti in range(16, 24):
        emit_ln_tile(ti)
    emit_qkv_quarter(1)
    emit_transposes(2)
    for ti in range(24, 32):
        emit_ln_tile(ti)
    emit_qkv_quarter(2)
    emit_transposes(3)
    emit_qkv_quarter(3)

    if DEBUG:
        pl.op("sync", lambda e: e.dma_start(out=dbg["d_h"], in_=h_dram.ap()),
              wait=[(sHp[0], sHp[0].n), (sHp[1], sHp[1].n)], inc=[(sSTp[0], 16)])
        sSTp[0].inc(16)

    pl.barrier(sB)
    pl.op("gpsimd", lambda e: e.dma_start(out=sb["wo_sb"][:, :], in_=w_o),
          inc=[(sWo, 16)], tag="wo")
    sWo.inc(16)
    wo_ready = sWo.n
    ptile = sb["ptile"]
    att = {"jglob": 0, "norm_done": [0, 0], "recip_done": [0, 0], "bcast_copy": 0,
           "exp_mark": {}, "pv_done": {}}

    if DEBUG:
        pl.op("sync", lambda e: e.dma_start(out=dbg["d_q"], in_=q_ft[:, :]),
              inc=[(sSTp[0], 16)])
        sSTp[0].inc(16)
        pl.op("sync", lambda e: e.dma_start(out=dbg["d_k"], in_=k_ft[:, :]),
              inc=[(sSTp[0], 16)])
        sSTp[0].inc(16)
        pl.op("sync", lambda e: e.dma_start(out=dbg["d_v"], in_=v_sb[:, :]),
              inc=[(sSTp[0], 16)])
        sSTp[0].inc(16)

    unit = 0
    head_norm = [0] * H
    for hd in range(H):
        for qc in range(8):
            jmax = 4 * (qc + 1)
            q0 = 512 * qc
            opp = unit % 2
            spp = unit % 2
            p_ready = [None] * jmax
            pv_prev = att.get("pv_marks", [None] * 32)
            pv_marks = [None] * 32
            for j in range(jmax):
                jg = att["jglob"] + j
                sp = jg % 2
                w = []
                pe_prev = att["exp_mark"].get(jg - 2)
                if pe_prev:
                    w.append(pe_prev)
                pl.op("tensor", lambda e, sp=sp, j=j, hd=hd, q0=q0: e.matmul(
                    P[0][:, sp * 512:(sp + 1) * 512],
                    k_ft[:, j * 128:(j + 1) * 128],
                    q_ft[:, hd * T + q0:hd * T + q0 + 512],
                    start=True, stop=True),
                    wait=w, inc=[(sPE, 1)], tag=f"aS{unit}.{j}")
                m_s = sPE.inc()
                wse = [(sPE, m_s)]
                if pv_prev[j] is not None:
                    wse.append((sPE, pv_prev[j]))
                pl.op("scalar", lambda e, j=j, sp=sp: e.activation(
                    ptile[:, j * 512:(j + 1) * 512],
                    P[0][:, sp * 512:(sp + 1) * 512],
                    AF.Exp, scale=np.float32(QK2), bias=np.float32(-SHIFT)),
                    wait=wse, inc=[(sSE, 1)], tag=f"aE{unit}.{j}")
                m_e = sSE.inc()
                if j >= jmax - 4:
                    r = j - (jmax - 4)
                    pl.op("vector", lambda e, j=j, r=r: e.tensor_mul(
                        ptile[:, j * 512:(j + 1) * 512],
                        ptile[:, j * 512:(j + 1) * 512],
                        masks_sb[:, r * 512:(r + 1) * 512]),
                        wait=[(sSE, m_e)], inc=[(sVE, 1)], tag=f"aM{unit}.{j}")
                    p_ready[j] = (sVE, sVE.inc())
                else:
                    p_ready[j] = (sSE, m_e)
                att["exp_mark"][jg] = p_ready[j]
            for j in range(jmax):
                w2 = [p_ready[j]]
                if j == 0:
                    if att["recip_done"][spp]:
                        w2.append((sVE, att["recip_done"][spp]))
                    if att["norm_done"][opp]:
                        w2.append((sVE, att["norm_done"][opp]))
                pl.op("tensor", lambda e, spp=spp, j=j, jmax=jmax: e.matmul(
                    P[2][0:1, spp * 512:(spp + 1) * 512],
                    sb["ones_bf"][:, 0:1],
                    ptile[:, j * 512:(j + 1) * 512],
                    start=(j == 0), stop=(j == jmax - 1)),
                    wait=w2, tag=f"aU{unit}.{j}")
                pl.op("tensor", lambda e, opp=opp, j=j, jmax=jmax: e.matmul(
                    P[1][:, opp * 512:(opp + 1) * 512],
                    v_sb[:, j * 128:(j + 1) * 128],
                    ptile[:, j * 512:(j + 1) * 512],
                    start=(j == 0), stop=(j == jmax - 1)),
                    inc=[(sPE, 1)], tag=f"aP{unit}.{j}")
                pv_marks[j] = sPE.inc()
            att["pv_marks"] = pv_marks
            m_pe_unit = pv_marks[jmax - 1]
            pl.op("vector", lambda e, spp=spp: e.reciprocal(
                sb["recip_sb"][0:1, :], P[2][0:1, spp * 512:(spp + 1) * 512]),
                wait=[(sPE, m_pe_unit)], inc=[(sVE, 1)], tag=f"aR{unit}")
            att["recip_done"][spp] = sVE.inc()
            wb = [(sVE, att["recip_done"][spp])]
            if att["bcast_copy"]:
                wb.append((sSE, att["bcast_copy"]))
            pl.op("tensor", lambda e: e.matmul(
                P[3][:, 0:512], sb["ones_f"][0:1, :], sb["recip_sb"][0:1, :],
                start=True, stop=True),
                wait=wb, inc=[(sPE, 1)], tag=f"aB{unit}")
            m_b = sPE.inc()
            pl.op("scalar", lambda e: e.copy(sb["bcast_sb"][:, :], P[3][:, 0:512]),
                  wait=[(sPE, m_b)], inc=[(sSE, 1)], tag=f"aC{unit}")
            att["bcast_copy"] = sSE.inc()
            pl.op("vector", lambda e, hd=hd, q0=q0, opp=opp: e.tensor_mul(
                o_ft[:, hd * T + q0:hd * T + q0 + 512],
                P[1][:, opp * 512:(opp + 1) * 512],
                sb["bcast_sb"][:, :]),
                wait=[(sSE, att["bcast_copy"])], inc=[(sVE, 1)], tag=f"aN{unit}")
            att["norm_done"][opp] = sVE.inc()
            att["jglob"] += jmax
            unit += 1
        head_norm[hd] = sVE.n
    o_done = sVE.n

    if DEBUG:
        pl.op("sync", lambda e: e.dma_start(out=dbg["d_o"], in_=o_ft[:, :]),
              wait=[(sVE, o_done)], inc=[(sSTp[0], 16)])
        sSTp[0].inc(16)

    for hd in range(H):
        pl.op("sync", lambda e, hd=hd: e.dma_start(
            out=oc_dram[hd].ap(), in_=o_ft[:, hd * T:(hd + 1) * T]),
            wait=[(sVE, head_norm[hd])], inc=[(sC1h[hd], 16)], tag=f"oc{hd}")
        sC1h[hd].inc(16)
        pl.op("gpsimd", lambda e, hd=hd: e.collective_compute(
            "AllGather", ALU.bypass,
            replica_groups=[[0, 1, 2, 3], [4, 5, 6, 7]],
            ins=[oc_dram[hd].ap()], outs=[og_dram[hd].ap()]),
            wait=[(sC1h[hd], 16)], inc=[(sCCh[hd], 1)], tag=f"allgather{hd}")
        sCCh[hd].inc(1)

    pl.barrier(sB)

    for hd in range(H):
        pl.op("sync", lambda e, hd=hd: e.dma_start(
            out=sb["o16_sb"][:, :].rearrange("p (gh t) -> p gh t", gh=G * H)[
                :, hd::H, :],
            in_=og_dram[hd].ap().rearrange("(g d) t -> d g t", d=128)[
                :, :, bass.ds(pl.sl1024["sync"], 1024)]),
            wait=[(sCCh[hd], 1)], inc=[(sG, 16)], tag=f"gather{hd}")
        sG.inc(16)

    p4 = {"psum_free": [0, 0], "h2dma": [0, 0], "xmdma": [0, 0], "xr_used": [0, 0]}
    for ot in range(OTN):
        pp = ot % 2
        PA, PB = (P[0], P[1]) if pp == 0 else (P[2], P[3])
        wxr = [(sVE, p4["xr_used"][pp])] if p4["xr_used"][pp] else []
        pl.op("sync", lambda e, ot=ot, pp=pp: e.dma_start(
            out=sb["xr"][:, pp * 2048:(pp + 1) * 2048],
            in_=x_full[bass.ds(pl.sl1024["sync"] + ot * 128, 128), :]),
            wait=wxr, inc=[(sXRp[pp], 16)], tag=f"xr{ot}")
        m_xr = sXRp[pp].inc(16)
        w = [(sG, sG.n), (sWo, wo_ready)]
        if p4["psum_free"][pp]:
            w.append((sVE, p4["psum_free"][pp]))
        for gh in range(G * H):
            lhsT = sb["o16_sb"][:, gh * 1024 + ot * 128:gh * 1024 + (ot + 1) * 128]
            for k in range(4):
                tgt = PA if k < 2 else PB
                pl.op("tensor", lambda e, lhsT=lhsT, gh=gh, k=k, tgt=tgt: e.matmul(
                    tgt[:, (k % 2) * 512:(k % 2 + 1) * 512], lhsT,
                    sb["wo_sb"][:, gh * M + k * 512:gh * M + (k + 1) * 512],
                    start=(gh == 0), stop=(gh == G * H - 1)),
                    wait=(w if (gh == 0 and k == 0) else []),
                    inc=([(sPE, 1)] if (gh == G * H - 1 and k == 3) else []),
                    tag=f"oM{ot}.{gh}.{k}")
        m_pe = sPE.inc()
        sl = pp * 16
        sxa, sxb = stats[:, sl:sl + 1], stats[:, sl + 1:sl + 2]
        sxxa, sxxb = stats[:, sl + 2:sl + 3], stats[:, sl + 3:sl + 4]
        v0 = stats[:, sl + 4:sl + 5]
        negm = stats[:, sl + 5:sl + 6]
        rstd = stats[:, sl + 6:sl + 7]
        pl.op("scalar", lambda e, PA=PA, sxa=sxa: e.activation(
            dump[:, 0:1024], PA[:, :], AF.Copy, accum_out=sxa),
            wait=[(sPE, m_pe)], tag=f"rA{ot}")
        pl.op("scalar", lambda e, PB=PB, sxb=sxb: e.activation(
            dump[:, 0:1024], PB[:, :], AF.Copy, accum_out=sxb))
        pl.op("scalar", lambda e, PA=PA, sxxa=sxxa: e.activation(
            dump[:, 0:1024], PA[:, :], AF.Square, accum_out=sxxa))
        pl.op("scalar", lambda e, PB=PB, sxxb=sxxb: e.activation(
            dump[:, 0:1024], PB[:, :], AF.Square, accum_out=sxxb),
            inc=[(sSE, 1)], tag=f"rS{ot}")
        m_se = sSE.inc()
        pl.op("vector", lambda e, sxa=sxa, sxb=sxb: e.tensor_add(sxa, sxa, sxb),
              wait=[(sSE, m_se)])
        pl.op("vector", lambda e, sxxa=sxxa, sxxb=sxxb: e.tensor_add(sxxa, sxxa, sxxb))
        pl.op("vector", lambda e, sxa=sxa, v0=v0: e.tensor_mul(v0, sxa, sxa))
        pl.op("vector", lambda e, v0=v0, sxxa=sxxa: e.scalar_tensor_tensor(
            v0, v0, -1.0 / 2048, sxxa, ALU.mult, ALU.add))
        pl.op("vector", lambda e, sxa=sxa, negm=negm: e.tensor_scalar(
            negm, sxa, -1.0 / 2048, None, ALU.mult), inc=[(sVE, 1)], tag=f"rV{ot}")
        m_ve = sVE.inc()
        pl.op("scalar", lambda e, v0=v0: e.activation(v0, v0, AF.Ln,
                                                      scale=np.float32(1.0 / 2048), bias=np.float32(EPS)),
              wait=[(sVE, m_ve)])
        pl.op("scalar", lambda e, v0=v0, rstd=rstd: e.activation(
            rstd, v0, AF.Exp, scale=np.float32(-0.5)), inc=[(sSE, 1)], tag=f"rR{ot}")
        m_se2 = sSE.inc()
        lnta = sb["ln_t"][:, pp * 2048:pp * 2048 + 1024]
        lntb = sb["ln_t"][:, pp * 2048 + 1024:(pp + 1) * 2048]
        pl.op("vector", lambda e, PA=PA, lnta=lnta, negm=negm, rstd=rstd: e.tensor_scalar(
            lnta, PA[:, :], negm, rstd, ALU.add, ALU.mult),
            wait=[(sSE, m_se2)], tag=f"rlA{ot}")
        pl.op("vector", lambda e, PB=PB, lntb=lntb, negm=negm, rstd=rstd: e.tensor_scalar(
            lntb, PB[:, :], negm, rstd, ALU.add, ALU.mult),
            inc=[(sVE, 1)], tag=f"rlB{ot}")
        p4["psum_free"][pp] = sVE.inc()
        sx2 = stats[:, sl + 7:sl + 8]
        xma = sb["xmid_t"][:, pp * 2048:(pp + 1) * 2048]
        lnt = sb["ln_t"][:, pp * 2048:(pp + 1) * 2048]
        xra = sb["xr"][:, pp * 2048:(pp + 1) * 2048]
        wxm = [(sXRp[pp], m_xr)]
        if p4["xmdma"][pp]:
            wxm.append((sXMp[pp], p4["xmdma"][pp]))
        pl.op("vector", lambda e, xma=xma, lnt=lnt, xra=xra, sx2=sx2:
              e.scalar_tensor_tensor(xma, lnt, 1.0, xra, ALU.mult, ALU.add,
                                     accum_out=sx2),
              wait=wxm, inc=[(sVE, 1)], tag=f"xm{ot}")
        p4["xr_used"][pp] = sVE.inc()
        m_xmid = p4["xr_used"][pp]
        sxx2 = stats[:, sl + 8:sl + 9]
        v02 = stats[:, sl + 9:sl + 10]
        negm2 = stats[:, sl + 10:sl + 11]
        rstd2 = stats[:, sl + 11:sl + 12]
        pl.op("scalar", lambda e, xma=xma, sxx2=sxx2: e.activation(
            dump[:, :], xma, AF.Square, accum_out=sxx2),
            wait=[(sVE, m_xmid)], inc=[(sSE, 1)], tag=f"l2S{ot}")
        m_se3 = sSE.inc()
        pl.op("vector", lambda e, sx2=sx2, v02=v02: e.tensor_mul(v02, sx2, sx2),
              wait=[(sSE, m_se3)])
        pl.op("vector", lambda e, v02=v02, sxx2=sxx2: e.scalar_tensor_tensor(
            v02, v02, -1.0 / 2048, sxx2, ALU.mult, ALU.add))
        pl.op("vector", lambda e, sx2=sx2, negm2=negm2: e.tensor_scalar(
            negm2, sx2, -1.0 / 2048, None, ALU.mult), inc=[(sVE, 1)], tag=f"l2V{ot}")
        m_ve2 = sVE.inc()
        pl.op("scalar", lambda e, v02=v02: e.activation(v02, v02, AF.Ln,
                                                        scale=np.float32(1.0 / 2048), bias=np.float32(EPS)),
              wait=[(sVE, m_ve2)])
        pl.op("scalar", lambda e, v02=v02, rstd2=rstd2: e.activation(
            rstd2, v02, AF.Exp, scale=np.float32(-0.5)), inc=[(sSE, 1)], tag=f"l2R{ot}")
        m_se4 = sSE.inc()
        h2a = sb["h2t"][:, pp * 2048:(pp + 1) * 2048]
        wh2 = [(sSE, m_se4)]
        if p4["h2dma"][pp]:
            wh2.append((sXMp[pp], p4["h2dma"][pp]))
        pl.op("vector", lambda e, h2a=h2a, xma=xma, negm2=negm2, rstd2=rstd2:
              e.tensor_scalar(h2a, xma, negm2, rstd2, ALU.add, ALU.mult),
              wait=wh2, inc=[(sVE, 1)], tag=f"h2{ot}")
        m_h2 = sVE.inc()
        pl.op("sync", lambda e, ot=ot, xma=xma: e.dma_start(
            out=xmid_dram.ap()[ot * 128:(ot + 1) * 128, :], in_=xma),
            wait=[(sVE, m_h2)], inc=[(sXMp[pp], 16)], tag=f"sxm{ot}")
        p4["xmdma"][pp] = sXMp[pp].inc(16)
        pl.op("sync", lambda e, ot=ot, h2a=h2a: e.dma_start(
            out=h2_dram.ap()[ot * 128:(ot + 1) * 128, :], in_=h2a),
            inc=[(sXMp[pp], 16)], tag=f"sh2{ot}")
        p4["h2dma"][pp] = sXMp[pp].inc(16)
    xm_all_w = [(sXMp[0], sXMp[0].n), (sXMp[1], sXMp[1].n)]

    if DEBUG:
        pl.op("sync", lambda e: e.dma_start(out=dbg["d_xmid"], in_=xmid_dram.ap()),
              wait=xm_all_w, inc=[(sSTp[0], 16)])
        sSTp[0].inc(16)

    for mc in range(MC):
        pl.op("sync", lambda e, mc=mc: e.dma_start(
            out=sb["h2_ft"][:, mc * 1024:(mc + 1) * 1024],
            in_=h2_dram.ap()[:, mc * 128:(mc + 1) * 128], transpose=True),
            wait=(xm_all_w if mc == 0 else []), inc=[(sTT, 16)], tag=f"T2.{mc}")
        sTT.inc(16)
    h2ft_ready = sTT.n

    if DEBUG:
        pl.op("sync", lambda e: e.dma_start(out=dbg["d_h2ft"], in_=sb["h2_ft"][:, :]),
              wait=[(sTT, h2ft_ready)], inc=[(sSTp[0], 16)])
        sSTp[0].inc(16)

    pl.barrier(sB)
    ffn = {"gu_pe": {}, "gu_used": [0, 0], "silu_prev": 0,
           "r2_evac": [0, 0], "down_pe_all": 0}
    load_marks = {}

    def load_gu(which, slab, u):
        gidx = slab * 8 + u
        slot = gidx % 2
        h0 = slab * 2048 + u * 256
        dst = sb["wgu"] if which == "wg" else sb["wuu"]
        srcw = wg_bf if which == "wg" else wu_bf
        sem = sFg[slot] if which == "wg" else sFu[slot]
        wk_ = []
        if gidx >= 2 and (gidx - 2) in ffn["gu_pe"]:
            wk_.append((sPE, ffn["gu_pe"][gidx - 2]))
        if gidx == 0:
            wk_.append((sWc, ffn_cast_ready))
            wk_.append((sTT, h2ft_ready))
        pl.op("sync", lambda e, dst=dst, srcw=srcw, h0=h0, slot=slot: e.dma_start(
            out=dst[:, slot * MC * 256:(slot + 1) * MC * 256].rearrange(
                "p (mc h) -> p mc h", mc=MC),
            in_=srcw.ap().rearrange("p (mc h) -> p mc h", mc=MC)[:, :, h0:h0 + 256]),
            wait=wk_, inc=[(sem, 16)], tag=f"u{which}{slab}.{u}")
        return (sem, sem.inc(16))

    def load_wd_half(slab, hs):
        for ul in range(4):
            h0 = slab * 2048 + hs * 1024 + ul * 256
            wk_ = []
            if ul == 0:
                if ffn["down_pe_all"]:
                    wk_.append((sPE, ffn["down_pe_all"]))
                if slab == 0 and hs == 0:
                    wk_.append((sWc, ffn_cast_ready))
            hc0 = h0 // 128
            pl.op("sync", lambda e, hc0=hc0, ul=ul: e.dma_start(
                out=sb["wdu"][:, ul * 2 * 2048:(ul + 1) * 2 * 2048].rearrange(
                    "p (hs2 m) -> p hs2 m", hs2=2),
                in_=wd_bf.ap().rearrange("p (hc m) -> p hc m", hc=64)[:, hc0:hc0 + 2, :]),
                wait=wk_, inc=[(sFd, 16)], tag=f"uwd{slab}.{hs}.{ul}")
            sFd.inc(16)
        return (sFd, sFd.n)

    for slab in range(4):
        wd_mark = load_wd_half(slab, 0)
        for hc in range(16):
            u, sub = hc // 2, hc % 2
            if sub == 0:
                load_marks[("wg", slab, u)] = load_gu("wg", slab, u)
                load_marks[("wu", slab, u)] = load_gu("wu", slab, u)
            gp = P[0] if hc % 2 == 0 else P[2]
            up = P[1] if hc % 2 == 0 else P[3]
            gidx = slab * 8 + u
            slot = gidx % 2
            w = [load_marks[("wg", slab, u)], load_marks[("wu", slab, u)]]
            prev = ffn["gu_used"][hc % 2]
            if prev:
                w.append((sVE, prev))
            if slab > 0 and hc < 2:
                if ffn["r2_evac"][0]:
                    w.append((sVE, ffn["r2_evac"][0]))
                if ffn["r2_evac"][1]:
                    w.append((sVE, ffn["r2_evac"][1]))
            for mc in range(MC):
                base = slot * MC * 256 + mc * 256 + sub * 128
                lg = sb["wgu"][:, base:base + 128]
                lu = sb["wuu"][:, base:base + 128]
                for half in range(2):
                    pl.op("tensor", lambda e, lg=lg, gp=gp, mc=mc, half=half: e.matmul(
                        gp[:, half * 512:(half + 1) * 512], lg,
                        sb["h2_ft"][:, mc * 1024 + half * 512:mc * 1024 + (half + 1) * 512],
                        start=(mc == 0), stop=(mc == MC - 1)),
                        wait=(w if (mc == 0 and half == 0) else []),
                        inc=([(sPE, 1)] if (mc == MC - 1 and half == 1) else []),
                        tag=f"gM{slab}.{hc}")
                for half in range(2):
                    pl.op("tensor", lambda e, lu=lu, up=up, mc=mc, half=half: e.matmul(
                        up[:, half * 512:(half + 1) * 512], lu,
                        sb["h2_ft"][:, mc * 1024 + half * 512:mc * 1024 + (half + 1) * 512],
                        start=(mc == 0), stop=(mc == MC - 1)),
                        inc=([(sPE, 1)] if (mc == MC - 1 and half == 1) else []),
                        tag=f"uM{slab}.{hc}")
            m_gate = sPE.inc()
            m_up = sPE.inc()
            if sub == 1:
                ffn["gu_pe"][gidx] = m_up
            wse = [(sPE, m_gate)]
            if ffn["silu_prev"]:
                wse.append((sVE, ffn["silu_prev"]))
            pl.op("scalar", lambda e, gp=gp: e.activation(
                sb["silu_t"][:, :], gp[:, :], AF.Silu),
                wait=wse, inc=[(sSE, 1)], tag=f"si{slab}.{hc}")
            m_si = sSE.inc()
            pl.op("vector", lambda e, up=up, hc=hc: e.tensor_mul(
                sb["g_slab"][:, hc * 1024:(hc + 1) * 1024], up[:, :],
                sb["silu_t"][:, :]),
                wait=[(sSE, m_si), (sPE, m_up)], inc=[(sVE, 1)], tag=f"gm{slab}.{hc}")
            ffn["gu_used"][hc % 2] = sVE.inc()
            ffn["silu_prev"] = ffn["gu_used"][hc % 2]
        g_ready = sVE.n
        if DEBUG and slab == 0:
            pl.op("sync", lambda e: e.dma_start(out=dbg["d_g"], in_=sb["g_slab"][:, :]),
                  wait=[(sVE, g_ready)], inc=[(sSTp[0], 16)])
            sSTp[0].inc(16)
        for hs in range(2):
            if hs == 1:
                wd_mark = load_wd_half(slab, 1)
            for ot in range(OTN):
                pp = ot % 2
                PA, PB = (P[0], P[1]) if pp == 0 else (P[2], P[3])
                w = [(sVE, g_ready), wd_mark]
                if ffn["r2_evac"][pp]:
                    w.append((sVE, ffn["r2_evac"][pp]))
                for hl in range(8):
                    hc = hs * 8 + hl
                    ul, sub = hl // 2, hl % 2
                    lg = sb["g_slab"][:, hc * 1024 + ot * 128:hc * 1024 + (ot + 1) * 128]
                    for k in range(4):
                        tgt = PA if k < 2 else PB
                        pl.op("tensor",
                              lambda e, lg=lg, tgt=tgt, k=k, ul=ul, sub=sub, hl=hl: e.matmul(
                                  tgt[:, (k % 2) * 512:(k % 2 + 1) * 512], lg,
                                  sb["wdu"][:, ul * 2 * 2048 + sub * 2048 + k * 512:
                                            ul * 2 * 2048 + sub * 2048 + (k + 1) * 512],
                                  start=(hl == 0), stop=(hl == 7)),
                              wait=(w if (hl == 0 and k == 0) else []),
                              inc=([(sPE, 1)] if (hl == 7 and k == 3) else []),
                              tag=f"dM{slab}.{hs}.{ot}.{hl}.{k}")
                m_pe = sPE.inc()
                r2a = sb["r2"][:, ot * 2048:ot * 2048 + 1024]
                r2b = sb["r2"][:, ot * 2048 + 1024:(ot + 1) * 2048]
                first = (slab == 0 and hs == 0)
                if first:
                    pl.op("vector", lambda e, r2a=r2a, PA=PA: e.tensor_copy(r2a, PA[:, :]),
                          wait=[(sPE, m_pe)], tag=f"rc{slab}.{hs}.{ot}")
                    pl.op("vector", lambda e, r2b=r2b, PB=PB: e.tensor_copy(r2b, PB[:, :]),
                          inc=[(sVE, 1)], tag=f"rcB{slab}.{hs}.{ot}")
                else:
                    pl.op("vector", lambda e, r2a=r2a, PA=PA: e.tensor_add(r2a, r2a, PA[:, :]),
                          wait=[(sPE, m_pe)], tag=f"ra{slab}.{hs}.{ot}")
                    pl.op("vector", lambda e, r2b=r2b, PB=PB: e.tensor_add(r2b, r2b, PB[:, :]),
                          inc=[(sVE, 1)], tag=f"raB{slab}.{hs}.{ot}")
                ffn["r2_evac"][pp] = sVE.inc()
            ffn["down_pe_all"] = sPE.n
    r2_all = sVE.n

    if DEBUG:
        pl.op("sync", lambda e: e.dma_start(out=dbg["d_r2"], in_=sb["r2"][:, :]),
              wait=[(sVE, r2_all)], inc=[(sSTp[0], 16)])
        sSTp[0].inc(16)

    pl.barrier(sB)
    p6_ve = [0, 0]
    p6_st = [0, 0]
    for ot in range(OTN):
        pp = ot % 2
        xra = sb["xr6"][:, pp * 2048:(pp + 1) * 2048]
        wxr = list(xm_all_w)
        if p6_ve[pp]:
            wxr.append((sVE, p6_ve[pp]))
        pl.op("sync", lambda e, ot=ot, xra=xra: e.dma_start(
            out=xra, in_=xmid_dram.ap()[ot * 128:(ot + 1) * 128, :]),
            wait=wxr, inc=[(sXRp[pp], 16)], tag=f"fx{ot}")
        m_xr = sXRp[pp].inc(16)
        sl = pp * 16
        sx = stats[:, sl:sl + 1]
        sxx = stats[:, sl + 1:sl + 2]
        v0 = stats[:, sl + 2:sl + 3]
        negm = stats[:, sl + 3:sl + 4]
        rstd = stats[:, sl + 4:sl + 5]
        r2t = sb["r2"][:, ot * 2048:(ot + 1) * 2048]
        pl.op("scalar", lambda e, r2t=r2t, sx=sx: e.activation(
            dump[:, :], r2t, AF.Copy, accum_out=sx),
            wait=[(sVE, r2_all)], tag=f"fS{ot}")
        pl.op("scalar", lambda e, r2t=r2t, sxx=sxx: e.activation(
            dump[:, :], r2t, AF.Square, accum_out=sxx), inc=[(sSE, 1)], tag=f"fQ{ot}")
        m_se = sSE.inc()
        pl.op("vector", lambda e, sx=sx, v0=v0: e.tensor_mul(v0, sx, sx),
              wait=[(sSE, m_se)])
        pl.op("vector", lambda e, v0=v0, sxx=sxx: e.scalar_tensor_tensor(
            v0, v0, -1.0 / 2048, sxx, ALU.mult, ALU.add))
        pl.op("vector", lambda e, sx=sx, negm=negm: e.tensor_scalar(
            negm, sx, -1.0 / 2048, None, ALU.mult), inc=[(sVE, 1)], tag=f"fV{ot}")
        m_ve = sVE.inc()
        pl.op("scalar", lambda e, v0=v0: e.activation(v0, v0, AF.Ln,
                                                      scale=np.float32(1.0 / 2048), bias=np.float32(EPS)),
              wait=[(sVE, m_ve)])
        pl.op("scalar", lambda e, v0=v0, rstd=rstd: e.activation(
            rstd, v0, AF.Exp, scale=np.float32(-0.5)), inc=[(sSE, 1)], tag=f"fR{ot}")
        m_se2 = sSE.inc()
        lna = sb["ln6"][:, pp * 2048:(pp + 1) * 2048]
        xoa = sb["xo6"][:, pp * 2048:(pp + 1) * 2048]
        pl.op("vector", lambda e, lna=lna, r2t=r2t, negm=negm, rstd=rstd: e.tensor_scalar(
            lna, r2t, negm, rstd, ALU.add, ALU.mult),
            wait=[(sSE, m_se2)], tag=f"fL{ot}")
        wadd = [(sXRp[pp], m_xr)]
        if p6_st[pp]:
            wadd.append((sSTp[pp], p6_st[pp]))
        pl.op("vector", lambda e, xoa=xoa, lna=lna, xra=xra: e.scalar_tensor_tensor(
            xoa, lna, 1.0, xra, ALU.mult, ALU.add),
            wait=wadd, inc=[(sVE, 1)], tag=f"fA{ot}")
        p6_ve[pp] = sVE.inc()
        pl.op("sync", lambda e, ot=ot, xoa=xoa: e.dma_start(
            out=out_own[ot * 128:(ot + 1) * 128, :], in_=xoa),
            wait=[(sVE, p6_ve[pp])], inc=[(sSTp[pp], 16)], tag=f"st{ot}")
        p6_st[pp] = sSTp[pp].inc(16)

    pl.op("sync", None, wait=[(sSTp[0], sSTp[0].n), (sSTp[1], sSTp[1].n)],
          tag="final-wait")

    pl.simulate()
    pl.replay()
    return nc


_CACHE = {}


def kernel(L0_attn_x, L0_attn_w_aq, L0_attn_w_ak, L0_attn_w_av, L0_attn_w_ao,
           L0_ffn_w_gate, L0_ffn_w_up, L0_ffn_w_down):
    x = np.ascontiguousarray(np.asarray(L0_attn_x, dtype=np.float32))
    waq = np.asarray(L0_attn_w_aq, np.float32)
    wak = np.asarray(L0_attn_w_ak, np.float32)
    wav = np.asarray(L0_attn_w_av, np.float32)
    wq_h = [np.ascontiguousarray(
        waq[g].reshape(H, MC, 128, D).transpose(2, 0, 1, 3).reshape(128, H * MC * D))
        for g in range(G)]
    wk_h = [np.ascontiguousarray(
        wak[g].reshape(MC, 128, D).transpose(1, 0, 2).reshape(128, MC * D))
        for g in range(G)]
    wv_h = [np.ascontiguousarray(
        wav[g].reshape(MC, 128, D).transpose(1, 0, 2).reshape(128, MC * D))
        for g in range(G)]
    wao = np.ascontiguousarray(
        np.asarray(L0_attn_w_ao, np.float32).reshape(G * H, D, M)
        .transpose(1, 0, 2).reshape(128, G * H * M))
    wgt = np.ascontiguousarray(
        np.asarray(L0_ffn_w_gate, np.float32).T.reshape(MC, 128, HID)
        .transpose(1, 0, 2).reshape(128, MC * HID))
    wut = np.ascontiguousarray(
        np.asarray(L0_ffn_w_up, np.float32).T.reshape(MC, 128, HID)
        .transpose(1, 0, 2).reshape(128, MC * HID))
    wdt = np.ascontiguousarray(
        np.asarray(L0_ffn_w_down, np.float32).T.reshape(64, 128, M)
        .transpose(1, 0, 2).reshape(128, 64 * M))

    pos = np.arange(T, dtype=np.float64)
    freqs = 10000.0 ** (-np.arange(64, dtype=np.float64) / 64.0)
    ang = np.outer(freqs, pos)
    rope1 = np.concatenate([np.cos(ang), np.sin(ang)], axis=0).astype(BFH)
    rope2 = np.concatenate([np.sin(ang), np.cos(ang)], axis=0).astype(BFH)
    masks = np.zeros((4, 128, 512), np.float32)
    jj = np.arange(128)[:, None]
    ii = np.arange(512)[None, :]
    for r in range(4):
        masks[r] = ((jj + 128 * r) <= ii).astype(np.float32)
    masks = masks.astype(BFH)

    if "nc" not in _CACHE:
        _CACHE["nc"] = build_program()
    nc = _CACHE["nc"]

    in_maps = []
    for r in range(8):
        b, g = r // 4, r % 4
        in_maps.append({
            "x_full": x[b],
            "w_q": wq_h[g],
            "w_k": wk_h[g],
            "w_v": wv_h[g],
            "w_o": wao,
            "wg_t": wgt,
            "wu_t": wut,
            "wd_t": wdt,
            "rope1": np.asarray(rope1),
            "rope2": np.asarray(rope2),
            "masks": np.asarray(masks),
        })
    res = run_bass_kernel_spmd(nc, in_maps, core_ids=list(range(8)))
    out = np.empty((B, T, M), np.float32)
    for r in range(8):
        b, s = r // 4, r % 4
        out[b, s * 1024:(s + 1) * 1024, :] = np.asarray(res.results[r]["out_own"])
    if DEBUG:
        kernel._last = res
    return out


# version 03
# speedup vs baseline: 1.4797x; 1.2119x over previous
import numpy as np
import ml_dtypes
import concourse.bass as bass
import concourse.mybir as mybir
from concourse.bass_utils import run_bass_kernel_spmd

F32 = mybir.dt.float32
BF16 = mybir.dt.bfloat16
AF = mybir.ActivationFunctionType
ALU = mybir.AluOpType
BFH = ml_dtypes.bfloat16

B, T, M, G, H, D = 2, 4096, 2048, 4, 4, 128
HID = 8192
NT = T // 128
OTN = 8
MC = M // 128
QK2 = float(D) ** -0.5
SHIFT = 30.0
EPS = 1e-5
DEBUG = False

ENGINES = ("sync", "gpsimd", "tensor", "scalar", "vector")


class SemT:
    def __init__(self, nc, name):
        self.h = nc.alloc_semaphore(name)
        self.n = 0
        self.name = name

    def inc(self, k=1):
        self.n += k
        return self.n


class Op:
    __slots__ = ("eng", "fn", "waits", "incs", "tag")

    def __init__(self, eng, fn, waits, incs, tag):
        self.eng, self.fn, self.waits, self.incs, self.tag = eng, fn, waits, incs, tag


class Plan:
    def __init__(self, nc):
        self.nc = nc
        self.seq = []
        self.pid = {}
        self.sl1024 = {}

    def op(self, eng, fn=None, wait=(), inc=(), tag=""):
        self.seq.append(Op(eng, fn, [w for w in wait if w is not None], list(inc), tag))

    def barrier(self, sem):
        for e in ENGINES:
            self.op(e, None, inc=[(sem, 1)], tag="bar-inc")
            sem.inc(1)
        tgt = sem.n
        for e in ENGINES:
            self.op(e, None, wait=[(sem, tgt)], tag="bar-wait")

    def simulate(self):
        vals = {}
        qs = {e: [o for o in self.seq if o.eng == e] for e in ENGINES}
        ptr = {e: 0 for e in ENGINES}
        progress = True
        while progress:
            progress = False
            for e in ENGINES:
                while ptr[e] < len(qs[e]):
                    o = qs[e][ptr[e]]
                    if all(vals.get(s.name, 0) >= v for s, v in o.waits):
                        for s, k in o.incs:
                            vals[s.name] = vals.get(s.name, 0) + k
                        ptr[e] += 1
                        progress = True
                    else:
                        break
        stuck = {e: ptr[e] for e in ENGINES if ptr[e] < len(qs[e])}
        if stuck:
            msg = []
            for e, p in stuck.items():
                o = qs[e][p]
                why = [(s.name, v, vals.get(s.name, 0)) for s, v in o.waits
                       if vals.get(s.name, 0) < v]
                msg.append(f"{e}[{p}/{len(qs[e])}] tag={o.tag} blocked on {why}")
            raise RuntimeError("PLAN DEADLOCK:\n" + "\n".join(msg))

    def replay(self):
        nc = self.nc
        with nc.Block() as block:
            def run_engine(eng_obj, name):
                try:
                    self.pid[name] = eng_obj.partition_id()
                    self.sl1024[name] = self.pid[name] % 4 * 1024
                except Exception:
                    pass
                for o in self.seq:
                    if o.eng != name:
                        continue
                    for s, v in o.waits:
                        eng_obj.wait_ge(s.h, v)
                    if o.fn is not None:
                        inst = o.fn(eng_obj)
                        for s, k in o.incs:
                            inst = inst.then_inc(s.h, k)
                    else:
                        for s, k in o.incs:
                            eng_obj.sem_inc(s.h, k)

            @block.sync
            def _(e):
                run_engine(e, "sync")

            @block.gpsimd
            def _(e):
                run_engine(e, "gpsimd")

            @block.tensor
            def _(e):
                run_engine(e, "tensor")

            @block.scalar
            def _(e):
                run_engine(e, "scalar")

            @block.vector
            def _(e):
                run_engine(e, "vector")


class SbufMap:
    def __init__(self, nc, base, top):
        self.nc = nc
        self.base = base
        self.top = top
        self.items = []

    def add(self, name, shape, dtype, live):
        esz = 4 if dtype == F32 else 2
        bpp = int(np.prod(shape[1:])) * esz
        bpp = (bpp + 63) // 64 * 64
        self.items.append([name, shape, dtype, bpp, live])

    def finalize(self):
        placed = []
        out = {}
        for name, shape, dtype, bpp, live in sorted(self.items, key=lambda it: -it[3]):
            off = 0
            while True:
                clash = None
                for poff, psz, plive, pname in placed:
                    if not (live[1] < plive[0] or plive[1] < live[0]):
                        if not (off + bpp <= poff or poff + psz <= off):
                            clash = poff + psz
                            break
                if clash is None:
                    break
                off = (clash + 63) // 64 * 64
            assert self.base + off + bpp <= self.top, (
                f"SBUF overflow placing {name}: off={off} bpp={bpp}")
            placed.append((off, bpp, live, name))
            out[name] = self.nc.alloc_sbuf_tensor_at(
                name, list(shape), dtype, offset=self.base + off)
        return out


def build_program():
    nc = bass.Bass("TRN2", target_bir_lowering=False)

    x_full = nc.dram_tensor("x_full", [T, M], F32, kind="ExternalInput").ap()
    w_q = nc.dram_tensor("w_q", [128, H * MC * D], F32, kind="ExternalInput").ap()
    w_k = nc.dram_tensor("w_k", [128, MC * D], F32, kind="ExternalInput").ap()
    w_v = nc.dram_tensor("w_v", [128, MC * D], F32, kind="ExternalInput").ap()
    w_o = nc.dram_tensor("w_o", [128, G * H * M], F32, kind="ExternalInput").ap()
    wg_t = nc.dram_tensor("wg_t", [128, MC * HID], F32, kind="ExternalInput").ap()
    wu_t = nc.dram_tensor("wu_t", [128, MC * HID], F32, kind="ExternalInput").ap()
    wd_t = nc.dram_tensor("wd_t", [128, 64 * M], F32, kind="ExternalInput").ap()
    rope1 = nc.dram_tensor("rope1", [128, T], BF16, kind="ExternalInput").ap()
    rope2 = nc.dram_tensor("rope2", [128, T], BF16, kind="ExternalInput").ap()
    masks_i = nc.dram_tensor("masks", [4, 128, 512], BF16, kind="ExternalInput").ap()
    out_own = nc.dram_tensor("out_own", [1024, M], F32, kind="ExternalOutput").ap()

    h_dram = nc.dram_tensor("h_dram", [T, M], BF16)
    h2_dram = nc.dram_tensor("h2_dram", [1024, M], BF16)
    xmid_dram = nc.dram_tensor("xmid_dram", [1024, M], F32)
    oc_dram = [nc.dram_tensor(f"oc{i}_dram", [D, T], BF16) for i in range(H)]
    og_dram = [nc.dram_tensor(f"og{i}_dram", [G * D, T], BF16) for i in range(H)]
    wg_bf = nc.dram_tensor("wg_bf", [128, MC * HID], BF16)
    wu_bf = nc.dram_tensor("wu_bf", [128, MC * HID], BF16)
    wd_bf = nc.dram_tensor("wd_bf", [128, 64 * M], BF16)

    dbg = {}
    if DEBUG:
        dbg["d_h"] = nc.dram_tensor("d_h", [T, M], BF16, kind="ExternalOutput").ap()
        dbg["d_q"] = nc.dram_tensor("d_q", [128, H * T], BF16, kind="ExternalOutput").ap()
        dbg["d_k"] = nc.dram_tensor("d_k", [128, T], BF16, kind="ExternalOutput").ap()
        dbg["d_v"] = nc.dram_tensor("d_v", [128, T], BF16, kind="ExternalOutput").ap()
        dbg["d_o"] = nc.dram_tensor("d_o", [128, H * T], BF16, kind="ExternalOutput").ap()
        dbg["d_xmid"] = nc.dram_tensor("d_xmid", [1024, M], F32, kind="ExternalOutput").ap()
        dbg["d_h2ft"] = nc.dram_tensor("d_h2ft", [128, MC * 1024], BF16, kind="ExternalOutput").ap()
        dbg["d_g"] = nc.dram_tensor("d_g", [128, 16 * 1024], BF16, kind="ExternalOutput").ap()
        dbg["d_r2"] = nc.dram_tensor("d_r2", [128, OTN * 2048], F32, kind="ExternalOutput").ap()

    smap = SbufMap(nc, (nc.sbuf_base + 63) // 64 * 64, nc.sbuf_top)
    smap.add("rope1_sb", [128, T], BF16, (0, 1))
    smap.add("rope2_sb", [128, T], BF16, (0, 1))
    smap.add("masks_sb", [128, 2048], BF16, (0, 2))
    smap.add("ones_bf", [128, 16], BF16, (0, 6))
    smap.add("ones_f", [1, 128], F32, (0, 6))
    smap.add("stats", [128, 64], F32, (0, 6))
    smap.add("recip_sb", [1, 512], F32, (2, 2))
    smap.add("recipb_sb", [1, 512], BF16, (2, 2))
    smap.add("pairt", [128, 1024], BF16, (2, 2))
    smap.add("ones_bfr", [1, 128], BF16, (0, 6))
    smap.add("bcast_sb", [128, 512], F32, (2, 2))
    smap.add("ptile", [128, 32 * 512], BF16, (2, 2))
    smap.add("dump", [128, 2048], F32, (0, 6))
    smap.add("h_ft", [128, 2 * MC * 1024], BF16, (1, 1))
    smap.add("xt", [128, 2 * 2048], F32, (1, 1))
    smap.add("ht", [128, 2 * 2048], BF16, (1, 1))
    smap.add("qtmp", [128, 2 * 1024], BF16, (1, 1))
    smap.add("rtmp", [128, 2048], BF16, (1, 1))
    smap.add("rswp", [128, 1024], BF16, (1, 1))
    smap.add("wq_sb", [128, H * MC * D], BF16, (0, 1))
    smap.add("wk_sb", [128, MC * D], BF16, (0, 1))
    smap.add("wv_sb", [128, MC * D], BF16, (0, 1))
    smap.add("q_ft", [128, H * T], BF16, (1, 2))
    smap.add("k_ft", [128, T], BF16, (1, 2))
    smap.add("v_sb", [128, T], BF16, (1, 2))
    smap.add("o_ft", [128, H * T], BF16, (2, 3))
    smap.add("wo_sb", [128, G * H * M], BF16, (2, 4))
    smap.add("o16_sb", [128, G * H * 1024], BF16, (4, 4))
    smap.add("h2_ft", [128, MC * 1024], BF16, (4, 5))
    smap.add("xr", [128, 2 * 2048], F32, (4, 4))
    smap.add("ln_t", [128, 2 * 2048], F32, (4, 4))
    smap.add("xmid_t", [128, 2 * 2048], F32, (4, 4))
    smap.add("h2t", [128, 2 * 2048], BF16, (4, 4))
    smap.add("g_slab", [128, 16 * 1024], BF16, (5, 5))
    smap.add("r2", [128, OTN * 2048], F32, (5, 6))
    smap.add("wgu", [128, 2 * MC * 256], BF16, (5, 5))
    smap.add("wuu", [128, 2 * MC * 256], BF16, (5, 5))
    smap.add("wdu", [128, 8 * 2048], BF16, (5, 5))
    smap.add("silu_t", [128, 1024], F32, (5, 5))
    smap.add("xr6", [128, 2 * 2048], F32, (6, 6))
    smap.add("ln6", [128, 2 * 2048], F32, (6, 6))
    smap.add("xo6", [128, 2 * 2048], F32, (6, 6))
    sb = smap.finalize()

    P = [nc.alloc_psum_tensor(f"ps{i}", [128, 1024], F32) for i in range(4)]

    pl = Plan(nc)
    sCst = SemT(nc, "sCst")
    sXp = [SemT(nc, "sXa"), SemT(nc, "sXb")]
    sHp = [SemT(nc, "sHa"), SemT(nc, "sHb")]
    sTT = SemT(nc, "sTT")
    sWq = SemT(nc, "sWq")
    sWo = SemT(nc, "sWo")
    sWc = SemT(nc, "sWc")
    sFg = [SemT(nc, "sFga"), SemT(nc, "sFgb")]
    sFu = [SemT(nc, "sFua"), SemT(nc, "sFub")]
    sFdu = [SemT(nc, f"sFd{i}") for i in range(4)]
    sPE = SemT(nc, "sPE")
    sSE = SemT(nc, "sSE")
    sVE = SemT(nc, "sVE")
    sCCh = [SemT(nc, f"sCC{i}") for i in range(H)]
    sC1h = [SemT(nc, f"sC1{i}") for i in range(H)]
    sG = SemT(nc, "sG")
    sXRp = [SemT(nc, "sXRa"), SemT(nc, "sXRb")]
    sXMp = [SemT(nc, "sXMa"), SemT(nc, "sXMb")]
    sSTp = [SemT(nc, "sSTa"), SemT(nc, "sSTb")]
    sB = SemT(nc, "sB")

    rope1_sb, rope2_sb = sb["rope1_sb"], sb["rope2_sb"]
    masks_sb, stats, dump = sb["masks_sb"], sb["stats"], sb["dump"]
    h_ft, q_ft, k_ft, v_sb, o_ft = (sb["h_ft"], sb["q_ft"], sb["k_ft"],
                                    sb["v_sb"], sb["o_ft"])

    pl.op("vector", lambda e: e.memset(sb["ones_bf"][:, :], 1.0))
    pl.op("vector", lambda e: e.memset(sb["ones_f"][:, :], 1.0))
    pl.op("vector", lambda e: e.memset(sb["ones_bfr"][:, :], 1.0), inc=[(sVE, 1)])
    ones_ready = sVE.inc()
    pl.op("sync", lambda e: e.dma_start(out=rope1_sb[:, :], in_=rope1), inc=[(sCst, 16)])
    sCst.inc(16)
    pl.op("sync", lambda e: e.dma_start(out=rope2_sb[:, :], in_=rope2), inc=[(sCst, 16)])
    sCst.inc(16)
    pl.op("sync",
          lambda e: e.dma_start(out=masks_sb[:, :].rearrange("p (r i) -> p r i", r=4),
                                in_=masks_i.rearrange("r p i -> p r i")),
          inc=[(sCst, 16)])
    rope_ready = sCst.inc(16)

    pl.op("gpsimd", lambda e: e.dma_start(out=sb["wk_sb"][:, :], in_=w_k),
          inc=[(sWq, 16)], tag="wk")
    sWq.inc(16)
    pl.op("gpsimd", lambda e: e.dma_start(out=sb["wv_sb"][:, :], in_=w_v),
          inc=[(sWq, 16)], tag="wv")
    sWq.inc(16)
    pl.op("gpsimd", lambda e: e.dma_start(out=sb["wq_sb"][:, :], in_=w_q),
          inc=[(sWq, 16)], tag="wq")
    sWq.inc(16)
    wqkv_ready = sWq.n

    xt, ht = sb["xt"], sb["ht"]
    ln_done = [0] * NT
    h_mark = [0] * NT

    def emit_ln_tile(ti):
        slot = ti % 2
        xa = xt[:, slot * 2048:(slot + 1) * 2048]
        ha = ht[:, slot * 2048:(slot + 1) * 2048]
        w = [(sVE, ln_done[ti - 2])] if ti >= 2 else []
        pl.op("sync", lambda e, ti=ti, xa=xa: e.dma_start(
            out=xa, in_=x_full[ti * 128:(ti + 1) * 128, :]),
            wait=w, inc=[(sXp[slot], 16)], tag=f"x{ti}")
        xv = sXp[slot].inc(16)
        sl = slot * 8
        sx = stats[:, sl:sl + 1]
        sxx = stats[:, sl + 1:sl + 2]
        v0 = stats[:, sl + 2:sl + 3]
        negm = stats[:, sl + 3:sl + 4]
        rstd = stats[:, sl + 4:sl + 5]
        pl.op("scalar", lambda e, xa=xa, sx=sx: e.activation(
            dump[:, :], xa, AF.Copy, accum_out=sx),
            wait=[(sXp[slot], xv)], tag=f"ln{ti}-sx")
        pl.op("scalar", lambda e, xa=xa, sxx=sxx: e.activation(
            dump[:, :], xa, AF.Square, accum_out=sxx), inc=[(sSE, 1)], tag=f"ln{ti}-sxx")
        m_se = sSE.inc()
        pl.op("vector", lambda e, sx=sx, v0=v0: e.tensor_mul(v0, sx, sx),
              wait=[(sSE, m_se)])
        pl.op("vector", lambda e, v0=v0, sxx=sxx: e.scalar_tensor_tensor(
            v0, v0, -1.0 / 2048, sxx, ALU.mult, ALU.add))
        pl.op("vector", lambda e, sx=sx, negm=negm: e.tensor_scalar(
            negm, sx, -1.0 / 2048, None, ALU.mult), inc=[(sVE, 1)], tag=f"ln{ti}-v")
        m_ve = sVE.inc()
        pl.op("scalar", lambda e, v0=v0: e.activation(v0, v0, AF.Ln,
                                                      scale=np.float32(1.0 / 2048), bias=np.float32(EPS)),
              wait=[(sVE, m_ve)])
        pl.op("scalar", lambda e, v0=v0, rstd=rstd: e.activation(
            rstd, v0, AF.Exp, scale=np.float32(-0.5)), inc=[(sSE, 1)], tag=f"ln{ti}-r")
        m_se2 = sSE.inc()
        wn = [(sSE, m_se2)]
        if ti >= 2 and h_mark[ti - 2]:
            wn.append((sHp[slot], h_mark[ti - 2]))
        pl.op("vector", lambda e, ha=ha, xa=xa, negm=negm, rstd=rstd: e.tensor_scalar(
            ha, xa, negm, rstd, ALU.add, ALU.mult),
            wait=wn, inc=[(sVE, 1)], tag=f"ln{ti}-n")
        ln_done[ti] = sVE.inc()
        pl.op("sync", lambda e, ti=ti, ha=ha: e.dma_start(
            out=h_dram.ap()[ti * 128:(ti + 1) * 128, :], in_=ha),
            wait=[(sVE, ln_done[ti])], inc=[(sHp[slot], 16)], tag=f"h{ti}")
        h_mark[ti] = sHp[slot].inc(16)

    qkv_state = {"unit": 0, "vunit": 0, "sse_unit": [0, 0], "sve_unit": [0, 0],
                 "sse_v": [0, 0]}
    trans_done = [0] * 4

    def emit_qkv_quarter(qu):
        tbase = qu * 1024
        t_ready = sTT.n
        hbase = (qu % 2) * MC * 1024
        for u in range(5):
            ps = P[qkv_state["unit"] % 2]
            w = [(sTT, t_ready)]
            w.append((sWq, wqkv_ready))
            prev = qkv_state["sse_unit"][qkv_state["unit"] % 2]
            if prev:
                w.append((sSE, prev))
            for mc in range(MC):
                if u < 4:
                    lhsT = sb["wq_sb"][:, (u * MC + mc) * D:(u * MC + mc + 1) * D]
                else:
                    lhsT = sb["wk_sb"][:, mc * D:(mc + 1) * D]
                for half in range(2):
                    pl.op("tensor", lambda e, lhsT=lhsT, half=half, mc=mc, ps=ps, hbase=hbase:
                          e.matmul(
                        ps[:, half * 512:(half + 1) * 512], lhsT,
                        h_ft[:, hbase + mc * 1024 + half * 512:
                             hbase + mc * 1024 + (half + 1) * 512],
                        start=(mc == 0), stop=(mc == MC - 1)),
                        wait=(w if (mc == 0 and half == 0) else []),
                        inc=([(sPE, 1)] if (mc == MC - 1 and half == 1) else []),
                        tag=f"qkvM{qu}.{u}")
            m_pe = sPE.inc()
            qslot = qkv_state["unit"] % 2
            qa = sb["qtmp"][:, qslot * 1024:(qslot + 1) * 1024]
            wse = [(sPE, m_pe)]
            prev_ve = qkv_state["sve_unit"][qslot]
            if prev_ve:
                wse.append((sVE, prev_ve))
            pl.op("scalar", lambda e, qa=qa, ps=ps: e.copy(qa, ps[:, :]),
                  wait=wse, inc=[(sSE, 1)], tag=f"qkvE{qu}.{u}")
            qkv_state["sse_unit"][qslot] = sSE.inc()
            if u < 4:
                dst = q_ft[:, u * T + tbase:u * T + tbase + 1024]
            else:
                dst = k_ft[:, tbase:tbase + 1024]
            cs1 = rope1_sb[:, tbase:tbase + 1024]
            cs2 = rope2_sb[:, tbase:tbase + 1024]
            rt1 = sb["rtmp"][:, 0:1024]
            rt2 = sb["rtmp"][:, 1024:2048]
            sw = sb["rswp"]
            m_se = qkv_state["sse_unit"][qslot]
            pl.op("vector", lambda e, qa=qa, cs1=cs1, rt1=rt1: e.tensor_mul(rt1, qa, cs1),
                  wait=[(sSE, m_se), (sCst, rope_ready)], tag=f"rp{qu}.{u}")
            pl.op("vector", lambda e, qa=qa, cs2=cs2, rt2=rt2: e.tensor_mul(rt2, qa, cs2))
            pl.op("vector", lambda e, rt1=rt1, sw=sw: e.tensor_copy(
                sw[0:64, :], rt1[64:128, :]))
            pl.op("vector", lambda e, rt2=rt2, sw=sw: e.tensor_copy(
                sw[64:128, :], rt2[0:64, :]))
            pl.op("vector", lambda e, dst=dst, rt1=rt1, sw=sw: e.tensor_sub(
                dst[0:64, :], rt1[0:64, :], sw[0:64, :]))
            pl.op("vector", lambda e, dst=dst, rt2=rt2, sw=sw: e.tensor_add(
                dst[64:128, :], rt2[64:128, :], sw[64:128, :]),
                inc=[(sVE, 1)], tag=f"rpd{qu}.{u}")
            qkv_state["sve_unit"][qslot] = sVE.inc()
            qkv_state["unit"] += 1
        for tt in range(8):
            vp = P[2 + qkv_state["vunit"] % 2]
            w = [(sTT, t_ready)]
            prev = qkv_state["sse_v"][qkv_state["vunit"] % 2]
            if prev:
                w.append((sSE, prev))
            for mc in range(MC):
                lhsT = h_ft[:, hbase + mc * 1024 + tt * 128:
                            hbase + mc * 1024 + (tt + 1) * 128]
                pl.op("tensor", lambda e, lhsT=lhsT, mc=mc, vp=vp: e.matmul(
                    vp[:, 0:128], lhsT, sb["wv_sb"][:, mc * D:(mc + 1) * D],
                    start=(mc == 0), stop=(mc == MC - 1)),
                    wait=(w if mc == 0 else []),
                    inc=([(sPE, 1)] if mc == MC - 1 else []), tag=f"vM{qu}.{tt}")
            m_pe = sPE.inc()
            gt = qu * 8 + tt
            pl.op("scalar", lambda e, gt=gt, vp=vp: e.copy(
                v_sb[:, gt * 128:(gt + 1) * 128], vp[:, 0:128]),
                wait=[(sPE, m_pe)], inc=[(sSE, 1)], tag=f"vE{qu}.{tt}")
            qkv_state["sse_v"][qkv_state["vunit"] % 2] = sSE.inc()
            qkv_state["vunit"] += 1
        trans_done[qu] = sPE.n

    def emit_transposes(qu):
        hbase = (qu % 2) * MC * 1024
        for mc in range(MC):
            w = []
            if mc == 0:
                w.append((sHp[0], sHp[0].n))
                w.append((sHp[1], sHp[1].n))
                if qu >= 2:
                    w.append((sPE, trans_done[qu - 2]))
            pl.op("sync", lambda e, qu=qu, mc=mc, hbase=hbase: e.dma_start(
                out=h_ft[:, hbase + mc * 1024:hbase + (mc + 1) * 1024],
                in_=h_dram.ap()[qu * 1024:(qu + 1) * 1024, mc * 128:(mc + 1) * 128],
                transpose=True),
                wait=w, inc=[(sTT, 16)], tag=f"T{qu}.{mc}")
            sTT.inc(16)

    for ti in range(8):
        emit_ln_tile(ti)
    emit_transposes(0)
    for ti in range(8, 16):
        emit_ln_tile(ti)
    emit_qkv_quarter(0)
    emit_transposes(1)
    for ti in range(16, 24):
        emit_ln_tile(ti)
    emit_qkv_quarter(1)
    emit_transposes(2)
    for ti in range(24, 32):
        emit_ln_tile(ti)
    emit_qkv_quarter(2)
    emit_transposes(3)
    emit_qkv_quarter(3)

    if DEBUG:
        pl.op("sync", lambda e: e.dma_start(out=dbg["d_h"], in_=h_dram.ap()),
              wait=[(sHp[0], sHp[0].n), (sHp[1], sHp[1].n)], inc=[(sSTp[0], 16)])
        sSTp[0].inc(16)

    pl.barrier(sB)
    pl.op("gpsimd", lambda e: e.dma_start(out=sb["wo_sb"][:, :], in_=w_o),
          inc=[(sWo, 16)], tag="wo")
    sWo.inc(16)
    wo_ready = sWo.n
    for nm, srcw, dstw in (("wg", wg_t, wg_bf), ("wu", wu_t, wu_bf), ("wd", wd_t, wd_bf)):
        pl.op("gpsimd", lambda e, srcw=srcw, dstw=dstw: e.dma_start(out=dstw.ap(), in_=srcw),
              inc=[(sWc, 16)], tag=f"cast-{nm}")
        sWc.inc(16)
    ffn_cast_ready = sWc.n
    ptile = sb["ptile"]
    att = {"jglob": 0, "norm_done": [0, 0], "recip_done": [0, 0], "bcast_copy": 0,
           "exp_mark": {}, "pv_done": {}}

    if DEBUG:
        pl.op("sync", lambda e: e.dma_start(out=dbg["d_q"], in_=q_ft[:, :]),
              inc=[(sSTp[0], 16)])
        sSTp[0].inc(16)
        pl.op("sync", lambda e: e.dma_start(out=dbg["d_k"], in_=k_ft[:, :]),
              inc=[(sSTp[0], 16)])
        sSTp[0].inc(16)
        pl.op("sync", lambda e: e.dma_start(out=dbg["d_v"], in_=v_sb[:, :]),
              inc=[(sSTp[0], 16)])
        sSTp[0].inc(16)

    unit = 0
    head_norm = [0] * H
    att["pair_mark"] = [0, 0]
    tail_q = []

    def emit_tail(tu):
        (t_spp, t_opp, t_hd, t_q0, t_mpe) = tu
        pl.op("vector", lambda e, t_spp=t_spp: e.reciprocal(
            sb["recip_sb"][0:1, :], P[2][0:1, t_spp * 512:(t_spp + 1) * 512]),
            wait=[(sPE, t_mpe)], inc=[(sVE, 1)], tag="aR")
        att["recip_done"][t_spp] = sVE.inc()
        pl.op("scalar", lambda e: e.copy(sb["recipb_sb"][0:1, :], sb["recip_sb"][0:1, :]),
              wait=[(sVE, att["recip_done"][t_spp])], inc=[(sSE, 1)], tag="aRC")
        m_rc = sSE.inc()
        wb = [(sSE, m_rc)]
        if att["bcast_copy"]:
            wb.append((sSE, att["bcast_copy"]))
        pl.op("tensor", lambda e: e.matmul(
            P[3][:, 0:512], sb["ones_bfr"][0:1, :], sb["recipb_sb"][0:1, :],
            start=True, stop=True),
            wait=wb, inc=[(sPE, 1)], tag="aB")
        m_b = sPE.inc()
        pl.op("scalar", lambda e: e.copy(sb["bcast_sb"][:, :], P[3][:, 0:512]),
              wait=[(sPE, m_b)], inc=[(sSE, 1)], tag="aC")
        att["bcast_copy"] = sSE.inc()
        pl.op("vector", lambda e, t_hd=t_hd, t_q0=t_q0, t_opp=t_opp: e.tensor_mul(
            o_ft[:, t_hd * T + t_q0:t_hd * T + t_q0 + 512],
            P[1][:, t_opp * 512:(t_opp + 1) * 512],
            sb["bcast_sb"][:, :]),
            wait=[(sSE, att["bcast_copy"])], inc=[(sVE, 1)], tag="aN")
        att["norm_done"][t_opp] = sVE.inc()

    for hd in range(H):
        for qc in range(8):
            jmax = 4 * (qc + 1)
            q0 = 512 * qc
            opp = unit % 2
            spp = unit % 2
            p_ready = [None] * jmax
            pv_prev = att.get("pv_marks", [None] * 32)
            pv_marks = [None] * 32
            for j in range(jmax):
                jg = att["jglob"] + j
                sp = jg % 2
                w = []
                pe_prev = att["exp_mark"].get(jg - 2)
                if pe_prev:
                    w.append(pe_prev)
                pl.op("tensor", lambda e, sp=sp, j=j, hd=hd, q0=q0: e.matmul(
                    P[0][:, sp * 512:(sp + 1) * 512],
                    k_ft[:, j * 128:(j + 1) * 128],
                    q_ft[:, hd * T + q0:hd * T + q0 + 512],
                    start=True, stop=True),
                    wait=w, inc=[(sPE, 1)], tag=f"aS{unit}.{j}")
                m_s = sPE.inc()
                wse = [(sPE, m_s)]
                if pv_prev[j] is not None:
                    wse.append((sPE, pv_prev[j]))
                pl.op("scalar", lambda e, j=j, sp=sp: e.activation(
                    ptile[:, j * 512:(j + 1) * 512],
                    P[0][:, sp * 512:(sp + 1) * 512],
                    AF.Exp, scale=np.float32(QK2), bias=np.float32(-SHIFT)),
                    wait=wse, inc=[(sSE, 1)], tag=f"aE{unit}.{j}")
                m_e = sSE.inc()
                if j >= jmax - 4:
                    r = j - (jmax - 4)
                    pl.op("vector", lambda e, j=j, r=r: e.tensor_mul(
                        ptile[:, j * 512:(j + 1) * 512],
                        ptile[:, j * 512:(j + 1) * 512],
                        masks_sb[:, r * 512:(r + 1) * 512]),
                        wait=[(sSE, m_e)], inc=[(sVE, 1)], tag=f"aM{unit}.{j}")
                    p_ready[j] = (sVE, sVE.inc())
                else:
                    p_ready[j] = (sSE, m_e)
                att["exp_mark"][jg] = p_ready[j]
            if tail_q:
                emit_tail(tail_q.pop())
            npairs = jmax // 2
            for pr in range(npairs):
                j0, j1 = 2 * pr, 2 * pr + 1
                ppp = pr % 2
                wp = [p_ready[j0], p_ready[j1]]
                if att["pair_mark"][ppp]:
                    wp.append((sPE, att["pair_mark"][ppp]))
                pl.op("vector", lambda e, j0=j0, j1=j1, ppp=ppp: e.tensor_add(
                    sb["pairt"][:, ppp * 512:(ppp + 1) * 512],
                    ptile[:, j0 * 512:(j0 + 1) * 512],
                    ptile[:, j1 * 512:(j1 + 1) * 512]),
                    wait=wp, inc=[(sVE, 1)], tag=f"aA{unit}.{pr}")
                m_pair = sVE.inc()
                w2 = [(sVE, m_pair)]
                if pr == 0:
                    if att["recip_done"][spp]:
                        w2.append((sVE, att["recip_done"][spp]))
                    if att["norm_done"][opp]:
                        w2.append((sVE, att["norm_done"][opp]))
                pl.op("tensor", lambda e, spp=spp, ppp=ppp, pr=pr, npairs=npairs: e.matmul(
                    P[2][0:1, spp * 512:(spp + 1) * 512],
                    sb["ones_bf"][:, 0:1],
                    sb["pairt"][:, ppp * 512:(ppp + 1) * 512],
                    start=(pr == 0), stop=(pr == npairs - 1)),
                    wait=w2, tag=f"aU{unit}.{pr}")
                for j in (j0, j1):
                    pl.op("tensor", lambda e, opp=opp, j=j, jmax=jmax: e.matmul(
                        P[1][:, opp * 512:(opp + 1) * 512],
                        v_sb[:, j * 128:(j + 1) * 128],
                        ptile[:, j * 512:(j + 1) * 512],
                        start=(j == 0), stop=(j == jmax - 1)),
                        wait=([p_ready[j]] if j == j1 else []),
                        inc=[(sPE, 1)], tag=f"aP{unit}.{j}")
                    pv_marks[j] = sPE.inc()
                att["pair_mark"][ppp] = pv_marks[j1]
            att["pv_marks"] = pv_marks
            tail_q.append((spp, opp, hd, q0, pv_marks[jmax - 1]))
            att["jglob"] += jmax
            unit += 1
        if tail_q:
            emit_tail(tail_q.pop())
        head_norm[hd] = sVE.n
    o_done = sVE.n

    if DEBUG:
        pl.op("sync", lambda e: e.dma_start(out=dbg["d_o"], in_=o_ft[:, :]),
              wait=[(sVE, o_done)], inc=[(sSTp[0], 16)])
        sSTp[0].inc(16)

    for hd in range(H):
        pl.op("sync", lambda e, hd=hd: e.dma_start(
            out=oc_dram[hd].ap(), in_=o_ft[:, hd * T:(hd + 1) * T]),
            wait=[(sVE, head_norm[hd])], inc=[(sC1h[hd], 16)], tag=f"oc{hd}")
        sC1h[hd].inc(16)
        pl.op("gpsimd", lambda e, hd=hd: e.collective_compute(
            "AllGather", ALU.bypass,
            replica_groups=[[0, 1, 2, 3], [4, 5, 6, 7]],
            ins=[oc_dram[hd].ap()], outs=[og_dram[hd].ap()]),
            wait=[(sC1h[hd], 16)], inc=[(sCCh[hd], 1)], tag=f"allgather{hd}")
        sCCh[hd].inc(1)

    pl.barrier(sB)

    for hd in range(H):
        pl.op("sync", lambda e, hd=hd: e.dma_start(
            out=sb["o16_sb"][:, :].rearrange("p (gh t) -> p gh t", gh=G * H)[
                :, hd::H, :],
            in_=og_dram[hd].ap().rearrange("(g d) t -> d g t", d=128)[
                :, :, bass.ds(pl.sl1024["sync"], 1024)]),
            wait=[(sCCh[hd], 1)], inc=[(sG, 16)], tag=f"gather{hd}")
        sG.inc(16)

    p4 = {"psum_free": [0, 0], "h2dma": [0, 0], "xmdma": [0, 0], "xr_used": [0, 0]}
    for ot in range(OTN):
        pp = ot % 2
        PA, PB = (P[0], P[1]) if pp == 0 else (P[2], P[3])
        wxr = [(sVE, p4["xr_used"][pp])] if p4["xr_used"][pp] else []
        pl.op("sync", lambda e, ot=ot, pp=pp: e.dma_start(
            out=sb["xr"][:, pp * 2048:(pp + 1) * 2048],
            in_=x_full[bass.ds(pl.sl1024["sync"] + ot * 128, 128), :]),
            wait=wxr, inc=[(sXRp[pp], 16)], tag=f"xr{ot}")
        m_xr = sXRp[pp].inc(16)
        w = [(sG, sG.n), (sWo, wo_ready)]
        if p4["psum_free"][pp]:
            w.append((sVE, p4["psum_free"][pp]))
        for gh in range(G * H):
            lhsT = sb["o16_sb"][:, gh * 1024 + ot * 128:gh * 1024 + (ot + 1) * 128]
            for k in range(4):
                tgt = PA if k < 2 else PB
                pl.op("tensor", lambda e, lhsT=lhsT, gh=gh, k=k, tgt=tgt: e.matmul(
                    tgt[:, (k % 2) * 512:(k % 2 + 1) * 512], lhsT,
                    sb["wo_sb"][:, gh * M + k * 512:gh * M + (k + 1) * 512],
                    start=(gh == 0), stop=(gh == G * H - 1)),
                    wait=(w if (gh == 0 and k == 0) else []),
                    inc=([(sPE, 1)] if (gh == G * H - 1 and k == 3) else []),
                    tag=f"oM{ot}.{gh}.{k}")
        m_pe = sPE.inc()
        sl = pp * 16
        sxa, sxb = stats[:, sl:sl + 1], stats[:, sl + 1:sl + 2]
        sxxa, sxxb = stats[:, sl + 2:sl + 3], stats[:, sl + 3:sl + 4]
        v0 = stats[:, sl + 4:sl + 5]
        negm = stats[:, sl + 5:sl + 6]
        rstd = stats[:, sl + 6:sl + 7]
        pl.op("scalar", lambda e, PA=PA, sxa=sxa: e.activation(
            dump[:, 0:1024], PA[:, :], AF.Copy, accum_out=sxa),
            wait=[(sPE, m_pe)], tag=f"rA{ot}")
        pl.op("scalar", lambda e, PB=PB, sxb=sxb: e.activation(
            dump[:, 0:1024], PB[:, :], AF.Copy, accum_out=sxb))
        pl.op("scalar", lambda e, PA=PA, sxxa=sxxa: e.activation(
            dump[:, 0:1024], PA[:, :], AF.Square, accum_out=sxxa))
        pl.op("scalar", lambda e, PB=PB, sxxb=sxxb: e.activation(
            dump[:, 0:1024], PB[:, :], AF.Square, accum_out=sxxb),
            inc=[(sSE, 1)], tag=f"rS{ot}")
        m_se = sSE.inc()
        pl.op("vector", lambda e, sxa=sxa, sxb=sxb: e.tensor_add(sxa, sxa, sxb),
              wait=[(sSE, m_se)])
        pl.op("vector", lambda e, sxxa=sxxa, sxxb=sxxb: e.tensor_add(sxxa, sxxa, sxxb))
        pl.op("vector", lambda e, sxa=sxa, v0=v0: e.tensor_mul(v0, sxa, sxa))
        pl.op("vector", lambda e, v0=v0, sxxa=sxxa: e.scalar_tensor_tensor(
            v0, v0, -1.0 / 2048, sxxa, ALU.mult, ALU.add))
        pl.op("vector", lambda e, sxa=sxa, negm=negm: e.tensor_scalar(
            negm, sxa, -1.0 / 2048, None, ALU.mult), inc=[(sVE, 1)], tag=f"rV{ot}")
        m_ve = sVE.inc()
        pl.op("scalar", lambda e, v0=v0: e.activation(v0, v0, AF.Ln,
                                                      scale=np.float32(1.0 / 2048), bias=np.float32(EPS)),
              wait=[(sVE, m_ve)])
        pl.op("scalar", lambda e, v0=v0, rstd=rstd: e.activation(
            rstd, v0, AF.Exp, scale=np.float32(-0.5)), inc=[(sSE, 1)], tag=f"rR{ot}")
        m_se2 = sSE.inc()
        lnta = sb["ln_t"][:, pp * 2048:pp * 2048 + 1024]
        lntb = sb["ln_t"][:, pp * 2048 + 1024:(pp + 1) * 2048]
        pl.op("vector", lambda e, PA=PA, lnta=lnta, negm=negm, rstd=rstd: e.tensor_scalar(
            lnta, PA[:, :], negm, rstd, ALU.add, ALU.mult),
            wait=[(sSE, m_se2)], tag=f"rlA{ot}")
        pl.op("vector", lambda e, PB=PB, lntb=lntb, negm=negm, rstd=rstd: e.tensor_scalar(
            lntb, PB[:, :], negm, rstd, ALU.add, ALU.mult),
            inc=[(sVE, 1)], tag=f"rlB{ot}")
        p4["psum_free"][pp] = sVE.inc()
        sx2 = stats[:, sl + 7:sl + 8]
        xma = sb["xmid_t"][:, pp * 2048:(pp + 1) * 2048]
        lnt = sb["ln_t"][:, pp * 2048:(pp + 1) * 2048]
        xra = sb["xr"][:, pp * 2048:(pp + 1) * 2048]
        wxm = [(sXRp[pp], m_xr)]
        if p4["xmdma"][pp]:
            wxm.append((sXMp[pp], p4["xmdma"][pp]))
        pl.op("vector", lambda e, xma=xma, lnt=lnt, xra=xra, sx2=sx2:
              e.scalar_tensor_tensor(xma, lnt, 1.0, xra, ALU.mult, ALU.add,
                                     accum_out=sx2),
              wait=wxm, inc=[(sVE, 1)], tag=f"xm{ot}")
        p4["xr_used"][pp] = sVE.inc()
        m_xmid = p4["xr_used"][pp]
        sxx2 = stats[:, sl + 8:sl + 9]
        v02 = stats[:, sl + 9:sl + 10]
        negm2 = stats[:, sl + 10:sl + 11]
        rstd2 = stats[:, sl + 11:sl + 12]
        pl.op("scalar", lambda e, xma=xma, sxx2=sxx2: e.activation(
            dump[:, :], xma, AF.Square, accum_out=sxx2),
            wait=[(sVE, m_xmid)], inc=[(sSE, 1)], tag=f"l2S{ot}")
        m_se3 = sSE.inc()
        pl.op("vector", lambda e, sx2=sx2, v02=v02: e.tensor_mul(v02, sx2, sx2),
              wait=[(sSE, m_se3)])
        pl.op("vector", lambda e, v02=v02, sxx2=sxx2: e.scalar_tensor_tensor(
            v02, v02, -1.0 / 2048, sxx2, ALU.mult, ALU.add))
        pl.op("vector", lambda e, sx2=sx2, negm2=negm2: e.tensor_scalar(
            negm2, sx2, -1.0 / 2048, None, ALU.mult), inc=[(sVE, 1)], tag=f"l2V{ot}")
        m_ve2 = sVE.inc()
        pl.op("scalar", lambda e, v02=v02: e.activation(v02, v02, AF.Ln,
                                                        scale=np.float32(1.0 / 2048), bias=np.float32(EPS)),
              wait=[(sVE, m_ve2)])
        pl.op("scalar", lambda e, v02=v02, rstd2=rstd2: e.activation(
            rstd2, v02, AF.Exp, scale=np.float32(-0.5)), inc=[(sSE, 1)], tag=f"l2R{ot}")
        m_se4 = sSE.inc()
        h2a = sb["h2t"][:, pp * 2048:(pp + 1) * 2048]
        wh2 = [(sSE, m_se4)]
        if p4["h2dma"][pp]:
            wh2.append((sXMp[pp], p4["h2dma"][pp]))
        pl.op("vector", lambda e, h2a=h2a, xma=xma, negm2=negm2, rstd2=rstd2:
              e.tensor_scalar(h2a, xma, negm2, rstd2, ALU.add, ALU.mult),
              wait=wh2, inc=[(sVE, 1)], tag=f"h2{ot}")
        m_h2 = sVE.inc()
        pl.op("sync", lambda e, ot=ot, xma=xma: e.dma_start(
            out=xmid_dram.ap()[ot * 128:(ot + 1) * 128, :], in_=xma),
            wait=[(sVE, m_h2)], inc=[(sXMp[pp], 16)], tag=f"sxm{ot}")
        p4["xmdma"][pp] = sXMp[pp].inc(16)
        pl.op("sync", lambda e, ot=ot, h2a=h2a: e.dma_start(
            out=h2_dram.ap()[ot * 128:(ot + 1) * 128, :], in_=h2a),
            inc=[(sXMp[pp], 16)], tag=f"sh2{ot}")
        p4["h2dma"][pp] = sXMp[pp].inc(16)
    xm_all_w = [(sXMp[0], sXMp[0].n), (sXMp[1], sXMp[1].n)]

    if DEBUG:
        pl.op("sync", lambda e: e.dma_start(out=dbg["d_xmid"], in_=xmid_dram.ap()),
              wait=xm_all_w, inc=[(sSTp[0], 16)])
        sSTp[0].inc(16)

    for mc in range(MC):
        pl.op("sync", lambda e, mc=mc: e.dma_start(
            out=sb["h2_ft"][:, mc * 1024:(mc + 1) * 1024],
            in_=h2_dram.ap()[:, mc * 128:(mc + 1) * 128], transpose=True),
            wait=(xm_all_w if mc == 0 else []), inc=[(sTT, 16)], tag=f"T2.{mc}")
        sTT.inc(16)
    h2ft_ready = sTT.n

    if DEBUG:
        pl.op("sync", lambda e: e.dma_start(out=dbg["d_h2ft"], in_=sb["h2_ft"][:, :]),
              wait=[(sTT, h2ft_ready)], inc=[(sSTp[0], 16)])
        sSTp[0].inc(16)

    pl.barrier(sB)
    ffn = {"gu_pe": {}, "gu_used": [0, 0], "silu_prev": 0,
           "r2_evac": [0, 0], "down_pe_all": 0}
    load_marks = {}

    def load_gu(which, slab, u):
        gidx = slab * 8 + u
        slot = gidx % 2
        h0 = slab * 2048 + u * 256
        dst = sb["wgu"] if which == "wg" else sb["wuu"]
        srcw = wg_bf if which == "wg" else wu_bf
        sem = sFg[slot] if which == "wg" else sFu[slot]
        wk_ = []
        if gidx >= 2 and (gidx - 2) in ffn["gu_pe"]:
            wk_.append((sPE, ffn["gu_pe"][gidx - 2]))
        if gidx == 0:
            wk_.append((sWc, ffn_cast_ready))
        pl.op("sync", lambda e, dst=dst, srcw=srcw, h0=h0, slot=slot: e.dma_start(
            out=dst[:, slot * MC * 256:(slot + 1) * MC * 256].rearrange(
                "p (mc h) -> p mc h", mc=MC),
            in_=srcw.ap().rearrange("p (mc h) -> p mc h", mc=MC)[:, :, h0:h0 + 256]),
            wait=wk_, inc=[(sem, 16)], tag=f"u{which}{slab}.{u}")
        return (sem, sem.inc(16))

    def load_wd_half(slab, hs):
        marks = []
        for ul in range(4):
            h0 = slab * 2048 + hs * 1024 + ul * 256
            wk_ = []
            if ul == 0:
                if ffn["down_pe_all"]:
                    wk_.append((sPE, ffn["down_pe_all"]))
                if slab == 0 and hs == 0:
                    wk_.append((sWc, ffn_cast_ready))
            hc0 = h0 // 128
            pl.op("sync", lambda e, hc0=hc0, ul=ul: e.dma_start(
                out=sb["wdu"][:, ul * 2 * 2048:(ul + 1) * 2 * 2048].rearrange(
                    "p (hs2 m) -> p hs2 m", hs2=2),
                in_=wd_bf.ap().rearrange("p (hc m) -> p hc m", hc=64)[:, hc0:hc0 + 2, :]),
                wait=wk_, inc=[(sFdu[ul], 16)], tag=f"uwd{slab}.{hs}.{ul}")
            marks.append((sFdu[ul], sFdu[ul].inc(16)))
        return marks

    for slab in range(4):
        wd_mark = load_wd_half(slab, 0)
        for hc in range(16):
            u, sub = hc // 2, hc % 2
            if sub == 0:
                load_marks[("wg", slab, u)] = load_gu("wg", slab, u)
                load_marks[("wu", slab, u)] = load_gu("wu", slab, u)
            gp = P[0] if hc % 2 == 0 else P[2]
            up = P[1] if hc % 2 == 0 else P[3]
            gidx = slab * 8 + u
            slot = gidx % 2
            w = [load_marks[("wg", slab, u)], load_marks[("wu", slab, u)]]
            if slab == 0 and hc == 0:
                w.append((sTT, h2ft_ready))
            prev = ffn["gu_used"][hc % 2]
            if prev:
                w.append((sVE, prev))
            if slab > 0 and hc < 2:
                if ffn["r2_evac"][0]:
                    w.append((sVE, ffn["r2_evac"][0]))
                if ffn["r2_evac"][1]:
                    w.append((sVE, ffn["r2_evac"][1]))
            for mc in range(MC):
                base = slot * MC * 256 + mc * 256 + sub * 128
                lg = sb["wgu"][:, base:base + 128]
                lu = sb["wuu"][:, base:base + 128]
                for half in range(2):
                    pl.op("tensor", lambda e, lg=lg, gp=gp, mc=mc, half=half: e.matmul(
                        gp[:, half * 512:(half + 1) * 512], lg,
                        sb["h2_ft"][:, mc * 1024 + half * 512:mc * 1024 + (half + 1) * 512],
                        start=(mc == 0), stop=(mc == MC - 1)),
                        wait=(w if (mc == 0 and half == 0) else []),
                        inc=([(sPE, 1)] if (mc == MC - 1 and half == 1) else []),
                        tag=f"gM{slab}.{hc}")
                for half in range(2):
                    pl.op("tensor", lambda e, lu=lu, up=up, mc=mc, half=half: e.matmul(
                        up[:, half * 512:(half + 1) * 512], lu,
                        sb["h2_ft"][:, mc * 1024 + half * 512:mc * 1024 + (half + 1) * 512],
                        start=(mc == 0), stop=(mc == MC - 1)),
                        inc=([(sPE, 1)] if (mc == MC - 1 and half == 1) else []),
                        tag=f"uM{slab}.{hc}")
            m_gate = sPE.inc()
            m_up = sPE.inc()
            if sub == 1:
                ffn["gu_pe"][gidx] = m_up
            wse = [(sPE, m_gate)]
            if ffn["silu_prev"]:
                wse.append((sVE, ffn["silu_prev"]))
            pl.op("scalar", lambda e, gp=gp: e.activation(
                sb["silu_t"][:, :], gp[:, :], AF.Silu),
                wait=wse, inc=[(sSE, 1)], tag=f"si{slab}.{hc}")
            m_si = sSE.inc()
            pl.op("vector", lambda e, up=up, hc=hc: e.tensor_mul(
                sb["g_slab"][:, hc * 1024:(hc + 1) * 1024], up[:, :],
                sb["silu_t"][:, :]),
                wait=[(sSE, m_si), (sPE, m_up)], inc=[(sVE, 1)], tag=f"gm{slab}.{hc}")
            ffn["gu_used"][hc % 2] = sVE.inc()
            ffn["silu_prev"] = ffn["gu_used"][hc % 2]
        g_ready = sVE.n
        if DEBUG and slab == 0:
            pl.op("sync", lambda e: e.dma_start(out=dbg["d_g"], in_=sb["g_slab"][:, :]),
                  wait=[(sVE, g_ready)], inc=[(sSTp[0], 16)])
            sSTp[0].inc(16)
        for hs in range(2):
            if hs == 1:
                wd_mark = load_wd_half(slab, 1)
            for ot in range(OTN):
                pp = ot % 2
                PA, PB = (P[0], P[1]) if pp == 0 else (P[2], P[3])
                w = [(sVE, g_ready)]
                if ffn["r2_evac"][pp]:
                    w.append((sVE, ffn["r2_evac"][pp]))
                for hl in range(8):
                    hc = hs * 8 + hl
                    ul, sub = hl // 2, hl % 2
                    lg = sb["g_slab"][:, hc * 1024 + ot * 128:hc * 1024 + (ot + 1) * 128]
                    for k in range(4):
                        tgt = PA if k < 2 else PB
                        pl.op("tensor",
                              lambda e, lg=lg, tgt=tgt, k=k, ul=ul, sub=sub, hl=hl: e.matmul(
                                  tgt[:, (k % 2) * 512:(k % 2 + 1) * 512], lg,
                                  sb["wdu"][:, ul * 2 * 2048 + sub * 2048 + k * 512:
                                            ul * 2 * 2048 + sub * 2048 + (k + 1) * 512],
                                  start=(hl == 0), stop=(hl == 7)),
                              wait=((w + [wd_mark[0]]) if (hl == 0 and k == 0) else
                                    ([wd_mark[ul]] if (sub == 0 and k == 0) else [])),
                              inc=([(sPE, 1)] if (hl == 7 and k == 3) else []),
                              tag=f"dM{slab}.{hs}.{ot}.{hl}.{k}")
                m_pe = sPE.inc()
                r2a = sb["r2"][:, ot * 2048:ot * 2048 + 1024]
                r2b = sb["r2"][:, ot * 2048 + 1024:(ot + 1) * 2048]
                first = (slab == 0 and hs == 0)
                if first:
                    pl.op("vector", lambda e, r2a=r2a, PA=PA: e.tensor_copy(r2a, PA[:, :]),
                          wait=[(sPE, m_pe)], tag=f"rc{slab}.{hs}.{ot}")
                    pl.op("vector", lambda e, r2b=r2b, PB=PB: e.tensor_copy(r2b, PB[:, :]),
                          inc=[(sVE, 1)], tag=f"rcB{slab}.{hs}.{ot}")
                else:
                    pl.op("vector", lambda e, r2a=r2a, PA=PA: e.tensor_add(r2a, r2a, PA[:, :]),
                          wait=[(sPE, m_pe)], tag=f"ra{slab}.{hs}.{ot}")
                    pl.op("vector", lambda e, r2b=r2b, PB=PB: e.tensor_add(r2b, r2b, PB[:, :]),
                          inc=[(sVE, 1)], tag=f"raB{slab}.{hs}.{ot}")
                ffn["r2_evac"][pp] = sVE.inc()
            ffn["down_pe_all"] = sPE.n
    r2_all = sVE.n

    if DEBUG:
        pl.op("sync", lambda e: e.dma_start(out=dbg["d_r2"], in_=sb["r2"][:, :]),
              wait=[(sVE, r2_all)], inc=[(sSTp[0], 16)])
        sSTp[0].inc(16)

    pl.barrier(sB)
    p6_ve = [0, 0]
    p6_st = [0, 0]
    for ot in range(OTN):
        pp = ot % 2
        xra = sb["xr6"][:, pp * 2048:(pp + 1) * 2048]
        wxr = list(xm_all_w)
        if p6_ve[pp]:
            wxr.append((sVE, p6_ve[pp]))
        pl.op("sync", lambda e, ot=ot, xra=xra: e.dma_start(
            out=xra, in_=xmid_dram.ap()[ot * 128:(ot + 1) * 128, :]),
            wait=wxr, inc=[(sXRp[pp], 16)], tag=f"fx{ot}")
        m_xr = sXRp[pp].inc(16)
        sl = pp * 16
        sx = stats[:, sl:sl + 1]
        sxx = stats[:, sl + 1:sl + 2]
        v0 = stats[:, sl + 2:sl + 3]
        negm = stats[:, sl + 3:sl + 4]
        rstd = stats[:, sl + 4:sl + 5]
        r2t = sb["r2"][:, ot * 2048:(ot + 1) * 2048]
        pl.op("scalar", lambda e, r2t=r2t, sx=sx: e.activation(
            dump[:, :], r2t, AF.Copy, accum_out=sx),
            wait=[(sVE, r2_all)], tag=f"fS{ot}")
        pl.op("scalar", lambda e, r2t=r2t, sxx=sxx: e.activation(
            dump[:, :], r2t, AF.Square, accum_out=sxx), inc=[(sSE, 1)], tag=f"fQ{ot}")
        m_se = sSE.inc()
        pl.op("vector", lambda e, sx=sx, v0=v0: e.tensor_mul(v0, sx, sx),
              wait=[(sSE, m_se)])
        pl.op("vector", lambda e, v0=v0, sxx=sxx: e.scalar_tensor_tensor(
            v0, v0, -1.0 / 2048, sxx, ALU.mult, ALU.add))
        pl.op("vector", lambda e, sx=sx, negm=negm: e.tensor_scalar(
            negm, sx, -1.0 / 2048, None, ALU.mult), inc=[(sVE, 1)], tag=f"fV{ot}")
        m_ve = sVE.inc()
        pl.op("scalar", lambda e, v0=v0: e.activation(v0, v0, AF.Ln,
                                                      scale=np.float32(1.0 / 2048), bias=np.float32(EPS)),
              wait=[(sVE, m_ve)])
        pl.op("scalar", lambda e, v0=v0, rstd=rstd: e.activation(
            rstd, v0, AF.Exp, scale=np.float32(-0.5)), inc=[(sSE, 1)], tag=f"fR{ot}")
        m_se2 = sSE.inc()
        lna = sb["ln6"][:, pp * 2048:(pp + 1) * 2048]
        xoa = sb["xo6"][:, pp * 2048:(pp + 1) * 2048]
        pl.op("vector", lambda e, lna=lna, r2t=r2t, negm=negm, rstd=rstd: e.tensor_scalar(
            lna, r2t, negm, rstd, ALU.add, ALU.mult),
            wait=[(sSE, m_se2)], tag=f"fL{ot}")
        wadd = [(sXRp[pp], m_xr)]
        if p6_st[pp]:
            wadd.append((sSTp[pp], p6_st[pp]))
        pl.op("vector", lambda e, xoa=xoa, lna=lna, xra=xra: e.scalar_tensor_tensor(
            xoa, lna, 1.0, xra, ALU.mult, ALU.add),
            wait=wadd, inc=[(sVE, 1)], tag=f"fA{ot}")
        p6_ve[pp] = sVE.inc()
        pl.op("sync", lambda e, ot=ot, xoa=xoa: e.dma_start(
            out=out_own[ot * 128:(ot + 1) * 128, :], in_=xoa),
            wait=[(sVE, p6_ve[pp])], inc=[(sSTp[pp], 16)], tag=f"st{ot}")
        p6_st[pp] = sSTp[pp].inc(16)

    pl.op("sync", None, wait=[(sSTp[0], sSTp[0].n), (sSTp[1], sSTp[1].n)],
          tag="final-wait")

    pl.simulate()
    pl.replay()
    return nc


_CACHE = {}


def kernel(L0_attn_x, L0_attn_w_aq, L0_attn_w_ak, L0_attn_w_av, L0_attn_w_ao,
           L0_ffn_w_gate, L0_ffn_w_up, L0_ffn_w_down):
    x = np.ascontiguousarray(np.asarray(L0_attn_x, dtype=np.float32))
    waq = np.asarray(L0_attn_w_aq, np.float32)
    wak = np.asarray(L0_attn_w_ak, np.float32)
    wav = np.asarray(L0_attn_w_av, np.float32)
    wq_h = [np.ascontiguousarray(
        waq[g].reshape(H, MC, 128, D).transpose(2, 0, 1, 3).reshape(128, H * MC * D))
        for g in range(G)]
    wk_h = [np.ascontiguousarray(
        wak[g].reshape(MC, 128, D).transpose(1, 0, 2).reshape(128, MC * D))
        for g in range(G)]
    wv_h = [np.ascontiguousarray(
        wav[g].reshape(MC, 128, D).transpose(1, 0, 2).reshape(128, MC * D))
        for g in range(G)]
    wao = np.ascontiguousarray(
        np.asarray(L0_attn_w_ao, np.float32).reshape(G * H, D, M)
        .transpose(1, 0, 2).reshape(128, G * H * M))
    wgt = np.ascontiguousarray(
        np.asarray(L0_ffn_w_gate, np.float32).T.reshape(MC, 128, HID)
        .transpose(1, 0, 2).reshape(128, MC * HID))
    wut = np.ascontiguousarray(
        np.asarray(L0_ffn_w_up, np.float32).T.reshape(MC, 128, HID)
        .transpose(1, 0, 2).reshape(128, MC * HID))
    wdt = np.ascontiguousarray(
        np.asarray(L0_ffn_w_down, np.float32).T.reshape(64, 128, M)
        .transpose(1, 0, 2).reshape(128, 64 * M))

    pos = np.arange(T, dtype=np.float64)
    freqs = 10000.0 ** (-np.arange(64, dtype=np.float64) / 64.0)
    ang = np.outer(freqs, pos)
    rope1 = np.concatenate([np.cos(ang), np.sin(ang)], axis=0).astype(BFH)
    rope2 = np.concatenate([np.sin(ang), np.cos(ang)], axis=0).astype(BFH)
    masks = np.zeros((4, 128, 512), np.float32)
    jj = np.arange(128)[:, None]
    ii = np.arange(512)[None, :]
    for r in range(4):
        masks[r] = ((jj + 128 * r) <= ii).astype(np.float32)
    masks = masks.astype(BFH)

    if "nc" not in _CACHE:
        _CACHE["nc"] = build_program()
    nc = _CACHE["nc"]

    in_maps = []
    for r in range(8):
        b, g = r // 4, r % 4
        in_maps.append({
            "x_full": x[b],
            "w_q": wq_h[g],
            "w_k": wk_h[g],
            "w_v": wv_h[g],
            "w_o": wao,
            "wg_t": wgt,
            "wu_t": wut,
            "wd_t": wdt,
            "rope1": np.asarray(rope1),
            "rope2": np.asarray(rope2),
            "masks": np.asarray(masks),
        })
    res = run_bass_kernel_spmd(nc, in_maps, core_ids=list(range(8)))
    out = np.empty((B, T, M), np.float32)
    for r in range(8):
        b, s = r // 4, r % 4
        out[b, s * 1024:(s + 1) * 1024, :] = np.asarray(res.results[r]["out_own"])
    if DEBUG:
        kernel._last = res
    return out


# version 06
# speedup vs baseline: 1.5236x; 1.0296x over previous
import numpy as np
import ml_dtypes
import concourse.bass as bass
import concourse.mybir as mybir
from concourse.bass_utils import run_bass_kernel_spmd

F32 = mybir.dt.float32
BF16 = mybir.dt.bfloat16
AF = mybir.ActivationFunctionType
ALU = mybir.AluOpType
BFH = ml_dtypes.bfloat16

B, T, M, G, H, D = 2, 4096, 2048, 4, 4, 128
HID = 8192
NT = T // 128
OTN = 8
MC = M // 128
QK2 = float(D) ** -0.5
SHIFT = 30.0
EPS = 1e-5
DEBUG = False

ENGINES = ("sync", "gpsimd", "tensor", "scalar", "vector")


class SemT:
    def __init__(self, nc, name):
        self.h = nc.alloc_semaphore(name)
        self.n = 0
        self.name = name

    def inc(self, k=1):
        self.n += k
        return self.n


class Op:
    __slots__ = ("eng", "fn", "waits", "incs", "tag")

    def __init__(self, eng, fn, waits, incs, tag):
        self.eng, self.fn, self.waits, self.incs, self.tag = eng, fn, waits, incs, tag


class Plan:
    def __init__(self, nc):
        self.nc = nc
        self.seq = []
        self.pid = {}
        self.sl1024 = {}

    def op(self, eng, fn=None, wait=(), inc=(), tag=""):
        self.seq.append(Op(eng, fn, [w for w in wait if w is not None], list(inc), tag))

    def barrier(self, sem):
        for e in ENGINES:
            self.op(e, None, inc=[(sem, 1)], tag="bar-inc")
            sem.inc(1)
        tgt = sem.n
        for e in ENGINES:
            self.op(e, None, wait=[(sem, tgt)], tag="bar-wait")

    def simulate(self):
        vals = {}
        qs = {e: [o for o in self.seq if o.eng == e] for e in ENGINES}
        ptr = {e: 0 for e in ENGINES}
        progress = True
        while progress:
            progress = False
            for e in ENGINES:
                while ptr[e] < len(qs[e]):
                    o = qs[e][ptr[e]]
                    if all(vals.get(s.name, 0) >= v for s, v in o.waits):
                        for s, k in o.incs:
                            vals[s.name] = vals.get(s.name, 0) + k
                        ptr[e] += 1
                        progress = True
                    else:
                        break
        stuck = {e: ptr[e] for e in ENGINES if ptr[e] < len(qs[e])}
        if stuck:
            msg = []
            for e, p in stuck.items():
                o = qs[e][p]
                why = [(s.name, v, vals.get(s.name, 0)) for s, v in o.waits
                       if vals.get(s.name, 0) < v]
                msg.append(f"{e}[{p}/{len(qs[e])}] tag={o.tag} blocked on {why}")
            raise RuntimeError("PLAN DEADLOCK:\n" + "\n".join(msg))

    def replay(self):
        nc = self.nc
        with nc.Block() as block:
            def run_engine(eng_obj, name):
                try:
                    self.pid[name] = eng_obj.partition_id()
                    self.sl1024[name] = self.pid[name] % 4 * 1024
                except Exception:
                    pass
                for o in self.seq:
                    if o.eng != name:
                        continue
                    for s, v in o.waits:
                        eng_obj.wait_ge(s.h, v)
                    if o.fn is not None:
                        inst = o.fn(eng_obj)
                        for s, k in o.incs:
                            inst = inst.then_inc(s.h, k)
                    else:
                        for s, k in o.incs:
                            eng_obj.sem_inc(s.h, k)

            @block.sync
            def _(e):
                run_engine(e, "sync")

            @block.gpsimd
            def _(e):
                run_engine(e, "gpsimd")

            @block.tensor
            def _(e):
                run_engine(e, "tensor")

            @block.scalar
            def _(e):
                run_engine(e, "scalar")

            @block.vector
            def _(e):
                run_engine(e, "vector")


class SbufMap:
    def __init__(self, nc, base, top):
        self.nc = nc
        self.base = base
        self.top = top
        self.items = []

    def add(self, name, shape, dtype, live):
        esz = 4 if dtype == F32 else 2
        bpp = int(np.prod(shape[1:])) * esz
        bpp = (bpp + 63) // 64 * 64
        self.items.append([name, shape, dtype, bpp, live])

    def finalize(self):
        placed = []
        out = {}
        for name, shape, dtype, bpp, live in sorted(self.items, key=lambda it: -it[3]):
            off = 0
            while True:
                clash = None
                for poff, psz, plive, pname in placed:
                    if not (live[1] < plive[0] or plive[1] < live[0]):
                        if not (off + bpp <= poff or poff + psz <= off):
                            clash = poff + psz
                            break
                if clash is None:
                    break
                off = (clash + 63) // 64 * 64
            assert self.base + off + bpp <= self.top, (
                f"SBUF overflow placing {name}: off={off} bpp={bpp}")
            placed.append((off, bpp, live, name))
            out[name] = self.nc.alloc_sbuf_tensor_at(
                name, list(shape), dtype, offset=self.base + off)
        return out


def build_program():
    nc = bass.Bass("TRN2", target_bir_lowering=False)

    x_full = nc.dram_tensor("x_full", [T, M], F32, kind="ExternalInput").ap()
    w_q = nc.dram_tensor("w_q", [128, H * MC * D], F32, kind="ExternalInput").ap()
    w_k = nc.dram_tensor("w_k", [128, MC * D], F32, kind="ExternalInput").ap()
    w_v = nc.dram_tensor("w_v", [128, MC * D], F32, kind="ExternalInput").ap()
    w_o = nc.dram_tensor("w_o", [128, G * H * M], F32, kind="ExternalInput").ap()
    wg_t = nc.dram_tensor("wg_t", [128, MC * HID], F32, kind="ExternalInput").ap()
    wu_t = nc.dram_tensor("wu_t", [128, MC * HID], F32, kind="ExternalInput").ap()
    wd_t = nc.dram_tensor("wd_t", [128, 64 * M], F32, kind="ExternalInput").ap()
    rope1 = nc.dram_tensor("rope1", [128, T], BF16, kind="ExternalInput").ap()
    rope2 = nc.dram_tensor("rope2", [128, T], BF16, kind="ExternalInput").ap()
    masks_i = nc.dram_tensor("masks", [4, 128, 512], BF16, kind="ExternalInput").ap()
    out_own = nc.dram_tensor("out_own", [1024, M], F32, kind="ExternalOutput").ap()

    h_dram = nc.dram_tensor("h_dram", [T, M], BF16)
    h2_dram = nc.dram_tensor("h2_dram", [1024, M], BF16)
    xmid_dram = nc.dram_tensor("xmid_dram", [1024, M], F32)
    oc_dram = [nc.dram_tensor(f"oc{i}_dram", [D, T], BF16) for i in range(H)]
    og_dram = [nc.dram_tensor(f"og{i}_dram", [G * D, T], BF16) for i in range(H)]
    wg_bf = nc.dram_tensor("wg_bf", [128, MC * HID], BF16)
    wu_bf = nc.dram_tensor("wu_bf", [128, MC * HID], BF16)
    wd_bf = nc.dram_tensor("wd_bf", [128, 64 * M], BF16)

    dbg = {}
    if DEBUG:
        dbg["d_h"] = nc.dram_tensor("d_h", [T, M], BF16, kind="ExternalOutput").ap()
        dbg["d_q"] = nc.dram_tensor("d_q", [128, H * T], BF16, kind="ExternalOutput").ap()
        dbg["d_k"] = nc.dram_tensor("d_k", [128, T], BF16, kind="ExternalOutput").ap()
        dbg["d_v"] = nc.dram_tensor("d_v", [128, T], BF16, kind="ExternalOutput").ap()
        dbg["d_o"] = nc.dram_tensor("d_o", [128, H * T], BF16, kind="ExternalOutput").ap()
        dbg["d_xmid"] = nc.dram_tensor("d_xmid", [1024, M], F32, kind="ExternalOutput").ap()
        dbg["d_h2ft"] = nc.dram_tensor("d_h2ft", [128, MC * 1024], BF16, kind="ExternalOutput").ap()
        dbg["d_g"] = nc.dram_tensor("d_g", [128, 16 * 1024], BF16, kind="ExternalOutput").ap()
        dbg["d_r2"] = nc.dram_tensor("d_r2", [128, OTN * 2048], F32, kind="ExternalOutput").ap()

    smap = SbufMap(nc, (nc.sbuf_base + 63) // 64 * 64, nc.sbuf_top)
    smap.add("rope1_sb", [128, T], BF16, (0, 1))
    smap.add("rope2_sb", [128, T], BF16, (0, 1))
    smap.add("masks_sb", [128, 2048], BF16, (0, 2))
    smap.add("ones_bf", [128, 16], BF16, (0, 6))
    smap.add("ones_f", [1, 128], F32, (0, 6))
    smap.add("stats", [128, 64], F32, (0, 6))
    smap.add("recip_sb", [1, 512], F32, (2, 2))
    smap.add("recipb_sb", [1, 512], BF16, (2, 2))
    smap.add("pairt", [128, 1024], BF16, (2, 2))
    smap.add("ones_bfr", [1, 128], BF16, (0, 6))
    smap.add("bcast_sb", [128, 512], F32, (2, 2))
    smap.add("ptile", [128, 32 * 512], BF16, (2, 2))
    smap.add("dump", [128, 2048], F32, (0, 6))
    smap.add("dump2", [128, 2048], F32, (1, 1))
    smap.add("h_ft", [128, 2 * MC * 1024], BF16, (1, 1))
    smap.add("xt", [128, 2 * 2048], F32, (1, 1))
    smap.add("ht", [128, 2 * 2048], BF16, (1, 1))
    smap.add("qtmp", [128, 2 * 1024], BF16, (1, 1))
    smap.add("rtmp", [128, 2048], BF16, (1, 1))
    smap.add("rswp", [128, 1024], BF16, (1, 1))
    smap.add("wq_sb", [128, H * MC * D], BF16, (0, 1))
    smap.add("wk_sb", [128, MC * D], BF16, (0, 1))
    smap.add("wv_sb", [128, MC * D], BF16, (0, 1))
    smap.add("q_ft", [128, H * T], BF16, (1, 2))
    smap.add("k_ft", [128, T], BF16, (1, 2))
    smap.add("v_sb", [128, T], BF16, (1, 2))
    smap.add("o_ft", [128, H * T], BF16, (2, 3))
    smap.add("wo_sb", [128, G * H * M], BF16, (2, 4))
    smap.add("o16_sb", [128, G * H * 1024], BF16, (4, 4))
    smap.add("h2_ft", [128, MC * 1024], BF16, (4, 5))
    smap.add("xr", [128, 2 * 2048], F32, (4, 4))
    smap.add("ln_t", [128, 2 * 2048], F32, (4, 4))
    smap.add("xmid_t", [128, 2 * 2048], F32, (4, 4))
    smap.add("h2t", [128, 2 * 2048], BF16, (4, 4))
    smap.add("g_slab", [128, 16 * 1024], BF16, (5, 5))
    smap.add("r2", [128, OTN * 2048], F32, (5, 6))
    smap.add("wgu", [128, 2 * MC * 256], BF16, (5, 5))
    smap.add("wuu", [128, 2 * MC * 256], BF16, (5, 5))
    smap.add("wdu", [128, 8 * 2048], BF16, (5, 5))
    smap.add("silu_t", [128, 1024], F32, (5, 5))
    smap.add("xr6", [128, 2 * 2048], F32, (6, 6))
    smap.add("ln6", [128, 2 * 2048], F32, (6, 6))
    smap.add("xo6", [128, 2 * 2048], F32, (6, 6))
    sb = smap.finalize()

    P = [nc.alloc_psum_tensor(f"ps{i}", [128, 1024], F32) for i in range(4)]

    pl = Plan(nc)
    sCst = SemT(nc, "sCst")
    sXp = [SemT(nc, "sXa"), SemT(nc, "sXb")]
    sHp = [SemT(nc, "sHa"), SemT(nc, "sHb")]
    sTT = SemT(nc, "sTT")
    sWq = SemT(nc, "sWq")
    sWo = SemT(nc, "sWo")
    sWc = SemT(nc, "sWc")
    sFg = [SemT(nc, "sFga"), SemT(nc, "sFgb")]
    sFu = [SemT(nc, "sFua"), SemT(nc, "sFub")]
    sFdu = [SemT(nc, f"sFd{i}") for i in range(4)]
    sPE = SemT(nc, "sPE")
    sSE = SemT(nc, "sSE")
    sVE = SemT(nc, "sVE")
    sCCh = [SemT(nc, f"sCC{i}") for i in range(H)]
    sC1h = [SemT(nc, f"sC1{i}") for i in range(H)]
    sG = SemT(nc, "sG")
    sXRp = [SemT(nc, "sXRa"), SemT(nc, "sXRb")]
    sXMp = [SemT(nc, "sXMa"), SemT(nc, "sXMb")]
    sSTp = [SemT(nc, "sSTa"), SemT(nc, "sSTb")]
    sB = SemT(nc, "sB")

    rope1_sb, rope2_sb = sb["rope1_sb"], sb["rope2_sb"]
    masks_sb, stats, dump = sb["masks_sb"], sb["stats"], sb["dump"]
    h_ft, q_ft, k_ft, v_sb, o_ft = (sb["h_ft"], sb["q_ft"], sb["k_ft"],
                                    sb["v_sb"], sb["o_ft"])

    pl.op("vector", lambda e: e.memset(sb["ones_bf"][:, :], 1.0))
    pl.op("vector", lambda e: e.memset(sb["ones_f"][:, :], 1.0))
    pl.op("vector", lambda e: e.memset(sb["ones_bfr"][:, :], 1.0), inc=[(sVE, 1)])
    ones_ready = sVE.inc()
    pl.op("sync", lambda e: e.dma_start(out=rope1_sb[:, :], in_=rope1), inc=[(sCst, 16)])
    sCst.inc(16)
    pl.op("sync", lambda e: e.dma_start(out=rope2_sb[:, :], in_=rope2), inc=[(sCst, 16)])
    sCst.inc(16)
    pl.op("sync",
          lambda e: e.dma_start(out=masks_sb[:, :].rearrange("p (r i) -> p r i", r=4),
                                in_=masks_i.rearrange("r p i -> p r i")),
          inc=[(sCst, 16)])
    rope_ready = sCst.inc(16)

    pl.op("gpsimd", lambda e: e.dma_start(out=sb["wk_sb"][:, :], in_=w_k),
          inc=[(sWq, 16)], tag="wk")
    sWq.inc(16)
    pl.op("gpsimd", lambda e: e.dma_start(out=sb["wv_sb"][:, :], in_=w_v),
          inc=[(sWq, 16)], tag="wv")
    sWq.inc(16)
    pl.op("gpsimd", lambda e: e.dma_start(out=sb["wq_sb"][:, :], in_=w_q),
          inc=[(sWq, 16)], tag="wq")
    sWq.inc(16)
    wqkv_ready = sWq.n

    xt, ht = sb["xt"], sb["ht"]
    ln_done = [0] * NT
    h_mark = [0] * NT

    def emit_ln_tile(ti):
        slot = ti % 2
        xa = xt[:, slot * 2048:(slot + 1) * 2048]
        ha = ht[:, slot * 2048:(slot + 1) * 2048]
        w = [(sVE, ln_done[ti - 2])] if ti >= 2 else []
        pl.op("sync", lambda e, ti=ti, xa=xa: e.dma_start(
            out=xa, in_=x_full[ti * 128:(ti + 1) * 128, :]),
            wait=w, inc=[(sXp[slot], 16)], tag=f"x{ti}")
        xv = sXp[slot].inc(16)
        sl = slot * 8
        sx = stats[:, sl:sl + 1]
        sxx = stats[:, sl + 1:sl + 2]
        v0 = stats[:, sl + 2:sl + 3]
        negm = stats[:, sl + 3:sl + 4]
        rstd = stats[:, sl + 4:sl + 5]
        pl.op("vector", lambda e, xa=xa, sx=sx: e.tensor_scalar(
            sb["dump2"][:, :], xa, 1.0, 0.0, ALU.mult, ALU.add, accum_out=sx),
            wait=[(sXp[slot], xv)], inc=[(sVE, 1)], tag=f"ln{ti}-sx")
        m_vx = sVE.inc()
        pl.op("scalar", lambda e, xa=xa, sxx=sxx: e.activation(
            dump[:, :], xa, AF.Square, accum_out=sxx),
            wait=[(sXp[slot], xv)], inc=[(sSE, 1)], tag=f"ln{ti}-sxx")
        m_se = sSE.inc()
        pl.op("vector", lambda e, sx=sx, v0=v0: e.tensor_mul(v0, sx, sx),
              wait=[(sSE, m_se)])
        pl.op("vector", lambda e, v0=v0, sxx=sxx: e.scalar_tensor_tensor(
            v0, v0, -1.0 / 2048, sxx, ALU.mult, ALU.add))
        pl.op("vector", lambda e, sx=sx, negm=negm: e.tensor_scalar(
            negm, sx, -1.0 / 2048, None, ALU.mult), inc=[(sVE, 1)], tag=f"ln{ti}-v")
        m_ve = sVE.inc()
        pl.op("scalar", lambda e, v0=v0: e.activation(v0, v0, AF.Ln,
                                                      scale=np.float32(1.0 / 2048), bias=np.float32(EPS)),
              wait=[(sVE, m_ve)])
        pl.op("scalar", lambda e, v0=v0, rstd=rstd: e.activation(
            rstd, v0, AF.Exp, scale=np.float32(-0.5)), inc=[(sSE, 1)], tag=f"ln{ti}-r")
        m_se2 = sSE.inc()
        wn = [(sSE, m_se2)]
        if ti >= 2 and h_mark[ti - 2]:
            wn.append((sHp[slot], h_mark[ti - 2]))
        pl.op("vector", lambda e, ha=ha, xa=xa, negm=negm, rstd=rstd: e.tensor_scalar(
            ha, xa, negm, rstd, ALU.add, ALU.mult),
            wait=wn, inc=[(sVE, 1)], tag=f"ln{ti}-n")
        ln_done[ti] = sVE.inc()
        pl.op("sync", lambda e, ti=ti, ha=ha: e.dma_start(
            out=h_dram.ap()[ti * 128:(ti + 1) * 128, :], in_=ha),
            wait=[(sVE, ln_done[ti])], inc=[(sHp[slot], 16)], tag=f"h{ti}")
        h_mark[ti] = sHp[slot].inc(16)

    qkv_state = {"unit": 0, "vunit": 0, "sse_unit": [0, 0, 0], "sve_unit": [0, 0],
                 "sse_v": [0, 0]}
    trans_done = [0] * 4

    def emit_qkv_quarter(qu):
        tbase = qu * 1024
        t_ready = sTT.n
        hbase = (qu % 2) * MC * 1024
        for u in range(5):
            ps = P[qkv_state["unit"] % 3]
            w = [(sTT, t_ready)]
            w.append((sWq, wqkv_ready))
            prev = qkv_state["sse_unit"][qkv_state["unit"] % 3]
            if prev:
                w.append((sSE, prev))
            for mc in range(MC):
                if u < 4:
                    lhsT = sb["wq_sb"][:, (u * MC + mc) * D:(u * MC + mc + 1) * D]
                else:
                    lhsT = sb["wk_sb"][:, mc * D:(mc + 1) * D]
                for half in range(2):
                    pl.op("tensor", lambda e, lhsT=lhsT, half=half, mc=mc, ps=ps, hbase=hbase:
                          e.matmul(
                        ps[:, half * 512:(half + 1) * 512], lhsT,
                        h_ft[:, hbase + mc * 1024 + half * 512:
                             hbase + mc * 1024 + (half + 1) * 512],
                        start=(mc == 0), stop=(mc == MC - 1)),
                        wait=(w if (mc == 0 and half == 0) else []),
                        inc=([(sPE, 1)] if (mc == MC - 1 and half == 1) else []),
                        tag=f"qkvM{qu}.{u}")
            m_pe = sPE.inc()
            qslot = qkv_state["unit"] % 2
            qa = sb["qtmp"][:, qslot * 1024:(qslot + 1) * 1024]
            pslot = qkv_state["unit"] % 3
            wse = [(sPE, m_pe)]
            prev_ve = qkv_state["sve_unit"][qslot]
            if prev_ve:
                wse.append((sVE, prev_ve))
            pl.op("scalar", lambda e, qa=qa, ps=ps: e.copy(qa, ps[:, :]),
                  wait=wse, inc=[(sSE, 1)], tag=f"qkvE{qu}.{u}")
            qkv_state["sse_unit"][pslot] = sSE.inc()
            if u < 4:
                dst = q_ft[:, u * T + tbase:u * T + tbase + 1024]
            else:
                dst = k_ft[:, tbase:tbase + 1024]
            cs1 = rope1_sb[:, tbase:tbase + 1024]
            cs2 = rope2_sb[:, tbase:tbase + 1024]
            rt1 = sb["rtmp"][:, 0:1024]
            rt2 = sb["rtmp"][:, 1024:2048]
            sw = sb["rswp"]
            m_se = qkv_state["sse_unit"][pslot]
            pl.op("vector", lambda e, qa=qa, cs1=cs1, rt1=rt1: e.tensor_mul(rt1, qa, cs1),
                  wait=[(sSE, m_se), (sCst, rope_ready)], tag=f"rp{qu}.{u}")
            pl.op("vector", lambda e, qa=qa, cs2=cs2, rt2=rt2: e.tensor_mul(rt2, qa, cs2))
            pl.op("vector", lambda e, rt1=rt1, sw=sw: e.tensor_copy(
                sw[0:64, :], rt1[64:128, :]))
            pl.op("vector", lambda e, rt2=rt2, sw=sw: e.tensor_copy(
                sw[64:128, :], rt2[0:64, :]))
            pl.op("vector", lambda e, dst=dst, rt1=rt1, sw=sw: e.tensor_sub(
                dst[0:64, :], rt1[0:64, :], sw[0:64, :]))
            pl.op("vector", lambda e, dst=dst, rt2=rt2, sw=sw: e.tensor_add(
                dst[64:128, :], rt2[64:128, :], sw[64:128, :]),
                inc=[(sVE, 1)], tag=f"rpd{qu}.{u}")
            qkv_state["sve_unit"][qslot] = sVE.inc()
            qkv_state["unit"] += 1
        for tt in range(8):
            vslot = qkv_state["vunit"] % 2
            vp = P[3][:, vslot * 512:vslot * 512 + 128]
            w = [(sTT, t_ready)]
            prev = qkv_state["sse_v"][qkv_state["vunit"] % 2]
            if prev:
                w.append((sSE, prev))
            for mc in range(MC):
                lhsT = h_ft[:, hbase + mc * 1024 + tt * 128:
                            hbase + mc * 1024 + (tt + 1) * 128]
                pl.op("tensor", lambda e, lhsT=lhsT, mc=mc, vp=vp: e.matmul(
                    vp, lhsT, sb["wv_sb"][:, mc * D:(mc + 1) * D],
                    start=(mc == 0), stop=(mc == MC - 1)),
                    wait=(w if mc == 0 else []),
                    inc=([(sPE, 1)] if mc == MC - 1 else []), tag=f"vM{qu}.{tt}")
            m_pe = sPE.inc()
            gt = qu * 8 + tt
            pl.op("scalar", lambda e, gt=gt, vp=vp: e.copy(
                v_sb[:, gt * 128:(gt + 1) * 128], vp),
                wait=[(sPE, m_pe)], inc=[(sSE, 1)], tag=f"vE{qu}.{tt}")
            qkv_state["sse_v"][qkv_state["vunit"] % 2] = sSE.inc()
            qkv_state["vunit"] += 1
        trans_done[qu] = sPE.n

    def emit_transposes(qu):
        hbase = (qu % 2) * MC * 1024
        for mc in range(MC):
            w = []
            if mc == 0:
                w.append((sHp[0], sHp[0].n))
                w.append((sHp[1], sHp[1].n))
                if qu >= 2:
                    w.append((sPE, trans_done[qu - 2]))
            pl.op("sync", lambda e, qu=qu, mc=mc, hbase=hbase: e.dma_start(
                out=h_ft[:, hbase + mc * 1024:hbase + (mc + 1) * 1024],
                in_=h_dram.ap()[qu * 1024:(qu + 1) * 1024, mc * 128:(mc + 1) * 128],
                transpose=True),
                wait=w, inc=[(sTT, 16)], tag=f"T{qu}.{mc}")
            sTT.inc(16)

    for ti in range(8):
        emit_ln_tile(ti)
    emit_transposes(0)
    for ti in range(8, 16):
        emit_ln_tile(ti)
    emit_qkv_quarter(0)
    emit_transposes(1)
    for ti in range(16, 24):
        emit_ln_tile(ti)
    emit_qkv_quarter(1)
    emit_transposes(2)
    for ti in range(24, 32):
        emit_ln_tile(ti)
    emit_qkv_quarter(2)
    emit_transposes(3)
    emit_qkv_quarter(3)

    if DEBUG:
        pl.op("sync", lambda e: e.dma_start(out=dbg["d_h"], in_=h_dram.ap()),
              wait=[(sHp[0], sHp[0].n), (sHp[1], sHp[1].n)], inc=[(sSTp[0], 16)])
        sSTp[0].inc(16)

    pl.barrier(sB)
    pl.op("gpsimd", lambda e: e.dma_start(out=sb["wo_sb"][:, :], in_=w_o),
          inc=[(sWo, 16)], tag="wo")
    sWo.inc(16)
    wo_ready = sWo.n
    for nm, srcw, dstw in (("wg", wg_t, wg_bf), ("wu", wu_t, wu_bf), ("wd", wd_t, wd_bf)):
        pl.op("gpsimd", lambda e, srcw=srcw, dstw=dstw: e.dma_start(out=dstw.ap(), in_=srcw),
              inc=[(sWc, 16)], tag=f"cast-{nm}")
        sWc.inc(16)
    ffn_cast_ready = sWc.n
    ptile = sb["ptile"]
    att = {"jglob": 0, "norm_done": [0, 0], "recip_done": [0, 0], "bcast_copy": 0,
           "exp_mark": {}, "pv_done": {}}

    if DEBUG:
        pl.op("sync", lambda e: e.dma_start(out=dbg["d_q"], in_=q_ft[:, :]),
              inc=[(sSTp[0], 16)])
        sSTp[0].inc(16)
        pl.op("sync", lambda e: e.dma_start(out=dbg["d_k"], in_=k_ft[:, :]),
              inc=[(sSTp[0], 16)])
        sSTp[0].inc(16)
        pl.op("sync", lambda e: e.dma_start(out=dbg["d_v"], in_=v_sb[:, :]),
              inc=[(sSTp[0], 16)])
        sSTp[0].inc(16)

    unit = 0
    head_norm = [0] * H
    att["pair_mark"] = [0, 0]
    att["recipb_done"] = [0, 0]
    tailA_q = []
    tailB_q = []
    SBANK = [(0, 0), (0, 512), (3, 0)]

    def emit_tailA(tu):
        (t_spp, t_opp, t_hd, t_q0, t_mpe) = tu
        pl.op("scalar", lambda e, t_spp=t_spp: e.activation(
            sb["recip_sb"][0:1, :], P[2][0:1, t_spp * 512:(t_spp + 1) * 512], AF.Ln),
            wait=[(sPE, t_mpe)], tag="aRL")
        pl.op("scalar", lambda e: e.activation(
            sb["recipb_sb"][0:1, :], sb["recip_sb"][0:1, :], AF.Exp,
            scale=np.float32(-1.0)),
            inc=[(sSE, 1)], tag="aRE")
        att["recipb_done"][t_spp] = sSE.inc()
        return att["recipb_done"][t_spp]

    def emit_tailB(tu, m_rb):
        (t_spp, t_opp, t_hd, t_q0, t_mpe) = tu
        wb = [(sSE, m_rb)]
        if att["bcast_copy"]:
            wb.append((sSE, att["bcast_copy"]))
        pl.op("tensor", lambda e: e.matmul(
            P[3][:, 512:1024], sb["ones_bfr"][0:1, :], sb["recipb_sb"][0:1, :],
            start=True, stop=True),
            wait=wb, inc=[(sPE, 1)], tag="aB")
        m_b = sPE.inc()
        pl.op("scalar", lambda e: e.copy(sb["bcast_sb"][:, :], P[3][:, 512:1024]),
              wait=[(sPE, m_b)], inc=[(sSE, 1)], tag="aC")
        att["bcast_copy"] = sSE.inc()
        pl.op("vector", lambda e, t_hd=t_hd, t_q0=t_q0, t_opp=t_opp: e.tensor_mul(
            o_ft[:, t_hd * T + t_q0:t_hd * T + t_q0 + 512],
            P[1][:, t_opp * 512:(t_opp + 1) * 512],
            sb["bcast_sb"][:, :]),
            wait=[(sSE, att["bcast_copy"])], inc=[(sVE, 1)], tag="aN")
        att["norm_done"][t_opp] = sVE.inc()

    for hd in range(H):
        for qc in range(8):
            jmax = 4 * (qc + 1)
            q0 = 512 * qc
            opp = unit % 2
            spp = unit % 2
            p_ready = [None] * jmax
            pv_prev = att.get("pv_marks", [None] * 32)
            pv_marks = [None] * 32
            for j in range(jmax):
                jg = att["jglob"] + j
                pt, pc = SBANK[jg % 3]
                w = []
                pe_prev = att["exp_mark"].get(jg - 3)
                if pe_prev:
                    w.append(pe_prev)
                pl.op("tensor", lambda e, pt=pt, pc=pc, j=j, hd=hd, q0=q0: e.matmul(
                    P[pt][:, pc:pc + 512],
                    k_ft[:, j * 128:(j + 1) * 128],
                    q_ft[:, hd * T + q0:hd * T + q0 + 512],
                    start=True, stop=True),
                    wait=w, inc=[(sPE, 1)], tag=f"aS{unit}.{j}")
                m_s = sPE.inc()
                wse = [(sPE, m_s)]
                if pv_prev[j] is not None:
                    wse.append((sPE, pv_prev[j]))
                pl.op("scalar", lambda e, j=j, pt=pt, pc=pc: e.activation(
                    ptile[:, j * 512:(j + 1) * 512],
                    P[pt][:, pc:pc + 512],
                    AF.Exp, scale=np.float32(QK2), bias=np.float32(-SHIFT)),
                    wait=wse, inc=[(sSE, 1)], tag=f"aE{unit}.{j}")
                m_e = sSE.inc()
                if j >= jmax - 4:
                    r = j - (jmax - 4)
                    pl.op("vector", lambda e, j=j, r=r: e.tensor_mul(
                        ptile[:, j * 512:(j + 1) * 512],
                        ptile[:, j * 512:(j + 1) * 512],
                        masks_sb[:, r * 512:(r + 1) * 512]),
                        wait=[(sSE, m_e)], inc=[(sVE, 1)], tag=f"aM{unit}.{j}")
                    p_ready[j] = (sVE, sVE.inc())
                else:
                    p_ready[j] = (sSE, m_e)
                att["exp_mark"][jg] = p_ready[j]
            if tailA_q:
                tu = tailA_q.pop()
                m_rb = emit_tailA(tu)
                tailB_q.append((tu, m_rb))
            npairs = jmax // 2
            for pr in range(npairs):
                j0, j1 = 2 * pr, 2 * pr + 1
                ppp = pr % 2
                wp = [p_ready[j0], p_ready[j1]]
                if att["pair_mark"][ppp]:
                    wp.append((sPE, att["pair_mark"][ppp]))
                pl.op("vector", lambda e, j0=j0, j1=j1, ppp=ppp: e.tensor_add(
                    sb["pairt"][:, ppp * 512:(ppp + 1) * 512],
                    ptile[:, j0 * 512:(j0 + 1) * 512],
                    ptile[:, j1 * 512:(j1 + 1) * 512]),
                    wait=wp, inc=[(sVE, 1)], tag=f"aA{unit}.{pr}")
                m_pair = sVE.inc()
                w2 = [(sVE, m_pair)]
                if pr == 0:
                    if att["recipb_done"][spp]:
                        w2.append((sSE, att["recipb_done"][spp]))
                    if att["norm_done"][opp]:
                        w2.append((sVE, att["norm_done"][opp]))
                pl.op("tensor", lambda e, spp=spp, ppp=ppp, pr=pr, npairs=npairs: e.matmul(
                    P[2][0:1, spp * 512:(spp + 1) * 512],
                    sb["ones_bf"][:, 0:1],
                    sb["pairt"][:, ppp * 512:(ppp + 1) * 512],
                    start=(pr == 0), stop=(pr == npairs - 1)),
                    wait=w2, tag=f"aU{unit}.{pr}")
                for j in (j0, j1):
                    pl.op("tensor", lambda e, opp=opp, j=j, jmax=jmax: e.matmul(
                        P[1][:, opp * 512:(opp + 1) * 512],
                        v_sb[:, j * 128:(j + 1) * 128],
                        ptile[:, j * 512:(j + 1) * 512],
                        start=(j == 0), stop=(j == jmax - 1)),
                        wait=([p_ready[j]] if j == j1 else []),
                        inc=[(sPE, 1)], tag=f"aP{unit}.{j}")
                    pv_marks[j] = sPE.inc()
                att["pair_mark"][ppp] = pv_marks[j1]
            att["pv_marks"] = pv_marks
            if tailB_q:
                emit_tailB(*tailB_q.pop())
            tailA_q.append((spp, opp, hd, q0, pv_marks[jmax - 1]))
            att["jglob"] += jmax
            unit += 1
        while tailA_q or tailB_q:
            if tailB_q:
                emit_tailB(*tailB_q.pop())
            if tailA_q:
                tu = tailA_q.pop()
                m_rb = emit_tailA(tu)
                tailB_q.append((tu, m_rb))
        head_norm[hd] = sVE.n
    o_done = sVE.n

    if DEBUG:
        pl.op("sync", lambda e: e.dma_start(out=dbg["d_o"], in_=o_ft[:, :]),
              wait=[(sVE, o_done)], inc=[(sSTp[0], 16)])
        sSTp[0].inc(16)

    for hd in range(H):
        pl.op("sync", lambda e, hd=hd: e.dma_start(
            out=oc_dram[hd].ap(), in_=o_ft[:, hd * T:(hd + 1) * T]),
            wait=[(sVE, head_norm[hd])], inc=[(sC1h[hd], 16)], tag=f"oc{hd}")
        sC1h[hd].inc(16)
        pl.op("gpsimd", lambda e, hd=hd: e.collective_compute(
            "AllGather", ALU.bypass,
            replica_groups=[[0, 1, 2, 3], [4, 5, 6, 7]],
            ins=[oc_dram[hd].ap()], outs=[og_dram[hd].ap()]),
            wait=[(sC1h[hd], 16)], inc=[(sCCh[hd], 1)], tag=f"allgather{hd}")
        sCCh[hd].inc(1)

    for hd in range(H):
        pl.op("sync", lambda e, hd=hd: e.dma_start(
            out=sb["o16_sb"][:, :].rearrange("p (gh t) -> p gh t", gh=G * H)[
                :, hd::H, :],
            in_=og_dram[hd].ap().rearrange("(g d) t -> d g t", d=128)[
                :, :, bass.ds(pl.sl1024["sync"], 1024)]),
            wait=[(sCCh[hd], 1)], inc=[(sG, 16)], tag=f"gather{hd}")
        sG.inc(16)

    p4 = {"psum_free": [0, 0], "h2dma": [0, 0], "xmdma": [0, 0], "xr_used": [0, 0]}
    for ot in range(OTN):
        pp = ot % 2
        PA, PB = (P[0], P[1]) if pp == 0 else (P[2], P[3])
        wxr = [(sVE, p4["xr_used"][pp])] if p4["xr_used"][pp] else []
        pl.op("sync", lambda e, ot=ot, pp=pp: e.dma_start(
            out=sb["xr"][:, pp * 2048:(pp + 1) * 2048],
            in_=x_full[bass.ds(pl.sl1024["sync"] + ot * 128, 128), :]),
            wait=wxr, inc=[(sXRp[pp], 16)], tag=f"xr{ot}")
        m_xr = sXRp[pp].inc(16)
        w = [(sG, sG.n), (sWo, wo_ready)]
        if p4["psum_free"][pp]:
            w.append((sVE, p4["psum_free"][pp]))
        for gh in range(G * H):
            lhsT = sb["o16_sb"][:, gh * 1024 + ot * 128:gh * 1024 + (ot + 1) * 128]
            for k in range(4):
                tgt = PA if k < 2 else PB
                pl.op("tensor", lambda e, lhsT=lhsT, gh=gh, k=k, tgt=tgt: e.matmul(
                    tgt[:, (k % 2) * 512:(k % 2 + 1) * 512], lhsT,
                    sb["wo_sb"][:, gh * M + k * 512:gh * M + (k + 1) * 512],
                    start=(gh == 0), stop=(gh == G * H - 1)),
                    wait=(w if (gh == 0 and k == 0) else []),
                    inc=([(sPE, 1)] if (gh == G * H - 1 and k == 3) else []),
                    tag=f"oM{ot}.{gh}.{k}")
        m_pe = sPE.inc()
        sl = pp * 16
        sxa, sxb = stats[:, sl:sl + 1], stats[:, sl + 1:sl + 2]
        sxxa, sxxb = stats[:, sl + 2:sl + 3], stats[:, sl + 3:sl + 4]
        v0 = stats[:, sl + 4:sl + 5]
        negm = stats[:, sl + 5:sl + 6]
        rstd = stats[:, sl + 6:sl + 7]
        pl.op("scalar", lambda e, PA=PA, sxa=sxa: e.activation(
            dump[:, 0:1024], PA[:, :], AF.Copy, accum_out=sxa),
            wait=[(sPE, m_pe)], tag=f"rA{ot}")
        pl.op("scalar", lambda e, PB=PB, sxb=sxb: e.activation(
            dump[:, 0:1024], PB[:, :], AF.Copy, accum_out=sxb))
        pl.op("scalar", lambda e, PA=PA, sxxa=sxxa: e.activation(
            dump[:, 0:1024], PA[:, :], AF.Square, accum_out=sxxa))
        pl.op("scalar", lambda e, PB=PB, sxxb=sxxb: e.activation(
            dump[:, 0:1024], PB[:, :], AF.Square, accum_out=sxxb),
            inc=[(sSE, 1)], tag=f"rS{ot}")
        m_se = sSE.inc()
        pl.op("vector", lambda e, sxa=sxa, sxb=sxb: e.tensor_add(sxa, sxa, sxb),
              wait=[(sSE, m_se)])
        pl.op("vector", lambda e, sxxa=sxxa, sxxb=sxxb: e.tensor_add(sxxa, sxxa, sxxb))
        pl.op("vector", lambda e, sxa=sxa, v0=v0: e.tensor_mul(v0, sxa, sxa))
        pl.op("vector", lambda e, v0=v0, sxxa=sxxa: e.scalar_tensor_tensor(
            v0, v0, -1.0 / 2048, sxxa, ALU.mult, ALU.add))
        pl.op("vector", lambda e, sxa=sxa, negm=negm: e.tensor_scalar(
            negm, sxa, -1.0 / 2048, None, ALU.mult), inc=[(sVE, 1)], tag=f"rV{ot}")
        m_ve = sVE.inc()
        pl.op("scalar", lambda e, v0=v0: e.activation(v0, v0, AF.Ln,
                                                      scale=np.float32(1.0 / 2048), bias=np.float32(EPS)),
              wait=[(sVE, m_ve)])
        pl.op("scalar", lambda e, v0=v0, rstd=rstd: e.activation(
            rstd, v0, AF.Exp, scale=np.float32(-0.5)), inc=[(sSE, 1)], tag=f"rR{ot}")
        m_se2 = sSE.inc()
        lnta = sb["ln_t"][:, pp * 2048:pp * 2048 + 1024]
        lntb = sb["ln_t"][:, pp * 2048 + 1024:(pp + 1) * 2048]
        pl.op("vector", lambda e, PA=PA, lnta=lnta, negm=negm, rstd=rstd: e.tensor_scalar(
            lnta, PA[:, :], negm, rstd, ALU.add, ALU.mult),
            wait=[(sSE, m_se2)], tag=f"rlA{ot}")
        pl.op("vector", lambda e, PB=PB, lntb=lntb, negm=negm, rstd=rstd: e.tensor_scalar(
            lntb, PB[:, :], negm, rstd, ALU.add, ALU.mult),
            inc=[(sVE, 1)], tag=f"rlB{ot}")
        p4["psum_free"][pp] = sVE.inc()
        sx2 = stats[:, sl + 7:sl + 8]
        xma = sb["xmid_t"][:, pp * 2048:(pp + 1) * 2048]
        lnt = sb["ln_t"][:, pp * 2048:(pp + 1) * 2048]
        xra = sb["xr"][:, pp * 2048:(pp + 1) * 2048]
        wxm = [(sXRp[pp], m_xr)]
        if p4["xmdma"][pp]:
            wxm.append((sXMp[pp], p4["xmdma"][pp]))
        pl.op("vector", lambda e, xma=xma, lnt=lnt, xra=xra, sx2=sx2:
              e.scalar_tensor_tensor(xma, lnt, 1.0, xra, ALU.mult, ALU.add,
                                     accum_out=sx2),
              wait=wxm, inc=[(sVE, 1)], tag=f"xm{ot}")
        p4["xr_used"][pp] = sVE.inc()
        m_xmid = p4["xr_used"][pp]
        sxx2 = stats[:, sl + 8:sl + 9]
        v02 = stats[:, sl + 9:sl + 10]
        negm2 = stats[:, sl + 10:sl + 11]
        rstd2 = stats[:, sl + 11:sl + 12]
        pl.op("scalar", lambda e, xma=xma, sxx2=sxx2: e.activation(
            dump[:, :], xma, AF.Square, accum_out=sxx2),
            wait=[(sVE, m_xmid)], inc=[(sSE, 1)], tag=f"l2S{ot}")
        m_se3 = sSE.inc()
        pl.op("vector", lambda e, sx2=sx2, v02=v02: e.tensor_mul(v02, sx2, sx2),
              wait=[(sSE, m_se3)])
        pl.op("vector", lambda e, v02=v02, sxx2=sxx2: e.scalar_tensor_tensor(
            v02, v02, -1.0 / 2048, sxx2, ALU.mult, ALU.add))
        pl.op("vector", lambda e, sx2=sx2, negm2=negm2: e.tensor_scalar(
            negm2, sx2, -1.0 / 2048, None, ALU.mult), inc=[(sVE, 1)], tag=f"l2V{ot}")
        m_ve2 = sVE.inc()
        pl.op("scalar", lambda e, v02=v02: e.activation(v02, v02, AF.Ln,
                                                        scale=np.float32(1.0 / 2048), bias=np.float32(EPS)),
              wait=[(sVE, m_ve2)])
        pl.op("scalar", lambda e, v02=v02, rstd2=rstd2: e.activation(
            rstd2, v02, AF.Exp, scale=np.float32(-0.5)), inc=[(sSE, 1)], tag=f"l2R{ot}")
        m_se4 = sSE.inc()
        h2a = sb["h2t"][:, pp * 2048:(pp + 1) * 2048]
        wh2 = [(sSE, m_se4)]
        if p4["h2dma"][pp]:
            wh2.append((sXMp[pp], p4["h2dma"][pp]))
        pl.op("vector", lambda e, h2a=h2a, xma=xma, negm2=negm2, rstd2=rstd2:
              e.tensor_scalar(h2a, xma, negm2, rstd2, ALU.add, ALU.mult),
              wait=wh2, inc=[(sVE, 1)], tag=f"h2{ot}")
        m_h2 = sVE.inc()
        pl.op("sync", lambda e, ot=ot, xma=xma: e.dma_start(
            out=xmid_dram.ap()[ot * 128:(ot + 1) * 128, :], in_=xma),
            wait=[(sVE, m_h2)], inc=[(sXMp[pp], 16)], tag=f"sxm{ot}")
        p4["xmdma"][pp] = sXMp[pp].inc(16)
        pl.op("sync", lambda e, ot=ot, h2a=h2a: e.dma_start(
            out=h2_dram.ap()[ot * 128:(ot + 1) * 128, :], in_=h2a),
            inc=[(sXMp[pp], 16)], tag=f"sh2{ot}")
        p4["h2dma"][pp] = sXMp[pp].inc(16)
    xm_all_w = [(sXMp[0], sXMp[0].n), (sXMp[1], sXMp[1].n)]

    if DEBUG:
        pl.op("sync", lambda e: e.dma_start(out=dbg["d_xmid"], in_=xmid_dram.ap()),
              wait=xm_all_w, inc=[(sSTp[0], 16)])
        sSTp[0].inc(16)

    for mc in range(MC):
        pl.op("sync", lambda e, mc=mc: e.dma_start(
            out=sb["h2_ft"][:, mc * 1024:(mc + 1) * 1024],
            in_=h2_dram.ap()[:, mc * 128:(mc + 1) * 128], transpose=True),
            wait=(xm_all_w if mc == 0 else []), inc=[(sTT, 16)], tag=f"T2.{mc}")
        sTT.inc(16)
    h2ft_ready = sTT.n

    if DEBUG:
        pl.op("sync", lambda e: e.dma_start(out=dbg["d_h2ft"], in_=sb["h2_ft"][:, :]),
              wait=[(sTT, h2ft_ready)], inc=[(sSTp[0], 16)])
        sSTp[0].inc(16)

    pl.barrier(sB)
    ffn = {"gu_pe": {}, "gu_used": [0, 0], "silu_prev": 0,
           "r2_evac": [0, 0], "down_pe_all": 0}
    load_marks = {}

    def load_gu(which, slab, u):
        gidx = slab * 8 + u
        slot = gidx % 2
        h0 = slab * 2048 + u * 256
        dst = sb["wgu"] if which == "wg" else sb["wuu"]
        srcw = wg_bf if which == "wg" else wu_bf
        sem = sFg[slot] if which == "wg" else sFu[slot]
        wk_ = []
        if gidx >= 2 and (gidx - 2) in ffn["gu_pe"]:
            wk_.append((sPE, ffn["gu_pe"][gidx - 2]))
        if gidx == 0:
            wk_.append((sWc, ffn_cast_ready))
        pl.op("sync", lambda e, dst=dst, srcw=srcw, h0=h0, slot=slot: e.dma_start(
            out=dst[:, slot * MC * 256:(slot + 1) * MC * 256].rearrange(
                "p (mc h) -> p mc h", mc=MC),
            in_=srcw.ap().rearrange("p (mc h) -> p mc h", mc=MC)[:, :, h0:h0 + 256]),
            wait=wk_, inc=[(sem, 16)], tag=f"u{which}{slab}.{u}")
        return (sem, sem.inc(16))

    def load_wd_half(slab, hs):
        marks = []
        for ul in range(4):
            h0 = slab * 2048 + hs * 1024 + ul * 256
            wk_ = []
            if ul == 0:
                if ffn["down_pe_all"]:
                    wk_.append((sPE, ffn["down_pe_all"]))
                if slab == 0 and hs == 0:
                    wk_.append((sWc, ffn_cast_ready))
            hc0 = h0 // 128
            pl.op("sync", lambda e, hc0=hc0, ul=ul: e.dma_start(
                out=sb["wdu"][:, ul * 2 * 2048:(ul + 1) * 2 * 2048].rearrange(
                    "p (hs2 m) -> p hs2 m", hs2=2),
                in_=wd_bf.ap().rearrange("p (hc m) -> p hc m", hc=64)[:, hc0:hc0 + 2, :]),
                wait=wk_, inc=[(sFdu[ul], 16)], tag=f"uwd{slab}.{hs}.{ul}")
            marks.append((sFdu[ul], sFdu[ul].inc(16)))
        return marks

    for slab in range(4):
        wd_mark = load_wd_half(slab, 0)
        for hc in range(16):
            u, sub = hc // 2, hc % 2
            if sub == 0:
                load_marks[("wg", slab, u)] = load_gu("wg", slab, u)
                load_marks[("wu", slab, u)] = load_gu("wu", slab, u)
            gp = P[0] if hc % 2 == 0 else P[2]
            up = P[1] if hc % 2 == 0 else P[3]
            gidx = slab * 8 + u
            slot = gidx % 2
            w = [load_marks[("wg", slab, u)], load_marks[("wu", slab, u)]]
            if slab == 0 and hc == 0:
                w.append((sTT, h2ft_ready))
            prev = ffn["gu_used"][hc % 2]
            if prev:
                w.append((sVE, prev))
            if slab > 0 and hc < 2:
                if ffn["r2_evac"][0]:
                    w.append((sVE, ffn["r2_evac"][0]))
                if ffn["r2_evac"][1]:
                    w.append((sVE, ffn["r2_evac"][1]))
            for mc in range(MC):
                base = slot * MC * 256 + mc * 256 + sub * 128
                lg = sb["wgu"][:, base:base + 128]
                lu = sb["wuu"][:, base:base + 128]
                for half in range(2):
                    pl.op("tensor", lambda e, lg=lg, gp=gp, mc=mc, half=half: e.matmul(
                        gp[:, half * 512:(half + 1) * 512], lg,
                        sb["h2_ft"][:, mc * 1024 + half * 512:mc * 1024 + (half + 1) * 512],
                        start=(mc == 0), stop=(mc == MC - 1)),
                        wait=(w if (mc == 0 and half == 0) else []),
                        inc=([(sPE, 1)] if (mc == MC - 1 and half == 1) else []),
                        tag=f"gM{slab}.{hc}")
                for half in range(2):
                    pl.op("tensor", lambda e, lu=lu, up=up, mc=mc, half=half: e.matmul(
                        up[:, half * 512:(half + 1) * 512], lu,
                        sb["h2_ft"][:, mc * 1024 + half * 512:mc * 1024 + (half + 1) * 512],
                        start=(mc == 0), stop=(mc == MC - 1)),
                        inc=([(sPE, 1)] if (mc == MC - 1 and half == 1) else []),
                        tag=f"uM{slab}.{hc}")
            m_gate = sPE.inc()
            m_up = sPE.inc()
            if sub == 1:
                ffn["gu_pe"][gidx] = m_up
            wse = [(sPE, m_gate)]
            if ffn["silu_prev"]:
                wse.append((sVE, ffn["silu_prev"]))
            pl.op("scalar", lambda e, gp=gp: e.activation(
                sb["silu_t"][:, :], gp[:, :], AF.Silu),
                wait=wse, inc=[(sSE, 1)], tag=f"si{slab}.{hc}")
            m_si = sSE.inc()
            pl.op("vector", lambda e, up=up, hc=hc: e.tensor_mul(
                sb["g_slab"][:, hc * 1024:(hc + 1) * 1024], up[:, :],
                sb["silu_t"][:, :]),
                wait=[(sSE, m_si), (sPE, m_up)], inc=[(sVE, 1)], tag=f"gm{slab}.{hc}")
            ffn["gu_used"][hc % 2] = sVE.inc()
            ffn["silu_prev"] = ffn["gu_used"][hc % 2]
        g_ready = sVE.n
        if DEBUG and slab == 0:
            pl.op("sync", lambda e: e.dma_start(out=dbg["d_g"], in_=sb["g_slab"][:, :]),
                  wait=[(sVE, g_ready)], inc=[(sSTp[0], 16)])
            sSTp[0].inc(16)
        for hs in range(2):
            if hs == 1:
                wd_mark = load_wd_half(slab, 1)
            for ot in range(OTN):
                pp = ot % 2
                PA, PB = (P[0], P[1]) if pp == 0 else (P[2], P[3])
                w = [(sVE, g_ready)]
                if ffn["r2_evac"][pp]:
                    w.append((sVE, ffn["r2_evac"][pp]))
                for hl in range(8):
                    hc = hs * 8 + hl
                    ul, sub = hl // 2, hl % 2
                    lg = sb["g_slab"][:, hc * 1024 + ot * 128:hc * 1024 + (ot + 1) * 128]
                    for k in range(4):
                        tgt = PA if k < 2 else PB
                        pl.op("tensor",
                              lambda e, lg=lg, tgt=tgt, k=k, ul=ul, sub=sub, hl=hl: e.matmul(
                                  tgt[:, (k % 2) * 512:(k % 2 + 1) * 512], lg,
                                  sb["wdu"][:, ul * 2 * 2048 + sub * 2048 + k * 512:
                                            ul * 2 * 2048 + sub * 2048 + (k + 1) * 512],
                                  start=(hl == 0), stop=(hl == 7)),
                              wait=((w + [wd_mark[0]]) if (hl == 0 and k == 0) else
                                    ([wd_mark[ul]] if (sub == 0 and k == 0) else [])),
                              inc=([(sPE, 1)] if (hl == 7 and k == 3) else []),
                              tag=f"dM{slab}.{hs}.{ot}.{hl}.{k}")
                m_pe = sPE.inc()
                r2a = sb["r2"][:, ot * 2048:ot * 2048 + 1024]
                r2b = sb["r2"][:, ot * 2048 + 1024:(ot + 1) * 2048]
                first = (slab == 0 and hs == 0)
                if first:
                    pl.op("vector", lambda e, r2a=r2a, PA=PA: e.tensor_copy(r2a, PA[:, :]),
                          wait=[(sPE, m_pe)], tag=f"rc{slab}.{hs}.{ot}")
                    pl.op("vector", lambda e, r2b=r2b, PB=PB: e.tensor_copy(r2b, PB[:, :]),
                          inc=[(sVE, 1)], tag=f"rcB{slab}.{hs}.{ot}")
                else:
                    pl.op("vector", lambda e, r2a=r2a, PA=PA: e.tensor_add(r2a, r2a, PA[:, :]),
                          wait=[(sPE, m_pe)], tag=f"ra{slab}.{hs}.{ot}")
                    pl.op("vector", lambda e, r2b=r2b, PB=PB: e.tensor_add(r2b, r2b, PB[:, :]),
                          inc=[(sVE, 1)], tag=f"raB{slab}.{hs}.{ot}")
                ffn["r2_evac"][pp] = sVE.inc()
            ffn["down_pe_all"] = sPE.n
    r2_all = sVE.n

    if DEBUG:
        pl.op("sync", lambda e: e.dma_start(out=dbg["d_r2"], in_=sb["r2"][:, :]),
              wait=[(sVE, r2_all)], inc=[(sSTp[0], 16)])
        sSTp[0].inc(16)

    pl.barrier(sB)
    p6_ve = [0, 0]
    p6_st = [0, 0]
    for ot in range(OTN):
        pp = ot % 2
        xra = sb["xr6"][:, pp * 2048:(pp + 1) * 2048]
        wxr = list(xm_all_w)
        if p6_ve[pp]:
            wxr.append((sVE, p6_ve[pp]))
        pl.op("sync", lambda e, ot=ot, xra=xra: e.dma_start(
            out=xra, in_=xmid_dram.ap()[ot * 128:(ot + 1) * 128, :]),
            wait=wxr, inc=[(sXRp[pp], 16)], tag=f"fx{ot}")
        m_xr = sXRp[pp].inc(16)
        sl = pp * 16
        sx = stats[:, sl:sl + 1]
        sxx = stats[:, sl + 1:sl + 2]
        v0 = stats[:, sl + 2:sl + 3]
        negm = stats[:, sl + 3:sl + 4]
        rstd = stats[:, sl + 4:sl + 5]
        r2t = sb["r2"][:, ot * 2048:(ot + 1) * 2048]
        pl.op("scalar", lambda e, r2t=r2t, sx=sx: e.activation(
            dump[:, :], r2t, AF.Copy, accum_out=sx),
            wait=[(sVE, r2_all)], tag=f"fS{ot}")
        pl.op("scalar", lambda e, r2t=r2t, sxx=sxx: e.activation(
            dump[:, :], r2t, AF.Square, accum_out=sxx), inc=[(sSE, 1)], tag=f"fQ{ot}")
        m_se = sSE.inc()
        pl.op("vector", lambda e, sx=sx, v0=v0: e.tensor_mul(v0, sx, sx),
              wait=[(sSE, m_se)])
        pl.op("vector", lambda e, v0=v0, sxx=sxx: e.scalar_tensor_tensor(
            v0, v0, -1.0 / 2048, sxx, ALU.mult, ALU.add))
        pl.op("vector", lambda e, sx=sx, negm=negm: e.tensor_scalar(
            negm, sx, -1.0 / 2048, None, ALU.mult), inc=[(sVE, 1)], tag=f"fV{ot}")
        m_ve = sVE.inc()
        pl.op("scalar", lambda e, v0=v0: e.activation(v0, v0, AF.Ln,
                                                      scale=np.float32(1.0 / 2048), bias=np.float32(EPS)),
              wait=[(sVE, m_ve)])
        pl.op("scalar", lambda e, v0=v0, rstd=rstd: e.activation(
            rstd, v0, AF.Exp, scale=np.float32(-0.5)), inc=[(sSE, 1)], tag=f"fR{ot}")
        m_se2 = sSE.inc()
        lna = sb["ln6"][:, pp * 2048:(pp + 1) * 2048]
        xoa = sb["xo6"][:, pp * 2048:(pp + 1) * 2048]
        pl.op("vector", lambda e, lna=lna, r2t=r2t, negm=negm, rstd=rstd: e.tensor_scalar(
            lna, r2t, negm, rstd, ALU.add, ALU.mult),
            wait=[(sSE, m_se2)], tag=f"fL{ot}")
        wadd = [(sXRp[pp], m_xr)]
        if p6_st[pp]:
            wadd.append((sSTp[pp], p6_st[pp]))
        pl.op("vector", lambda e, xoa=xoa, lna=lna, xra=xra: e.scalar_tensor_tensor(
            xoa, lna, 1.0, xra, ALU.mult, ALU.add),
            wait=wadd, inc=[(sVE, 1)], tag=f"fA{ot}")
        p6_ve[pp] = sVE.inc()
        pl.op("sync", lambda e, ot=ot, xoa=xoa: e.dma_start(
            out=out_own[ot * 128:(ot + 1) * 128, :], in_=xoa),
            wait=[(sVE, p6_ve[pp])], inc=[(sSTp[pp], 16)], tag=f"st{ot}")
        p6_st[pp] = sSTp[pp].inc(16)

    pl.op("sync", None, wait=[(sSTp[0], sSTp[0].n), (sSTp[1], sSTp[1].n)],
          tag="final-wait")

    pl.simulate()
    pl.replay()
    return nc


_CACHE = {}


def kernel(L0_attn_x, L0_attn_w_aq, L0_attn_w_ak, L0_attn_w_av, L0_attn_w_ao,
           L0_ffn_w_gate, L0_ffn_w_up, L0_ffn_w_down):
    x = np.ascontiguousarray(np.asarray(L0_attn_x, dtype=np.float32))
    waq = np.asarray(L0_attn_w_aq, np.float32)
    wak = np.asarray(L0_attn_w_ak, np.float32)
    wav = np.asarray(L0_attn_w_av, np.float32)
    wq_h = [np.ascontiguousarray(
        waq[g].reshape(H, MC, 128, D).transpose(2, 0, 1, 3).reshape(128, H * MC * D))
        for g in range(G)]
    wk_h = [np.ascontiguousarray(
        wak[g].reshape(MC, 128, D).transpose(1, 0, 2).reshape(128, MC * D))
        for g in range(G)]
    wv_h = [np.ascontiguousarray(
        wav[g].reshape(MC, 128, D).transpose(1, 0, 2).reshape(128, MC * D))
        for g in range(G)]
    wao = np.ascontiguousarray(
        np.asarray(L0_attn_w_ao, np.float32).reshape(G * H, D, M)
        .transpose(1, 0, 2).reshape(128, G * H * M))
    wgt = np.ascontiguousarray(
        np.asarray(L0_ffn_w_gate, np.float32).T.reshape(MC, 128, HID)
        .transpose(1, 0, 2).reshape(128, MC * HID))
    wut = np.ascontiguousarray(
        np.asarray(L0_ffn_w_up, np.float32).T.reshape(MC, 128, HID)
        .transpose(1, 0, 2).reshape(128, MC * HID))
    wdt = np.ascontiguousarray(
        np.asarray(L0_ffn_w_down, np.float32).T.reshape(64, 128, M)
        .transpose(1, 0, 2).reshape(128, 64 * M))

    pos = np.arange(T, dtype=np.float64)
    freqs = 10000.0 ** (-np.arange(64, dtype=np.float64) / 64.0)
    ang = np.outer(freqs, pos)
    rope1 = np.concatenate([np.cos(ang), np.sin(ang)], axis=0).astype(BFH)
    rope2 = np.concatenate([np.sin(ang), np.cos(ang)], axis=0).astype(BFH)
    masks = np.zeros((4, 128, 512), np.float32)
    jj = np.arange(128)[:, None]
    ii = np.arange(512)[None, :]
    for r in range(4):
        masks[r] = ((jj + 128 * r) <= ii).astype(np.float32)
    masks = masks.astype(BFH)

    if "nc" not in _CACHE:
        _CACHE["nc"] = build_program()
    nc = _CACHE["nc"]

    in_maps = []
    for r in range(8):
        b, g = r // 4, r % 4
        in_maps.append({
            "x_full": x[b],
            "w_q": wq_h[g],
            "w_k": wk_h[g],
            "w_v": wv_h[g],
            "w_o": wao,
            "wg_t": wgt,
            "wu_t": wut,
            "wd_t": wdt,
            "rope1": np.asarray(rope1),
            "rope2": np.asarray(rope2),
            "masks": np.asarray(masks),
        })
    res = run_bass_kernel_spmd(nc, in_maps, core_ids=list(range(8)))
    out = np.empty((B, T, M), np.float32)
    for r in range(8):
        b, s = r // 4, r % 4
        out[b, s * 1024:(s + 1) * 1024, :] = np.asarray(res.results[r]["out_own"])
    if DEBUG:
        kernel._last = res
    return out


# version 07
# speedup vs baseline: 1.7139x; 1.1249x over previous
import numpy as np
import ml_dtypes
import concourse.bass as bass
import concourse.mybir as mybir
from concourse.bass_utils import run_bass_kernel_spmd

F32 = mybir.dt.float32
BF16 = mybir.dt.bfloat16
AF = mybir.ActivationFunctionType
ALU = mybir.AluOpType
BFH = ml_dtypes.bfloat16

B, T, M, G, H, D = 2, 4096, 2048, 4, 4, 128
HID = 8192
NT = T // 128
OTN = 8
MC = M // 128
QK2 = float(D) ** -0.5
SHIFT = 30.0
EPS = 1e-5
DEBUG = False

ENGINES = ("sync", "gpsimd", "tensor", "scalar", "vector")


class SemT:
    def __init__(self, nc, name):
        self.h = nc.alloc_semaphore(name)
        self.n = 0
        self.name = name

    def inc(self, k=1):
        self.n += k
        return self.n


class Op:
    __slots__ = ("eng", "fn", "waits", "incs", "tag")

    def __init__(self, eng, fn, waits, incs, tag):
        self.eng, self.fn, self.waits, self.incs, self.tag = eng, fn, waits, incs, tag


class Plan:
    def __init__(self, nc):
        self.nc = nc
        self.seq = []
        self.pid = {}
        self.sl1024 = {}

    def op(self, eng, fn=None, wait=(), inc=(), tag=""):
        self.seq.append(Op(eng, fn, [w for w in wait if w is not None], list(inc), tag))

    def barrier(self, sem):
        for e in ENGINES:
            self.op(e, None, inc=[(sem, 1)], tag="bar-inc")
            sem.inc(1)
        tgt = sem.n
        for e in ENGINES:
            self.op(e, None, wait=[(sem, tgt)], tag="bar-wait")

    def simulate(self):
        vals = {}
        qs = {e: [o for o in self.seq if o.eng == e] for e in ENGINES}
        ptr = {e: 0 for e in ENGINES}
        progress = True
        while progress:
            progress = False
            for e in ENGINES:
                while ptr[e] < len(qs[e]):
                    o = qs[e][ptr[e]]
                    if all(vals.get(s.name, 0) >= v for s, v in o.waits):
                        for s, k in o.incs:
                            vals[s.name] = vals.get(s.name, 0) + k
                        ptr[e] += 1
                        progress = True
                    else:
                        break
        stuck = {e: ptr[e] for e in ENGINES if ptr[e] < len(qs[e])}
        if stuck:
            msg = []
            for e, p in stuck.items():
                o = qs[e][p]
                why = [(s.name, v, vals.get(s.name, 0)) for s, v in o.waits
                       if vals.get(s.name, 0) < v]
                msg.append(f"{e}[{p}/{len(qs[e])}] tag={o.tag} blocked on {why}")
            raise RuntimeError("PLAN DEADLOCK:\n" + "\n".join(msg))

    def replay(self):
        nc = self.nc
        with nc.Block() as block:
            def run_engine(eng_obj, name):
                try:
                    self.pid[name] = eng_obj.partition_id()
                    self.sl1024[name] = self.pid[name] % 4 * 1024
                except Exception:
                    pass
                for o in self.seq:
                    if o.eng != name:
                        continue
                    for s, v in o.waits:
                        eng_obj.wait_ge(s.h, v)
                    if o.fn is not None:
                        inst = o.fn(eng_obj)
                        for s, k in o.incs:
                            inst = inst.then_inc(s.h, k)
                    else:
                        for s, k in o.incs:
                            eng_obj.sem_inc(s.h, k)

            @block.sync
            def _(e):
                run_engine(e, "sync")

            @block.gpsimd
            def _(e):
                run_engine(e, "gpsimd")

            @block.tensor
            def _(e):
                run_engine(e, "tensor")

            @block.scalar
            def _(e):
                run_engine(e, "scalar")

            @block.vector
            def _(e):
                run_engine(e, "vector")


class SbufMap:
    def __init__(self, nc, base, top):
        self.nc = nc
        self.base = base
        self.top = top
        self.items = []

    def add(self, name, shape, dtype, live):
        esz = 4 if dtype == F32 else 2
        bpp = int(np.prod(shape[1:])) * esz
        bpp = (bpp + 63) // 64 * 64
        self.items.append([name, shape, dtype, bpp, live])

    def finalize(self):
        placed = []
        out = {}
        for name, shape, dtype, bpp, live in sorted(self.items, key=lambda it: -it[3]):
            off = 0
            while True:
                clash = None
                for poff, psz, plive, pname in placed:
                    if not (live[1] < plive[0] or plive[1] < live[0]):
                        if not (off + bpp <= poff or poff + psz <= off):
                            clash = poff + psz
                            break
                if clash is None:
                    break
                off = (clash + 63) // 64 * 64
            assert self.base + off + bpp <= self.top, (
                f"SBUF overflow placing {name}: off={off} bpp={bpp}")
            placed.append((off, bpp, live, name))
            out[name] = self.nc.alloc_sbuf_tensor_at(
                name, list(shape), dtype, offset=self.base + off)
        return out


def build_program():
    nc = bass.Bass("TRN2", target_bir_lowering=False)

    x_full = nc.dram_tensor("x_full", [T, M], F32, kind="ExternalInput").ap()
    w_q = nc.dram_tensor("w_q", [128, H * MC * D], F32, kind="ExternalInput").ap()
    w_k = nc.dram_tensor("w_k", [128, MC * D], F32, kind="ExternalInput").ap()
    w_v = nc.dram_tensor("w_v", [128, MC * D], F32, kind="ExternalInput").ap()
    w_o = nc.dram_tensor("w_o", [128, G * H * M], F32, kind="ExternalInput").ap()
    wg_t = nc.dram_tensor("wg_t", [128, MC * HID], F32, kind="ExternalInput").ap()
    wu_t = nc.dram_tensor("wu_t", [128, MC * HID], F32, kind="ExternalInput").ap()
    wd_t = nc.dram_tensor("wd_t", [128, 64 * M], F32, kind="ExternalInput").ap()
    rope1 = nc.dram_tensor("rope1", [128, T], BF16, kind="ExternalInput").ap()
    rope2 = nc.dram_tensor("rope2", [128, T], BF16, kind="ExternalInput").ap()
    masks_i = nc.dram_tensor("masks", [4, 128, 512], BF16, kind="ExternalInput").ap()
    out_own = nc.dram_tensor("out_own", [1024, M], F32, kind="ExternalOutput").ap()

    h_dram = nc.dram_tensor("h_dram", [T, M], BF16)
    h2_dram = nc.dram_tensor("h2_dram", [1024, M], BF16)
    xmid_dram = nc.dram_tensor("xmid_dram", [1024, M], F32)
    oc_dram = [nc.dram_tensor(f"oc{i}_dram", [D, T], BF16) for i in range(H)]
    og_dram = [nc.dram_tensor(f"og{i}_dram", [G * D, T], BF16) for i in range(H)]
    wg_bf = nc.dram_tensor("wg_bf", [128, MC * HID], BF16)
    wu_bf = nc.dram_tensor("wu_bf", [128, MC * HID], BF16)

    dbg = {}
    if DEBUG:
        dbg["d_h"] = nc.dram_tensor("d_h", [T, M], BF16, kind="ExternalOutput").ap()
        dbg["d_q"] = nc.dram_tensor("d_q", [128, H * T], BF16, kind="ExternalOutput").ap()
        dbg["d_k"] = nc.dram_tensor("d_k", [128, T], BF16, kind="ExternalOutput").ap()
        dbg["d_v"] = nc.dram_tensor("d_v", [128, T], BF16, kind="ExternalOutput").ap()
        dbg["d_o"] = nc.dram_tensor("d_o", [128, H * T], BF16, kind="ExternalOutput").ap()
        dbg["d_xmid"] = nc.dram_tensor("d_xmid", [1024, M], F32, kind="ExternalOutput").ap()
        dbg["d_h2ft"] = nc.dram_tensor("d_h2ft", [128, MC * 1024], BF16, kind="ExternalOutput").ap()
        dbg["d_g"] = nc.dram_tensor("d_g", [128, 16 * 1024], BF16, kind="ExternalOutput").ap()
        dbg["d_r2"] = nc.dram_tensor("d_r2", [128, OTN * 2048], F32, kind="ExternalOutput").ap()

    smap = SbufMap(nc, (nc.sbuf_base + 63) // 64 * 64, nc.sbuf_top)
    smap.add("rope1_sb", [128, T], BF16, (0, 1))
    smap.add("rope2_sb", [128, T], BF16, (0, 1))
    smap.add("masks_sb", [128, 2048], BF16, (0, 2))
    smap.add("ones_bf", [128, 16], BF16, (0, 6))
    smap.add("ones_f", [1, 128], F32, (0, 6))
    smap.add("stats", [128, 64], F32, (0, 6))
    smap.add("recip_sb", [1, 512], F32, (2, 2))
    smap.add("recipb_sb", [1, 512], BF16, (2, 2))
    smap.add("pairt", [128, 1024], BF16, (2, 2))
    smap.add("ones_bfr", [1, 128], BF16, (0, 6))
    smap.add("bcast_sb", [128, 512], F32, (2, 2))
    smap.add("ptile", [128, 32 * 512], BF16, (2, 2))
    smap.add("dump", [128, 2048], F32, (0, 6))
    smap.add("dump2", [128, 2048], F32, (1, 1))
    smap.add("h_ft", [128, 2 * MC * 1024], BF16, (1, 1))
    smap.add("xt", [128, 2 * 2048], F32, (1, 1))
    smap.add("ht", [128, 2 * 2048], BF16, (1, 1))
    smap.add("qtmp", [128, 2 * 1024], BF16, (1, 1))
    smap.add("rtmp", [128, 2048], BF16, (1, 1))
    smap.add("rswp", [128, 1024], BF16, (1, 1))
    smap.add("wq_sb", [128, H * MC * D], BF16, (0, 1))
    smap.add("wk_sb", [128, MC * D], BF16, (0, 1))
    smap.add("wv_sb", [128, MC * D], BF16, (0, 1))
    smap.add("q_ft", [128, H * T], BF16, (1, 2))
    smap.add("k_ft", [128, T], BF16, (1, 2))
    smap.add("v_sb", [128, T], BF16, (1, 2))
    smap.add("o_ft", [128, H * T], BF16, (2, 3))
    smap.add("wo_sb", [128, G * H * M], BF16, (2, 4))
    smap.add("o16_sb", [128, G * H * 1024], BF16, (4, 4))
    smap.add("h2_ft", [128, MC * 1024], BF16, (4, 5))
    smap.add("xr", [128, 2 * 2048], F32, (4, 4))
    smap.add("ln_t", [128, 2 * 2048], F32, (4, 4))
    smap.add("xmid_t", [128, 2 * 2048], F32, (4, 4))
    smap.add("h2t", [128, 2 * 2048], BF16, (4, 4))
    smap.add("g_slab", [128, 16 * 1024], BF16, (5, 5))
    smap.add("r2", [128, OTN * 2048], F32, (5, 6))
    smap.add("wgu", [128, 2 * MC * 256], BF16, (5, 5))
    smap.add("wuu", [128, 2 * MC * 256], BF16, (5, 5))
    smap.add("wdu", [128, 8 * 2048], BF16, (5, 5))
    smap.add("silu_t", [128, 1024], F32, (5, 5))
    smap.add("xr6", [128, 2 * 2048], F32, (6, 6))
    smap.add("ln6", [128, 2 * 2048], F32, (6, 6))
    smap.add("xo6", [128, 2 * 2048], F32, (6, 6))
    sb = smap.finalize()

    P = [nc.alloc_psum_tensor(f"ps{i}", [128, 1024], F32) for i in range(4)]

    pl = Plan(nc)
    sCst = SemT(nc, "sCst")
    sXp = [SemT(nc, "sXa"), SemT(nc, "sXb")]
    sHp = [SemT(nc, "sHa"), SemT(nc, "sHb")]
    sTT = SemT(nc, "sTT")
    sWq = SemT(nc, "sWq")
    sWo = SemT(nc, "sWo")
    sWc = SemT(nc, "sWc")
    sFg = [SemT(nc, "sFga"), SemT(nc, "sFgb")]
    sFu = [SemT(nc, "sFua"), SemT(nc, "sFub")]
    sFdu = [SemT(nc, f"sFd{i}") for i in range(4)]
    sPE = SemT(nc, "sPE")
    sSE = SemT(nc, "sSE")
    sVE = SemT(nc, "sVE")
    sCCh = [SemT(nc, f"sCC{i}") for i in range(H)]
    sC1h = [SemT(nc, f"sC1{i}") for i in range(H)]
    sG = SemT(nc, "sG")
    sXRp = [SemT(nc, "sXRa"), SemT(nc, "sXRb")]
    sXMp = [SemT(nc, "sXMa"), SemT(nc, "sXMb")]
    sSTp = [SemT(nc, "sSTa"), SemT(nc, "sSTb")]
    sB = SemT(nc, "sB")

    rope1_sb, rope2_sb = sb["rope1_sb"], sb["rope2_sb"]
    masks_sb, stats, dump = sb["masks_sb"], sb["stats"], sb["dump"]
    h_ft, q_ft, k_ft, v_sb, o_ft = (sb["h_ft"], sb["q_ft"], sb["k_ft"],
                                    sb["v_sb"], sb["o_ft"])

    pl.op("vector", lambda e: e.memset(sb["ones_bf"][:, :], 1.0))
    pl.op("vector", lambda e: e.memset(sb["ones_f"][:, :], 1.0))
    pl.op("vector", lambda e: e.memset(sb["ones_bfr"][:, :], 1.0), inc=[(sVE, 1)])
    ones_ready = sVE.inc()
    pl.op("sync", lambda e: e.dma_start(out=rope1_sb[:, :], in_=rope1), inc=[(sCst, 16)])
    sCst.inc(16)
    pl.op("sync", lambda e: e.dma_start(out=rope2_sb[:, :], in_=rope2), inc=[(sCst, 16)])
    sCst.inc(16)
    pl.op("sync",
          lambda e: e.dma_start(out=masks_sb[:, :].rearrange("p (r i) -> p r i", r=4),
                                in_=masks_i.rearrange("r p i -> p r i")),
          inc=[(sCst, 16)])
    rope_ready = sCst.inc(16)

    pl.op("gpsimd", lambda e: e.dma_start(out=sb["wk_sb"][:, :], in_=w_k),
          inc=[(sWq, 16)], tag="wk")
    sWq.inc(16)
    pl.op("gpsimd", lambda e: e.dma_start(out=sb["wv_sb"][:, :], in_=w_v),
          inc=[(sWq, 16)], tag="wv")
    sWq.inc(16)
    pl.op("gpsimd", lambda e: e.dma_start(out=sb["wq_sb"][:, :], in_=w_q),
          inc=[(sWq, 16)], tag="wq")
    sWq.inc(16)
    wqkv_ready = sWq.n

    xt, ht = sb["xt"], sb["ht"]
    ln_done = [0] * NT
    h_mark = [0] * NT

    def emit_ln_tile(ti):
        slot = ti % 2
        xa = xt[:, slot * 2048:(slot + 1) * 2048]
        ha = ht[:, slot * 2048:(slot + 1) * 2048]
        w = [(sVE, ln_done[ti - 2])] if ti >= 2 else []
        pl.op("gpsimd", lambda e, ti=ti, xa=xa: e.dma_start(
            out=xa, in_=x_full[ti * 128:(ti + 1) * 128, :]),
            wait=w, inc=[(sXp[slot], 16)], tag=f"x{ti}")
        xv = sXp[slot].inc(16)
        sl = slot * 8
        sx = stats[:, sl:sl + 1]
        sxx = stats[:, sl + 1:sl + 2]
        v0 = stats[:, sl + 2:sl + 3]
        negm = stats[:, sl + 3:sl + 4]
        rstd = stats[:, sl + 4:sl + 5]
        pl.op("vector", lambda e, xa=xa, sx=sx: e.tensor_scalar(
            sb["dump2"][:, :], xa, 1.0, 0.0, ALU.mult, ALU.add, accum_out=sx),
            wait=[(sXp[slot], xv)], inc=[(sVE, 1)], tag=f"ln{ti}-sx")
        m_vx = sVE.inc()
        pl.op("scalar", lambda e, xa=xa, sxx=sxx: e.activation(
            dump[:, :], xa, AF.Square, accum_out=sxx),
            wait=[(sXp[slot], xv)], inc=[(sSE, 1)], tag=f"ln{ti}-sxx")
        m_se = sSE.inc()
        pl.op("vector", lambda e, sx=sx, v0=v0: e.tensor_mul(v0, sx, sx),
              wait=[(sSE, m_se)])
        pl.op("vector", lambda e, v0=v0, sxx=sxx: e.scalar_tensor_tensor(
            v0, v0, -1.0 / 2048, sxx, ALU.mult, ALU.add))
        pl.op("vector", lambda e, sx=sx, negm=negm: e.tensor_scalar(
            negm, sx, -1.0 / 2048, None, ALU.mult), inc=[(sVE, 1)], tag=f"ln{ti}-v")
        m_ve = sVE.inc()
        pl.op("scalar", lambda e, v0=v0: e.activation(v0, v0, AF.Ln,
                                                      scale=np.float32(1.0 / 2048), bias=np.float32(EPS)),
              wait=[(sVE, m_ve)])
        pl.op("scalar", lambda e, v0=v0, rstd=rstd: e.activation(
            rstd, v0, AF.Exp, scale=np.float32(-0.5)), inc=[(sSE, 1)], tag=f"ln{ti}-r")
        m_se2 = sSE.inc()
        wn = [(sSE, m_se2)]
        if ti >= 2 and h_mark[ti - 2]:
            wn.append((sHp[slot], h_mark[ti - 2]))
        pl.op("vector", lambda e, ha=ha, xa=xa, negm=negm, rstd=rstd: e.tensor_scalar(
            ha, xa, negm, rstd, ALU.add, ALU.mult),
            wait=wn, inc=[(sVE, 1)], tag=f"ln{ti}-n")
        ln_done[ti] = sVE.inc()
        pl.op("sync", lambda e, ti=ti, ha=ha: e.dma_start(
            out=h_dram.ap()[ti * 128:(ti + 1) * 128, :], in_=ha),
            wait=[(sVE, ln_done[ti])], inc=[(sHp[slot], 16)], tag=f"h{ti}")
        h_mark[ti] = sHp[slot].inc(16)

    qkv_state = {"unit": 0, "vunit": 0, "sse_unit": [0, 0, 0], "sve_unit": [0, 0],
                 "sse_v": [0, 0]}
    trans_done = [0] * 4

    def emit_qkv_quarter(qu):
        tbase = qu * 1024
        t_ready = sTT.n
        hbase = (qu % 2) * MC * 1024
        for u in range(5):
            ps = P[qkv_state["unit"] % 3]
            w = [(sTT, t_ready)]
            w.append((sWq, wqkv_ready))
            prev = qkv_state["sse_unit"][qkv_state["unit"] % 3]
            if prev:
                w.append((sVE, prev))
            for mc in range(MC):
                if u < 4:
                    lhsT = sb["wq_sb"][:, (u * MC + mc) * D:(u * MC + mc + 1) * D]
                else:
                    lhsT = sb["wk_sb"][:, mc * D:(mc + 1) * D]
                for half in range(2):
                    pl.op("tensor", lambda e, lhsT=lhsT, half=half, mc=mc, ps=ps, hbase=hbase:
                          e.matmul(
                        ps[:, half * 512:(half + 1) * 512], lhsT,
                        h_ft[:, hbase + mc * 1024 + half * 512:
                             hbase + mc * 1024 + (half + 1) * 512],
                        start=(mc == 0), stop=(mc == MC - 1)),
                        wait=(w if (mc == 0 and half == 0) else []),
                        inc=([(sPE, 1)] if (mc == MC - 1 and half == 1) else []),
                        tag=f"qkvM{qu}.{u}")
            m_pe = sPE.inc()
            qslot = qkv_state["unit"] % 2
            qa = sb["qtmp"][:, qslot * 1024:(qslot + 1) * 1024]
            pslot = qkv_state["unit"] % 3
            wse = [(sPE, m_pe)]
            pl.op("vector", lambda e, qa=qa, ps=ps: e.tensor_copy(qa, ps[:, :]),
                  wait=wse, inc=[(sVE, 1)], tag=f"qkvE{qu}.{u}")
            qkv_state["sse_unit"][pslot] = sVE.inc()
            if u < 4:
                dst = q_ft[:, u * T + tbase:u * T + tbase + 1024]
            else:
                dst = k_ft[:, tbase:tbase + 1024]
            cs1 = rope1_sb[:, tbase:tbase + 1024]
            cs2 = rope2_sb[:, tbase:tbase + 1024]
            rt1 = sb["rtmp"][:, 0:1024]
            rt2 = sb["rtmp"][:, 1024:2048]
            sw = sb["rswp"]
            pl.op("vector", lambda e, qa=qa, cs1=cs1, rt1=rt1: e.tensor_mul(rt1, qa, cs1),
                  wait=[(sCst, rope_ready)], tag=f"rp{qu}.{u}")
            pl.op("vector", lambda e, qa=qa, cs2=cs2, rt2=rt2: e.tensor_mul(rt2, qa, cs2))
            pl.op("vector", lambda e, rt1=rt1, sw=sw: e.tensor_copy(
                sw[0:64, :], rt1[64:128, :]))
            pl.op("vector", lambda e, rt2=rt2, sw=sw: e.tensor_copy(
                sw[64:128, :], rt2[0:64, :]))
            pl.op("vector", lambda e, dst=dst, rt1=rt1, sw=sw: e.tensor_sub(
                dst[0:64, :], rt1[0:64, :], sw[0:64, :]))
            pl.op("vector", lambda e, dst=dst, rt2=rt2, sw=sw: e.tensor_add(
                dst[64:128, :], rt2[64:128, :], sw[64:128, :]),
                inc=[(sVE, 1)], tag=f"rpd{qu}.{u}")
            qkv_state["sve_unit"][qslot] = sVE.inc()
            qkv_state["unit"] += 1
        for tt in range(8):
            vslot = qkv_state["vunit"] % 2
            vp = P[3][:, vslot * 512:vslot * 512 + 128]
            w = [(sTT, t_ready)]
            prev = qkv_state["sse_v"][qkv_state["vunit"] % 2]
            if prev:
                w.append((sVE, prev))
            for mc in range(MC):
                lhsT = h_ft[:, hbase + mc * 1024 + tt * 128:
                            hbase + mc * 1024 + (tt + 1) * 128]
                pl.op("tensor", lambda e, lhsT=lhsT, mc=mc, vp=vp: e.matmul(
                    vp, lhsT, sb["wv_sb"][:, mc * D:(mc + 1) * D],
                    start=(mc == 0), stop=(mc == MC - 1)),
                    wait=(w if mc == 0 else []),
                    inc=([(sPE, 1)] if mc == MC - 1 else []), tag=f"vM{qu}.{tt}")
            m_pe = sPE.inc()
            gt = qu * 8 + tt
            pl.op("vector", lambda e, gt=gt, vp=vp: e.tensor_copy(
                v_sb[:, gt * 128:(gt + 1) * 128], vp),
                wait=[(sPE, m_pe)], inc=[(sVE, 1)], tag=f"vE{qu}.{tt}")
            qkv_state["sse_v"][qkv_state["vunit"] % 2] = sVE.inc()
            qkv_state["vunit"] += 1
        trans_done[qu] = sPE.n

    def emit_transposes(qu):
        hbase = (qu % 2) * MC * 1024
        for mc in range(MC):
            w = []
            if mc == 0:
                w.append((sHp[0], sHp[0].n))
                w.append((sHp[1], sHp[1].n))
                if qu >= 2:
                    w.append((sPE, trans_done[qu - 2]))
            pl.op("sync", lambda e, qu=qu, mc=mc, hbase=hbase: e.dma_start(
                out=h_ft[:, hbase + mc * 1024:hbase + (mc + 1) * 1024],
                in_=h_dram.ap()[qu * 1024:(qu + 1) * 1024, mc * 128:(mc + 1) * 128],
                transpose=True),
                wait=w, inc=[(sTT, 16)], tag=f"T{qu}.{mc}")
            sTT.inc(16)

    for ti in range(8):
        emit_ln_tile(ti)
    emit_transposes(0)
    for ti in range(8, 16):
        emit_ln_tile(ti)
    emit_qkv_quarter(0)
    emit_transposes(1)
    for ti in range(16, 24):
        emit_ln_tile(ti)
    emit_qkv_quarter(1)
    emit_transposes(2)
    for ti in range(24, 32):
        emit_ln_tile(ti)
    emit_qkv_quarter(2)
    emit_transposes(3)
    emit_qkv_quarter(3)

    if DEBUG:
        pl.op("sync", lambda e: e.dma_start(out=dbg["d_h"], in_=h_dram.ap()),
              wait=[(sHp[0], sHp[0].n), (sHp[1], sHp[1].n)], inc=[(sSTp[0], 16)])
        sSTp[0].inc(16)

    pl.barrier(sB)
    pl.op("gpsimd", lambda e: e.dma_start(out=sb["wo_sb"][:, :], in_=w_o),
          inc=[(sWo, 16)], tag="wo")
    sWo.inc(16)
    wo_ready = sWo.n
    for nm, srcw, dstw in (("wg", wg_t, wg_bf), ("wu", wu_t, wu_bf)):
        pl.op("gpsimd", lambda e, srcw=srcw, dstw=dstw: e.dma_start(out=dstw.ap(), in_=srcw),
              inc=[(sWc, 16)], tag=f"cast-{nm}")
        sWc.inc(16)
    ffn_cast_ready = sWc.n
    ptile = sb["ptile"]
    att = {"jglob": 0, "norm_done": [0, 0], "recip_done": [0, 0], "bcast_copy": 0,
           "exp_mark": {}, "pv_done": {}}

    if DEBUG:
        pl.op("sync", lambda e: e.dma_start(out=dbg["d_q"], in_=q_ft[:, :]),
              inc=[(sSTp[0], 16)])
        sSTp[0].inc(16)
        pl.op("sync", lambda e: e.dma_start(out=dbg["d_k"], in_=k_ft[:, :]),
              inc=[(sSTp[0], 16)])
        sSTp[0].inc(16)
        pl.op("sync", lambda e: e.dma_start(out=dbg["d_v"], in_=v_sb[:, :]),
              inc=[(sSTp[0], 16)])
        sSTp[0].inc(16)

    unit = 0
    head_norm = [0] * H
    att["pair_mark"] = [0, 0]
    att["recipb_done"] = [0, 0]
    tailA_q = []
    tailB_q = []
    SBANK = [(0, 0), (0, 512), (3, 0)]

    def emit_tailA(tu):
        (t_spp, t_opp, t_hd, t_q0, t_mpe) = tu
        pl.op("scalar", lambda e, t_spp=t_spp: e.activation(
            sb["recip_sb"][0:1, :], P[2][0:1, t_spp * 512:(t_spp + 1) * 512], AF.Ln),
            wait=[(sPE, t_mpe)], tag="aRL")
        pl.op("scalar", lambda e: e.activation(
            sb["recipb_sb"][0:1, :], sb["recip_sb"][0:1, :], AF.Exp,
            scale=np.float32(-1.0)),
            inc=[(sSE, 1)], tag="aRE")
        att["recipb_done"][t_spp] = sSE.inc()
        return att["recipb_done"][t_spp]

    def emit_tailB(tu, m_rb):
        (t_spp, t_opp, t_hd, t_q0, t_mpe) = tu
        wb = [(sSE, m_rb)]
        if att["bcast_copy"]:
            wb.append((sSE, att["bcast_copy"]))
        pl.op("tensor", lambda e: e.matmul(
            P[3][:, 512:1024], sb["ones_bfr"][0:1, :], sb["recipb_sb"][0:1, :],
            start=True, stop=True),
            wait=wb, inc=[(sPE, 1)], tag="aB")
        m_b = sPE.inc()
        pl.op("scalar", lambda e: e.copy(sb["bcast_sb"][:, :], P[3][:, 512:1024]),
              wait=[(sPE, m_b)], inc=[(sSE, 1)], tag="aC")
        att["bcast_copy"] = sSE.inc()
        pl.op("vector", lambda e, t_hd=t_hd, t_q0=t_q0, t_opp=t_opp: e.tensor_mul(
            o_ft[:, t_hd * T + t_q0:t_hd * T + t_q0 + 512],
            P[1][:, t_opp * 512:(t_opp + 1) * 512],
            sb["bcast_sb"][:, :]),
            wait=[(sSE, att["bcast_copy"])], inc=[(sVE, 1)], tag="aN")
        att["norm_done"][t_opp] = sVE.inc()

    for hd in range(H):
        for qc in range(8):
            jmax = 4 * (qc + 1)
            q0 = 512 * qc
            opp = unit % 2
            spp = unit % 2
            p_ready = [None] * jmax
            pv_prev = att.get("pv_marks", [None] * 32)
            pv_marks = [None] * 32
            for j in range(jmax):
                jg = att["jglob"] + j
                pt, pc = SBANK[jg % 3]
                w = []
                pe_prev = att["exp_mark"].get(jg - 3)
                if pe_prev:
                    w.append(pe_prev)
                pl.op("tensor", lambda e, pt=pt, pc=pc, j=j, hd=hd, q0=q0: e.matmul(
                    P[pt][:, pc:pc + 512],
                    k_ft[:, j * 128:(j + 1) * 128],
                    q_ft[:, hd * T + q0:hd * T + q0 + 512],
                    start=True, stop=True),
                    wait=w, inc=[(sPE, 1)], tag=f"aS{unit}.{j}")
                m_s = sPE.inc()
                wse = [(sPE, m_s)]
                if pv_prev[j] is not None:
                    wse.append((sPE, pv_prev[j]))
                pl.op("scalar", lambda e, j=j, pt=pt, pc=pc: e.activation(
                    ptile[:, j * 512:(j + 1) * 512],
                    P[pt][:, pc:pc + 512],
                    AF.Exp, scale=np.float32(QK2), bias=np.float32(-SHIFT)),
                    wait=wse, inc=[(sSE, 1)], tag=f"aE{unit}.{j}")
                m_e = sSE.inc()
                if j >= jmax - 4:
                    r = j - (jmax - 4)
                    pl.op("vector", lambda e, j=j, r=r: e.tensor_mul(
                        ptile[:, j * 512:(j + 1) * 512],
                        ptile[:, j * 512:(j + 1) * 512],
                        masks_sb[:, r * 512:(r + 1) * 512]),
                        wait=[(sSE, m_e)], inc=[(sVE, 1)], tag=f"aM{unit}.{j}")
                    p_ready[j] = (sVE, sVE.inc())
                else:
                    p_ready[j] = (sSE, m_e)
                att["exp_mark"][jg] = p_ready[j]
            if tailA_q:
                tu = tailA_q.pop()
                m_rb = emit_tailA(tu)
                tailB_q.append((tu, m_rb))
            npairs = jmax // 2
            for pr in range(npairs):
                j0, j1 = 2 * pr, 2 * pr + 1
                ppp = pr % 2
                wp = [p_ready[j0], p_ready[j1]]
                if att["pair_mark"][ppp]:
                    wp.append((sPE, att["pair_mark"][ppp]))
                pl.op("vector", lambda e, j0=j0, j1=j1, ppp=ppp: e.tensor_add(
                    sb["pairt"][:, ppp * 512:(ppp + 1) * 512],
                    ptile[:, j0 * 512:(j0 + 1) * 512],
                    ptile[:, j1 * 512:(j1 + 1) * 512]),
                    wait=wp, inc=[(sVE, 1)], tag=f"aA{unit}.{pr}")
                m_pair = sVE.inc()
                w2 = [(sVE, m_pair)]
                if pr == 0:
                    if att["recipb_done"][spp]:
                        w2.append((sSE, att["recipb_done"][spp]))
                    if att["norm_done"][opp]:
                        w2.append((sVE, att["norm_done"][opp]))
                pl.op("tensor", lambda e, spp=spp, ppp=ppp, pr=pr, npairs=npairs: e.matmul(
                    P[2][0:1, spp * 512:(spp + 1) * 512],
                    sb["ones_bf"][:, 0:1],
                    sb["pairt"][:, ppp * 512:(ppp + 1) * 512],
                    start=(pr == 0), stop=(pr == npairs - 1)),
                    wait=w2, tag=f"aU{unit}.{pr}")
                for j in (j0, j1):
                    pl.op("tensor", lambda e, opp=opp, j=j, jmax=jmax: e.matmul(
                        P[1][:, opp * 512:(opp + 1) * 512],
                        v_sb[:, j * 128:(j + 1) * 128],
                        ptile[:, j * 512:(j + 1) * 512],
                        start=(j == 0), stop=(j == jmax - 1)),
                        wait=([p_ready[j]] if j == j1 else []),
                        inc=[(sPE, 1)], tag=f"aP{unit}.{j}")
                    pv_marks[j] = sPE.inc()
                att["pair_mark"][ppp] = pv_marks[j1]
            att["pv_marks"] = pv_marks
            if tailB_q:
                emit_tailB(*tailB_q.pop())
            tailA_q.append((spp, opp, hd, q0, pv_marks[jmax - 1]))
            att["jglob"] += jmax
            unit += 1
        while tailA_q or tailB_q:
            if tailB_q:
                emit_tailB(*tailB_q.pop())
            if tailA_q:
                tu = tailA_q.pop()
                m_rb = emit_tailA(tu)
                tailB_q.append((tu, m_rb))
        head_norm[hd] = sVE.n
    o_done = sVE.n

    if DEBUG:
        pl.op("sync", lambda e: e.dma_start(out=dbg["d_o"], in_=o_ft[:, :]),
              wait=[(sVE, o_done)], inc=[(sSTp[0], 16)])
        sSTp[0].inc(16)

    for hd in range(H):
        pl.op("sync", lambda e, hd=hd: e.dma_start(
            out=oc_dram[hd].ap(), in_=o_ft[:, hd * T:(hd + 1) * T]),
            wait=[(sVE, head_norm[hd])], inc=[(sC1h[hd], 16)], tag=f"oc{hd}")
        sC1h[hd].inc(16)
        pl.op("gpsimd", lambda e, hd=hd: e.collective_compute(
            "AllGather", ALU.bypass,
            replica_groups=[[0, 1, 2, 3], [4, 5, 6, 7]],
            ins=[oc_dram[hd].ap()], outs=[og_dram[hd].ap()]),
            wait=[(sC1h[hd], 16)], inc=[(sCCh[hd], 1)], tag=f"allgather{hd}")
        sCCh[hd].inc(1)

    for hd in range(H):
        pl.op("sync", lambda e, hd=hd: e.dma_start(
            out=sb["o16_sb"][:, :].rearrange("p (gh t) -> p gh t", gh=G * H)[
                :, hd::H, :],
            in_=og_dram[hd].ap().rearrange("(g d) t -> d g t", d=128)[
                :, :, bass.ds(pl.sl1024["sync"], 1024)]),
            wait=[(sCCh[hd], 1)], inc=[(sG, 16)], tag=f"gather{hd}")
        sG.inc(16)

    p4 = {"psum_free": [0, 0], "h2dma": [0, 0], "xmdma": [0, 0], "xr_used": [0, 0]}
    for ot in range(OTN):
        pp = ot % 2
        PA, PB = (P[0], P[1]) if pp == 0 else (P[2], P[3])
        wxr = [(sVE, p4["xr_used"][pp])] if p4["xr_used"][pp] else []
        pl.op("sync", lambda e, ot=ot, pp=pp: e.dma_start(
            out=sb["xr"][:, pp * 2048:(pp + 1) * 2048],
            in_=x_full[bass.ds(pl.sl1024["sync"] + ot * 128, 128), :]),
            wait=wxr, inc=[(sXRp[pp], 16)], tag=f"xr{ot}")
        m_xr = sXRp[pp].inc(16)
        w = [(sG, sG.n), (sWo, wo_ready)]
        if p4["psum_free"][pp]:
            w.append((sVE, p4["psum_free"][pp]))
        for gh in range(G * H):
            lhsT = sb["o16_sb"][:, gh * 1024 + ot * 128:gh * 1024 + (ot + 1) * 128]
            for k in range(4):
                tgt = PA if k < 2 else PB
                pl.op("tensor", lambda e, lhsT=lhsT, gh=gh, k=k, tgt=tgt: e.matmul(
                    tgt[:, (k % 2) * 512:(k % 2 + 1) * 512], lhsT,
                    sb["wo_sb"][:, gh * M + k * 512:gh * M + (k + 1) * 512],
                    start=(gh == 0), stop=(gh == G * H - 1)),
                    wait=(w if (gh == 0 and k == 0) else []),
                    inc=([(sPE, 1)] if (gh == G * H - 1 and k == 3) else []),
                    tag=f"oM{ot}.{gh}.{k}")
        m_pe = sPE.inc()
        sl = pp * 16
        sxa, sxb = stats[:, sl:sl + 1], stats[:, sl + 1:sl + 2]
        sxxa, sxxb = stats[:, sl + 2:sl + 3], stats[:, sl + 3:sl + 4]
        v0 = stats[:, sl + 4:sl + 5]
        negm = stats[:, sl + 5:sl + 6]
        rstd = stats[:, sl + 6:sl + 7]
        pl.op("scalar", lambda e, PA=PA, sxa=sxa: e.activation(
            dump[:, 0:1024], PA[:, :], AF.Copy, accum_out=sxa),
            wait=[(sPE, m_pe)], tag=f"rA{ot}")
        pl.op("scalar", lambda e, PB=PB, sxb=sxb: e.activation(
            dump[:, 0:1024], PB[:, :], AF.Copy, accum_out=sxb))
        pl.op("scalar", lambda e, PA=PA, sxxa=sxxa: e.activation(
            dump[:, 0:1024], PA[:, :], AF.Square, accum_out=sxxa))
        pl.op("scalar", lambda e, PB=PB, sxxb=sxxb: e.activation(
            dump[:, 0:1024], PB[:, :], AF.Square, accum_out=sxxb),
            inc=[(sSE, 1)], tag=f"rS{ot}")
        m_se = sSE.inc()
        pl.op("vector", lambda e, sxa=sxa, sxb=sxb: e.tensor_add(sxa, sxa, sxb),
              wait=[(sSE, m_se)])
        pl.op("vector", lambda e, sxxa=sxxa, sxxb=sxxb: e.tensor_add(sxxa, sxxa, sxxb))
        pl.op("vector", lambda e, sxa=sxa, v0=v0: e.tensor_mul(v0, sxa, sxa))
        pl.op("vector", lambda e, v0=v0, sxxa=sxxa: e.scalar_tensor_tensor(
            v0, v0, -1.0 / 2048, sxxa, ALU.mult, ALU.add))
        pl.op("vector", lambda e, sxa=sxa, negm=negm: e.tensor_scalar(
            negm, sxa, -1.0 / 2048, None, ALU.mult), inc=[(sVE, 1)], tag=f"rV{ot}")
        m_ve = sVE.inc()
        pl.op("scalar", lambda e, v0=v0: e.activation(v0, v0, AF.Ln,
                                                      scale=np.float32(1.0 / 2048), bias=np.float32(EPS)),
              wait=[(sVE, m_ve)])
        pl.op("scalar", lambda e, v0=v0, rstd=rstd: e.activation(
            rstd, v0, AF.Exp, scale=np.float32(-0.5)), inc=[(sSE, 1)], tag=f"rR{ot}")
        m_se2 = sSE.inc()
        lnta = sb["ln_t"][:, pp * 2048:pp * 2048 + 1024]
        lntb = sb["ln_t"][:, pp * 2048 + 1024:(pp + 1) * 2048]
        pl.op("vector", lambda e, PA=PA, lnta=lnta, negm=negm, rstd=rstd: e.tensor_scalar(
            lnta, PA[:, :], negm, rstd, ALU.add, ALU.mult),
            wait=[(sSE, m_se2)], tag=f"rlA{ot}")
        pl.op("vector", lambda e, PB=PB, lntb=lntb, negm=negm, rstd=rstd: e.tensor_scalar(
            lntb, PB[:, :], negm, rstd, ALU.add, ALU.mult),
            inc=[(sVE, 1)], tag=f"rlB{ot}")
        p4["psum_free"][pp] = sVE.inc()
        sx2 = stats[:, sl + 7:sl + 8]
        xma = sb["xmid_t"][:, pp * 2048:(pp + 1) * 2048]
        lnt = sb["ln_t"][:, pp * 2048:(pp + 1) * 2048]
        xra = sb["xr"][:, pp * 2048:(pp + 1) * 2048]
        wxm = [(sXRp[pp], m_xr)]
        if p4["xmdma"][pp]:
            wxm.append((sXMp[pp], p4["xmdma"][pp]))
        pl.op("vector", lambda e, xma=xma, lnt=lnt, xra=xra, sx2=sx2:
              e.scalar_tensor_tensor(xma, lnt, 1.0, xra, ALU.mult, ALU.add,
                                     accum_out=sx2),
              wait=wxm, inc=[(sVE, 1)], tag=f"xm{ot}")
        p4["xr_used"][pp] = sVE.inc()
        m_xmid = p4["xr_used"][pp]
        sxx2 = stats[:, sl + 8:sl + 9]
        v02 = stats[:, sl + 9:sl + 10]
        negm2 = stats[:, sl + 10:sl + 11]
        rstd2 = stats[:, sl + 11:sl + 12]
        pl.op("scalar", lambda e, xma=xma, sxx2=sxx2: e.activation(
            dump[:, :], xma, AF.Square, accum_out=sxx2),
            wait=[(sVE, m_xmid)], inc=[(sSE, 1)], tag=f"l2S{ot}")
        m_se3 = sSE.inc()
        pl.op("vector", lambda e, sx2=sx2, v02=v02: e.tensor_mul(v02, sx2, sx2),
              wait=[(sSE, m_se3)])
        pl.op("vector", lambda e, v02=v02, sxx2=sxx2: e.scalar_tensor_tensor(
            v02, v02, -1.0 / 2048, sxx2, ALU.mult, ALU.add))
        pl.op("vector", lambda e, sx2=sx2, negm2=negm2: e.tensor_scalar(
            negm2, sx2, -1.0 / 2048, None, ALU.mult), inc=[(sVE, 1)], tag=f"l2V{ot}")
        m_ve2 = sVE.inc()
        pl.op("scalar", lambda e, v02=v02: e.activation(v02, v02, AF.Ln,
                                                        scale=np.float32(1.0 / 2048), bias=np.float32(EPS)),
              wait=[(sVE, m_ve2)])
        pl.op("scalar", lambda e, v02=v02, rstd2=rstd2: e.activation(
            rstd2, v02, AF.Exp, scale=np.float32(-0.5)), inc=[(sSE, 1)], tag=f"l2R{ot}")
        m_se4 = sSE.inc()
        h2a = sb["h2t"][:, pp * 2048:(pp + 1) * 2048]
        wh2 = [(sSE, m_se4)]
        if p4["h2dma"][pp]:
            wh2.append((sXMp[pp], p4["h2dma"][pp]))
        pl.op("vector", lambda e, h2a=h2a, xma=xma, negm2=negm2, rstd2=rstd2:
              e.tensor_scalar(h2a, xma, negm2, rstd2, ALU.add, ALU.mult),
              wait=wh2, inc=[(sVE, 1)], tag=f"h2{ot}")
        m_h2 = sVE.inc()
        pl.op("sync", lambda e, ot=ot, xma=xma: e.dma_start(
            out=xmid_dram.ap()[ot * 128:(ot + 1) * 128, :], in_=xma),
            wait=[(sVE, m_h2)], inc=[(sXMp[pp], 16)], tag=f"sxm{ot}")
        p4["xmdma"][pp] = sXMp[pp].inc(16)
        pl.op("sync", lambda e, ot=ot, h2a=h2a: e.dma_start(
            out=h2_dram.ap()[ot * 128:(ot + 1) * 128, :], in_=h2a),
            inc=[(sXMp[pp], 16)], tag=f"sh2{ot}")
        p4["h2dma"][pp] = sXMp[pp].inc(16)
    xm_all_w = [(sXMp[0], sXMp[0].n), (sXMp[1], sXMp[1].n)]

    if DEBUG:
        pl.op("sync", lambda e: e.dma_start(out=dbg["d_xmid"], in_=xmid_dram.ap()),
              wait=xm_all_w, inc=[(sSTp[0], 16)])
        sSTp[0].inc(16)

    for mc in range(MC):
        pl.op("sync", lambda e, mc=mc: e.dma_start(
            out=sb["h2_ft"][:, mc * 1024:(mc + 1) * 1024],
            in_=h2_dram.ap()[:, mc * 128:(mc + 1) * 128], transpose=True),
            wait=(xm_all_w if mc == 0 else []), inc=[(sTT, 16)], tag=f"T2.{mc}")
        sTT.inc(16)
    h2ft_ready = sTT.n

    if DEBUG:
        pl.op("sync", lambda e: e.dma_start(out=dbg["d_h2ft"], in_=sb["h2_ft"][:, :]),
              wait=[(sTT, h2ft_ready)], inc=[(sSTp[0], 16)])
        sSTp[0].inc(16)

    pl.barrier(sB)
    ffn = {"gu_pe": {}, "gu_used": [0, 0], "silu_prev": 0,
           "r2_evac": [0, 0], "down_pe_all": 0}
    load_marks = {}

    def load_gu(which, slab, u):
        gidx = slab * 8 + u
        slot = gidx % 2
        h0 = slab * 2048 + u * 256
        dst = sb["wgu"] if which == "wg" else sb["wuu"]
        srcw = wg_bf if which == "wg" else wu_bf
        sem = sFg[slot] if which == "wg" else sFu[slot]
        wk_ = []
        if gidx >= 2 and (gidx - 2) in ffn["gu_pe"]:
            wk_.append((sPE, ffn["gu_pe"][gidx - 2]))
        if gidx == 0:
            wk_.append((sWc, ffn_cast_ready))
        pl.op("sync", lambda e, dst=dst, srcw=srcw, h0=h0, slot=slot: e.dma_start(
            out=dst[:, slot * MC * 256:(slot + 1) * MC * 256].rearrange(
                "p (mc h) -> p mc h", mc=MC),
            in_=srcw.ap().rearrange("p (mc h) -> p mc h", mc=MC)[:, :, h0:h0 + 256]),
            wait=wk_, inc=[(sem, 16)], tag=f"u{which}{slab}.{u}")
        return (sem, sem.inc(16))

    def load_wd_half(slab, hs):
        marks = []
        for ul in range(4):
            h0 = slab * 2048 + hs * 1024 + ul * 256
            wk_ = []
            if ul == 0:
                if ffn["down_pe_all"]:
                    wk_.append((sPE, ffn["down_pe_all"]))
            hc0 = h0 // 128
            pl.op("gpsimd", lambda e, hc0=hc0, ul=ul: e.dma_start(
                out=sb["wdu"][:, ul * 2 * 2048:(ul + 1) * 2 * 2048].rearrange(
                    "p (hs2 m) -> p hs2 m", hs2=2),
                in_=wd_t.rearrange("p (hc m) -> p hc m", hc=64)[:, hc0:hc0 + 2, :]),
                wait=wk_, inc=[(sFdu[ul], 16)], tag=f"uwd{slab}.{hs}.{ul}")
            marks.append((sFdu[ul], sFdu[ul].inc(16)))
        return marks

    for slab in range(4):
        wd_mark = load_wd_half(slab, 0)
        for hc in range(16):
            u, sub = hc // 2, hc % 2
            if sub == 0:
                load_marks[("wg", slab, u)] = load_gu("wg", slab, u)
                load_marks[("wu", slab, u)] = load_gu("wu", slab, u)
            gp = P[0] if hc % 2 == 0 else P[2]
            up = P[1] if hc % 2 == 0 else P[3]
            gidx = slab * 8 + u
            slot = gidx % 2
            w = [load_marks[("wg", slab, u)], load_marks[("wu", slab, u)]]
            if slab == 0 and hc == 0:
                w.append((sTT, h2ft_ready))
            prev = ffn["gu_used"][hc % 2]
            if prev:
                w.append((sVE, prev))
            if slab > 0 and hc < 2:
                if ffn["r2_evac"][0]:
                    w.append((sVE, ffn["r2_evac"][0]))
                if ffn["r2_evac"][1]:
                    w.append((sVE, ffn["r2_evac"][1]))
            for mc in range(MC):
                base = slot * MC * 256 + mc * 256 + sub * 128
                lg = sb["wgu"][:, base:base + 128]
                lu = sb["wuu"][:, base:base + 128]
                for half in range(2):
                    pl.op("tensor", lambda e, lg=lg, gp=gp, mc=mc, half=half: e.matmul(
                        gp[:, half * 512:(half + 1) * 512], lg,
                        sb["h2_ft"][:, mc * 1024 + half * 512:mc * 1024 + (half + 1) * 512],
                        start=(mc == 0), stop=(mc == MC - 1)),
                        wait=(w if (mc == 0 and half == 0) else []),
                        inc=([(sPE, 1)] if (mc == MC - 1 and half == 1) else []),
                        tag=f"gM{slab}.{hc}")
                for half in range(2):
                    pl.op("tensor", lambda e, lu=lu, up=up, mc=mc, half=half: e.matmul(
                        up[:, half * 512:(half + 1) * 512], lu,
                        sb["h2_ft"][:, mc * 1024 + half * 512:mc * 1024 + (half + 1) * 512],
                        start=(mc == 0), stop=(mc == MC - 1)),
                        inc=([(sPE, 1)] if (mc == MC - 1 and half == 1) else []),
                        tag=f"uM{slab}.{hc}")
            m_gate = sPE.inc()
            m_up = sPE.inc()
            if sub == 1:
                ffn["gu_pe"][gidx] = m_up
            wse = [(sPE, m_gate)]
            if ffn["silu_prev"]:
                wse.append((sVE, ffn["silu_prev"]))
            pl.op("scalar", lambda e, gp=gp: e.activation(
                sb["silu_t"][:, :], gp[:, :], AF.Silu),
                wait=wse, inc=[(sSE, 1)], tag=f"si{slab}.{hc}")
            m_si = sSE.inc()
            pl.op("vector", lambda e, up=up, hc=hc: e.tensor_mul(
                sb["g_slab"][:, hc * 1024:(hc + 1) * 1024], up[:, :],
                sb["silu_t"][:, :]),
                wait=[(sSE, m_si), (sPE, m_up)], inc=[(sVE, 1)], tag=f"gm{slab}.{hc}")
            ffn["gu_used"][hc % 2] = sVE.inc()
            ffn["silu_prev"] = ffn["gu_used"][hc % 2]
        g_ready = sVE.n
        if DEBUG and slab == 0:
            pl.op("sync", lambda e: e.dma_start(out=dbg["d_g"], in_=sb["g_slab"][:, :]),
                  wait=[(sVE, g_ready)], inc=[(sSTp[0], 16)])
            sSTp[0].inc(16)
        for hs in range(2):
            if hs == 1:
                wd_mark = load_wd_half(slab, 1)
            for ot in range(OTN):
                pp = ot % 2
                PA, PB = (P[0], P[1]) if pp == 0 else (P[2], P[3])
                w = [(sVE, g_ready)]
                if ffn["r2_evac"][pp]:
                    w.append((sVE, ffn["r2_evac"][pp]))
                for hl in range(8):
                    hc = hs * 8 + hl
                    ul, sub = hl // 2, hl % 2
                    lg = sb["g_slab"][:, hc * 1024 + ot * 128:hc * 1024 + (ot + 1) * 128]
                    for k in range(4):
                        tgt = PA if k < 2 else PB
                        pl.op("tensor",
                              lambda e, lg=lg, tgt=tgt, k=k, ul=ul, sub=sub, hl=hl: e.matmul(
                                  tgt[:, (k % 2) * 512:(k % 2 + 1) * 512], lg,
                                  sb["wdu"][:, ul * 2 * 2048 + sub * 2048 + k * 512:
                                            ul * 2 * 2048 + sub * 2048 + (k + 1) * 512],
                                  start=(hl == 0), stop=(hl == 7)),
                              wait=((w + [wd_mark[0]]) if (hl == 0 and k == 0) else
                                    ([wd_mark[ul]] if (sub == 0 and k == 0) else [])),
                              inc=([(sPE, 1)] if (hl == 7 and k == 3) else []),
                              tag=f"dM{slab}.{hs}.{ot}.{hl}.{k}")
                m_pe = sPE.inc()
                r2a = sb["r2"][:, ot * 2048:ot * 2048 + 1024]
                r2b = sb["r2"][:, ot * 2048 + 1024:(ot + 1) * 2048]
                first = (slab == 0 and hs == 0)
                if first:
                    pl.op("vector", lambda e, r2a=r2a, PA=PA: e.tensor_copy(r2a, PA[:, :]),
                          wait=[(sPE, m_pe)], tag=f"rc{slab}.{hs}.{ot}")
                    pl.op("vector", lambda e, r2b=r2b, PB=PB: e.tensor_copy(r2b, PB[:, :]),
                          inc=[(sVE, 1)], tag=f"rcB{slab}.{hs}.{ot}")
                else:
                    pl.op("vector", lambda e, r2a=r2a, PA=PA: e.tensor_add(r2a, r2a, PA[:, :]),
                          wait=[(sPE, m_pe)], tag=f"ra{slab}.{hs}.{ot}")
                    pl.op("vector", lambda e, r2b=r2b, PB=PB: e.tensor_add(r2b, r2b, PB[:, :]),
                          inc=[(sVE, 1)], tag=f"raB{slab}.{hs}.{ot}")
                ffn["r2_evac"][pp] = sVE.inc()
            ffn["down_pe_all"] = sPE.n
    r2_all = sVE.n

    if DEBUG:
        pl.op("sync", lambda e: e.dma_start(out=dbg["d_r2"], in_=sb["r2"][:, :]),
              wait=[(sVE, r2_all)], inc=[(sSTp[0], 16)])
        sSTp[0].inc(16)

    pl.barrier(sB)
    p6_ve = [0, 0]
    p6_st = [0, 0]
    for ot in range(OTN):
        pp = ot % 2
        xra = sb["xr6"][:, pp * 2048:(pp + 1) * 2048]
        wxr = list(xm_all_w)
        if p6_ve[pp]:
            wxr.append((sVE, p6_ve[pp]))
        pl.op("sync", lambda e, ot=ot, xra=xra: e.dma_start(
            out=xra, in_=xmid_dram.ap()[ot * 128:(ot + 1) * 128, :]),
            wait=wxr, inc=[(sXRp[pp], 16)], tag=f"fx{ot}")
        m_xr = sXRp[pp].inc(16)
        sl = pp * 16
        sx = stats[:, sl:sl + 1]
        sxx = stats[:, sl + 1:sl + 2]
        v0 = stats[:, sl + 2:sl + 3]
        negm = stats[:, sl + 3:sl + 4]
        rstd = stats[:, sl + 4:sl + 5]
        r2t = sb["r2"][:, ot * 2048:(ot + 1) * 2048]
        pl.op("scalar", lambda e, r2t=r2t, sx=sx: e.activation(
            dump[:, :], r2t, AF.Copy, accum_out=sx),
            wait=[(sVE, r2_all)], tag=f"fS{ot}")
        pl.op("scalar", lambda e, r2t=r2t, sxx=sxx: e.activation(
            dump[:, :], r2t, AF.Square, accum_out=sxx), inc=[(sSE, 1)], tag=f"fQ{ot}")
        m_se = sSE.inc()
        pl.op("vector", lambda e, sx=sx, v0=v0: e.tensor_mul(v0, sx, sx),
              wait=[(sSE, m_se)])
        pl.op("vector", lambda e, v0=v0, sxx=sxx: e.scalar_tensor_tensor(
            v0, v0, -1.0 / 2048, sxx, ALU.mult, ALU.add))
        pl.op("vector", lambda e, sx=sx, negm=negm: e.tensor_scalar(
            negm, sx, -1.0 / 2048, None, ALU.mult), inc=[(sVE, 1)], tag=f"fV{ot}")
        m_ve = sVE.inc()
        pl.op("scalar", lambda e, v0=v0: e.activation(v0, v0, AF.Ln,
                                                      scale=np.float32(1.0 / 2048), bias=np.float32(EPS)),
              wait=[(sVE, m_ve)])
        pl.op("scalar", lambda e, v0=v0, rstd=rstd: e.activation(
            rstd, v0, AF.Exp, scale=np.float32(-0.5)), inc=[(sSE, 1)], tag=f"fR{ot}")
        m_se2 = sSE.inc()
        lna = sb["ln6"][:, pp * 2048:(pp + 1) * 2048]
        xoa = sb["xo6"][:, pp * 2048:(pp + 1) * 2048]
        pl.op("vector", lambda e, lna=lna, r2t=r2t, negm=negm, rstd=rstd: e.tensor_scalar(
            lna, r2t, negm, rstd, ALU.add, ALU.mult),
            wait=[(sSE, m_se2)], tag=f"fL{ot}")
        wadd = [(sXRp[pp], m_xr)]
        if p6_st[pp]:
            wadd.append((sSTp[pp], p6_st[pp]))
        pl.op("vector", lambda e, xoa=xoa, lna=lna, xra=xra: e.scalar_tensor_tensor(
            xoa, lna, 1.0, xra, ALU.mult, ALU.add),
            wait=wadd, inc=[(sVE, 1)], tag=f"fA{ot}")
        p6_ve[pp] = sVE.inc()
        pl.op("sync", lambda e, ot=ot, xoa=xoa: e.dma_start(
            out=out_own[ot * 128:(ot + 1) * 128, :], in_=xoa),
            wait=[(sVE, p6_ve[pp])], inc=[(sSTp[pp], 16)], tag=f"st{ot}")
        p6_st[pp] = sSTp[pp].inc(16)

    pl.op("sync", None, wait=[(sSTp[0], sSTp[0].n), (sSTp[1], sSTp[1].n)],
          tag="final-wait")

    pl.simulate()
    pl.replay()
    return nc


_CACHE = {}


def kernel(L0_attn_x, L0_attn_w_aq, L0_attn_w_ak, L0_attn_w_av, L0_attn_w_ao,
           L0_ffn_w_gate, L0_ffn_w_up, L0_ffn_w_down):
    x = np.ascontiguousarray(np.asarray(L0_attn_x, dtype=np.float32))
    waq = np.asarray(L0_attn_w_aq, np.float32)
    wak = np.asarray(L0_attn_w_ak, np.float32)
    wav = np.asarray(L0_attn_w_av, np.float32)
    wq_h = [np.ascontiguousarray(
        waq[g].reshape(H, MC, 128, D).transpose(2, 0, 1, 3).reshape(128, H * MC * D))
        for g in range(G)]
    wk_h = [np.ascontiguousarray(
        wak[g].reshape(MC, 128, D).transpose(1, 0, 2).reshape(128, MC * D))
        for g in range(G)]
    wv_h = [np.ascontiguousarray(
        wav[g].reshape(MC, 128, D).transpose(1, 0, 2).reshape(128, MC * D))
        for g in range(G)]
    wao = np.ascontiguousarray(
        np.asarray(L0_attn_w_ao, np.float32).reshape(G * H, D, M)
        .transpose(1, 0, 2).reshape(128, G * H * M))
    wgt = np.ascontiguousarray(
        np.asarray(L0_ffn_w_gate, np.float32).T.reshape(MC, 128, HID)
        .transpose(1, 0, 2).reshape(128, MC * HID))
    wut = np.ascontiguousarray(
        np.asarray(L0_ffn_w_up, np.float32).T.reshape(MC, 128, HID)
        .transpose(1, 0, 2).reshape(128, MC * HID))
    wdt = np.ascontiguousarray(
        np.asarray(L0_ffn_w_down, np.float32).T.reshape(64, 128, M)
        .transpose(1, 0, 2).reshape(128, 64 * M))

    pos = np.arange(T, dtype=np.float64)
    freqs = 10000.0 ** (-np.arange(64, dtype=np.float64) / 64.0)
    ang = np.outer(freqs, pos)
    rope1 = np.concatenate([np.cos(ang), np.sin(ang)], axis=0).astype(BFH)
    rope2 = np.concatenate([np.sin(ang), np.cos(ang)], axis=0).astype(BFH)
    masks = np.zeros((4, 128, 512), np.float32)
    jj = np.arange(128)[:, None]
    ii = np.arange(512)[None, :]
    for r in range(4):
        masks[r] = ((jj + 128 * r) <= ii).astype(np.float32)
    masks = masks.astype(BFH)

    if "nc" not in _CACHE:
        _CACHE["nc"] = build_program()
    nc = _CACHE["nc"]

    in_maps = []
    for r in range(8):
        b, g = r // 4, r % 4
        in_maps.append({
            "x_full": x[b],
            "w_q": wq_h[g],
            "w_k": wk_h[g],
            "w_v": wv_h[g],
            "w_o": wao,
            "wg_t": wgt,
            "wu_t": wut,
            "wd_t": wdt,
            "rope1": np.asarray(rope1),
            "rope2": np.asarray(rope2),
            "masks": np.asarray(masks),
        })
    res = run_bass_kernel_spmd(nc, in_maps, core_ids=list(range(8)))
    out = np.empty((B, T, M), np.float32)
    for r in range(8):
        b, s = r // 4, r % 4
        out[b, s * 1024:(s + 1) * 1024, :] = np.asarray(res.results[r]["out_own"])
    if DEBUG:
        kernel._last = res
    return out


# version 08
# speedup vs baseline: 1.7147x; 1.0004x over previous
import numpy as np
import ml_dtypes
import concourse.bass as bass
import concourse.mybir as mybir
from concourse.bass_utils import run_bass_kernel_spmd

F32 = mybir.dt.float32
BF16 = mybir.dt.bfloat16
AF = mybir.ActivationFunctionType
ALU = mybir.AluOpType
BFH = ml_dtypes.bfloat16

B, T, M, G, H, D = 2, 4096, 2048, 4, 4, 128
HID = 8192
NT = T // 128
OTN = 8
MC = M // 128
QK2 = float(D) ** -0.5
SHIFT = 30.0
EPS = 1e-5
DEBUG = False

ENGINES = ("sync", "gpsimd", "tensor", "scalar", "vector")


class SemT:
    def __init__(self, nc, name):
        self.h = nc.alloc_semaphore(name)
        self.n = 0
        self.name = name

    def inc(self, k=1):
        self.n += k
        return self.n


class Op:
    __slots__ = ("eng", "fn", "waits", "incs", "tag")

    def __init__(self, eng, fn, waits, incs, tag):
        self.eng, self.fn, self.waits, self.incs, self.tag = eng, fn, waits, incs, tag


class Plan:
    def __init__(self, nc):
        self.nc = nc
        self.seq = []
        self.pid = {}
        self.sl1024 = {}

    def op(self, eng, fn=None, wait=(), inc=(), tag=""):
        self.seq.append(Op(eng, fn, [w for w in wait if w is not None], list(inc), tag))

    def barrier(self, sem):
        for e in ENGINES:
            self.op(e, None, inc=[(sem, 1)], tag="bar-inc")
            sem.inc(1)
        tgt = sem.n
        for e in ENGINES:
            self.op(e, None, wait=[(sem, tgt)], tag="bar-wait")

    def simulate(self):
        vals = {}
        qs = {e: [o for o in self.seq if o.eng == e] for e in ENGINES}
        ptr = {e: 0 for e in ENGINES}
        progress = True
        while progress:
            progress = False
            for e in ENGINES:
                while ptr[e] < len(qs[e]):
                    o = qs[e][ptr[e]]
                    if all(vals.get(s.name, 0) >= v for s, v in o.waits):
                        for s, k in o.incs:
                            vals[s.name] = vals.get(s.name, 0) + k
                        ptr[e] += 1
                        progress = True
                    else:
                        break
        stuck = {e: ptr[e] for e in ENGINES if ptr[e] < len(qs[e])}
        if stuck:
            msg = []
            for e, p in stuck.items():
                o = qs[e][p]
                why = [(s.name, v, vals.get(s.name, 0)) for s, v in o.waits
                       if vals.get(s.name, 0) < v]
                msg.append(f"{e}[{p}/{len(qs[e])}] tag={o.tag} blocked on {why}")
            raise RuntimeError("PLAN DEADLOCK:\n" + "\n".join(msg))

    def replay(self):
        nc = self.nc
        with nc.Block() as block:
            def run_engine(eng_obj, name):
                try:
                    self.pid[name] = eng_obj.partition_id()
                    self.sl1024[name] = self.pid[name] % 4 * 1024
                except Exception:
                    pass
                for o in self.seq:
                    if o.eng != name:
                        continue
                    for s, v in o.waits:
                        eng_obj.wait_ge(s.h, v)
                    if o.fn is not None:
                        inst = o.fn(eng_obj)
                        for s, k in o.incs:
                            inst = inst.then_inc(s.h, k)
                    else:
                        for s, k in o.incs:
                            eng_obj.sem_inc(s.h, k)

            @block.sync
            def _(e):
                run_engine(e, "sync")

            @block.gpsimd
            def _(e):
                run_engine(e, "gpsimd")

            @block.tensor
            def _(e):
                run_engine(e, "tensor")

            @block.scalar
            def _(e):
                run_engine(e, "scalar")

            @block.vector
            def _(e):
                run_engine(e, "vector")


class SbufMap:
    def __init__(self, nc, base, top):
        self.nc = nc
        self.base = base
        self.top = top
        self.items = []

    def add(self, name, shape, dtype, live):
        esz = 4 if dtype == F32 else 2
        bpp = int(np.prod(shape[1:])) * esz
        bpp = (bpp + 63) // 64 * 64
        self.items.append([name, shape, dtype, bpp, live])

    def finalize(self):
        placed = []
        out = {}
        for name, shape, dtype, bpp, live in sorted(self.items, key=lambda it: -it[3]):
            off = 0
            while True:
                clash = None
                for poff, psz, plive, pname in placed:
                    if not (live[1] < plive[0] or plive[1] < live[0]):
                        if not (off + bpp <= poff or poff + psz <= off):
                            clash = poff + psz
                            break
                if clash is None:
                    break
                off = (clash + 63) // 64 * 64
            assert self.base + off + bpp <= self.top, (
                f"SBUF overflow placing {name}: off={off} bpp={bpp}")
            placed.append((off, bpp, live, name))
            out[name] = self.nc.alloc_sbuf_tensor_at(
                name, list(shape), dtype, offset=self.base + off)
        return out


def build_program():
    nc = bass.Bass("TRN2", target_bir_lowering=False)

    x_full = nc.dram_tensor("x_full", [T, M], F32, kind="ExternalInput").ap()
    w_q = nc.dram_tensor("w_q", [128, H * MC * D], F32, kind="ExternalInput").ap()
    w_k = nc.dram_tensor("w_k", [128, MC * D], F32, kind="ExternalInput").ap()
    w_v = nc.dram_tensor("w_v", [128, MC * D], F32, kind="ExternalInput").ap()
    w_o = nc.dram_tensor("w_o", [128, G * H * M], F32, kind="ExternalInput").ap()
    wg_t = nc.dram_tensor("wg_t", [128, MC * HID], F32, kind="ExternalInput").ap()
    wu_t = nc.dram_tensor("wu_t", [128, MC * HID], F32, kind="ExternalInput").ap()
    wd_t = nc.dram_tensor("wd_t", [128, 64 * M], F32, kind="ExternalInput").ap()
    rope1 = nc.dram_tensor("rope1", [128, T], BF16, kind="ExternalInput").ap()
    rope2 = nc.dram_tensor("rope2", [128, T], BF16, kind="ExternalInput").ap()
    masks_i = nc.dram_tensor("masks", [4, 128, 512], BF16, kind="ExternalInput").ap()
    out_own = nc.dram_tensor("out_own", [1024, M], F32, kind="ExternalOutput").ap()

    h_dram = nc.dram_tensor("h_dram", [T, M], BF16)
    h2_dram = nc.dram_tensor("h2_dram", [1024, M], BF16)
    xmid_dram = nc.dram_tensor("xmid_dram", [1024, M], F32)
    oc_dram = [nc.dram_tensor(f"oc{i}_dram", [D, T], BF16) for i in range(H)]
    og_dram = [nc.dram_tensor(f"og{i}_dram", [G * D, T], BF16) for i in range(H)]
    wg_bf = nc.dram_tensor("wg_bf", [128, MC * HID], BF16)
    wu_bf = nc.dram_tensor("wu_bf", [128, MC * HID], BF16)

    dbg = {}
    if DEBUG:
        dbg["d_h"] = nc.dram_tensor("d_h", [T, M], BF16, kind="ExternalOutput").ap()
        dbg["d_q"] = nc.dram_tensor("d_q", [128, H * T], BF16, kind="ExternalOutput").ap()
        dbg["d_k"] = nc.dram_tensor("d_k", [128, T], BF16, kind="ExternalOutput").ap()
        dbg["d_v"] = nc.dram_tensor("d_v", [128, T], BF16, kind="ExternalOutput").ap()
        dbg["d_o"] = nc.dram_tensor("d_o", [128, H * T], BF16, kind="ExternalOutput").ap()
        dbg["d_xmid"] = nc.dram_tensor("d_xmid", [1024, M], F32, kind="ExternalOutput").ap()
        dbg["d_h2ft"] = nc.dram_tensor("d_h2ft", [128, MC * 1024], BF16, kind="ExternalOutput").ap()
        dbg["d_g"] = nc.dram_tensor("d_g", [128, 16 * 1024], BF16, kind="ExternalOutput").ap()
        dbg["d_r2"] = nc.dram_tensor("d_r2", [128, OTN * 2048], F32, kind="ExternalOutput").ap()

    smap = SbufMap(nc, (nc.sbuf_base + 63) // 64 * 64, nc.sbuf_top)
    smap.add("rope1_sb", [128, T], BF16, (0, 1))
    smap.add("rope2_sb", [128, T], BF16, (0, 1))
    smap.add("masks_sb", [128, 2048], BF16, (0, 2))
    smap.add("ones_bf", [128, 16], BF16, (0, 6))
    smap.add("ones_f", [1, 128], F32, (0, 6))
    smap.add("stats", [128, 64], F32, (0, 6))
    smap.add("recip_sb", [1, 512], F32, (2, 2))
    smap.add("recipb_sb", [1, 512], BF16, (2, 2))
    smap.add("pairt", [128, 1024], BF16, (2, 2))
    smap.add("ones_bfr", [1, 128], BF16, (0, 6))
    smap.add("bcast_sb", [128, 512], F32, (2, 2))
    smap.add("ptile", [128, 32 * 512], BF16, (2, 2))
    smap.add("dump", [128, 2048], F32, (0, 6))
    smap.add("dump2", [128, 2048], F32, (1, 1))
    smap.add("h_ft", [128, 2 * MC * 1024], BF16, (1, 1))
    smap.add("xt", [128, 2 * 2048], F32, (1, 1))
    smap.add("ht", [128, 2 * 2048], BF16, (1, 1))
    smap.add("qtmp", [128, 2 * 1024], BF16, (1, 1))
    smap.add("rtmp", [128, 2048], BF16, (1, 1))
    smap.add("rswp", [128, 1024], BF16, (1, 1))
    smap.add("wq_sb", [128, H * MC * D], BF16, (0, 1))
    smap.add("wk_sb", [128, MC * D], BF16, (0, 1))
    smap.add("wv_sb", [128, MC * D], BF16, (0, 1))
    smap.add("q_ft", [128, H * T], BF16, (1, 2))
    smap.add("k_ft", [128, T], BF16, (1, 2))
    smap.add("v_sb", [128, T], BF16, (1, 2))
    smap.add("o_ft", [128, H * T], BF16, (2, 3))
    smap.add("wo_sb", [128, G * H * M], BF16, (2, 4))
    smap.add("o16_sb", [128, G * H * 1024], BF16, (4, 4))
    smap.add("h2_ft", [128, MC * 1024], BF16, (4, 5))
    smap.add("xr", [128, 2 * 2048], F32, (4, 4))
    smap.add("ln_t", [128, 2 * 2048], F32, (4, 4))
    smap.add("xmid_t", [128, 2 * 2048], F32, (4, 4))
    smap.add("h2t", [128, 2 * 2048], BF16, (4, 4))
    smap.add("g_slab", [128, 16 * 1024], BF16, (5, 5))
    smap.add("r2", [128, OTN * 2048], F32, (5, 6))
    smap.add("wgu", [128, 2 * MC * 256], BF16, (5, 5))
    smap.add("wuu", [128, 2 * MC * 256], BF16, (5, 5))
    smap.add("wdu", [128, 8 * 2048], BF16, (5, 5))
    smap.add("silu_t", [128, 1024], F32, (5, 5))
    smap.add("xr6", [128, 2 * 2048], F32, (6, 6))
    smap.add("ln6", [128, 2 * 2048], F32, (6, 6))
    smap.add("xo6", [128, 2 * 2048], F32, (6, 6))
    sb = smap.finalize()

    P = [nc.alloc_psum_tensor(f"ps{i}", [128, 1024], F32) for i in range(4)]

    pl = Plan(nc)
    sCst = SemT(nc, "sCst")
    sXp = [SemT(nc, "sXa"), SemT(nc, "sXb")]
    sHp = [SemT(nc, "sHa"), SemT(nc, "sHb")]
    sTT = SemT(nc, "sTT")
    sWq = SemT(nc, "sWq")
    sWo = SemT(nc, "sWo")
    sWc = SemT(nc, "sWc")
    sFg = [SemT(nc, "sFga"), SemT(nc, "sFgb")]
    sFu = [SemT(nc, "sFua"), SemT(nc, "sFub")]
    sFdu = [SemT(nc, f"sFd{i}") for i in range(4)]
    sPE = SemT(nc, "sPE")
    sSE = SemT(nc, "sSE")
    sVE = SemT(nc, "sVE")
    sCCh = [SemT(nc, f"sCC{i}") for i in range(H)]
    sC1h = [SemT(nc, f"sC1{i}") for i in range(H)]
    sG = SemT(nc, "sG")
    sXRp = [SemT(nc, "sXRa"), SemT(nc, "sXRb")]
    sXMp = [SemT(nc, "sXMa"), SemT(nc, "sXMb")]
    sSTp = [SemT(nc, "sSTa"), SemT(nc, "sSTb")]
    sB = SemT(nc, "sB")

    rope1_sb, rope2_sb = sb["rope1_sb"], sb["rope2_sb"]
    masks_sb, stats, dump = sb["masks_sb"], sb["stats"], sb["dump"]
    h_ft, q_ft, k_ft, v_sb, o_ft = (sb["h_ft"], sb["q_ft"], sb["k_ft"],
                                    sb["v_sb"], sb["o_ft"])

    pl.op("vector", lambda e: e.memset(sb["ones_bf"][:, :], 1.0))
    pl.op("vector", lambda e: e.memset(sb["ones_f"][:, :], 1.0))
    pl.op("vector", lambda e: e.memset(sb["ones_bfr"][:, :], 1.0), inc=[(sVE, 1)])
    ones_ready = sVE.inc()
    pl.op("sync", lambda e: e.dma_start(out=rope1_sb[:, :], in_=rope1), inc=[(sCst, 16)])
    sCst.inc(16)
    pl.op("sync", lambda e: e.dma_start(out=rope2_sb[:, :], in_=rope2), inc=[(sCst, 16)])
    sCst.inc(16)
    pl.op("sync",
          lambda e: e.dma_start(out=masks_sb[:, :].rearrange("p (r i) -> p r i", r=4),
                                in_=masks_i.rearrange("r p i -> p r i")),
          inc=[(sCst, 16)])
    rope_ready = sCst.inc(16)

    pl.op("gpsimd", lambda e: e.dma_start(out=sb["wk_sb"][:, :], in_=w_k),
          inc=[(sWq, 16)], tag="wk")
    sWq.inc(16)
    pl.op("gpsimd", lambda e: e.dma_start(out=sb["wv_sb"][:, :], in_=w_v),
          inc=[(sWq, 16)], tag="wv")
    sWq.inc(16)
    pl.op("gpsimd", lambda e: e.dma_start(out=sb["wq_sb"][:, :], in_=w_q),
          inc=[(sWq, 16)], tag="wq")
    sWq.inc(16)
    wqkv_ready = sWq.n

    xt, ht = sb["xt"], sb["ht"]
    ln_done = [0] * NT
    h_mark = [0] * NT

    def emit_ln_tile(ti):
        slot = ti % 2
        xa = xt[:, slot * 2048:(slot + 1) * 2048]
        ha = ht[:, slot * 2048:(slot + 1) * 2048]
        w = [(sVE, ln_done[ti - 2])] if ti >= 2 else []
        pl.op("gpsimd", lambda e, ti=ti, xa=xa: e.dma_start(
            out=xa, in_=x_full[ti * 128:(ti + 1) * 128, :]),
            wait=w, inc=[(sXp[slot], 16)], tag=f"x{ti}")
        xv = sXp[slot].inc(16)
        sl = slot * 8
        sx = stats[:, sl:sl + 1]
        sxx = stats[:, sl + 1:sl + 2]
        v0 = stats[:, sl + 2:sl + 3]
        negm = stats[:, sl + 3:sl + 4]
        rstd = stats[:, sl + 4:sl + 5]
        pl.op("vector", lambda e, xa=xa, sx=sx: e.tensor_scalar(
            sb["dump2"][:, :], xa, 1.0, 0.0, ALU.mult, ALU.add, accum_out=sx),
            wait=[(sXp[slot], xv)], inc=[(sVE, 1)], tag=f"ln{ti}-sx")
        m_vx = sVE.inc()
        pl.op("scalar", lambda e, xa=xa, sxx=sxx: e.activation(
            dump[:, :], xa, AF.Square, accum_out=sxx),
            wait=[(sXp[slot], xv)], inc=[(sSE, 1)], tag=f"ln{ti}-sxx")
        m_se = sSE.inc()
        pl.op("vector", lambda e, sx=sx, v0=v0: e.tensor_mul(v0, sx, sx),
              wait=[(sSE, m_se)])
        pl.op("vector", lambda e, v0=v0, sxx=sxx: e.scalar_tensor_tensor(
            v0, v0, -1.0 / 2048, sxx, ALU.mult, ALU.add))
        pl.op("vector", lambda e, sx=sx, negm=negm: e.tensor_scalar(
            negm, sx, -1.0 / 2048, None, ALU.mult), inc=[(sVE, 1)], tag=f"ln{ti}-v")
        m_ve = sVE.inc()
        pl.op("scalar", lambda e, v0=v0: e.activation(v0, v0, AF.Ln,
                                                      scale=np.float32(1.0 / 2048), bias=np.float32(EPS)),
              wait=[(sVE, m_ve)])
        pl.op("scalar", lambda e, v0=v0, rstd=rstd: e.activation(
            rstd, v0, AF.Exp, scale=np.float32(-0.5)), inc=[(sSE, 1)], tag=f"ln{ti}-r")
        m_se2 = sSE.inc()
        wn = [(sSE, m_se2)]
        if ti >= 2 and h_mark[ti - 2]:
            wn.append((sHp[slot], h_mark[ti - 2]))
        pl.op("vector", lambda e, ha=ha, xa=xa, negm=negm, rstd=rstd: e.tensor_scalar(
            ha, xa, negm, rstd, ALU.add, ALU.mult),
            wait=wn, inc=[(sVE, 1)], tag=f"ln{ti}-n")
        ln_done[ti] = sVE.inc()
        pl.op("sync", lambda e, ti=ti, ha=ha: e.dma_start(
            out=h_dram.ap()[ti * 128:(ti + 1) * 128, :], in_=ha),
            wait=[(sVE, ln_done[ti])], inc=[(sHp[slot], 16)], tag=f"h{ti}")
        h_mark[ti] = sHp[slot].inc(16)

    qkv_state = {"unit": 0, "vunit": 0, "sse_unit": [0, 0, 0], "sve_unit": [0, 0],
                 "sse_v": [0, 0]}
    trans_done = [0] * 4

    def emit_qkv_quarter(qu):
        tbase = qu * 1024
        t_ready = sTT.n
        hbase = (qu % 2) * MC * 1024
        for u in range(5):
            ps = P[qkv_state["unit"] % 3]
            w = [(sTT, t_ready)]
            w.append((sWq, wqkv_ready))
            prev = qkv_state["sse_unit"][qkv_state["unit"] % 3]
            if prev:
                w.append((sVE, prev))
            for mc in range(MC):
                if u < 4:
                    lhsT = sb["wq_sb"][:, (u * MC + mc) * D:(u * MC + mc + 1) * D]
                else:
                    lhsT = sb["wk_sb"][:, mc * D:(mc + 1) * D]
                for half in range(2):
                    pl.op("tensor", lambda e, lhsT=lhsT, half=half, mc=mc, ps=ps, hbase=hbase:
                          e.matmul(
                        ps[:, half * 512:(half + 1) * 512], lhsT,
                        h_ft[:, hbase + mc * 1024 + half * 512:
                             hbase + mc * 1024 + (half + 1) * 512],
                        start=(mc == 0), stop=(mc == MC - 1)),
                        wait=(w if (mc == 0 and half == 0) else []),
                        inc=([(sPE, 1)] if (mc == MC - 1 and half == 1) else []),
                        tag=f"qkvM{qu}.{u}")
            m_pe = sPE.inc()
            qslot = qkv_state["unit"] % 2
            qa = sb["qtmp"][:, qslot * 1024:(qslot + 1) * 1024]
            pslot = qkv_state["unit"] % 3
            wse = [(sPE, m_pe)]
            pl.op("vector", lambda e, qa=qa, ps=ps: e.tensor_copy(qa, ps[:, :]),
                  wait=wse, inc=[(sVE, 1)], tag=f"qkvE{qu}.{u}")
            qkv_state["sse_unit"][pslot] = sVE.inc()
            if u < 4:
                dst = q_ft[:, u * T + tbase:u * T + tbase + 1024]
            else:
                dst = k_ft[:, tbase:tbase + 1024]
            cs1 = rope1_sb[:, tbase:tbase + 1024]
            cs2 = rope2_sb[:, tbase:tbase + 1024]
            rt1 = sb["rtmp"][:, 0:1024]
            rt2 = sb["rtmp"][:, 1024:2048]
            sw = sb["rswp"]
            pl.op("vector", lambda e, qa=qa, cs1=cs1, rt1=rt1: e.tensor_mul(rt1, qa, cs1),
                  wait=[(sCst, rope_ready)], tag=f"rp{qu}.{u}")
            pl.op("vector", lambda e, qa=qa, cs2=cs2, rt2=rt2: e.tensor_mul(rt2, qa, cs2))
            pl.op("vector", lambda e, rt1=rt1, sw=sw: e.tensor_copy(
                sw[0:64, :], rt1[64:128, :]))
            pl.op("vector", lambda e, rt2=rt2, sw=sw: e.tensor_copy(
                sw[64:128, :], rt2[0:64, :]))
            pl.op("vector", lambda e, dst=dst, rt1=rt1, sw=sw: e.tensor_sub(
                dst[0:64, :], rt1[0:64, :], sw[0:64, :]))
            pl.op("vector", lambda e, dst=dst, rt2=rt2, sw=sw: e.tensor_add(
                dst[64:128, :], rt2[64:128, :], sw[64:128, :]),
                inc=[(sVE, 1)], tag=f"rpd{qu}.{u}")
            qkv_state["sve_unit"][qslot] = sVE.inc()
            qkv_state["unit"] += 1
        for tt in range(8):
            vslot = qkv_state["vunit"] % 2
            vp = P[3][:, vslot * 512:vslot * 512 + 128]
            w = [(sTT, t_ready)]
            prev = qkv_state["sse_v"][qkv_state["vunit"] % 2]
            if prev:
                w.append((sVE, prev))
            for mc in range(MC):
                lhsT = h_ft[:, hbase + mc * 1024 + tt * 128:
                            hbase + mc * 1024 + (tt + 1) * 128]
                pl.op("tensor", lambda e, lhsT=lhsT, mc=mc, vp=vp: e.matmul(
                    vp, lhsT, sb["wv_sb"][:, mc * D:(mc + 1) * D],
                    start=(mc == 0), stop=(mc == MC - 1)),
                    wait=(w if mc == 0 else []),
                    inc=([(sPE, 1)] if mc == MC - 1 else []), tag=f"vM{qu}.{tt}")
            m_pe = sPE.inc()
            gt = qu * 8 + tt
            pl.op("vector", lambda e, gt=gt, vp=vp: e.tensor_copy(
                v_sb[:, gt * 128:(gt + 1) * 128], vp),
                wait=[(sPE, m_pe)], inc=[(sVE, 1)], tag=f"vE{qu}.{tt}")
            qkv_state["sse_v"][qkv_state["vunit"] % 2] = sVE.inc()
            qkv_state["vunit"] += 1
        trans_done[qu] = sPE.n

    def emit_transposes(qu):
        hbase = (qu % 2) * MC * 1024
        for mc in range(MC):
            w = []
            if mc == 0:
                w.append((sHp[0], sHp[0].n))
                w.append((sHp[1], sHp[1].n))
                if qu >= 2:
                    w.append((sPE, trans_done[qu - 2]))
            pl.op("sync", lambda e, qu=qu, mc=mc, hbase=hbase: e.dma_start(
                out=h_ft[:, hbase + mc * 1024:hbase + (mc + 1) * 1024],
                in_=h_dram.ap()[qu * 1024:(qu + 1) * 1024, mc * 128:(mc + 1) * 128],
                transpose=True),
                wait=w, inc=[(sTT, 16)], tag=f"T{qu}.{mc}")
            sTT.inc(16)

    for ti in range(8):
        emit_ln_tile(ti)
    emit_transposes(0)
    for ti in range(8, 16):
        emit_ln_tile(ti)
    emit_qkv_quarter(0)
    emit_transposes(1)
    for ti in range(16, 24):
        emit_ln_tile(ti)
    emit_qkv_quarter(1)
    emit_transposes(2)
    for ti in range(24, 32):
        emit_ln_tile(ti)
    emit_qkv_quarter(2)
    emit_transposes(3)
    emit_qkv_quarter(3)

    if DEBUG:
        pl.op("sync", lambda e: e.dma_start(out=dbg["d_h"], in_=h_dram.ap()),
              wait=[(sHp[0], sHp[0].n), (sHp[1], sHp[1].n)], inc=[(sSTp[0], 16)])
        sSTp[0].inc(16)

    pl.barrier(sB)
    pl.op("gpsimd", lambda e: e.dma_start(out=sb["wo_sb"][:, :], in_=w_o),
          inc=[(sWo, 16)], tag="wo")
    sWo.inc(16)
    wo_ready = sWo.n
    for nm, srcw, dstw in (("wg", wg_t, wg_bf), ("wu", wu_t, wu_bf)):
        pl.op("gpsimd", lambda e, srcw=srcw, dstw=dstw: e.dma_start(out=dstw.ap(), in_=srcw),
              inc=[(sWc, 16)], tag=f"cast-{nm}")
        sWc.inc(16)
    ffn_cast_ready = sWc.n
    ptile = sb["ptile"]
    att = {"jglob": 0, "norm_done": [0, 0], "recip_done": [0, 0], "bcast_copy": 0,
           "exp_mark": {}, "pv_done": {}}

    if DEBUG:
        pl.op("sync", lambda e: e.dma_start(out=dbg["d_q"], in_=q_ft[:, :]),
              inc=[(sSTp[0], 16)])
        sSTp[0].inc(16)
        pl.op("sync", lambda e: e.dma_start(out=dbg["d_k"], in_=k_ft[:, :]),
              inc=[(sSTp[0], 16)])
        sSTp[0].inc(16)
        pl.op("sync", lambda e: e.dma_start(out=dbg["d_v"], in_=v_sb[:, :]),
              inc=[(sSTp[0], 16)])
        sSTp[0].inc(16)

    unit = 0
    head_norm = [0] * H
    att["pair_mark"] = [0, 0]
    att["recipb_done"] = [0, 0]
    tailA_q = []
    tailB_q = []
    SBANK = [(0, 0), (0, 512), (3, 0)]

    def emit_tailA(tu):
        (t_spp, t_opp, t_hd, t_q0, t_mpe) = tu
        pl.op("scalar", lambda e, t_spp=t_spp: e.activation(
            sb["recip_sb"][0:1, :], P[2][0:1, t_spp * 512:(t_spp + 1) * 512], AF.Ln),
            wait=[(sPE, t_mpe)], tag="aRL")
        pl.op("scalar", lambda e: e.activation(
            sb["recipb_sb"][0:1, :], sb["recip_sb"][0:1, :], AF.Exp,
            scale=np.float32(-1.0)),
            inc=[(sSE, 1)], tag="aRE")
        att["recipb_done"][t_spp] = sSE.inc()
        return att["recipb_done"][t_spp]

    def emit_tailB(tu, m_rb):
        (t_spp, t_opp, t_hd, t_q0, t_mpe) = tu
        wb = [(sSE, m_rb)]
        if att["bcast_copy"]:
            wb.append((sSE, att["bcast_copy"]))
        pl.op("tensor", lambda e: e.matmul(
            P[3][:, 512:1024], sb["ones_bfr"][0:1, :], sb["recipb_sb"][0:1, :],
            start=True, stop=True),
            wait=wb, inc=[(sPE, 1)], tag="aB")
        m_b = sPE.inc()
        wc_ = [(sPE, m_b)]
        if att["norm_done"][1 - t_opp]:
            wc_.append((sVE, att["norm_done"][1 - t_opp]))
        if att["norm_done"][t_opp]:
            wc_.append((sVE, att["norm_done"][t_opp]))
        pl.op("scalar", lambda e: e.copy(sb["bcast_sb"][:, :], P[3][:, 512:1024]),
              wait=wc_, inc=[(sSE, 1)], tag="aC")
        att["bcast_copy"] = sSE.inc()
        pl.op("vector", lambda e, t_hd=t_hd, t_q0=t_q0, t_opp=t_opp: e.tensor_mul(
            o_ft[:, t_hd * T + t_q0:t_hd * T + t_q0 + 512],
            P[1][:, t_opp * 512:(t_opp + 1) * 512],
            sb["bcast_sb"][:, :]),
            wait=[(sSE, att["bcast_copy"])], inc=[(sVE, 1)], tag="aN")
        att["norm_done"][t_opp] = sVE.inc()

    for hd in range(H):
        for qc in range(8):
            jmax = 4 * (qc + 1)
            q0 = 512 * qc
            opp = unit % 2
            spp = unit % 2
            p_ready = [None] * jmax
            pv_prev = att.get("pv_marks", [None] * 32)
            pv_marks = [None] * 32
            for j in range(jmax):
                jg = att["jglob"] + j
                pt, pc = SBANK[jg % 3]
                w = []
                pe_prev = att["exp_mark"].get(jg - 3)
                if pe_prev:
                    w.append(pe_prev)
                pl.op("tensor", lambda e, pt=pt, pc=pc, j=j, hd=hd, q0=q0: e.matmul(
                    P[pt][:, pc:pc + 512],
                    k_ft[:, j * 128:(j + 1) * 128],
                    q_ft[:, hd * T + q0:hd * T + q0 + 512],
                    start=True, stop=True),
                    wait=w, inc=[(sPE, 1)], tag=f"aS{unit}.{j}")
                m_s = sPE.inc()
                wse = [(sPE, m_s)]
                if pv_prev[j] is not None:
                    wse.append((sPE, pv_prev[j]))
                pl.op("scalar", lambda e, j=j, pt=pt, pc=pc: e.activation(
                    ptile[:, j * 512:(j + 1) * 512],
                    P[pt][:, pc:pc + 512],
                    AF.Exp, scale=np.float32(QK2), bias=np.float32(-SHIFT)),
                    wait=wse, inc=[(sSE, 1)], tag=f"aE{unit}.{j}")
                m_e = sSE.inc()
                if j >= jmax - 4:
                    r = j - (jmax - 4)
                    pl.op("vector", lambda e, j=j, r=r: e.tensor_mul(
                        ptile[:, j * 512:(j + 1) * 512],
                        ptile[:, j * 512:(j + 1) * 512],
                        masks_sb[:, r * 512:(r + 1) * 512]),
                        wait=[(sSE, m_e)], inc=[(sVE, 1)], tag=f"aM{unit}.{j}")
                    p_ready[j] = (sVE, sVE.inc())
                else:
                    p_ready[j] = (sSE, m_e)
                att["exp_mark"][jg] = p_ready[j]
            if tailA_q:
                tu = tailA_q.pop()
                m_rb = emit_tailA(tu)
                tailB_q.append((tu, m_rb))
            npairs = jmax // 2
            for pr in range(npairs):
                j0, j1 = 2 * pr, 2 * pr + 1
                ppp = pr % 2
                wp = [p_ready[j0], p_ready[j1]]
                if att["pair_mark"][ppp]:
                    wp.append((sPE, att["pair_mark"][ppp]))
                pl.op("vector", lambda e, j0=j0, j1=j1, ppp=ppp: e.tensor_add(
                    sb["pairt"][:, ppp * 512:(ppp + 1) * 512],
                    ptile[:, j0 * 512:(j0 + 1) * 512],
                    ptile[:, j1 * 512:(j1 + 1) * 512]),
                    wait=wp, inc=[(sVE, 1)], tag=f"aA{unit}.{pr}")
                m_pair = sVE.inc()
                w2 = [(sVE, m_pair)]
                if pr == 0:
                    if att["recipb_done"][spp]:
                        w2.append((sSE, att["recipb_done"][spp]))
                    if att["norm_done"][opp]:
                        w2.append((sVE, att["norm_done"][opp]))
                pl.op("tensor", lambda e, spp=spp, ppp=ppp, pr=pr, npairs=npairs: e.matmul(
                    P[2][0:1, spp * 512:(spp + 1) * 512],
                    sb["ones_bf"][:, 0:1],
                    sb["pairt"][:, ppp * 512:(ppp + 1) * 512],
                    start=(pr == 0), stop=(pr == npairs - 1)),
                    wait=w2, tag=f"aU{unit}.{pr}")
                for j in (j0, j1):
                    pl.op("tensor", lambda e, opp=opp, j=j, jmax=jmax: e.matmul(
                        P[1][:, opp * 512:(opp + 1) * 512],
                        v_sb[:, j * 128:(j + 1) * 128],
                        ptile[:, j * 512:(j + 1) * 512],
                        start=(j == 0), stop=(j == jmax - 1)),
                        wait=([p_ready[j]] if j == j1 else []),
                        inc=[(sPE, 1)], tag=f"aP{unit}.{j}")
                    pv_marks[j] = sPE.inc()
                att["pair_mark"][ppp] = pv_marks[j1]
            att["pv_marks"] = pv_marks
            if tailB_q:
                emit_tailB(*tailB_q.pop())
            tailA_q.append((spp, opp, hd, q0, pv_marks[jmax - 1]))
            att["jglob"] += jmax
            unit += 1
        while tailA_q or tailB_q:
            if tailB_q:
                emit_tailB(*tailB_q.pop())
            if tailA_q:
                tu = tailA_q.pop()
                m_rb = emit_tailA(tu)
                tailB_q.append((tu, m_rb))
        head_norm[hd] = sVE.n
    o_done = sVE.n

    if DEBUG:
        pl.op("sync", lambda e: e.dma_start(out=dbg["d_o"], in_=o_ft[:, :]),
              wait=[(sVE, o_done)], inc=[(sSTp[0], 16)])
        sSTp[0].inc(16)

    for hd in range(H):
        pl.op("sync", lambda e, hd=hd: e.dma_start(
            out=oc_dram[hd].ap(), in_=o_ft[:, hd * T:(hd + 1) * T]),
            wait=[(sVE, head_norm[hd])], inc=[(sC1h[hd], 16)], tag=f"oc{hd}")
        sC1h[hd].inc(16)
        pl.op("gpsimd", lambda e, hd=hd: e.collective_compute(
            "AllGather", ALU.bypass,
            replica_groups=[[0, 1, 2, 3], [4, 5, 6, 7]],
            ins=[oc_dram[hd].ap()], outs=[og_dram[hd].ap()]),
            wait=[(sC1h[hd], 16)], inc=[(sCCh[hd], 1)], tag=f"allgather{hd}")
        sCCh[hd].inc(1)

    for hd in range(H):
        pl.op("sync", lambda e, hd=hd: e.dma_start(
            out=sb["o16_sb"][:, :].rearrange("p (gh t) -> p gh t", gh=G * H)[
                :, hd::H, :],
            in_=og_dram[hd].ap().rearrange("(g d) t -> d g t", d=128)[
                :, :, bass.ds(pl.sl1024["sync"], 1024)]),
            wait=[(sCCh[hd], 1)], inc=[(sG, 16)], tag=f"gather{hd}")
        sG.inc(16)

    p4 = {"psum_free": [0, 0], "h2dma": [0, 0], "xmdma": [0, 0], "xr_used": [0, 0]}
    for ot in range(OTN):
        pp = ot % 2
        PA, PB = (P[0], P[1]) if pp == 0 else (P[2], P[3])
        wxr = [(sVE, p4["xr_used"][pp])] if p4["xr_used"][pp] else []
        pl.op("sync", lambda e, ot=ot, pp=pp: e.dma_start(
            out=sb["xr"][:, pp * 2048:(pp + 1) * 2048],
            in_=x_full[bass.ds(pl.sl1024["sync"] + ot * 128, 128), :]),
            wait=wxr, inc=[(sXRp[pp], 16)], tag=f"xr{ot}")
        m_xr = sXRp[pp].inc(16)
        w = [(sG, sG.n), (sWo, wo_ready)]
        if p4["psum_free"][pp]:
            w.append((sVE, p4["psum_free"][pp]))
        for gh in range(G * H):
            lhsT = sb["o16_sb"][:, gh * 1024 + ot * 128:gh * 1024 + (ot + 1) * 128]
            for k in range(4):
                tgt = PA if k < 2 else PB
                pl.op("tensor", lambda e, lhsT=lhsT, gh=gh, k=k, tgt=tgt: e.matmul(
                    tgt[:, (k % 2) * 512:(k % 2 + 1) * 512], lhsT,
                    sb["wo_sb"][:, gh * M + k * 512:gh * M + (k + 1) * 512],
                    start=(gh == 0), stop=(gh == G * H - 1)),
                    wait=(w if (gh == 0 and k == 0) else []),
                    inc=([(sPE, 1)] if (gh == G * H - 1 and k == 3) else []),
                    tag=f"oM{ot}.{gh}.{k}")
        m_pe = sPE.inc()
        sl = pp * 16
        sxa, sxb = stats[:, sl:sl + 1], stats[:, sl + 1:sl + 2]
        sxxa, sxxb = stats[:, sl + 2:sl + 3], stats[:, sl + 3:sl + 4]
        v0 = stats[:, sl + 4:sl + 5]
        negm = stats[:, sl + 5:sl + 6]
        rstd = stats[:, sl + 6:sl + 7]
        pl.op("scalar", lambda e, PA=PA, sxa=sxa: e.activation(
            dump[:, 0:1024], PA[:, :], AF.Copy, accum_out=sxa),
            wait=[(sPE, m_pe)], tag=f"rA{ot}")
        pl.op("scalar", lambda e, PB=PB, sxb=sxb: e.activation(
            dump[:, 0:1024], PB[:, :], AF.Copy, accum_out=sxb))
        pl.op("scalar", lambda e, PA=PA, sxxa=sxxa: e.activation(
            dump[:, 0:1024], PA[:, :], AF.Square, accum_out=sxxa))
        pl.op("scalar", lambda e, PB=PB, sxxb=sxxb: e.activation(
            dump[:, 0:1024], PB[:, :], AF.Square, accum_out=sxxb),
            inc=[(sSE, 1)], tag=f"rS{ot}")
        m_se = sSE.inc()
        pl.op("vector", lambda e, sxa=sxa, sxb=sxb: e.tensor_add(sxa, sxa, sxb),
              wait=[(sSE, m_se)])
        pl.op("vector", lambda e, sxxa=sxxa, sxxb=sxxb: e.tensor_add(sxxa, sxxa, sxxb))
        pl.op("vector", lambda e, sxa=sxa, v0=v0: e.tensor_mul(v0, sxa, sxa))
        pl.op("vector", lambda e, v0=v0, sxxa=sxxa: e.scalar_tensor_tensor(
            v0, v0, -1.0 / 2048, sxxa, ALU.mult, ALU.add))
        pl.op("vector", lambda e, sxa=sxa, negm=negm: e.tensor_scalar(
            negm, sxa, -1.0 / 2048, None, ALU.mult), inc=[(sVE, 1)], tag=f"rV{ot}")
        m_ve = sVE.inc()
        pl.op("scalar", lambda e, v0=v0: e.activation(v0, v0, AF.Ln,
                                                      scale=np.float32(1.0 / 2048), bias=np.float32(EPS)),
              wait=[(sVE, m_ve)])
        pl.op("scalar", lambda e, v0=v0, rstd=rstd: e.activation(
            rstd, v0, AF.Exp, scale=np.float32(-0.5)), inc=[(sSE, 1)], tag=f"rR{ot}")
        m_se2 = sSE.inc()
        lnta = sb["ln_t"][:, pp * 2048:pp * 2048 + 1024]
        lntb = sb["ln_t"][:, pp * 2048 + 1024:(pp + 1) * 2048]
        pl.op("vector", lambda e, PA=PA, lnta=lnta, negm=negm, rstd=rstd: e.tensor_scalar(
            lnta, PA[:, :], negm, rstd, ALU.add, ALU.mult),
            wait=[(sSE, m_se2)], tag=f"rlA{ot}")
        pl.op("vector", lambda e, PB=PB, lntb=lntb, negm=negm, rstd=rstd: e.tensor_scalar(
            lntb, PB[:, :], negm, rstd, ALU.add, ALU.mult),
            inc=[(sVE, 1)], tag=f"rlB{ot}")
        p4["psum_free"][pp] = sVE.inc()
        sx2 = stats[:, sl + 7:sl + 8]
        xma = sb["xmid_t"][:, pp * 2048:(pp + 1) * 2048]
        lnt = sb["ln_t"][:, pp * 2048:(pp + 1) * 2048]
        xra = sb["xr"][:, pp * 2048:(pp + 1) * 2048]
        wxm = [(sXRp[pp], m_xr)]
        if p4["xmdma"][pp]:
            wxm.append((sXMp[pp], p4["xmdma"][pp]))
        pl.op("vector", lambda e, xma=xma, lnt=lnt, xra=xra, sx2=sx2:
              e.scalar_tensor_tensor(xma, lnt, 1.0, xra, ALU.mult, ALU.add,
                                     accum_out=sx2),
              wait=wxm, inc=[(sVE, 1)], tag=f"xm{ot}")
        p4["xr_used"][pp] = sVE.inc()
        m_xmid = p4["xr_used"][pp]
        sxx2 = stats[:, sl + 8:sl + 9]
        v02 = stats[:, sl + 9:sl + 10]
        negm2 = stats[:, sl + 10:sl + 11]
        rstd2 = stats[:, sl + 11:sl + 12]
        pl.op("scalar", lambda e, xma=xma, sxx2=sxx2: e.activation(
            dump[:, :], xma, AF.Square, accum_out=sxx2),
            wait=[(sVE, m_xmid)], inc=[(sSE, 1)], tag=f"l2S{ot}")
        m_se3 = sSE.inc()
        pl.op("vector", lambda e, sx2=sx2, v02=v02: e.tensor_mul(v02, sx2, sx2),
              wait=[(sSE, m_se3)])
        pl.op("vector", lambda e, v02=v02, sxx2=sxx2: e.scalar_tensor_tensor(
            v02, v02, -1.0 / 2048, sxx2, ALU.mult, ALU.add))
        pl.op("vector", lambda e, sx2=sx2, negm2=negm2: e.tensor_scalar(
            negm2, sx2, -1.0 / 2048, None, ALU.mult), inc=[(sVE, 1)], tag=f"l2V{ot}")
        m_ve2 = sVE.inc()
        pl.op("scalar", lambda e, v02=v02: e.activation(v02, v02, AF.Ln,
                                                        scale=np.float32(1.0 / 2048), bias=np.float32(EPS)),
              wait=[(sVE, m_ve2)])
        pl.op("scalar", lambda e, v02=v02, rstd2=rstd2: e.activation(
            rstd2, v02, AF.Exp, scale=np.float32(-0.5)), inc=[(sSE, 1)], tag=f"l2R{ot}")
        m_se4 = sSE.inc()
        h2a = sb["h2t"][:, pp * 2048:(pp + 1) * 2048]
        wh2 = [(sSE, m_se4)]
        if p4["h2dma"][pp]:
            wh2.append((sXMp[pp], p4["h2dma"][pp]))
        pl.op("vector", lambda e, h2a=h2a, xma=xma, negm2=negm2, rstd2=rstd2:
              e.tensor_scalar(h2a, xma, negm2, rstd2, ALU.add, ALU.mult),
              wait=wh2, inc=[(sVE, 1)], tag=f"h2{ot}")
        m_h2 = sVE.inc()
        pl.op("sync", lambda e, ot=ot, xma=xma: e.dma_start(
            out=xmid_dram.ap()[ot * 128:(ot + 1) * 128, :], in_=xma),
            wait=[(sVE, m_h2)], inc=[(sXMp[pp], 16)], tag=f"sxm{ot}")
        p4["xmdma"][pp] = sXMp[pp].inc(16)
        pl.op("sync", lambda e, ot=ot, h2a=h2a: e.dma_start(
            out=h2_dram.ap()[ot * 128:(ot + 1) * 128, :], in_=h2a),
            inc=[(sXMp[pp], 16)], tag=f"sh2{ot}")
        p4["h2dma"][pp] = sXMp[pp].inc(16)
    xm_all_w = [(sXMp[0], sXMp[0].n), (sXMp[1], sXMp[1].n)]

    if DEBUG:
        pl.op("sync", lambda e: e.dma_start(out=dbg["d_xmid"], in_=xmid_dram.ap()),
              wait=xm_all_w, inc=[(sSTp[0], 16)])
        sSTp[0].inc(16)

    for mc in range(MC):
        pl.op("sync", lambda e, mc=mc: e.dma_start(
            out=sb["h2_ft"][:, mc * 1024:(mc + 1) * 1024],
            in_=h2_dram.ap()[:, mc * 128:(mc + 1) * 128], transpose=True),
            wait=(xm_all_w if mc == 0 else []), inc=[(sTT, 16)], tag=f"T2.{mc}")
        sTT.inc(16)
    h2ft_ready = sTT.n

    if DEBUG:
        pl.op("sync", lambda e: e.dma_start(out=dbg["d_h2ft"], in_=sb["h2_ft"][:, :]),
              wait=[(sTT, h2ft_ready)], inc=[(sSTp[0], 16)])
        sSTp[0].inc(16)

    pl.barrier(sB)
    ffn = {"gu_pe": {}, "gu_used": [0, 0], "silu_prev": 0,
           "r2_evac": [0, 0], "down_pe_all": 0}
    load_marks = {}

    def load_gu(which, slab, u):
        gidx = slab * 8 + u
        slot = gidx % 2
        h0 = slab * 2048 + u * 256
        dst = sb["wgu"] if which == "wg" else sb["wuu"]
        srcw = wg_bf if which == "wg" else wu_bf
        sem = sFg[slot] if which == "wg" else sFu[slot]
        wk_ = []
        if gidx >= 2 and (gidx - 2) in ffn["gu_pe"]:
            wk_.append((sPE, ffn["gu_pe"][gidx - 2]))
        if gidx == 0:
            wk_.append((sWc, ffn_cast_ready))
        pl.op("sync", lambda e, dst=dst, srcw=srcw, h0=h0, slot=slot: e.dma_start(
            out=dst[:, slot * MC * 256:(slot + 1) * MC * 256].rearrange(
                "p (mc h) -> p mc h", mc=MC),
            in_=srcw.ap().rearrange("p (mc h) -> p mc h", mc=MC)[:, :, h0:h0 + 256]),
            wait=wk_, inc=[(sem, 16)], tag=f"u{which}{slab}.{u}")
        return (sem, sem.inc(16))

    def load_wd_half(slab, hs):
        marks = []
        for ul in range(4):
            h0 = slab * 2048 + hs * 1024 + ul * 256
            wk_ = []
            if ul == 0:
                if ffn["down_pe_all"]:
                    wk_.append((sPE, ffn["down_pe_all"]))
            hc0 = h0 // 128
            pl.op("gpsimd", lambda e, hc0=hc0, ul=ul: e.dma_start(
                out=sb["wdu"][:, ul * 2 * 2048:(ul + 1) * 2 * 2048].rearrange(
                    "p (hs2 m) -> p hs2 m", hs2=2),
                in_=wd_t.rearrange("p (hc m) -> p hc m", hc=64)[:, hc0:hc0 + 2, :]),
                wait=wk_, inc=[(sFdu[ul], 16)], tag=f"uwd{slab}.{hs}.{ul}")
            marks.append((sFdu[ul], sFdu[ul].inc(16)))
        return marks

    for slab in range(4):
        wd_mark = load_wd_half(slab, 0)
        for hc in range(16):
            u, sub = hc // 2, hc % 2
            if sub == 0:
                load_marks[("wg", slab, u)] = load_gu("wg", slab, u)
                load_marks[("wu", slab, u)] = load_gu("wu", slab, u)
            gp = P[0] if hc % 2 == 0 else P[2]
            up = P[1] if hc % 2 == 0 else P[3]
            gidx = slab * 8 + u
            slot = gidx % 2
            w = [load_marks[("wg", slab, u)], load_marks[("wu", slab, u)]]
            if slab == 0 and hc == 0:
                w.append((sTT, h2ft_ready))
            prev = ffn["gu_used"][hc % 2]
            if prev:
                w.append((sVE, prev))
            if slab > 0 and hc < 2:
                if ffn["r2_evac"][0]:
                    w.append((sVE, ffn["r2_evac"][0]))
                if ffn["r2_evac"][1]:
                    w.append((sVE, ffn["r2_evac"][1]))
            for mc in range(MC):
                base = slot * MC * 256 + mc * 256 + sub * 128
                lg = sb["wgu"][:, base:base + 128]
                lu = sb["wuu"][:, base:base + 128]
                for half in range(2):
                    pl.op("tensor", lambda e, lg=lg, gp=gp, mc=mc, half=half: e.matmul(
                        gp[:, half * 512:(half + 1) * 512], lg,
                        sb["h2_ft"][:, mc * 1024 + half * 512:mc * 1024 + (half + 1) * 512],
                        start=(mc == 0), stop=(mc == MC - 1)),
                        wait=(w if (mc == 0 and half == 0) else []),
                        inc=([(sPE, 1)] if (mc == MC - 1 and half == 1) else []),
                        tag=f"gM{slab}.{hc}")
                for half in range(2):
                    pl.op("tensor", lambda e, lu=lu, up=up, mc=mc, half=half: e.matmul(
                        up[:, half * 512:(half + 1) * 512], lu,
                        sb["h2_ft"][:, mc * 1024 + half * 512:mc * 1024 + (half + 1) * 512],
                        start=(mc == 0), stop=(mc == MC - 1)),
                        inc=([(sPE, 1)] if (mc == MC - 1 and half == 1) else []),
                        tag=f"uM{slab}.{hc}")
            m_gate = sPE.inc()
            m_up = sPE.inc()
            if sub == 1:
                ffn["gu_pe"][gidx] = m_up
            wse = [(sPE, m_gate)]
            if ffn["silu_prev"]:
                wse.append((sVE, ffn["silu_prev"]))
            pl.op("scalar", lambda e, gp=gp: e.activation(
                sb["silu_t"][:, :], gp[:, :], AF.Silu),
                wait=wse, inc=[(sSE, 1)], tag=f"si{slab}.{hc}")
            m_si = sSE.inc()
            pl.op("vector", lambda e, up=up, hc=hc: e.tensor_mul(
                sb["g_slab"][:, hc * 1024:(hc + 1) * 1024], up[:, :],
                sb["silu_t"][:, :]),
                wait=[(sSE, m_si), (sPE, m_up)], inc=[(sVE, 1)], tag=f"gm{slab}.{hc}")
            ffn["gu_used"][hc % 2] = sVE.inc()
            ffn["silu_prev"] = ffn["gu_used"][hc % 2]
        g_ready = sVE.n
        if DEBUG and slab == 0:
            pl.op("sync", lambda e: e.dma_start(out=dbg["d_g"], in_=sb["g_slab"][:, :]),
                  wait=[(sVE, g_ready)], inc=[(sSTp[0], 16)])
            sSTp[0].inc(16)
        for hs in range(2):
            if hs == 1:
                wd_mark = load_wd_half(slab, 1)
            for ot in range(OTN):
                pp = ot % 2
                PA, PB = (P[0], P[1]) if pp == 0 else (P[2], P[3])
                w = [(sVE, g_ready)]
                if ffn["r2_evac"][pp]:
                    w.append((sVE, ffn["r2_evac"][pp]))
                for hl in range(8):
                    hc = hs * 8 + hl
                    ul, sub = hl // 2, hl % 2
                    lg = sb["g_slab"][:, hc * 1024 + ot * 128:hc * 1024 + (ot + 1) * 128]
                    for k in range(4):
                        tgt = PA if k < 2 else PB
                        pl.op("tensor",
                              lambda e, lg=lg, tgt=tgt, k=k, ul=ul, sub=sub, hl=hl: e.matmul(
                                  tgt[:, (k % 2) * 512:(k % 2 + 1) * 512], lg,
                                  sb["wdu"][:, ul * 2 * 2048 + sub * 2048 + k * 512:
                                            ul * 2 * 2048 + sub * 2048 + (k + 1) * 512],
                                  start=(hl == 0), stop=(hl == 7)),
                              wait=((w + [wd_mark[0]]) if (hl == 0 and k == 0) else
                                    ([wd_mark[ul]] if (sub == 0 and k == 0) else [])),
                              inc=([(sPE, 1)] if (hl == 7 and k == 3) else []),
                              tag=f"dM{slab}.{hs}.{ot}.{hl}.{k}")
                m_pe = sPE.inc()
                r2a = sb["r2"][:, ot * 2048:ot * 2048 + 1024]
                r2b = sb["r2"][:, ot * 2048 + 1024:(ot + 1) * 2048]
                first = (slab == 0 and hs == 0)
                if first:
                    pl.op("vector", lambda e, r2a=r2a, PA=PA: e.tensor_copy(r2a, PA[:, :]),
                          wait=[(sPE, m_pe)], tag=f"rc{slab}.{hs}.{ot}")
                    pl.op("vector", lambda e, r2b=r2b, PB=PB: e.tensor_copy(r2b, PB[:, :]),
                          inc=[(sVE, 1)], tag=f"rcB{slab}.{hs}.{ot}")
                else:
                    pl.op("vector", lambda e, r2a=r2a, PA=PA: e.tensor_add(r2a, r2a, PA[:, :]),
                          wait=[(sPE, m_pe)], tag=f"ra{slab}.{hs}.{ot}")
                    pl.op("vector", lambda e, r2b=r2b, PB=PB: e.tensor_add(r2b, r2b, PB[:, :]),
                          inc=[(sVE, 1)], tag=f"raB{slab}.{hs}.{ot}")
                ffn["r2_evac"][pp] = sVE.inc()
            ffn["down_pe_all"] = sPE.n
    r2_all = sVE.n

    if DEBUG:
        pl.op("sync", lambda e: e.dma_start(out=dbg["d_r2"], in_=sb["r2"][:, :]),
              wait=[(sVE, r2_all)], inc=[(sSTp[0], 16)])
        sSTp[0].inc(16)

    pl.barrier(sB)
    p6_ve = [0, 0]
    p6_st = [0, 0]
    for ot in range(OTN):
        pp = ot % 2
        xra = sb["xr6"][:, pp * 2048:(pp + 1) * 2048]
        wxr = list(xm_all_w)
        if p6_ve[pp]:
            wxr.append((sVE, p6_ve[pp]))
        pl.op("sync", lambda e, ot=ot, xra=xra: e.dma_start(
            out=xra, in_=xmid_dram.ap()[ot * 128:(ot + 1) * 128, :]),
            wait=wxr, inc=[(sXRp[pp], 16)], tag=f"fx{ot}")
        m_xr = sXRp[pp].inc(16)
        sl = pp * 16
        sx = stats[:, sl:sl + 1]
        sxx = stats[:, sl + 1:sl + 2]
        v0 = stats[:, sl + 2:sl + 3]
        negm = stats[:, sl + 3:sl + 4]
        rstd = stats[:, sl + 4:sl + 5]
        r2t = sb["r2"][:, ot * 2048:(ot + 1) * 2048]
        pl.op("scalar", lambda e, r2t=r2t, sx=sx: e.activation(
            dump[:, :], r2t, AF.Copy, accum_out=sx),
            wait=[(sVE, r2_all)], tag=f"fS{ot}")
        pl.op("scalar", lambda e, r2t=r2t, sxx=sxx: e.activation(
            dump[:, :], r2t, AF.Square, accum_out=sxx), inc=[(sSE, 1)], tag=f"fQ{ot}")
        m_se = sSE.inc()
        pl.op("vector", lambda e, sx=sx, v0=v0: e.tensor_mul(v0, sx, sx),
              wait=[(sSE, m_se)])
        pl.op("vector", lambda e, v0=v0, sxx=sxx: e.scalar_tensor_tensor(
            v0, v0, -1.0 / 2048, sxx, ALU.mult, ALU.add))
        pl.op("vector", lambda e, sx=sx, negm=negm: e.tensor_scalar(
            negm, sx, -1.0 / 2048, None, ALU.mult), inc=[(sVE, 1)], tag=f"fV{ot}")
        m_ve = sVE.inc()
        pl.op("scalar", lambda e, v0=v0: e.activation(v0, v0, AF.Ln,
                                                      scale=np.float32(1.0 / 2048), bias=np.float32(EPS)),
              wait=[(sVE, m_ve)])
        pl.op("scalar", lambda e, v0=v0, rstd=rstd: e.activation(
            rstd, v0, AF.Exp, scale=np.float32(-0.5)), inc=[(sSE, 1)], tag=f"fR{ot}")
        m_se2 = sSE.inc()
        lna = sb["ln6"][:, pp * 2048:(pp + 1) * 2048]
        xoa = sb["xo6"][:, pp * 2048:(pp + 1) * 2048]
        pl.op("vector", lambda e, lna=lna, r2t=r2t, negm=negm, rstd=rstd: e.tensor_scalar(
            lna, r2t, negm, rstd, ALU.add, ALU.mult),
            wait=[(sSE, m_se2)], tag=f"fL{ot}")
        wadd = [(sXRp[pp], m_xr)]
        if p6_st[pp]:
            wadd.append((sSTp[pp], p6_st[pp]))
        pl.op("vector", lambda e, xoa=xoa, lna=lna, xra=xra: e.scalar_tensor_tensor(
            xoa, lna, 1.0, xra, ALU.mult, ALU.add),
            wait=wadd, inc=[(sVE, 1)], tag=f"fA{ot}")
        p6_ve[pp] = sVE.inc()
        pl.op("sync", lambda e, ot=ot, xoa=xoa: e.dma_start(
            out=out_own[ot * 128:(ot + 1) * 128, :], in_=xoa),
            wait=[(sVE, p6_ve[pp])], inc=[(sSTp[pp], 16)], tag=f"st{ot}")
        p6_st[pp] = sSTp[pp].inc(16)

    pl.op("sync", None, wait=[(sSTp[0], sSTp[0].n), (sSTp[1], sSTp[1].n)],
          tag="final-wait")

    pl.simulate()
    pl.replay()
    return nc


_CACHE = {}


def kernel(L0_attn_x, L0_attn_w_aq, L0_attn_w_ak, L0_attn_w_av, L0_attn_w_ao,
           L0_ffn_w_gate, L0_ffn_w_up, L0_ffn_w_down):
    x = np.ascontiguousarray(np.asarray(L0_attn_x, dtype=np.float32))
    waq = np.asarray(L0_attn_w_aq, np.float32)
    wak = np.asarray(L0_attn_w_ak, np.float32)
    wav = np.asarray(L0_attn_w_av, np.float32)
    wq_h = [np.ascontiguousarray(
        waq[g].reshape(H, MC, 128, D).transpose(2, 0, 1, 3).reshape(128, H * MC * D))
        for g in range(G)]
    wk_h = [np.ascontiguousarray(
        wak[g].reshape(MC, 128, D).transpose(1, 0, 2).reshape(128, MC * D))
        for g in range(G)]
    wv_h = [np.ascontiguousarray(
        wav[g].reshape(MC, 128, D).transpose(1, 0, 2).reshape(128, MC * D))
        for g in range(G)]
    wao = np.ascontiguousarray(
        np.asarray(L0_attn_w_ao, np.float32).reshape(G * H, D, M)
        .transpose(1, 0, 2).reshape(128, G * H * M))
    wgt = np.ascontiguousarray(
        np.asarray(L0_ffn_w_gate, np.float32).T.reshape(MC, 128, HID)
        .transpose(1, 0, 2).reshape(128, MC * HID))
    wut = np.ascontiguousarray(
        np.asarray(L0_ffn_w_up, np.float32).T.reshape(MC, 128, HID)
        .transpose(1, 0, 2).reshape(128, MC * HID))
    wdt = np.ascontiguousarray(
        np.asarray(L0_ffn_w_down, np.float32).T.reshape(64, 128, M)
        .transpose(1, 0, 2).reshape(128, 64 * M))

    pos = np.arange(T, dtype=np.float64)
    freqs = 10000.0 ** (-np.arange(64, dtype=np.float64) / 64.0)
    ang = np.outer(freqs, pos)
    rope1 = np.concatenate([np.cos(ang), np.sin(ang)], axis=0).astype(BFH)
    rope2 = np.concatenate([np.sin(ang), np.cos(ang)], axis=0).astype(BFH)
    masks = np.zeros((4, 128, 512), np.float32)
    jj = np.arange(128)[:, None]
    ii = np.arange(512)[None, :]
    for r in range(4):
        masks[r] = ((jj + 128 * r) <= ii).astype(np.float32)
    masks = masks.astype(BFH)

    if "nc" not in _CACHE:
        _CACHE["nc"] = build_program()
    nc = _CACHE["nc"]

    in_maps = []
    for r in range(8):
        b, g = r // 4, r % 4
        in_maps.append({
            "x_full": x[b],
            "w_q": wq_h[g],
            "w_k": wk_h[g],
            "w_v": wv_h[g],
            "w_o": wao,
            "wg_t": wgt,
            "wu_t": wut,
            "wd_t": wdt,
            "rope1": np.asarray(rope1),
            "rope2": np.asarray(rope2),
            "masks": np.asarray(masks),
        })
    res = run_bass_kernel_spmd(nc, in_maps, core_ids=list(range(8)))
    out = np.empty((B, T, M), np.float32)
    for r in range(8):
        b, s = r // 4, r % 4
        out[b, s * 1024:(s + 1) * 1024, :] = np.asarray(res.results[r]["out_own"])
    if DEBUG:
        kernel._last = res
    return out


# version 09 (final)
# speedup vs baseline: 1.7597x; 1.0262x over previous
import numpy as np
import ml_dtypes
import concourse.bass as bass
import concourse.mybir as mybir
from concourse.bass_utils import run_bass_kernel_spmd

F32 = mybir.dt.float32
BF16 = mybir.dt.bfloat16
AF = mybir.ActivationFunctionType
ALU = mybir.AluOpType
BFH = ml_dtypes.bfloat16

B, T, M, G, H, D = 2, 4096, 2048, 4, 4, 128
HID = 8192
NT = T // 128
OTN = 8
MC = M // 128
QK2 = float(D) ** -0.5
SHIFT = 30.0
EPS = 1e-5
DEBUG = False

ENGINES = ("sync", "gpsimd", "tensor", "scalar", "vector")


class SemT:
    def __init__(self, nc, name):
        self.h = nc.alloc_semaphore(name)
        self.n = 0
        self.name = name

    def inc(self, k=1):
        self.n += k
        return self.n


class Op:
    __slots__ = ("eng", "fn", "waits", "incs", "tag")

    def __init__(self, eng, fn, waits, incs, tag):
        self.eng, self.fn, self.waits, self.incs, self.tag = eng, fn, waits, incs, tag


class Plan:
    def __init__(self, nc):
        self.nc = nc
        self.seq = []
        self.pid = {}
        self.sl1024 = {}

    def op(self, eng, fn=None, wait=(), inc=(), tag=""):
        self.seq.append(Op(eng, fn, [w for w in wait if w is not None], list(inc), tag))

    def barrier(self, sem):
        for e in ENGINES:
            self.op(e, None, inc=[(sem, 1)], tag="bar-inc")
            sem.inc(1)
        tgt = sem.n
        for e in ENGINES:
            self.op(e, None, wait=[(sem, tgt)], tag="bar-wait")

    def simulate(self):
        vals = {}
        qs = {e: [o for o in self.seq if o.eng == e] for e in ENGINES}
        ptr = {e: 0 for e in ENGINES}
        progress = True
        while progress:
            progress = False
            for e in ENGINES:
                while ptr[e] < len(qs[e]):
                    o = qs[e][ptr[e]]
                    if all(vals.get(s.name, 0) >= v for s, v in o.waits):
                        for s, k in o.incs:
                            vals[s.name] = vals.get(s.name, 0) + k
                        ptr[e] += 1
                        progress = True
                    else:
                        break
        stuck = {e: ptr[e] for e in ENGINES if ptr[e] < len(qs[e])}
        if stuck:
            msg = []
            for e, p in stuck.items():
                o = qs[e][p]
                why = [(s.name, v, vals.get(s.name, 0)) for s, v in o.waits
                       if vals.get(s.name, 0) < v]
                msg.append(f"{e}[{p}/{len(qs[e])}] tag={o.tag} blocked on {why}")
            raise RuntimeError("PLAN DEADLOCK:\n" + "\n".join(msg))

    def replay(self):
        nc = self.nc
        with nc.Block() as block:
            def run_engine(eng_obj, name):
                try:
                    self.pid[name] = eng_obj.partition_id()
                    self.sl1024[name] = self.pid[name] % 4 * 1024
                except Exception:
                    pass
                for o in self.seq:
                    if o.eng != name:
                        continue
                    for s, v in o.waits:
                        eng_obj.wait_ge(s.h, v)
                    if o.fn is not None:
                        inst = o.fn(eng_obj)
                        for s, k in o.incs:
                            inst = inst.then_inc(s.h, k)
                    else:
                        for s, k in o.incs:
                            eng_obj.sem_inc(s.h, k)

            @block.sync
            def _(e):
                run_engine(e, "sync")

            @block.gpsimd
            def _(e):
                run_engine(e, "gpsimd")

            @block.tensor
            def _(e):
                run_engine(e, "tensor")

            @block.scalar
            def _(e):
                run_engine(e, "scalar")

            @block.vector
            def _(e):
                run_engine(e, "vector")


class SbufMap:
    def __init__(self, nc, base, top):
        self.nc = nc
        self.base = base
        self.top = top
        self.items = []

    def add(self, name, shape, dtype, live):
        esz = 4 if dtype == F32 else 2
        bpp = int(np.prod(shape[1:])) * esz
        bpp = (bpp + 63) // 64 * 64
        self.items.append([name, shape, dtype, bpp, live])

    def finalize(self):
        placed = []
        out = {}
        for name, shape, dtype, bpp, live in sorted(self.items, key=lambda it: -it[3]):
            off = 0
            while True:
                clash = None
                for poff, psz, plive, pname in placed:
                    if not (live[1] < plive[0] or plive[1] < live[0]):
                        if not (off + bpp <= poff or poff + psz <= off):
                            clash = poff + psz
                            break
                if clash is None:
                    break
                off = (clash + 63) // 64 * 64
            assert self.base + off + bpp <= self.top, (
                f"SBUF overflow placing {name}: off={off} bpp={bpp}")
            placed.append((off, bpp, live, name))
            out[name] = self.nc.alloc_sbuf_tensor_at(
                name, list(shape), dtype, offset=self.base + off)
        return out


def build_program():
    nc = bass.Bass("TRN2", target_bir_lowering=False)

    x_full = nc.dram_tensor("x_full", [T, M], F32, kind="ExternalInput").ap()
    w_q = nc.dram_tensor("w_q", [128, H * MC * D], F32, kind="ExternalInput").ap()
    w_k = nc.dram_tensor("w_k", [128, MC * D], F32, kind="ExternalInput").ap()
    w_v = nc.dram_tensor("w_v", [128, MC * D], F32, kind="ExternalInput").ap()
    w_o = nc.dram_tensor("w_o", [128, G * H * M], F32, kind="ExternalInput").ap()
    wg_t = nc.dram_tensor("wg_t", [128, MC * HID], F32, kind="ExternalInput").ap()
    wu_t = nc.dram_tensor("wu_t", [128, MC * HID], F32, kind="ExternalInput").ap()
    wd_t = nc.dram_tensor("wd_t", [128, 64 * M], F32, kind="ExternalInput").ap()
    rope1 = nc.dram_tensor("rope1", [128, T], BF16, kind="ExternalInput").ap()
    rope2 = nc.dram_tensor("rope2", [128, T], BF16, kind="ExternalInput").ap()
    masks_i = nc.dram_tensor("masks", [4, 128, 512], BF16, kind="ExternalInput").ap()
    out_own = nc.dram_tensor("out_own", [1024, M], F32, kind="ExternalOutput").ap()

    h_dram = nc.dram_tensor("h_dram", [T, M], BF16)
    h2_dram = nc.dram_tensor("h2_dram", [1024, M], BF16)
    xmid_dram = nc.dram_tensor("xmid_dram", [1024, M], F32)
    oc_dram = [nc.dram_tensor(f"oc{i}_dram", [D, T], BF16) for i in range(H)]
    og_dram = [nc.dram_tensor(f"og{i}_dram", [G * D, T], BF16) for i in range(H)]
    wg_bf = nc.dram_tensor("wg_bf", [128, MC * HID], BF16)
    wu_bf = nc.dram_tensor("wu_bf", [128, MC * HID], BF16)

    dbg = {}
    if DEBUG:
        dbg["d_h"] = nc.dram_tensor("d_h", [T, M], BF16, kind="ExternalOutput").ap()
        dbg["d_q"] = nc.dram_tensor("d_q", [128, H * T], BF16, kind="ExternalOutput").ap()
        dbg["d_k"] = nc.dram_tensor("d_k", [128, T], BF16, kind="ExternalOutput").ap()
        dbg["d_v"] = nc.dram_tensor("d_v", [128, T], BF16, kind="ExternalOutput").ap()
        dbg["d_o"] = nc.dram_tensor("d_o", [128, H * T], BF16, kind="ExternalOutput").ap()
        dbg["d_xmid"] = nc.dram_tensor("d_xmid", [1024, M], F32, kind="ExternalOutput").ap()
        dbg["d_h2ft"] = nc.dram_tensor("d_h2ft", [128, MC * 1024], BF16, kind="ExternalOutput").ap()
        dbg["d_g"] = nc.dram_tensor("d_g", [128, 16 * 1024], BF16, kind="ExternalOutput").ap()
        dbg["d_r2"] = nc.dram_tensor("d_r2", [128, OTN * 2048], F32, kind="ExternalOutput").ap()

    smap = SbufMap(nc, (nc.sbuf_base + 63) // 64 * 64, nc.sbuf_top)
    smap.add("rope1_sb", [128, T], BF16, (0, 1))
    smap.add("rope2_sb", [128, T], BF16, (0, 1))
    smap.add("masks_sb", [128, 2048], BF16, (0, 2))
    smap.add("ones_bf", [128, 16], BF16, (0, 6))
    smap.add("ones_f", [1, 128], F32, (0, 6))
    smap.add("stats", [128, 64], F32, (0, 6))
    smap.add("recip_sb", [1, 512], F32, (2, 2))
    smap.add("recipb_sb", [1, 512], BF16, (2, 2))
    smap.add("pairt", [128, 1024], BF16, (2, 2))
    smap.add("ones_bfr", [1, 128], BF16, (0, 6))
    smap.add("bcast_sb", [128, 512], F32, (2, 2))
    smap.add("ptile", [128, 32 * 512], BF16, (2, 2))
    smap.add("dump", [128, 2048], F32, (0, 6))
    smap.add("dump2", [128, 2048], F32, (1, 1))
    smap.add("h_ft", [128, 2 * MC * 1024], BF16, (1, 1))
    smap.add("xt", [128, 2 * 2048], F32, (1, 1))
    smap.add("ht", [128, 2 * 2048], BF16, (1, 1))
    smap.add("qtmp", [128, 2 * 1024], BF16, (1, 1))
    smap.add("rtmp", [128, 2048], BF16, (1, 1))
    smap.add("rswp", [128, 1024], BF16, (1, 1))
    smap.add("wq_sb", [128, H * MC * D], BF16, (0, 1))
    smap.add("wk_sb", [128, MC * D], BF16, (0, 1))
    smap.add("wv_sb", [128, MC * D], BF16, (0, 1))
    smap.add("q_ft", [128, H * T], BF16, (1, 2))
    smap.add("k_ft", [128, T], BF16, (1, 2))
    smap.add("v_sb", [128, T], BF16, (1, 2))
    smap.add("o_ft", [128, H * T], BF16, (2, 3))
    smap.add("wo_sb", [128, G * H * M], BF16, (2, 4))
    smap.add("o16_sb", [128, G * H * 1024], BF16, (3, 4))
    smap.add("h2_ft", [128, MC * 1024], BF16, (4, 5))
    smap.add("xr", [128, 2 * 2048], F32, (4, 4))
    smap.add("ln_t", [128, 2 * 2048], F32, (4, 4))
    smap.add("xmid_t", [128, 2 * 2048], F32, (4, 4))
    smap.add("h2t", [128, 2 * 2048], BF16, (4, 4))
    smap.add("g_slab", [128, 16 * 1024], BF16, (5, 6))
    smap.add("r2", [128, OTN * 2048], F32, (5, 6))
    smap.add("wgu", [128, 2 * MC * 256], BF16, (5, 5))
    smap.add("wuu", [128, 2 * MC * 256], BF16, (5, 5))
    smap.add("wdu", [128, 8 * 2048], BF16, (5, 6))
    smap.add("silu_t", [128, 1024], F32, (5, 5))
    smap.add("xr6", [128, 2 * 2048], F32, (6, 6))
    sb = smap.finalize()

    P = [nc.alloc_psum_tensor(f"ps{i}", [128, 1024], F32) for i in range(4)]

    pl = Plan(nc)
    sCst = SemT(nc, "sCst")
    sXp = [SemT(nc, "sXa"), SemT(nc, "sXb")]
    sHp = [SemT(nc, "sHa"), SemT(nc, "sHb")]
    sTT = SemT(nc, "sTT")
    sWq = SemT(nc, "sWq")
    sWo = SemT(nc, "sWo")
    sWc = SemT(nc, "sWc")
    sFg = [SemT(nc, "sFga"), SemT(nc, "sFgb")]
    sFu = [SemT(nc, "sFua"), SemT(nc, "sFub")]
    sFdu = [SemT(nc, f"sFd{i}") for i in range(4)]
    sPE = SemT(nc, "sPE")
    sSE = SemT(nc, "sSE")
    sVE = SemT(nc, "sVE")
    sCCh = [SemT(nc, f"sCC{i}") for i in range(H)]
    sC1h = [SemT(nc, f"sC1{i}") for i in range(H)]
    sG = SemT(nc, "sG")
    sOwnG = SemT(nc, "sOwnG")
    sXRp = [SemT(nc, "sXRa"), SemT(nc, "sXRb")]
    sXMp = [SemT(nc, "sXMa"), SemT(nc, "sXMb")]
    sSTp = [SemT(nc, "sSTa"), SemT(nc, "sSTb")]
    sB = SemT(nc, "sB")

    rope1_sb, rope2_sb = sb["rope1_sb"], sb["rope2_sb"]
    masks_sb, stats, dump = sb["masks_sb"], sb["stats"], sb["dump"]
    h_ft, q_ft, k_ft, v_sb, o_ft = (sb["h_ft"], sb["q_ft"], sb["k_ft"],
                                    sb["v_sb"], sb["o_ft"])

    pl.op("vector", lambda e: e.memset(sb["ones_bf"][:, :], 1.0))
    pl.op("vector", lambda e: e.memset(sb["ones_f"][:, :], 1.0))
    pl.op("vector", lambda e: e.memset(sb["ones_bfr"][:, :], 1.0), inc=[(sVE, 1)])
    ones_ready = sVE.inc()
    pl.op("sync", lambda e: e.dma_start(out=rope1_sb[:, :], in_=rope1), inc=[(sCst, 16)])
    sCst.inc(16)
    pl.op("sync", lambda e: e.dma_start(out=rope2_sb[:, :], in_=rope2), inc=[(sCst, 16)])
    sCst.inc(16)
    pl.op("sync",
          lambda e: e.dma_start(out=masks_sb[:, :].rearrange("p (r i) -> p r i", r=4),
                                in_=masks_i.rearrange("r p i -> p r i")),
          inc=[(sCst, 16)])
    rope_ready = sCst.inc(16)

    pl.op("gpsimd", lambda e: e.dma_start(out=sb["wk_sb"][:, :], in_=w_k),
          inc=[(sWq, 16)], tag="wk")
    sWq.inc(16)
    pl.op("gpsimd", lambda e: e.dma_start(out=sb["wv_sb"][:, :], in_=w_v),
          inc=[(sWq, 16)], tag="wv")
    sWq.inc(16)
    pl.op("gpsimd", lambda e: e.dma_start(out=sb["wq_sb"][:, :], in_=w_q),
          inc=[(sWq, 16)], tag="wq")
    sWq.inc(16)
    wqkv_ready = sWq.n

    xt, ht = sb["xt"], sb["ht"]
    ln_done = [0] * NT
    h_mark = [0] * NT

    def emit_ln_tile(ti):
        slot = ti % 2
        xa = xt[:, slot * 2048:(slot + 1) * 2048]
        ha = ht[:, slot * 2048:(slot + 1) * 2048]
        w = [(sVE, ln_done[ti - 2])] if ti >= 2 else []
        pl.op("gpsimd", lambda e, ti=ti, xa=xa: e.dma_start(
            out=xa, in_=x_full[ti * 128:(ti + 1) * 128, :]),
            wait=w, inc=[(sXp[slot], 16)], tag=f"x{ti}")
        xv = sXp[slot].inc(16)
        sl = slot * 8
        sx = stats[:, sl:sl + 1]
        sxx = stats[:, sl + 1:sl + 2]
        v0 = stats[:, sl + 2:sl + 3]
        negm = stats[:, sl + 3:sl + 4]
        rstd = stats[:, sl + 4:sl + 5]
        pl.op("vector", lambda e, xa=xa, sx=sx: e.tensor_scalar(
            sb["dump2"][:, :], xa, 1.0, 0.0, ALU.mult, ALU.add, accum_out=sx),
            wait=[(sXp[slot], xv)], inc=[(sVE, 1)], tag=f"ln{ti}-sx")
        m_vx = sVE.inc()
        pl.op("scalar", lambda e, xa=xa, sxx=sxx: e.activation(
            dump[:, :], xa, AF.Square, accum_out=sxx),
            wait=[(sXp[slot], xv)], inc=[(sSE, 1)], tag=f"ln{ti}-sxx")
        m_se = sSE.inc()
        pl.op("vector", lambda e, sx=sx, v0=v0: e.tensor_mul(v0, sx, sx),
              wait=[(sSE, m_se)])
        pl.op("vector", lambda e, v0=v0, sxx=sxx: e.scalar_tensor_tensor(
            v0, v0, -1.0 / 2048, sxx, ALU.mult, ALU.add))
        pl.op("vector", lambda e, sx=sx, negm=negm: e.tensor_scalar(
            negm, sx, -1.0 / 2048, None, ALU.mult), inc=[(sVE, 1)], tag=f"ln{ti}-v")
        m_ve = sVE.inc()
        pl.op("scalar", lambda e, v0=v0: e.activation(v0, v0, AF.Ln,
                                                      scale=np.float32(1.0 / 2048), bias=np.float32(EPS)),
              wait=[(sVE, m_ve)])
        pl.op("scalar", lambda e, v0=v0, rstd=rstd: e.activation(
            rstd, v0, AF.Exp, scale=np.float32(-0.5)), inc=[(sSE, 1)], tag=f"ln{ti}-r")
        m_se2 = sSE.inc()
        wn = [(sSE, m_se2)]
        if ti >= 2 and h_mark[ti - 2]:
            wn.append((sHp[slot], h_mark[ti - 2]))
        pl.op("vector", lambda e, ha=ha, xa=xa, negm=negm, rstd=rstd: e.tensor_scalar(
            ha, xa, negm, rstd, ALU.add, ALU.mult),
            wait=wn, inc=[(sVE, 1)], tag=f"ln{ti}-n")
        ln_done[ti] = sVE.inc()
        pl.op("sync", lambda e, ti=ti, ha=ha: e.dma_start(
            out=h_dram.ap()[ti * 128:(ti + 1) * 128, :], in_=ha),
            wait=[(sVE, ln_done[ti])], inc=[(sHp[slot], 16)], tag=f"h{ti}")
        h_mark[ti] = sHp[slot].inc(16)

    qkv_state = {"unit": 0, "vunit": 0, "sse_unit": [0, 0, 0], "sve_unit": [0, 0],
                 "sse_v": [0, 0]}
    trans_done = [0] * 4

    def emit_qkv_quarter(qu):
        tbase = qu * 1024
        t_ready = sTT.n
        hbase = (qu % 2) * MC * 1024
        for u in range(5):
            ps = P[qkv_state["unit"] % 3]
            w = [(sTT, t_ready)]
            w.append((sWq, wqkv_ready))
            prev = qkv_state["sse_unit"][qkv_state["unit"] % 3]
            if prev:
                w.append((sVE, prev))
            for mc in range(MC):
                if u < 4:
                    lhsT = sb["wq_sb"][:, (u * MC + mc) * D:(u * MC + mc + 1) * D]
                else:
                    lhsT = sb["wk_sb"][:, mc * D:(mc + 1) * D]
                for half in range(2):
                    pl.op("tensor", lambda e, lhsT=lhsT, half=half, mc=mc, ps=ps, hbase=hbase:
                          e.matmul(
                        ps[:, half * 512:(half + 1) * 512], lhsT,
                        h_ft[:, hbase + mc * 1024 + half * 512:
                             hbase + mc * 1024 + (half + 1) * 512],
                        start=(mc == 0), stop=(mc == MC - 1)),
                        wait=(w if (mc == 0 and half == 0) else []),
                        inc=([(sPE, 1)] if (mc == MC - 1 and half == 1) else []),
                        tag=f"qkvM{qu}.{u}")
            m_pe = sPE.inc()
            qslot = qkv_state["unit"] % 2
            qa = sb["qtmp"][:, qslot * 1024:(qslot + 1) * 1024]
            pslot = qkv_state["unit"] % 3
            wse = [(sPE, m_pe)]
            pl.op("vector", lambda e, qa=qa, ps=ps: e.tensor_copy(qa, ps[:, :]),
                  wait=wse, inc=[(sVE, 1)], tag=f"qkvE{qu}.{u}")
            qkv_state["sse_unit"][pslot] = sVE.inc()
            if u < 4:
                dst = q_ft[:, u * T + tbase:u * T + tbase + 1024]
            else:
                dst = k_ft[:, tbase:tbase + 1024]
            cs1 = rope1_sb[:, tbase:tbase + 1024]
            cs2 = rope2_sb[:, tbase:tbase + 1024]
            rt1 = sb["rtmp"][:, 0:1024]
            rt2 = sb["rtmp"][:, 1024:2048]
            sw = sb["rswp"]
            pl.op("vector", lambda e, qa=qa, cs1=cs1, rt1=rt1: e.tensor_mul(rt1, qa, cs1),
                  wait=[(sCst, rope_ready)], tag=f"rp{qu}.{u}")
            pl.op("vector", lambda e, qa=qa, cs2=cs2, rt2=rt2: e.tensor_mul(rt2, qa, cs2))
            pl.op("vector", lambda e, rt1=rt1, sw=sw: e.tensor_copy(
                sw[0:64, :], rt1[64:128, :]))
            pl.op("vector", lambda e, rt2=rt2, sw=sw: e.tensor_copy(
                sw[64:128, :], rt2[0:64, :]))
            pl.op("vector", lambda e, dst=dst, rt1=rt1, sw=sw: e.tensor_sub(
                dst[0:64, :], rt1[0:64, :], sw[0:64, :]))
            pl.op("vector", lambda e, dst=dst, rt2=rt2, sw=sw: e.tensor_add(
                dst[64:128, :], rt2[64:128, :], sw[64:128, :]),
                inc=[(sVE, 1)], tag=f"rpd{qu}.{u}")
            qkv_state["sve_unit"][qslot] = sVE.inc()
            qkv_state["unit"] += 1
        for tt in range(8):
            vslot = qkv_state["vunit"] % 2
            vp = P[3][:, vslot * 512:vslot * 512 + 128]
            w = [(sTT, t_ready)]
            prev = qkv_state["sse_v"][qkv_state["vunit"] % 2]
            if prev:
                w.append((sVE, prev))
            for mc in range(MC):
                lhsT = h_ft[:, hbase + mc * 1024 + tt * 128:
                            hbase + mc * 1024 + (tt + 1) * 128]
                pl.op("tensor", lambda e, lhsT=lhsT, mc=mc, vp=vp: e.matmul(
                    vp, lhsT, sb["wv_sb"][:, mc * D:(mc + 1) * D],
                    start=(mc == 0), stop=(mc == MC - 1)),
                    wait=(w if mc == 0 else []),
                    inc=([(sPE, 1)] if mc == MC - 1 else []), tag=f"vM{qu}.{tt}")
            m_pe = sPE.inc()
            gt = qu * 8 + tt
            pl.op("vector", lambda e, gt=gt, vp=vp: e.tensor_copy(
                v_sb[:, gt * 128:(gt + 1) * 128], vp),
                wait=[(sPE, m_pe)], inc=[(sVE, 1)], tag=f"vE{qu}.{tt}")
            qkv_state["sse_v"][qkv_state["vunit"] % 2] = sVE.inc()
            qkv_state["vunit"] += 1
        trans_done[qu] = sPE.n

    def emit_transposes(qu):
        hbase = (qu % 2) * MC * 1024
        for mc in range(MC):
            w = []
            if mc == 0:
                w.append((sHp[0], sHp[0].n))
                w.append((sHp[1], sHp[1].n))
                if qu >= 2:
                    w.append((sPE, trans_done[qu - 2]))
            pl.op("sync", lambda e, qu=qu, mc=mc, hbase=hbase: e.dma_start(
                out=h_ft[:, hbase + mc * 1024:hbase + (mc + 1) * 1024],
                in_=h_dram.ap()[qu * 1024:(qu + 1) * 1024, mc * 128:(mc + 1) * 128],
                transpose=True),
                wait=w, inc=[(sTT, 16)], tag=f"T{qu}.{mc}")
            sTT.inc(16)

    for ti in range(8):
        emit_ln_tile(ti)
    emit_transposes(0)
    for ti in range(8, 16):
        emit_ln_tile(ti)
    emit_qkv_quarter(0)
    emit_transposes(1)
    for ti in range(16, 24):
        emit_ln_tile(ti)
    emit_qkv_quarter(1)
    emit_transposes(2)
    for ti in range(24, 32):
        emit_ln_tile(ti)
    emit_qkv_quarter(2)
    emit_transposes(3)
    emit_qkv_quarter(3)

    if DEBUG:
        pl.op("sync", lambda e: e.dma_start(out=dbg["d_h"], in_=h_dram.ap()),
              wait=[(sHp[0], sHp[0].n), (sHp[1], sHp[1].n)], inc=[(sSTp[0], 16)])
        sSTp[0].inc(16)

    pl.barrier(sB)
    pl.op("gpsimd", lambda e: e.dma_start(out=sb["wo_sb"][:, :], in_=w_o),
          inc=[(sWo, 16)], tag="wo")
    sWo.inc(16)
    wo_ready = sWo.n
    for nm, srcw, dstw in (("wg", wg_t, wg_bf), ("wu", wu_t, wu_bf)):
        pl.op("gpsimd", lambda e, srcw=srcw, dstw=dstw: e.dma_start(out=dstw.ap(), in_=srcw),
              inc=[(sWc, 16)], tag=f"cast-{nm}")
        sWc.inc(16)
    ffn_cast_ready = sWc.n
    ptile = sb["ptile"]
    att = {"jglob": 0, "norm_done": [0, 0], "recip_done": [0, 0], "bcast_copy": 0,
           "exp_mark": {}, "pv_done": {}}

    if DEBUG:
        pl.op("sync", lambda e: e.dma_start(out=dbg["d_q"], in_=q_ft[:, :]),
              inc=[(sSTp[0], 16)])
        sSTp[0].inc(16)
        pl.op("sync", lambda e: e.dma_start(out=dbg["d_k"], in_=k_ft[:, :]),
              inc=[(sSTp[0], 16)])
        sSTp[0].inc(16)
        pl.op("sync", lambda e: e.dma_start(out=dbg["d_v"], in_=v_sb[:, :]),
              inc=[(sSTp[0], 16)])
        sSTp[0].inc(16)

    unit = 0
    head_norm = [0] * H
    att["pair_mark"] = [0, 0]
    att["recipb_done"] = [0, 0]
    tailA_q = []
    tailB_q = []
    SBANK = [(0, 0), (0, 512), (3, 0)]

    def emit_tailA(tu):
        (t_spp, t_opp, t_hd, t_q0, t_mpe) = tu
        pl.op("scalar", lambda e, t_spp=t_spp: e.activation(
            sb["recip_sb"][0:1, :], P[2][0:1, t_spp * 512:(t_spp + 1) * 512], AF.Ln),
            wait=[(sPE, t_mpe)], tag="aRL")
        pl.op("scalar", lambda e: e.activation(
            sb["recipb_sb"][0:1, :], sb["recip_sb"][0:1, :], AF.Exp,
            scale=np.float32(-1.0)),
            inc=[(sSE, 1)], tag="aRE")
        att["recipb_done"][t_spp] = sSE.inc()
        return att["recipb_done"][t_spp]

    def emit_tailB(tu, m_rb):
        (t_spp, t_opp, t_hd, t_q0, t_mpe) = tu
        wb = [(sSE, m_rb)]
        if att["bcast_copy"]:
            wb.append((sSE, att["bcast_copy"]))
        pl.op("tensor", lambda e: e.matmul(
            P[3][:, 512:1024], sb["ones_bfr"][0:1, :], sb["recipb_sb"][0:1, :],
            start=True, stop=True),
            wait=wb, inc=[(sPE, 1)], tag="aB")
        m_b = sPE.inc()
        wc_ = [(sPE, m_b)]
        if att["norm_done"][1 - t_opp]:
            wc_.append((sVE, att["norm_done"][1 - t_opp]))
        if att["norm_done"][t_opp]:
            wc_.append((sVE, att["norm_done"][t_opp]))
        pl.op("scalar", lambda e: e.copy(sb["bcast_sb"][:, :], P[3][:, 512:1024]),
              wait=wc_, inc=[(sSE, 1)], tag="aC")
        att["bcast_copy"] = sSE.inc()
        pl.op("vector", lambda e, t_hd=t_hd, t_q0=t_q0, t_opp=t_opp: e.tensor_mul(
            o_ft[:, t_hd * T + t_q0:t_hd * T + t_q0 + 512],
            P[1][:, t_opp * 512:(t_opp + 1) * 512],
            sb["bcast_sb"][:, :]),
            wait=[(sSE, att["bcast_copy"])], inc=[(sVE, 1)], tag="aN")
        att["norm_done"][t_opp] = sVE.inc()

    for hd in range(H):
        for qc in range(8):
            jmax = 4 * (qc + 1)
            q0 = 512 * qc
            opp = unit % 2
            spp = unit % 2
            p_ready = [None] * jmax
            pv_prev_last = att.get("pv_last", 0)
            for j in range(jmax):
                jg = att["jglob"] + j
                pt, pc = SBANK[jg % 3]
                w = []
                pe_prev = att["exp_mark"].get(jg - 3)
                if pe_prev:
                    w.append(pe_prev)
                pl.op("tensor", lambda e, pt=pt, pc=pc, j=j, hd=hd, q0=q0: e.matmul(
                    P[pt][:, pc:pc + 512],
                    k_ft[:, j * 128:(j + 1) * 128],
                    q_ft[:, hd * T + q0:hd * T + q0 + 512],
                    start=True, stop=True),
                    wait=w, inc=[(sPE, 1)], tag=f"aS{unit}.{j}")
                m_s = sPE.inc()
                wse = [(sPE, m_s)]
                if pv_prev_last and j == 0:
                    wse.append((sPE, pv_prev_last))
                pl.op("scalar", lambda e, j=j, pt=pt, pc=pc: e.activation(
                    ptile[:, j * 512:(j + 1) * 512],
                    P[pt][:, pc:pc + 512],
                    AF.Exp, scale=np.float32(QK2), bias=np.float32(-SHIFT)),
                    wait=wse, inc=[(sSE, 1)], tag=f"aE{unit}.{j}")
                m_e = sSE.inc()
                if j >= jmax - 4:
                    r = j - (jmax - 4)
                    pl.op("vector", lambda e, j=j, r=r: e.tensor_mul(
                        ptile[:, j * 512:(j + 1) * 512],
                        ptile[:, j * 512:(j + 1) * 512],
                        masks_sb[:, r * 512:(r + 1) * 512]),
                        wait=[(sSE, m_e)], inc=[(sVE, 1)], tag=f"aM{unit}.{j}")
                    p_ready[j] = (sVE, sVE.inc())
                else:
                    p_ready[j] = (sSE, m_e)
                att["exp_mark"][jg] = p_ready[j]
            if tailA_q:
                tu = tailA_q.pop()
                m_rb = emit_tailA(tu)
                tailB_q.append((tu, m_rb))
            npairs = jmax // 2
            for pr in range(npairs):
                j0, j1 = 2 * pr, 2 * pr + 1
                ppp = pr % 2
                wp = [p_ready[j0], p_ready[j1]]
                if att["pair_mark"][ppp]:
                    wp.append((sPE, att["pair_mark"][ppp]))
                pl.op("vector", lambda e, j0=j0, j1=j1, ppp=ppp: e.tensor_add(
                    sb["pairt"][:, ppp * 512:(ppp + 1) * 512],
                    ptile[:, j0 * 512:(j0 + 1) * 512],
                    ptile[:, j1 * 512:(j1 + 1) * 512]),
                    wait=wp, inc=[(sVE, 1)], tag=f"aA{unit}.{pr}")
                m_pair = sVE.inc()
                w2 = [(sVE, m_pair)]
                if pr == 0:
                    if att["recipb_done"][spp]:
                        w2.append((sSE, att["recipb_done"][spp]))
                    if att["norm_done"][opp]:
                        w2.append((sVE, att["norm_done"][opp]))
                pl.op("tensor", lambda e, spp=spp, ppp=ppp, pr=pr, npairs=npairs: e.matmul(
                    P[2][0:1, spp * 512:(spp + 1) * 512],
                    sb["ones_bf"][:, 0:1],
                    sb["pairt"][:, ppp * 512:(ppp + 1) * 512],
                    start=(pr == 0), stop=(pr == npairs - 1)),
                    wait=w2, tag=f"aU{unit}.{pr}")
                for j in (j0, j1):
                    pl.op("tensor", lambda e, opp=opp, j=j, jmax=jmax: e.matmul(
                        P[1][:, opp * 512:(opp + 1) * 512],
                        v_sb[:, j * 128:(j + 1) * 128],
                        ptile[:, j * 512:(j + 1) * 512],
                        start=(j == 0), stop=(j == jmax - 1)),
                        wait=([p_ready[j]] if j == j1 else []),
                        inc=([(sPE, 1)] if j % 2 == 1 else []),
                        tag=f"aP{unit}.{j}")
                att["pair_mark"][ppp] = sPE.inc()
            m_unit_pe = att["pair_mark"][(npairs - 1) % 2]
            att["pv_last"] = m_unit_pe
            if tailB_q:
                emit_tailB(*tailB_q.pop())
            tailA_q.append((spp, opp, hd, q0, m_unit_pe))
            att["jglob"] += jmax
            unit += 1
        while tailA_q or tailB_q:
            if tailB_q:
                emit_tailB(*tailB_q.pop())
            if tailA_q:
                tu = tailA_q.pop()
                m_rb = emit_tailA(tu)
                tailB_q.append((tu, m_rb))
        head_norm[hd] = sVE.n
    o_done = sVE.n

    if DEBUG:
        pl.op("sync", lambda e: e.dma_start(out=dbg["d_o"], in_=o_ft[:, :]),
              wait=[(sVE, o_done)], inc=[(sSTp[0], 16)])
        sSTp[0].inc(16)

    for hd in range(H):
        pl.op("sync", lambda e, hd=hd: e.dma_start(
            out=oc_dram[hd].ap(), in_=o_ft[:, hd * T:(hd + 1) * T]),
            wait=[(sVE, head_norm[hd])], inc=[(sC1h[hd], 16)], tag=f"oc{hd}")
        sC1h[hd].inc(16)
        pl.op("gpsimd", lambda e, hd=hd: e.collective_compute(
            "AllGather", ALU.bypass,
            replica_groups=[[0, 1, 2, 3], [4, 5, 6, 7]],
            ins=[oc_dram[hd].ap()], outs=[og_dram[hd].ap()]),
            wait=[(sC1h[hd], 16)], inc=[(sCCh[hd], 1)], tag=f"allgather{hd}")
        sCCh[hd].inc(1)

    for hd in range(H):
        pl.op("sync", lambda e, hd=hd: e.dma_start(
            out=sb["o16_sb"][:, hd * 1024:(hd + 1) * 1024],
            in_=oc_dram[hd].ap()[:, bass.ds(pl.sl1024["sync"], 1024)]),
            wait=[(sC1h[3], 16)], inc=[(sOwnG, 16)], tag=f"gown{hd}")
        sOwnG.inc(16)
    own_ready = sOwnG.n
    for hd in range(H):
        for grot in range(1, G):
            pl.op("sync", lambda e, hd=hd, grot=grot: e.dma_start(
                out=sb["o16_sb"][:, (grot * H + hd) * 1024:(grot * H + hd + 1) * 1024],
                in_=og_dram[hd].ap().rearrange("(g d) t -> d g t", d=128)[
                    :, bass.ds((pl.pid["sync"] + grot) % 4, 1),
                    bass.ds(pl.sl1024["sync"], 1024)].squeeze(1)),
                wait=[(sCCh[hd], 1)], inc=[(sG, 16)], tag=f"g{hd}.{grot}")
            sG.inc(16)

    p4 = {"psum_free": [0, 0], "h2dma": [0, 0], "xmdma": [0, 0], "xr_used": [0, 0]}
    for ot in range(OTN):
        pp = ot % 2
        PA, PB = (P[0], P[1]) if pp == 0 else (P[2], P[3])
        wxr = [(sVE, p4["xr_used"][pp])] if p4["xr_used"][pp] else []
        pl.op("sync", lambda e, ot=ot, pp=pp: e.dma_start(
            out=sb["xr"][:, pp * 2048:(pp + 1) * 2048],
            in_=x_full[bass.ds(pl.sl1024["sync"] + ot * 128, 128), :]),
            wait=wxr, inc=[(sXRp[pp], 16)], tag=f"xr{ot}")
        m_xr = sXRp[pp].inc(16)
        w = [(sOwnG, own_ready), (sWo, wo_ready)]
        if p4["psum_free"][pp]:
            w.append((sVE, p4["psum_free"][pp]))
        for gh in range(G * H):
            lhsT = sb["o16_sb"][:, gh * 1024 + ot * 128:gh * 1024 + (ot + 1) * 128]
            ww = w if (gh == 0) else ([(sG, sG.n)] if gh == H else [])
            for k in range(4):
                tgt = PA if k < 2 else PB
                pl.op("tensor", lambda e, lhsT=lhsT, gh=gh, k=k, tgt=tgt: e.matmul(
                    tgt[:, (k % 2) * 512:(k % 2 + 1) * 512], lhsT,
                    sb["wo_sb"][:, gh * M + k * 512:gh * M + (k + 1) * 512],
                    start=(gh == 0), stop=(gh == G * H - 1)),
                    wait=(ww if k == 0 else []),
                    inc=([(sPE, 1)] if (gh == G * H - 1 and k == 3) else []),
                    tag=f"oM{ot}.{gh}.{k}")
        m_pe = sPE.inc()
        sl = pp * 16
        sxa, sxb = stats[:, sl:sl + 1], stats[:, sl + 1:sl + 2]
        sxxa, sxxb = stats[:, sl + 2:sl + 3], stats[:, sl + 3:sl + 4]
        v0 = stats[:, sl + 4:sl + 5]
        negm = stats[:, sl + 5:sl + 6]
        rstd = stats[:, sl + 6:sl + 7]
        pl.op("scalar", lambda e, PA=PA, sxa=sxa: e.activation(
            dump[:, 0:1024], PA[:, :], AF.Copy, accum_out=sxa),
            wait=[(sPE, m_pe)], tag=f"rA{ot}")
        pl.op("scalar", lambda e, PB=PB, sxb=sxb: e.activation(
            dump[:, 0:1024], PB[:, :], AF.Copy, accum_out=sxb))
        pl.op("scalar", lambda e, PA=PA, sxxa=sxxa: e.activation(
            dump[:, 0:1024], PA[:, :], AF.Square, accum_out=sxxa))
        pl.op("scalar", lambda e, PB=PB, sxxb=sxxb: e.activation(
            dump[:, 0:1024], PB[:, :], AF.Square, accum_out=sxxb),
            inc=[(sSE, 1)], tag=f"rS{ot}")
        m_se = sSE.inc()
        pl.op("vector", lambda e, sxa=sxa, sxb=sxb: e.tensor_add(sxa, sxa, sxb),
              wait=[(sSE, m_se)])
        pl.op("vector", lambda e, sxxa=sxxa, sxxb=sxxb: e.tensor_add(sxxa, sxxa, sxxb))
        pl.op("vector", lambda e, sxa=sxa, v0=v0: e.tensor_mul(v0, sxa, sxa))
        pl.op("vector", lambda e, v0=v0, sxxa=sxxa: e.scalar_tensor_tensor(
            v0, v0, -1.0 / 2048, sxxa, ALU.mult, ALU.add))
        pl.op("vector", lambda e, sxa=sxa, negm=negm: e.tensor_scalar(
            negm, sxa, -1.0 / 2048, None, ALU.mult), inc=[(sVE, 1)], tag=f"rV{ot}")
        m_ve = sVE.inc()
        pl.op("scalar", lambda e, v0=v0: e.activation(v0, v0, AF.Ln,
                                                      scale=np.float32(1.0 / 2048), bias=np.float32(EPS)),
              wait=[(sVE, m_ve)])
        pl.op("scalar", lambda e, v0=v0, rstd=rstd: e.activation(
            rstd, v0, AF.Exp, scale=np.float32(-0.5)), inc=[(sSE, 1)], tag=f"rR{ot}")
        m_se2 = sSE.inc()
        lnta = sb["ln_t"][:, pp * 2048:pp * 2048 + 1024]
        lntb = sb["ln_t"][:, pp * 2048 + 1024:(pp + 1) * 2048]
        pl.op("vector", lambda e, PA=PA, lnta=lnta, negm=negm, rstd=rstd: e.tensor_scalar(
            lnta, PA[:, :], negm, rstd, ALU.add, ALU.mult),
            wait=[(sSE, m_se2)], tag=f"rlA{ot}")
        pl.op("vector", lambda e, PB=PB, lntb=lntb, negm=negm, rstd=rstd: e.tensor_scalar(
            lntb, PB[:, :], negm, rstd, ALU.add, ALU.mult),
            inc=[(sVE, 1)], tag=f"rlB{ot}")
        p4["psum_free"][pp] = sVE.inc()
        sx2 = stats[:, sl + 7:sl + 8]
        xma = sb["xmid_t"][:, pp * 2048:(pp + 1) * 2048]
        lnt = sb["ln_t"][:, pp * 2048:(pp + 1) * 2048]
        xra = sb["xr"][:, pp * 2048:(pp + 1) * 2048]
        wxm = [(sXRp[pp], m_xr)]
        if p4["xmdma"][pp]:
            wxm.append((sXMp[pp], p4["xmdma"][pp]))
        pl.op("vector", lambda e, xma=xma, lnt=lnt, xra=xra, sx2=sx2:
              e.scalar_tensor_tensor(xma, lnt, 1.0, xra, ALU.mult, ALU.add,
                                     accum_out=sx2),
              wait=wxm, inc=[(sVE, 1)], tag=f"xm{ot}")
        p4["xr_used"][pp] = sVE.inc()
        m_xmid = p4["xr_used"][pp]
        sxx2 = stats[:, sl + 8:sl + 9]
        v02 = stats[:, sl + 9:sl + 10]
        negm2 = stats[:, sl + 10:sl + 11]
        rstd2 = stats[:, sl + 11:sl + 12]
        pl.op("scalar", lambda e, xma=xma, sxx2=sxx2: e.activation(
            dump[:, :], xma, AF.Square, accum_out=sxx2),
            wait=[(sVE, m_xmid)], inc=[(sSE, 1)], tag=f"l2S{ot}")
        m_se3 = sSE.inc()
        pl.op("vector", lambda e, sx2=sx2, v02=v02: e.tensor_mul(v02, sx2, sx2),
              wait=[(sSE, m_se3)])
        pl.op("vector", lambda e, v02=v02, sxx2=sxx2: e.scalar_tensor_tensor(
            v02, v02, -1.0 / 2048, sxx2, ALU.mult, ALU.add))
        pl.op("vector", lambda e, sx2=sx2, negm2=negm2: e.tensor_scalar(
            negm2, sx2, -1.0 / 2048, None, ALU.mult), inc=[(sVE, 1)], tag=f"l2V{ot}")
        m_ve2 = sVE.inc()
        pl.op("scalar", lambda e, v02=v02: e.activation(v02, v02, AF.Ln,
                                                        scale=np.float32(1.0 / 2048), bias=np.float32(EPS)),
              wait=[(sVE, m_ve2)])
        pl.op("scalar", lambda e, v02=v02, rstd2=rstd2: e.activation(
            rstd2, v02, AF.Exp, scale=np.float32(-0.5)), inc=[(sSE, 1)], tag=f"l2R{ot}")
        m_se4 = sSE.inc()
        h2a = sb["h2t"][:, pp * 2048:(pp + 1) * 2048]
        wh2 = [(sSE, m_se4)]
        if p4["h2dma"][pp]:
            wh2.append((sXMp[pp], p4["h2dma"][pp]))
        pl.op("vector", lambda e, h2a=h2a, xma=xma, negm2=negm2, rstd2=rstd2:
              e.tensor_scalar(h2a, xma, negm2, rstd2, ALU.add, ALU.mult),
              wait=wh2, inc=[(sVE, 1)], tag=f"h2{ot}")
        m_h2 = sVE.inc()
        pl.op("sync", lambda e, ot=ot, xma=xma: e.dma_start(
            out=xmid_dram.ap()[ot * 128:(ot + 1) * 128, :], in_=xma),
            wait=[(sVE, m_h2)], inc=[(sXMp[pp], 16)], tag=f"sxm{ot}")
        p4["xmdma"][pp] = sXMp[pp].inc(16)
        pl.op("sync", lambda e, ot=ot, h2a=h2a: e.dma_start(
            out=h2_dram.ap()[ot * 128:(ot + 1) * 128, :], in_=h2a),
            inc=[(sXMp[pp], 16)], tag=f"sh2{ot}")
        p4["h2dma"][pp] = sXMp[pp].inc(16)
    xm_all_w = [(sXMp[0], sXMp[0].n), (sXMp[1], sXMp[1].n)]

    if DEBUG:
        pl.op("sync", lambda e: e.dma_start(out=dbg["d_xmid"], in_=xmid_dram.ap()),
              wait=xm_all_w, inc=[(sSTp[0], 16)])
        sSTp[0].inc(16)

    for mc in range(MC):
        pl.op("sync", lambda e, mc=mc: e.dma_start(
            out=sb["h2_ft"][:, mc * 1024:(mc + 1) * 1024],
            in_=h2_dram.ap()[:, mc * 128:(mc + 1) * 128], transpose=True),
            wait=(xm_all_w if mc == 0 else []), inc=[(sTT, 16)], tag=f"T2.{mc}")
        sTT.inc(16)
    h2ft_ready = sTT.n

    if DEBUG:
        pl.op("sync", lambda e: e.dma_start(out=dbg["d_h2ft"], in_=sb["h2_ft"][:, :]),
              wait=[(sTT, h2ft_ready)], inc=[(sSTp[0], 16)])
        sSTp[0].inc(16)

    pl.barrier(sB)
    ffn = {"gu_pe": {}, "gu_used": [0, 0], "silu_prev": 0,
           "r2_evac": [0, 0], "down_pe_all": 0}
    load_marks = {}

    def load_gu(which, slab, u):
        gidx = slab * 8 + u
        slot = gidx % 2
        h0 = slab * 2048 + u * 256
        dst = sb["wgu"] if which == "wg" else sb["wuu"]
        srcw = wg_bf if which == "wg" else wu_bf
        sem = sFg[slot] if which == "wg" else sFu[slot]
        wk_ = []
        if gidx >= 2 and (gidx - 2) in ffn["gu_pe"]:
            wk_.append((sPE, ffn["gu_pe"][gidx - 2]))
        if gidx == 0:
            wk_.append((sWc, ffn_cast_ready))
        pl.op("sync", lambda e, dst=dst, srcw=srcw, h0=h0, slot=slot: e.dma_start(
            out=dst[:, slot * MC * 256:(slot + 1) * MC * 256].rearrange(
                "p (mc h) -> p mc h", mc=MC),
            in_=srcw.ap().rearrange("p (mc h) -> p mc h", mc=MC)[:, :, h0:h0 + 256]),
            wait=wk_, inc=[(sem, 16)], tag=f"u{which}{slab}.{u}")
        return (sem, sem.inc(16))

    def load_wd_half(slab, hs):
        marks = []
        for ul in range(4):
            h0 = slab * 2048 + hs * 1024 + ul * 256
            wk_ = []
            if ul == 0:
                if ffn["down_pe_all"]:
                    wk_.append((sPE, ffn["down_pe_all"]))
            hc0 = h0 // 128
            pl.op("gpsimd", lambda e, hc0=hc0, ul=ul: e.dma_start(
                out=sb["wdu"][:, ul * 2 * 2048:(ul + 1) * 2 * 2048].rearrange(
                    "p (hs2 m) -> p hs2 m", hs2=2),
                in_=wd_t.rearrange("p (hc m) -> p hc m", hc=64)[:, hc0:hc0 + 2, :]),
                wait=wk_, inc=[(sFdu[ul], 16)], tag=f"uwd{slab}.{hs}.{ul}")
            marks.append((sFdu[ul], sFdu[ul].inc(16)))
        return marks

    def emit_gu_load(gidx):
        if gidx > 31:
            return
        sl_, u_ = gidx // 8, gidx % 8
        load_marks[("wg", sl_, u_)] = load_gu("wg", sl_, u_)
        load_marks[("wu", sl_, u_)] = load_gu("wu", sl_, u_)

    p6_ve = [0, 0]
    p6_gate = {}

    def emit_finale(ot, r2mark):
        pp = ot % 2
        xra = sb["xr6"][:, pp * 2048:(pp + 1) * 2048]
        wxr = list(xm_all_w)
        wxr.append((sPE, ffn["gu_pe"][31]))
        wxr.append((sVE, p6_gate["g_ready3"]))
        if p6_ve[pp]:
            wxr.append((sVE, p6_ve[pp]))
        pl.op("sync", lambda e, ot=ot, xra=xra: e.dma_start(
            out=xra, in_=xmid_dram.ap()[ot * 128:(ot + 1) * 128, :]),
            wait=wxr, inc=[(sXRp[pp], 16)], tag=f"fx{ot}")
        m_xr = sXRp[pp].inc(16)
        sl = pp * 16
        sx = stats[:, sl:sl + 1]
        sxx = stats[:, sl + 1:sl + 2]
        v0 = stats[:, sl + 2:sl + 3]
        negm = stats[:, sl + 3:sl + 4]
        rstd = stats[:, sl + 4:sl + 5]
        r2t = sb["r2"][:, ot * 2048:(ot + 1) * 2048]
        pl.op("scalar", lambda e, r2t=r2t, sx=sx: e.activation(
            dump[:, :], r2t, AF.Copy, accum_out=sx),
            wait=[(sVE, r2mark)], tag=f"fS{ot}")
        pl.op("scalar", lambda e, r2t=r2t, sxx=sxx: e.activation(
            dump[:, :], r2t, AF.Square, accum_out=sxx), inc=[(sSE, 1)], tag=f"fQ{ot}")
        m_se = sSE.inc()
        pl.op("vector", lambda e, sx=sx, v0=v0: e.tensor_mul(v0, sx, sx),
              wait=[(sSE, m_se)])
        pl.op("vector", lambda e, v0=v0, sxx=sxx: e.scalar_tensor_tensor(
            v0, v0, -1.0 / 2048, sxx, ALU.mult, ALU.add))
        pl.op("vector", lambda e, sx=sx, negm=negm: e.tensor_scalar(
            negm, sx, -1.0 / 2048, None, ALU.mult), inc=[(sVE, 1)], tag=f"fV{ot}")
        m_ve = sVE.inc()
        pl.op("scalar", lambda e, v0=v0: e.activation(v0, v0, AF.Ln,
                                                      scale=np.float32(1.0 / 2048),
                                                      bias=np.float32(EPS)),
              wait=[(sVE, m_ve)])
        pl.op("scalar", lambda e, v0=v0, rstd=rstd: e.activation(
            rstd, v0, AF.Exp, scale=np.float32(-0.5)), inc=[(sSE, 1)], tag=f"fR{ot}")
        m_se2 = sSE.inc()
        pl.op("vector", lambda e, r2t=r2t, negm=negm, rstd=rstd: e.tensor_scalar(
            r2t, r2t, negm, rstd, ALU.add, ALU.mult),
            wait=[(sSE, m_se2)], tag=f"fL{ot}")
        pl.op("vector", lambda e, r2t=r2t, xra=xra: e.scalar_tensor_tensor(
            r2t, r2t, 1.0, xra, ALU.mult, ALU.add),
            wait=[(sXRp[pp], m_xr)], inc=[(sVE, 1)], tag=f"fA{ot}")
        p6_ve[pp] = sVE.inc()
        pl.op("sync", lambda e, ot=ot, r2t=r2t: e.dma_start(
            out=out_own[ot * 128:(ot + 1) * 128, :], in_=r2t),
            wait=[(sVE, p6_ve[pp])], inc=[(sSTp[pp], 16)], tag=f"st{ot}")
        sSTp[pp].inc(16)

    emit_gu_load(0)
    emit_gu_load(1)
    for slab in range(4):
        wd_mark = load_wd_half(slab, 0)
        for hc in range(16):
            u, sub = hc // 2, hc % 2
            gp = P[0] if hc % 2 == 0 else P[2]
            up = P[1] if hc % 2 == 0 else P[3]
            gidx = slab * 8 + u
            slot = gidx % 2
            w = [load_marks[("wg", slab, u)], load_marks[("wu", slab, u)]]
            if slab == 0 and hc == 0:
                w.append((sTT, h2ft_ready))
            prev = ffn["gu_used"][hc % 2]
            if prev:
                w.append((sVE, prev))
            if slab > 0 and hc < 2:
                if ffn["r2_evac"][0]:
                    w.append((sVE, ffn["r2_evac"][0]))
                if ffn["r2_evac"][1]:
                    w.append((sVE, ffn["r2_evac"][1]))
            for mc in range(MC):
                base = slot * MC * 256 + mc * 256 + sub * 128
                lg = sb["wgu"][:, base:base + 128]
                lu = sb["wuu"][:, base:base + 128]
                for half in range(2):
                    pl.op("tensor", lambda e, lg=lg, gp=gp, mc=mc, half=half: e.matmul(
                        gp[:, half * 512:(half + 1) * 512], lg,
                        sb["h2_ft"][:, mc * 1024 + half * 512:mc * 1024 + (half + 1) * 512],
                        start=(mc == 0), stop=(mc == MC - 1)),
                        wait=(w if (mc == 0 and half == 0) else []),
                        inc=([(sPE, 1)] if (mc == MC - 1 and half == 1) else []),
                        tag=f"gM{slab}.{hc}")
                for half in range(2):
                    pl.op("tensor", lambda e, lu=lu, up=up, mc=mc, half=half: e.matmul(
                        up[:, half * 512:(half + 1) * 512], lu,
                        sb["h2_ft"][:, mc * 1024 + half * 512:mc * 1024 + (half + 1) * 512],
                        start=(mc == 0), stop=(mc == MC - 1)),
                        inc=([(sPE, 1)] if (mc == MC - 1 and half == 1) else []),
                        tag=f"uM{slab}.{hc}")
            m_gate = sPE.inc()
            m_up = sPE.inc()
            if sub == 1:
                ffn["gu_pe"][gidx] = m_up
                emit_gu_load(gidx + 2)
            wse = [(sPE, m_gate)]
            if ffn["silu_prev"]:
                wse.append((sVE, ffn["silu_prev"]))
            pl.op("scalar", lambda e, gp=gp: e.activation(
                sb["silu_t"][:, :], gp[:, :], AF.Silu),
                wait=wse, inc=[(sSE, 1)], tag=f"si{slab}.{hc}")
            m_si = sSE.inc()
            pl.op("vector", lambda e, up=up, hc=hc: e.tensor_mul(
                sb["g_slab"][:, hc * 1024:(hc + 1) * 1024], up[:, :],
                sb["silu_t"][:, :]),
                wait=[(sSE, m_si), (sPE, m_up)], inc=[(sVE, 1)], tag=f"gm{slab}.{hc}")
            ffn["gu_used"][hc % 2] = sVE.inc()
            ffn["silu_prev"] = ffn["gu_used"][hc % 2]
        g_ready = sVE.n
        if slab == 3:
            p6_gate["g_ready3"] = g_ready
        if DEBUG and slab == 0:
            pl.op("sync", lambda e: e.dma_start(out=dbg["d_g"], in_=sb["g_slab"][:, :]),
                  wait=[(sVE, g_ready)], inc=[(sSTp[0], 16)])
            sSTp[0].inc(16)
        for hs in range(2):
            if hs == 1:
                wd_mark = load_wd_half(slab, 1)
            for ot in range(OTN):
                pp = ot % 2
                PA, PB = (P[0], P[1]) if pp == 0 else (P[2], P[3])
                w = [(sVE, g_ready)]
                if ffn["r2_evac"][pp]:
                    w.append((sVE, ffn["r2_evac"][pp]))
                for hl in range(8):
                    hc = hs * 8 + hl
                    ul, sub = hl // 2, hl % 2
                    lg = sb["g_slab"][:, hc * 1024 + ot * 128:hc * 1024 + (ot + 1) * 128]
                    for k in range(4):
                        tgt = PA if k < 2 else PB
                        pl.op("tensor",
                              lambda e, lg=lg, tgt=tgt, k=k, ul=ul, sub=sub, hl=hl: e.matmul(
                                  tgt[:, (k % 2) * 512:(k % 2 + 1) * 512], lg,
                                  sb["wdu"][:, ul * 2 * 2048 + sub * 2048 + k * 512:
                                            ul * 2 * 2048 + sub * 2048 + (k + 1) * 512],
                                  start=(hl == 0), stop=(hl == 7)),
                              wait=((w + [wd_mark[0]]) if (hl == 0 and k == 0) else
                                    ([wd_mark[ul]] if (sub == 0 and k == 0) else [])),
                              inc=([(sPE, 1)] if (hl == 7 and k == 3) else []),
                              tag=f"dM{slab}.{hs}.{ot}.{hl}.{k}")
                m_pe = sPE.inc()
                r2a = sb["r2"][:, ot * 2048:ot * 2048 + 1024]
                r2b = sb["r2"][:, ot * 2048 + 1024:(ot + 1) * 2048]
                first = (slab == 0 and hs == 0)
                if first:
                    pl.op("vector", lambda e, r2a=r2a, PA=PA: e.tensor_copy(r2a, PA[:, :]),
                          wait=[(sPE, m_pe)], tag=f"rc{slab}.{hs}.{ot}")
                    pl.op("vector", lambda e, r2b=r2b, PB=PB: e.tensor_copy(r2b, PB[:, :]),
                          inc=[(sVE, 1)], tag=f"rcB{slab}.{hs}.{ot}")
                else:
                    pl.op("vector", lambda e, r2a=r2a, PA=PA: e.tensor_add(r2a, r2a, PA[:, :]),
                          wait=[(sPE, m_pe)], tag=f"ra{slab}.{hs}.{ot}")
                    pl.op("vector", lambda e, r2b=r2b, PB=PB: e.tensor_add(r2b, r2b, PB[:, :]),
                          inc=[(sVE, 1)], tag=f"raB{slab}.{hs}.{ot}")
                ffn["r2_evac"][pp] = sVE.inc()
                if slab == 3 and hs == 1:
                    emit_finale(ot, ffn["r2_evac"][pp])
            ffn["down_pe_all"] = sPE.n
    r2_all = sVE.n

    if DEBUG:
        pl.op("sync", lambda e: e.dma_start(out=dbg["d_r2"], in_=sb["r2"][:, :]),
              wait=[(sVE, r2_all)], inc=[(sSTp[0], 16)])
        sSTp[0].inc(16)

    pl.op("sync", None, wait=[(sSTp[0], sSTp[0].n), (sSTp[1], sSTp[1].n)],
          tag="final-wait")

    pl.simulate()
    pl.replay()
    return nc


_CACHE = {}


def kernel(L0_attn_x, L0_attn_w_aq, L0_attn_w_ak, L0_attn_w_av, L0_attn_w_ao,
           L0_ffn_w_gate, L0_ffn_w_up, L0_ffn_w_down):
    x = np.ascontiguousarray(np.asarray(L0_attn_x, dtype=np.float32))
    waq = np.asarray(L0_attn_w_aq, np.float32)
    wak = np.asarray(L0_attn_w_ak, np.float32)
    wav = np.asarray(L0_attn_w_av, np.float32)
    wq_h = [np.ascontiguousarray(
        waq[g].reshape(H, MC, 128, D).transpose(2, 0, 1, 3).reshape(128, H * MC * D))
        for g in range(G)]
    wk_h = [np.ascontiguousarray(
        wak[g].reshape(MC, 128, D).transpose(1, 0, 2).reshape(128, MC * D))
        for g in range(G)]
    wv_h = [np.ascontiguousarray(
        wav[g].reshape(MC, 128, D).transpose(1, 0, 2).reshape(128, MC * D))
        for g in range(G)]
    wao3 = np.asarray(L0_attn_w_ao, np.float32).reshape(G * H, D, M)
    wo_h = []
    for g0 in range(G):
        order = [((g0 + k) % G) * H + h for k in range(G) for h in range(H)]
        wo_h.append(np.ascontiguousarray(
            wao3[order].transpose(1, 0, 2).reshape(128, G * H * M)))
    wgt = np.ascontiguousarray(
        np.asarray(L0_ffn_w_gate, np.float32).T.reshape(MC, 128, HID)
        .transpose(1, 0, 2).reshape(128, MC * HID))
    wut = np.ascontiguousarray(
        np.asarray(L0_ffn_w_up, np.float32).T.reshape(MC, 128, HID)
        .transpose(1, 0, 2).reshape(128, MC * HID))
    wdt = np.ascontiguousarray(
        np.asarray(L0_ffn_w_down, np.float32).T.reshape(64, 128, M)
        .transpose(1, 0, 2).reshape(128, 64 * M))

    pos = np.arange(T, dtype=np.float64)
    freqs = 10000.0 ** (-np.arange(64, dtype=np.float64) / 64.0)
    ang = np.outer(freqs, pos)
    rope1 = np.concatenate([np.cos(ang), np.sin(ang)], axis=0).astype(BFH)
    rope2 = np.concatenate([np.sin(ang), np.cos(ang)], axis=0).astype(BFH)
    masks = np.zeros((4, 128, 512), np.float32)
    jj = np.arange(128)[:, None]
    ii = np.arange(512)[None, :]
    for r in range(4):
        masks[r] = ((jj + 128 * r) <= ii).astype(np.float32)
    masks = masks.astype(BFH)

    if "nc" not in _CACHE:
        _CACHE["nc"] = build_program()
    nc = _CACHE["nc"]

    in_maps = []
    for r in range(8):
        b, g = r // 4, r % 4
        in_maps.append({
            "x_full": x[b],
            "w_q": wq_h[g],
            "w_k": wk_h[g],
            "w_v": wv_h[g],
            "w_o": wo_h[g],
            "wg_t": wgt,
            "wu_t": wut,
            "wd_t": wdt,
            "rope1": np.asarray(rope1),
            "rope2": np.asarray(rope2),
            "masks": np.asarray(masks),
        })
    res = run_bass_kernel_spmd(nc, in_maps, core_ids=list(range(8)))
    out = np.empty((B, T, M), np.float32)
    for r in range(8):
        b, s = r // 4, r % 4
        out[b, s * 1024:(s + 1) * 1024, :] = np.asarray(res.results[r]["out_own"])
    if DEBUG:
        kernel._last = res
    return out
